# version 04
# baseline (speedup 1.0000x reference)
import contextlib
import numpy as np

import concourse.bass as bass
import concourse.mybir as mybir
from concourse.bass_utils import run_bass_kernel_spmd
from concourse.bass_core_rust import IndirectOffsetOnAxis

F32 = mybir.dt.float32
BF16 = mybir.dt.bfloat16
I32 = mybir.dt.int32
AF = mybir.ActivationFunctionType
ALU = mybir.AluOpType
AX = mybir.AxisListType

D = 1024
T = 4096
NC = 8
CH = 512
NCHUNK = T // CH
NDT = D // 128
NKT = T // 128
NH = 16
HD = 64
VW = 65 * NH


class P:
    ENGS = ("tensor", "vector", "scalar", "gpsimd", "sync")

    def __init__(self, nc, ctx):
        self.nc = nc
        self.steps = []
        self.clk = {e: ctx.enter_context(nc.semaphore(f"clk_{e}")) for e in self.ENGS}
        self.dsem = {e: ctx.enter_context(nc.semaphore(f"dma_{e}")) for e in self.ENGS}
        self.clk_val = {e: 0 for e in self.ENGS}
        self.dma_val = {e: 0 for e in self.ENGS}
        self.waited = {e: {} for e in self.ENGS}
        self._emitting = None
        self._dma_seen = 0
        self._fence_deps = {}

    def fence(self):
        refs = ([(self.clk[e], self.clk_val[e]) for e in self.ENGS if self.clk_val[e]]
                + [(self.dsem[e], self.dma_val[e]) for e in self.ENGS if self.dma_val[e]])
        for e in self.ENGS:
            self._fence_deps[e] = refs

    def _mk_waits(self, eng, deps):
        waits = []
        for d in deps:
            if d is None:
                continue
            sem, val = d
            if val <= self.waited[eng].get(id(sem), 0):
                continue
            self.waited[eng][id(sem)] = val
            waits.append((sem, val))
        return waits

    def step(self, eng, fn, deps=(), n_dma=0, tag=""):
        deps = list(deps) + self._fence_deps.pop(eng, [])
        waits = self._mk_waits(eng, deps)
        if n_dma > 0:
            self.dma_val[eng] += 16 * n_dma
            ref = (self.dsem[eng], self.dma_val[eng])
        else:
            self.clk_val[eng] += 1
            ref = (self.clk[eng], self.clk_val[eng])
        self.steps.append(dict(eng=eng, fn=fn, waits=waits, n_dma=n_dma, tag=tag))
        return ref

    def wait_only(self, eng, deps, tag=""):
        deps = list(deps) + self._fence_deps.pop(eng, [])
        waits = self._mk_waits(eng, deps)
        self.steps.append(dict(eng=eng, fn=None, waits=waits, n_dma=0, tag=tag))

    def dma(self, e, **kw):
        ins = e.dma_start(**kw)
        ins.then_inc(self.dsem[self._emitting], 16)
        self._dma_seen += 1
        return ins

    def dma_t(self, e, out, in_):
        ins = e.dma_start_transpose(out=out, in_=in_)
        ins.then_inc(self.dsem[self._emitting], 16)
        self._dma_seen += 1
        return ins

    def emit(self, block):
        def make(eng_name):
            def run(e):
                for s in self.steps:
                    if s["eng"] != eng_name:
                        continue
                    for sem, val in s["waits"]:
                        e.wait_ge(sem, val)
                    if s["fn"] is None:
                        continue
                    self._emitting = eng_name
                    self._dma_seen = 0
                    last = s["fn"](e, self)
                    assert self._dma_seen == s["n_dma"], (
                        f"step {s['tag']}: declared {s['n_dma']} dma, emitted {self._dma_seen}")
                    if s["n_dma"] == 0:
                        assert last is not None, f"step {s['tag']} returned no instruction"
                        last.then_inc(self.clk[eng_name], 1)
            return run
        block.tensor(make("tensor"))
        block.vector(make("vector"))
        block.scalar(make("scalar"))
        block.gpsimd(make("gpsimd"))
        block.sync(make("sync"))


def build_l0_attn(upto="C", dbg=False, fctx=None):
    nc = fctx["nc"] if fctx else bass.Bass("TRN2", target_bir_lowering=False, debug=False, detect_race_conditions=False)
    xT = fctx["xin"] if fctx else nc.dram_tensor("xT", [D, T], F32, kind="ExternalInput").ap()
    Wk = nc.dram_tensor("Wk", [D, D], F32, kind="ExternalInput").ap()
    Wq = nc.dram_tensor("Wq", [D, D], F32, kind="ExternalInput").ap()
    Wo = nc.dram_tensor("Wo", [D, D], F32, kind="ExternalInput").ap()
    bo_t = nc.dram_tensor("bo_t", [128, NDT], F32, kind="ExternalInput").ap()
    ident = nc.dram_tensor("ident", [128, 128], BF16, kind="ExternalInput").ap()
    x1T = fctx["xout"] if fctx else nc.dram_tensor("x1T", [D, T], F32, kind="ExternalOutput").ap()
    xn_h = nc.dram_tensor("xn_h", [NCHUNK * D, CH], BF16).ap()
    qT_h = nc.dram_tensor("qT_h", [NCHUNK * D, CH], BF16).ap()
    ao_h = nc.dram_tensor("ao_h", [T, D], BF16, kind="ExternalOutput" if dbg else "Internal").ap()
    st_dbg = nc.dram_tensor("st_dbg", [128, 2048], F32, kind="ExternalOutput" if dbg else "Internal").ap()
    xn_dbg = nc.dram_tensor("xn_dbg", [NCHUNK * D, CH], BF16, kind="ExternalOutput" if dbg else "Internal").ap()

    with contextlib.ExitStack() as ctx:
        pref = fctx["tag"] if fctx else ""
        sb = lambda name, p_, f, dt=BF16: ctx.enter_context(nc.sbuf_tensor(pref + name, [p_, f], dt))
        c_ident = sb("c_ident", 128, 96)
        c_ones = sb("c_ones", 128, 1)
        c_onesrf = sb("c_onesrf", 1, 128, F32)
        wk_s = [sb(f"wk{i}", 128, D) for i in range(NDT)]
        wq_s = [sb(f"wq{i}", 128, D) for i in range(NDT)]
        bo_s = sb("bo_s", 128, NDT, F32)
        kT = [sb(f"kT{i}", 128, T) for i in range(NDT)]
        vbuf = sb("vbuf", 128, NKT * VW)
        sxb = sb("sxb", 128, NDT * CH)
        sqp = [sb(f"sqp{i}", 128, CH) for i in range(2)]
        xnb = sb("xnb", 128, NDT * CH)
        tmpf = sb("tmpf", 128, CH, F32)
        mrow = sb("mrow", 1, CH, F32)
        rrow = sb("rrow", 1, 2 * CH, F32)
        qcb = sb("qcb", 128, NDT * CH)
        ptb = [sb(f"ptb{i}", 128, 1024) for i in range(4)]
        anb = sxb
        recb = sb("recb", 128, 16, F32)
        resf = tmpf

        P0 = ctx.enter_context(nc.psum_tensor(pref + "P0", [128, 2048], F32))
        P2 = ctx.enter_context(nc.psum_tensor(pref + "P2", [128, 512], F32))
        P3 = ctx.enter_context(nc.psum_tensor(pref + "P3", [128, 512], F32))
        P4 = ctx.enter_context(nc.psum_tensor(pref + "P4", [128, 512], F32))
        P5 = ctx.enter_context(nc.psum_tensor(pref + "P5", [128, 512], F32))

        p = fctx["p"] if fctx else P(nc, ctx)

        def ld_w(e, p_):
            p_.dma(e, out=bo_s[:, :], in_=bo_t)
            p_.dma(e, out=c_ident[:, 0:64], in_=ident[:, 0:64])
            for i in range(NDT):
                p_.dma(e, out=wk_s[i][:, :], in_=Wk[i * 128:(i + 1) * 128, :])
                p_.dma(e, out=wq_s[i][:, :], in_=Wq[i * 128:(i + 1) * 128, :])
        s_wld = p.step("gpsimd", ld_w, n_dma=2 + 2 * NDT, tag="wld")

        def consts(e, p_):
            e.memset(c_ones[:, :], 1.0)
            e.memset(c_onesrf[0:1, :], 1.0)
            return e.memset(vbuf[:, :], 1.0)
        s_const = p.step("vector", consts, tag="consts")

        def init_ptb(e, p_):
            e.memset(ptb[0][:, :], 0.0)
            e.memset(ptb[1][:, :], 0.0)
            e.memset(ptb[2][:, :], 0.0)
            return e.memset(ptb[3][:, :], 0.0)
        s_iptb = p.step("vector", init_ptb, deps=[], tag="init_ptb")

        def warm_exp(e, p_):
            return e.activation(recb[:, 0:1], c_ones[:, 0:1], AF.Exp)
        s_warm = p.step("scalar", warm_exp, deps=[s_const], tag="warm_exp")

        def xcopy(e, p_):
            p_.dma(e, out=x1T, in_=xT)
        s_xcopy = p.step("sync", xcopy, n_dma=1, tag="xcopy")

        bca_m = P0[:, 0:CH]
        bca_r = P0[:, CH:2 * CH]
        kq_rot = [P0[:, 1024:1536], P0[:, 1536:2048], P2[:, :], P3[:, :]]
        s_xn = [None] * NCHUNK
        s_kq_ev = {}
        s_kq_pe_last = None
        s_qdma = [None] * NCHUNK
        s_xndma = [None] * NCHUNK
        prev = dict(statm=None, bca=None, stats_pe=None, sq=None)
        s_sqs = {}
        s_pes = {}

        for c in range(NCHUNK):
            def ldxb(e, p_, cc=c):
                p_.dma(e, out=sxb[:, :].rearrange("p (a n) -> p a n", a=NDT),
                       in_=xT[:, cc * CH:(cc + 1) * CH].rearrange("(a p) n -> p a n", p=128))
            s_ldxb = p.step("gpsimd", ldxb, n_dma=1, tag=f"A{c}_ldxb",
                            deps=[s_sqs.get((c - 1, NDT - 1)), prev["stats_pe"],
                                  s_xn[c - 1] if c else None])

            for dt in range(NDT):
                def sq(e, p_, dt=dt):
                    return e.tensor_tensor(sqp[dt % 2][:, :], sxb[:, dt * CH:(dt + 1) * CH],
                                           sxb[:, dt * CH:(dt + 1) * CH], ALU.mult)
                s_sqs[(c, dt)] = p.step("vector", sq, tag=f"A{c}_sq{dt}",
                                        deps=[s_ldxb, s_pes.get((c, dt - 2)),
                                              prev["stats_pe"] if dt < 2 else None])

                def pe2(e, p_, dt=dt):
                    e.matmul(P4[0:1, :], c_ones[:, :], sxb[:, dt * CH:(dt + 1) * CH],
                             start=(dt == 0), stop=(dt == NDT - 1))
                    return e.matmul(P5[0:1, :], c_ones[:, :], sqp[dt % 2][:, :],
                                    start=(dt == 0), stop=(dt == NDT - 1))
                s_pes[(c, dt)] = p.step("tensor", pe2, tag=f"A{c}_pe{dt}",
                                        deps=[s_sqs[(c, dt)], s_const,
                                              prev["statm"] if dt == 0 else None])
            s_stats_pe = s_pes[(c, NDT - 1)]
            prev["stats_pe"] = s_stats_pe

            def statm1(e, p_):
                e.tensor_scalar(mrow[0:1, :], P4[0:1, :], 1.0 / D, None, ALU.mult)
                e.tensor_scalar(rrow[0:1, 0:CH], P5[0:1, :], 1.0 / D, None, ALU.mult)
                e.tensor_tensor(rrow[0:1, CH:], mrow[0:1, :], mrow[0:1, :], ALU.mult)
                e.tensor_tensor(rrow[0:1, 0:CH], rrow[0:1, 0:CH], rrow[0:1, CH:], ALU.subtract)
                return e.tensor_scalar(rrow[0:1, 0:CH], rrow[0:1, 0:CH], 1e-5, None, ALU.add)
            s_statm1 = p.step("vector", statm1, deps=[s_stats_pe, prev["bca"]], tag=f"A{c}_sm1")

            def sqrt(e, p_):
                return e.activation(rrow[0:1, CH:], rrow[0:1, 0:CH], AF.Sqrt)
            s_sqrt = p.step("scalar", sqrt, deps=[s_statm1], tag=f"A{c}_sqrt")

            def recip(e, p_):
                return e.reciprocal(rrow[0:1, 0:CH], rrow[0:1, CH:])
            s_recip = p.step("vector", recip, deps=[s_sqrt], tag=f"A{c}_rec")
            prev["statm"] = s_recip

            def bca(e, p_):
                e.matmul(bca_m, c_onesrf[0:1, :], mrow[0:1, :], start=True, stop=True)
                return e.matmul(bca_r, c_onesrf[0:1, :], rrow[0:1, 0:CH], start=True, stop=True)
            s_bca = p.step("tensor", bca, deps=[s_recip, s_xn[c - 1] if c else None],
                           tag=f"A{c}_bca")
            prev["bca"] = s_bca

            def xn(e, p_):
                for dt in range(NDT):
                    e.tensor_tensor(tmpf[:, :], sxb[:, dt * CH:(dt + 1) * CH], bca_m, ALU.subtract)
                    ins = e.tensor_tensor(xnb[:, dt * CH:(dt + 1) * CH], tmpf[:, :], bca_r, ALU.mult)
                return ins
            s_xn[c] = p.step("vector", xn,
                             deps=[s_bca, s_xndma[c - 1] if c else None,
                                   s_kq_ev.get((c - 1, 2, 1))], tag=f"A{c}_xn")

            def xndma(e, p_, cc=c):
                p_.dma(e, out=xn_h[cc * D:(cc + 1) * D, :].rearrange("(a p) n -> p a n", p=128),
                       in_=xnb[:, :].rearrange("p (a n) -> p a n", a=NDT))
                if dbg:
                    p_.dma(e, out=xn_dbg[cc * D:(cc + 1) * D, :].rearrange("(a p) n -> p a n", p=128),
                           in_=xnb[:, :].rearrange("p (a n) -> p a n", a=NDT))
            s_xndma[c] = p.step("gpsimd", xndma, n_dma=2 if dbg else 1, deps=[s_xn[c]], tag=f"A{c}_xndma")

            for half, tag in ((0, "k"), (1, "q"), (2, "v")):
                for grp in range(2):
                    def kq_pe(e, p_, half=half, grp=grp, cc=c):
                        for j in range(4):
                            mt = grp * 4 + j
                            for ktd in range(NDT):
                                if half == 0 or half == 1:
                                    wgt = wk_s if half == 0 else wq_s
                                    ins = e.matmul(kq_rot[j],
                                                   wgt[ktd][:, mt * 128:(mt + 1) * 128],
                                                   xnb[:, ktd * CH:(ktd + 1) * CH],
                                                   start=(ktd == 0), stop=(ktd == NDT - 1))
                                else:
                                    s_, dh = mt % 4, mt // 4
                                    ins = e.matmul(kq_rot[j],
                                                   xnb[:, ktd * CH + s_ * 128:
                                                       ktd * CH + s_ * 128 + 128],
                                                   wk_s[ktd][:, dh * 512:(dh + 1) * 512],
                                                   start=(ktd == 0), stop=(ktd == NDT - 1))
                        return ins
                    dep_ev = (s_kq_ev.get((c, half, grp - 1)) if grp else
                              (s_kq_ev.get((c, half - 1, 1)) if half else
                               s_kq_ev.get((c - 1, 2, 1))))
                    s_pe = p.step("tensor", kq_pe, deps=[s_xn[c], dep_ev, s_wld],
                                  tag=f"A{c}_{tag}pe{grp}")
                    if half == 2 and grp == 1:
                        s_kq_pe_last = s_pe

                    def kq_ev(e, p_, half=half, grp=grp, cc=c):
                        for j in range(4):
                            mt = grp * 4 + j
                            if half == 0:
                                ins = e.tensor_copy(kT[mt][:, cc * CH:(cc + 1) * CH], kq_rot[j])
                            elif half == 1:
                                ins = e.tensor_copy(qcb[:, mt * CH:(mt + 1) * CH], kq_rot[j])
                            else:
                                s_, dh = mt % 4, mt // 4
                                tile = cc * 4 + s_
                                dst = vbuf[:, tile * VW + dh * 8 * 65:
                                           tile * VW + dh * 8 * 65 + 520]
                                ins = e.tensor_copy(
                                    dst.rearrange("p (h w) -> p h w", w=65)[:, :, 0:64],
                                    kq_rot[j][:, :].rearrange("p (h w) -> p h w", w=64))
                        return ins
                    dep_q = s_qdma[c - 1] if (half == 1 and grp == 0 and c) else None
                    dep_v = s_const if half == 2 else None
                    s_kq_ev[(c, half, grp)] = p.step("vector", kq_ev, deps=[s_pe, dep_q, dep_v],
                                                     tag=f"A{c}_{tag}ev{grp}")

            def qdma(e, p_, cc=c):
                p_.dma(e, out=qT_h[cc * D:(cc + 1) * D, :].rearrange("(a p) n -> p a n", p=128),
                       in_=qcb[:, :].rearrange("p (a n) -> p a n", a=NDT))
            s_qdma[c] = p.step("gpsimd", qdma, n_dma=1,
                               deps=[s_kq_ev[(c, 1, 0)], s_kq_ev[(c, 1, 1)]], tag=f"A{c}_qdma")

        s_a_done = s_kq_ev[(NCHUNK - 1, 2, 1)]

        def ld_wo(e, p_):
            for i in range(NDT):
                p_.dma(e, out=wk_s[i][:, :], in_=Wo[i * 128:(i + 1) * 128, :])
        s_wold = p.step("gpsimd", ld_wo, n_dma=NDT, deps=[s_kq_pe_last, s_a_done], tag="wold")

        if upto == "A":
            p.wait_only("gpsimd", [s_qdma[NCHUNK - 1], s_xndma[NCHUNK - 1], s_xcopy], tag="final")
            with nc.Block() as block:
                p.emit(block)
            return nc

        def ld_wo(e, p_):
            for i in range(NDT):
                p_.dma(e, out=wk_s[i][:, :], in_=Wo[i * 128:(i + 1) * 128, :])
        s_wold = p.step("gpsimd", ld_wo, n_dma=NDT, deps=[s_kq_pe_last, s_a_done], tag="wold")
        s_a2_done = s_a_done

        st = P0
        pv = [P2, P3]
        s_div = {}
        s_ao_dma = [None] * NCHUNK
        s_c_done = [None] * NCHUNK
        st_steps = []
        exp_steps = []
        pv_steps = []
        s_cev_last = [None] * NCHUNK

        for qc in range(NCHUNK):
            def qld(e, p_, qc=qc):
                p_.dma(e, out=qcb[:, :].rearrange("p (a n) -> p a n", a=NDT),
                       in_=qT_h[qc * D:(qc + 1) * D, :].rearrange("(a p) n -> p a n", p=128))
            s_qld = p.step("sync", qld, n_dma=1, tag=f"B{qc}_qld",
                           deps=[s_qdma[qc],
                                 s_qdma[NCHUNK - 1] if qc == 0 else st_steps[-1]])

            for hp in range(8):
                pend_pv = []
                for g in range(32):
                    G = (qc * 8 + hp) * 32 + g
                    sthalf = st[:, (g % 2) * 1024:(g % 2) * 1024 + 1024]

                    def st_mm(e, p_, hp=hp, g=g, sthalf=sthalf):
                        for hh in (2 * hp, 2 * hp + 1):
                            ins = e.matmul(
                                sthalf[:, (hh % 2) * 512:(hh % 2) * 512 + 512],
                                kT[hh // 2][(hh % 2) * 64:(hh % 2) * 64 + 64,
                                            g * 128:(g + 1) * 128],
                                qcb[(hh % 2) * 64:(hh % 2) * 64 + 64,
                                    (hh // 2) * CH:(hh // 2 + 1) * CH],
                                start=True, stop=True)
                        return ins
                    s_stm = p.step("tensor", st_mm,
                                   deps=[s_qld, s_a2_done,
                                         exp_steps[G - 2] if G >= 2 else None,
                                         s_div.get((qc, hp - 1)) if g == 0 else None,
                                         s_div.get((qc - 1, 7)) if (g == 0 and hp == 0) else None,
                                         s_cev_last[qc - 1] if (g == 0 and hp == 0 and qc) else None],
                                   tag=f"B{qc}_{hp}_st{g}")
                    st_steps.append(s_stm)

                    def expf(e, p_, g=g, sthalf=sthalf):
                        return e.activation(ptb[g % 4][:, :], sthalf, AF.Exp, scale=1.0 / 32)
                    s_e = p.step("scalar", expf,
                                 deps=[s_stm, pv_steps[G - 4] if G >= 4 else None],
                                 tag=f"B{qc}_{hp}_e{g}")
                    exp_steps.append(s_e)
                    if False and dbg and qc == 0 and hp == 0 and g == 2:
                        def dbg_ev(e, p_, g=g):
                            e.tensor_copy(sxb[:, 0:1024], ptb[g % 2][:, :])
                            return e.tensor_copy(sxb[:, 1024:2048],
                                                 st[:, (g % 2) * 1024:(g % 2) * 1024 + 1024])
                        s_dbg = p.step("vector", dbg_ev, deps=[s_e], tag="dbg_ev")
                        def dbg_dma(e, p_):
                            p_.dma(e, out=st_dbg[0:128, :], in_=sxb[:, 0:2048])
                        s_dbgd = p.step("gpsimd", dbg_dma, n_dma=1, deps=[s_dbg], tag="dbg_dma")
                        exp_steps[-1] = s_dbgd

                    def pv_mm(e, p_, hp=hp, g=g):
                        for hh in (2 * hp, 2 * hp + 1):
                            for sub in range(4):
                                ins = e.matmul(
                                    pv[hh % 2][:, sub * 65:(sub + 1) * 65],
                                    ptb[g % 4][:, (hh % 2) * 512 + sub * 128:
                                               (hh % 2) * 512 + sub * 128 + 128],
                                    vbuf[:, g * VW + hh * 65: g * VW + hh * 65 + 65],
                                    start=(g == 0 and sub == 0),
                                    stop=(g == 31 and sub == 3))
                        return ins
                    pend_pv.append((pv_mm, f"B{qc}_{hp}_pv{g}"))
                    if len(pend_pv) == 3:
                        fn0, tag0 = pend_pv.pop(0)
                        pv_steps.append(p.step("tensor", fn0,
                                               deps=[exp_steps[G - 1]], tag=tag0))

                def echo(e, p_, hp=hp):
                    return e.activation(sqp[0][0:1, 0:1], ptb[1][0:1, 1023:1024], AF.Exp)
                s_echo = p.step("scalar", echo, deps=[], tag=f"B{qc}_{hp}_echo")
                for fn0, tag0 in pend_pv:
                    pv_steps.append(p.step("tensor", fn0, deps=[s_echo], tag=tag0))
                pend_pv = []

                if dbg and qc == 0 and hp == 0:
                    def dbg_pv(e, p_):
                        e.tensor_copy(sxb[:, 0:512].rearrange("p (a b) -> p a b", a=1), pv[0][:, 0:512].rearrange("p (a b) -> p a b", a=1))
                        return e.tensor_copy(sxb[:, 512:1024], pv[1][:, 0:512])
                    s_dpv = p.step("vector", dbg_pv, deps=[pv_steps[-1]], tag="dbg_pv")
                    def dbg_pvd(e, p_):
                        p_.dma(e, out=st_dbg[0:128, 0:1024], in_=sxb[:, 0:1024])
                    s_dpvd = p.step("gpsimd", dbg_pvd, n_dma=1, deps=[s_dpv], tag="dbg_pvd")
                    pv_steps[-1] = s_dpvd

                def rec_act(e, p_):
                    for b in range(2):
                        e.activation(recb[:, 8 + b * 4:12 + b * 4].rearrange("p (s w) -> p s w", w=1),
                                     pv[b][:, 0:260].rearrange("p (s w) -> p s w", w=65)[:, :, 64:65],
                                     AF.Ln)
                    return e.activation(recb[:, 0:8], recb[:, 8:16], AF.Exp, scale=-1.0)
                s_rec = p.step("scalar", rec_act, deps=[pv_steps[-1]], tag=f"B{qc}_{hp}_rec")

                def div(e, p_, hp=hp):
                    for hh in (2 * hp, 2 * hp + 1):
                        for sub in range(4):
                            i = (hh % 2) * 4 + sub
                            ins = e.tensor_scalar(
                                anb[:, sub * D + hh * 64: sub * D + hh * 64 + 64],
                                pv[hh % 2][:, sub * 65:sub * 65 + 64],
                                recb[:, i:i + 1], None, ALU.mult)
                    return ins
                s_div[(qc, hp)] = p.step(
                    "vector", div,
                    deps=[s_rec, s_ao_dma[qc - 1] if qc >= 1 else None],
                    tag=f"B{qc}_{hp}_div")

            def ao_dma(e, p_, qc=qc):
                p_.dma(e, out=ao_h[qc * CH:(qc + 1) * CH, :].rearrange("(s p) d -> p s d", p=128),
                       in_=anb[:, :].rearrange("p (s d) -> p s d", s=4))
            s_ao_dma[qc] = p.step("gpsimd", ao_dma, n_dma=1,
                                  deps=[s_div[(qc, hp)] for hp in range(8)], tag=f"B{qc}_ao")

            if upto == "B":
                s_cev_last[qc] = s_ao_dma[qc]
                s_c_done[qc] = s_ao_dma[qc]
                continue
            def cldT(e, p_, qc=qc):
                for dt in range(NDT):
                    p_.dma_t(e, out=xnb[:, dt * CH:(dt + 1) * CH],
                             in_=ao_h[qc * CH:(qc + 1) * CH, dt * 128:(dt + 1) * 128])
            s_cldT = p.step("sync", cldT, n_dma=NDT,
                            deps=[s_ao_dma[qc],
                                  s_c_done[qc - 1] if qc else s_a_done,
                                  None if qc else s_xndma[NCHUNK - 1]],
                            tag=f"C{qc}_ldT")

            s_cev = None
            s_cout = None
            for dt in range(NDT):
                def c_pe(e, p_, dt=dt):
                    yo = P4[:, :] if dt % 2 == 0 else P5[:, :]
                    for ktd in range(NDT):
                        ins = e.matmul(yo, wk_s[ktd][:, dt * 128:(dt + 1) * 128],
                                       xnb[:, ktd * CH:(ktd + 1) * CH],
                                       start=(ktd == 0), stop=(ktd == NDT - 1))
                    return ins
                dep_stats = prev["statm"] if (qc == 0 and dt < 2) else None
                dep_war = s_cev if dt >= 2 else None
                s_cpe = p.step("tensor", c_pe, deps=[s_cldT, dep_stats, dep_war, s_wold],
                               tag=f"C{qc}_pe{dt}")

                def c_ev(e, p_, dt=dt):
                    yo = P4[:, :] if dt % 2 == 0 else P5[:, :]
                    return e.tensor_scalar(resf[:, :], yo, bo_s[:, dt:dt + 1], None, ALU.add)
                s_cev = p.step("vector", c_ev, deps=[s_cpe, s_cout], tag=f"C{qc}_ev{dt}")

                def c_out(e, p_, dt=dt, qc=qc):
                    p_.dma(e, out=x1T[dt * 128:(dt + 1) * 128, qc * CH:(qc + 1) * CH],
                           in_=resf[:, :], accum_op=ALU.add)
                s_cout = p.step("gpsimd", c_out, n_dma=1, deps=[s_cev, s_xcopy],
                                tag=f"C{qc}_out{dt}")
            s_cev_last[qc] = s_cev
            s_c_done[qc] = s_cout

        p.wait_only("gpsimd", [s_c_done[NCHUNK - 1], s_xcopy], tag="final")
        if fctx is not None:
            p.fence()
            return nc
        with nc.Block() as block:
            p.emit(block)
    return nc


def host_l0_attn_inputs(x, Wk, Wq, Wo, bo):
    import ml_dtypes
    ident_bf = np.eye(128, dtype=np.float32).astype(ml_dtypes.bfloat16)
    bo_t = np.ascontiguousarray(np.asarray(bo, np.float32).reshape(NDT, 128).T)
    maps = []
    for c in range(NC):
        maps.append({
            "xT": np.ascontiguousarray(np.asarray(x[c], np.float32).T),
            "Wk": np.asarray(Wk, np.float32), "Wq": np.asarray(Wq, np.float32),
            "Wo": np.asarray(Wo, np.float32),
            "bo_t": bo_t, "ident": ident_bf,
        })
    return maps




def build_fused():
    nc = bass.Bass("TRN2", target_bir_lowering=False, debug=False, detect_race_conditions=False)
    xin = nc.dram_tensor("xT", [D, T], F32, kind="ExternalInput").ap()
    x1 = nc.dram_tensor("x1b", [D, T], F32).ap()
    x2 = nc.dram_tensor("x2b", [D, T], F32).ap()
    x3 = nc.dram_tensor("x3b", [D, T], F32).ap()
    xout = nc.dram_tensor("x4T", [D, T], F32, kind="ExternalOutput").ap()
    with contextlib.ExitStack() as gctx:
        p = P(nc, gctx)
        build_l0_attn("C", fctx=dict(nc=nc, p=p, xin=xin, xout=x1, tag="a_"))
        build_l0_moe(fctx=dict(nc=nc, p=p, xin=x1, xout=x2, tag="m_"))
        build_l1_attn("C", fctx=dict(nc=nc, p=p, xin=x2, xout=x3, tag="b_"))
        build_l1_ffn(fctx=dict(nc=nc, p=p, xin=x3, xout=xout, tag="f_"))
        p.wait_only("sync", [(p.dsem[e], p.dma_val[e]) for e in P.ENGS if p.dma_val[e]]
                    + [(p.clk[e], p.clk_val[e]) for e in P.ENGS if p.clk_val[e]],
                    tag="final")
        with nc.Block() as block:
            p.emit(block)
    return nc


def host_fused_inputs(inputs):
    x = np.asarray(inputs["L0_attn_x"], np.float32)
    m1 = host_l0_attn_inputs(x, inputs["L0_attn_Wk"], inputs["L0_attn_Wq"],
                             inputs["L0_attn_Wo"], inputs["L0_attn_bo"])
    dummy = [np.zeros((1, 1), np.float32)] * NC
    m2 = host_l0_moe_inputs(dummy, inputs["L0_ffn_rms_w"], inputs["L0_ffn_gate_w"],
                            inputs["L0_ffn_w13"], inputs["L0_ffn_w2"],
                            inputs["L0_ffn_sw_gate"], inputs["L0_ffn_sw_up"],
                            inputs["L0_ffn_sw_down"])
    m3 = host_l1_attn_inputs(None, inputs["L1_attn_Wqa"], inputs["L1_attn_Wqb"],
                             inputs["L1_attn_Wkva"], inputs["L1_attn_Wkvb"],
                             inputs["L1_attn_Wout"], inputs["L1_attn_bout"], xs=dummy)
    m4 = host_l1_ffn_inputs(inputs["L1_ffn_ln_g"], inputs["L1_ffn_ln_b"],
                            inputs["L1_ffn_bn_g"], inputs["L1_ffn_bn_b"],
                            inputs["L1_ffn_bn_mean"], inputs["L1_ffn_bn_var"],
                            inputs["L1_ffn_Wr"], inputs["L1_ffn_br"],
                            inputs["L1_ffn_W_down"], inputs["L1_ffn_b_down"],
                            inputs["L1_ffn_W_up"], inputs["L1_ffn_b_up"], xs=dummy)
    maps = []
    for c in range(NC):
        for mm in (m2, m3, m4):
            mm[c].pop("xT", None)
        maps.append({**m1[c], **m2[c], **m3[c], **m4[c]})
    return maps


_prog_cache = {}


def _get_prog(name, builder):
    if name not in _prog_cache:
        _prog_cache[name] = builder()
    return _prog_cache[name]


def kernel(**inputs):
    nc = _get_prog("fused", build_fused)
    maps = host_fused_inputs(inputs)
    res = run_bass_kernel_spmd(nc, maps, core_ids=list(range(NC)))
    return np.ascontiguousarray(
        np.stack([res.results[c]["x4T"].T for c in range(NC)]).astype(np.float32))


def kernel_multilaunch(**inputs):
    x = np.asarray(inputs["L0_attn_x"], np.float32)

    nc1 = _get_prog("l0a", lambda: build_l0_attn("C"))
    maps1 = host_l0_attn_inputs(x, inputs["L0_attn_Wk"], inputs["L0_attn_Wq"],
                                inputs["L0_attn_Wo"], inputs["L0_attn_bo"])
    r1 = run_bass_kernel_spmd(nc1, maps1, core_ids=list(range(NC)))
    x1T = [r1.results[c]["x1T"] for c in range(NC)]

    nc2 = _get_prog("l0m", lambda: build_l0_moe())
    maps2 = host_l0_moe_inputs([a.T for a in x1T], inputs["L0_ffn_rms_w"],
                               inputs["L0_ffn_gate_w"], inputs["L0_ffn_w13"],
                               inputs["L0_ffn_w2"], inputs["L0_ffn_sw_gate"],
                               inputs["L0_ffn_sw_up"], inputs["L0_ffn_sw_down"])
    r2 = run_bass_kernel_spmd(nc2, maps2, core_ids=list(range(NC)))
    x2T = [r2.results[c]["x2T"] for c in range(NC)]

    nc3 = _get_prog("l1a", lambda: build_l1_attn("C"))
    maps3 = host_l1_attn_inputs(None, inputs["L1_attn_Wqa"], inputs["L1_attn_Wqb"],
                                inputs["L1_attn_Wkva"], inputs["L1_attn_Wkvb"],
                                inputs["L1_attn_Wout"], inputs["L1_attn_bout"],
                                xs=[a.T for a in x2T])
    r3 = run_bass_kernel_spmd(nc3, maps3, core_ids=list(range(NC)))
    x3T = [r3.results[c]["x2T"] for c in range(NC)]

    nc4 = _get_prog("l1f", lambda: build_l1_ffn())
    maps4 = host_l1_ffn_inputs(inputs["L1_ffn_ln_g"], inputs["L1_ffn_ln_b"],
                               inputs["L1_ffn_bn_g"], inputs["L1_ffn_bn_b"],
                               inputs["L1_ffn_bn_mean"], inputs["L1_ffn_bn_var"],
                               inputs["L1_ffn_Wr"], inputs["L1_ffn_br"],
                               inputs["L1_ffn_W_down"], inputs["L1_ffn_b_down"],
                               inputs["L1_ffn_W_up"], inputs["L1_ffn_b_up"],
                               xs=[a.T for a in x3T])
    r4 = run_bass_kernel_spmd(nc4, maps4, core_ids=list(range(NC)))
    return np.ascontiguousarray(
        np.stack([r4.results[c]["x4T"].T for c in range(NC)]).astype(np.float32))
L1_SLOT = {}
for _h in range(16):
    _g = _h // 4
    _idx = _h % 4 + (4 if _h >= 8 else 0)
    L1_SLOT[_h] = 2 * _idx + (_g % 2)
L1_PAIRS = [([0, 1, 2, 3, 8, 9, 10, 11][j], [4, 5, 6, 7, 12, 13, 14, 15][j])
            for j in range(8)]
VW1 = 65 * 4


def build_l1_attn(upto="C", fctx=None):
    nc = fctx["nc"] if fctx else bass.Bass("TRN2", target_bir_lowering=False, debug=False, detect_race_conditions=False)
    xT = fctx["xin"] if fctx else nc.dram_tensor("xT", [D, T], F32, kind="ExternalInput").ap()
    Wqa = nc.dram_tensor("Wqa", [D, 64], F32, kind="ExternalInput").ap()
    Wqb = nc.dram_tensor("Wqb", [64, D], F32, kind="ExternalInput").ap()
    Wkva = nc.dram_tensor("Wkva", [D, 32], F32, kind="ExternalInput").ap()
    Wkvb_k = nc.dram_tensor("Wkvb_k", [32, 256], F32, kind="ExternalInput").ap()
    Wkvb_v = nc.dram_tensor("Wkvb_v", [32, 256], F32, kind="ExternalInput").ap()
    Wout = nc.dram_tensor("Wout", [D, D], F32, kind="ExternalInput").ap()
    bout_t = nc.dram_tensor("bout_t", [128, NDT], F32, kind="ExternalInput").ap()
    cosT = nc.dram_tensor("cosT", [128, T], F32, kind="ExternalInput").ap()
    sinT = nc.dram_tensor("sinT", [128, T], F32, kind="ExternalInput").ap()
    cmask = nc.dram_tensor("cmask", [128, 4 * 1024], BF16, kind="ExternalInput").ap()
    x2T = fctx["xout"] if fctx else nc.dram_tensor("x2T", [D, T], F32, kind="ExternalOutput").ap()
    qT_h = nc.dram_tensor("qT_h1", [NCHUNK * D, CH], BF16).ap()
    kv_h = nc.dram_tensor("kv_h1", [NCHUNK * 256, CH], BF16).ap()
    ao_h = nc.dram_tensor("ao_h1", [T, D], BF16).ap()

    with contextlib.ExitStack() as ctx:
        pref = fctx["tag"] if fctx else ""
        sb = lambda name, p_, f, dt=BF16: ctx.enter_context(nc.sbuf_tensor(pref + name, [p_, f], dt))
        c_ones = sb("c_ones", 128, 1)
        c_onesrf = sb("c_onesrf", 1, 128, F32)
        wqa_s = [sb(f"wqa{i}", 128, 64) for i in range(NDT)]
        wqb_s = sb("wqb_s", 64, D)
        wkva_s = [sb(f"wkva{i}", 128, 32) for i in range(NDT)]
        wkvbk_s = sb("wkvbk_s", 32, 256)
        wkvbv_s = sb("wkvbv_s", 32, 256)
        wout_s = [sb(f"wout{i}", 128, D) for i in range(NDT)]
        bout_s = sb("bout_s", 128, NDT, F32)
        cos_s = sb("cos_s", 128, T, F32)
        sin_s = sb("sin_s", 128, T, F32)
        cm_s = sb("cm_s", 128, 4 * 1024)
        kT = [sb(f"kTl{i}", 128, T) for i in range(2)]
        vbuf = sb("vbufl", 128, NKT * VW1)
        sxb = sb("sxb", 128, NDT * CH)
        sqp = [sb(f"sqp{i}", 128, CH) for i in range(2)]
        xnb = sb("xnb", 128, NDT * CH)
        tmpf = sb("tmpf", 128, CH, F32)
        t2f = sb("t2f", 128, CH, F32)
        mrow = sb("mrow", 1, CH, F32)
        rrow = sb("rrow", 1, 2 * CH, F32)
        t1b = sb("t1b", 64, CH)
        t1k0 = sb("t1k0", 32, CH)
        qre = sb("qre", 128, CH)
        qro = sb("qro", 128, CH)
        qcb = sb("qcb", 128, NDT * CH)
        ptb = [sb(f"ptb{i}", 128, 1024) for i in range(4)]
        anb = sxb
        recb = sb("recb", 128, 16, F32)
        resf = tmpf

        P0 = ctx.enter_context(nc.psum_tensor(pref + "P0", [128, 2048], F32))
        P2 = ctx.enter_context(nc.psum_tensor(pref + "P2", [128, 512], F32))
        P3 = ctx.enter_context(nc.psum_tensor(pref + "P3", [128, 512], F32))
        P4 = ctx.enter_context(nc.psum_tensor(pref + "P4", [128, 512], F32))
        P5 = ctx.enter_context(nc.psum_tensor(pref + "P5", [128, 512], F32))

        p = fctx["p"] if fctx else P(nc, ctx)

        def ld_w(e, p_):
            p_.dma(e, out=bout_s[:, :], in_=bout_t)
            p_.dma(e, out=wqb_s[:, :], in_=Wqb)
            p_.dma(e, out=wkvbk_s[:, :], in_=Wkvb_k)
            p_.dma(e, out=wkvbv_s[:, :], in_=Wkvb_v)
            p_.dma(e, out=cm_s[:, :], in_=cmask)
            for i in range(NDT):
                p_.dma(e, out=wqa_s[i][:, :], in_=Wqa[i * 128:(i + 1) * 128, :])
                p_.dma(e, out=wkva_s[i][:, :], in_=Wkva[i * 128:(i + 1) * 128, :])
                p_.dma(e, out=wout_s[i][:, :], in_=Wout[i * 128:(i + 1) * 128, :])
        s_wld = p.step("gpsimd", ld_w, n_dma=5 + 3 * NDT, tag="wld")

        def ld_rope(e, p_):
            p_.dma(e, out=cos_s[:, :], in_=cosT)
            p_.dma(e, out=sin_s[:, :], in_=sinT)
        s_rope = p.step("sync", ld_rope, n_dma=2, tag="ldrope")

        def consts(e, p_):
            e.memset(c_ones[:, :], 1.0)
            e.memset(c_onesrf[0:1, :], 1.0)
            return e.memset(vbuf[:, :], 1.0)
        s_const = p.step("vector", consts, tag="consts")

        def xcopy(e, p_):
            p_.dma(e, out=x2T, in_=xT)
        s_xcopy = p.step("sync", xcopy, n_dma=1, tag="xcopy")

        bca_m = P0[:, 0:CH]
        bca_r = P0[:, CH:2 * CH]
        s_xn = [None] * NCHUNK
        s_qdma = [None] * NCHUNK
        s_kvld = [None] * NCHUNK
        prev = dict(statm=None, bca=None, stats_pe=None)
        s_sqs = {}
        s_pes = {}
        s_va_ev = {}
        last_dma_q = None

        for c in range(NCHUNK):
            def ldxb(e, p_, cc=c):
                p_.dma(e, out=sxb[:, :].rearrange("p (a n) -> p a n", a=NDT),
                       in_=xT[:, cc * CH:(cc + 1) * CH].rearrange("(a p) n -> p a n", p=128))
            s_ldxb = p.step("gpsimd", ldxb, n_dma=1, tag=f"A{c}_ldxb",
                            deps=[s_sqs.get((c - 1, NDT - 1)), prev["stats_pe"],
                                  s_xn[c - 1] if c else None])

            for dt in range(NDT):
                def sq(e, p_, dt=dt):
                    return e.tensor_tensor(sqp[dt % 2][:, :], sxb[:, dt * CH:(dt + 1) * CH],
                                           sxb[:, dt * CH:(dt + 1) * CH], ALU.mult)
                s_sqs[(c, dt)] = p.step("vector", sq, tag=f"A{c}_sq{dt}",
                                        deps=[s_ldxb, s_pes.get((c, dt - 2)),
                                              prev["stats_pe"] if dt < 2 else None])

                def pe2(e, p_, dt=dt):
                    e.matmul(P4[0:1, :], c_ones[:, :], sxb[:, dt * CH:(dt + 1) * CH],
                             start=(dt == 0), stop=(dt == NDT - 1))
                    return e.matmul(P5[0:1, :], c_ones[:, :], sqp[dt % 2][:, :],
                                    start=(dt == 0), stop=(dt == NDT - 1))
                s_pes[(c, dt)] = p.step("tensor", pe2, tag=f"A{c}_pe{dt}",
                                        deps=[s_sqs[(c, dt)], s_const,
                                              prev["statm"] if dt == 0 else None])
            s_stats_pe = s_pes[(c, NDT - 1)]
            prev["stats_pe"] = s_stats_pe

            def statm1(e, p_):
                e.tensor_scalar(mrow[0:1, :], P4[0:1, :], 1.0 / D, None, ALU.mult)
                e.tensor_scalar(rrow[0:1, 0:CH], P5[0:1, :], 1.0 / D, None, ALU.mult)
                e.tensor_tensor(rrow[0:1, CH:], mrow[0:1, :], mrow[0:1, :], ALU.mult)
                e.tensor_tensor(rrow[0:1, 0:CH], rrow[0:1, 0:CH], rrow[0:1, CH:], ALU.subtract)
                return e.tensor_scalar(rrow[0:1, 0:CH], rrow[0:1, 0:CH], 1e-5, None, ALU.add)
            s_statm1 = p.step("vector", statm1, deps=[s_stats_pe, prev["bca"]], tag=f"A{c}_sm1")

            def sqrt(e, p_):
                return e.activation(rrow[0:1, CH:], rrow[0:1, 0:CH], AF.Sqrt)
            s_sqrt = p.step("scalar", sqrt, deps=[s_statm1], tag=f"A{c}_sqrt")

            def recip(e, p_):
                return e.reciprocal(rrow[0:1, 0:CH], rrow[0:1, CH:])
            s_recip = p.step("vector", recip, deps=[s_sqrt], tag=f"A{c}_rec")
            prev["statm"] = s_recip

            def bca(e, p_):
                e.matmul(bca_m, c_onesrf[0:1, :], mrow[0:1, :], start=True, stop=True)
                return e.matmul(bca_r, c_onesrf[0:1, :], rrow[0:1, 0:CH], start=True, stop=True)
            s_bca = p.step("tensor", bca, deps=[s_recip, s_xn[c - 1] if c else None],
                           tag=f"A{c}_bca")
            prev["bca"] = s_bca

            def xn(e, p_):
                for dt in range(NDT):
                    e.tensor_tensor(tmpf[:, :], sxb[:, dt * CH:(dt + 1) * CH], bca_m, ALU.subtract)
                    ins = e.tensor_tensor(xnb[:, dt * CH:(dt + 1) * CH], tmpf[:, :], bca_r, ALU.mult)
                return ins
            s_xn[c] = p.step("vector", xn, deps=[s_bca, s_va_ev.get((c - 1, 3))],
                             tag=f"A{c}_xn")

            def st1_pe(e, p_):
                for ktd in range(NDT):
                    e.matmul(P0[0:64, 1024:1536], wqa_s[ktd][:, :],
                             xnb[:, ktd * CH:(ktd + 1) * CH],
                             start=(ktd == 0), stop=(ktd == NDT - 1))
                for ktd in range(NDT):
                    ins = e.matmul(P0[0:32, 1536:2048], wkva_s[ktd][:, :],
                                   xnb[:, ktd * CH:(ktd + 1) * CH],
                                   start=(ktd == 0), stop=(ktd == NDT - 1))
                return ins
            s_st1 = p.step("tensor", st1_pe, deps=[s_xn[c], s_wld,
                                                   s_va_ev.get((c - 1, 3))], tag=f"A{c}_st1")

            def st1_ev(e, p_):
                e.tensor_copy(t1b[0:64, :], P0[0:64, 1024:1536])
                return e.tensor_copy(t1k0[0:32, :], P0[0:32, 1536:2048])
            s_st1e = p.step("vector", st1_ev, deps=[s_st1], tag=f"A{c}_st1e")

            s_rope_prev = None
            for m in range(4):
                def q2_pe(e, p_, m=m):
                    e.matmul(P2[:, :], wqb_s[:, m * 128:(m + 1) * 128], t1b[0:64, :],
                             start=True, stop=True)
                    return e.matmul(P3[:, :], wqb_s[:, 512 + m * 128: 512 + (m + 1) * 128],
                                    t1b[0:64, :], start=True, stop=True)
                s_q2 = p.step("tensor", q2_pe, deps=[s_st1e, s_rope_prev], tag=f"A{c}_q2_{m}")

                def ropeq(e, p_, m=m, cc=c):
                    cslice = cos_s[:, cc * CH:(cc + 1) * CH]
                    sslice = sin_s[:, cc * CH:(cc + 1) * CH]
                    e.tensor_tensor(tmpf[:, :], P2[:, :], cslice, ALU.mult)
                    e.tensor_tensor(t2f[:, :], P3[:, :], sslice, ALU.mult)
                    e.tensor_tensor(qre[:, :], tmpf[:, :], t2f[:, :], ALU.subtract)
                    e.tensor_tensor(tmpf[:, :], P2[:, :], sslice, ALU.mult)
                    e.tensor_tensor(t2f[:, :], P3[:, :], cslice, ALU.mult)
                    return e.tensor_tensor(qro[:, :], tmpf[:, :], t2f[:, :], ALU.add)
                s_ropeq = p.step("vector", ropeq, deps=[s_q2, last_dma_q], tag=f"A{c}_ropeq{m}")
                s_rope_prev = s_ropeq

                def qdma(e, p_, m=m, cc=c):
                    base = (m % 2) + 8 * (m // 2)
                    region = qT_h[cc * D:(cc + 1) * D, :]
                    rr2 = region.rearrange("(sp two half i) n -> half two sp i n",
                                           two=2, half=2, i=32)
                    so = 4 if base >= 8 else 0
                    for half, srcbuf in ((0, qre), (1, qro)):
                        dst = rr2[half, base % 2, so:so + 4, :, :]
                        p_.dma(e, out=dst, in_=srcbuf[:, :])
                s_qd = p.step("gpsimd", qdma, n_dma=2, deps=[s_ropeq], tag=f"A{c}_qdma{m}")
                last_dma_q = s_qd
            s_qdma[c] = last_dma_q

            def k2_pe(e, p_):
                e.matmul(P2[:, :], wkvbk_s[:, 0:128], t1k0[0:32, :], start=True, stop=True)
                return e.matmul(P3[:, :], wkvbk_s[:, 128:256], t1k0[0:32, :], start=True, stop=True)
            s_k2 = p.step("tensor", k2_pe, deps=[s_st1e, s_rope_prev], tag=f"A{c}_k2")

            def ropek(e, p_, cc=c):
                cslice = cos_s[:, cc * CH:(cc + 1) * CH]
                sslice = sin_s[:, cc * CH:(cc + 1) * CH]
                e.tensor_tensor(tmpf[:, :], P2[:, :], cslice, ALU.mult)
                e.tensor_tensor(t2f[:, :], P3[:, :], sslice, ALU.mult)
                e.tensor_tensor(qre[:, :], tmpf[:, :], t2f[:, :], ALU.subtract)
                e.tensor_tensor(tmpf[:, :], P2[:, :], sslice, ALU.mult)
                e.tensor_tensor(t2f[:, :], P3[:, :], cslice, ALU.mult)
                return e.tensor_tensor(qro[:, :], tmpf[:, :], t2f[:, :], ALU.add)
            s_ropek = p.step("vector", ropek, deps=[s_k2, last_dma_q], tag=f"A{c}_ropek")

            def kdma(e, p_, cc=c):
                region = kv_h[cc * 256:(cc + 1) * 256, :]
                rr = region.rearrange("(g half i) n -> half g i n", half=2, i=32)
                for half, srcbuf in ((0, qre), (1, qro)):
                    p_.dma(e, out=rr[half, :, :, :], in_=srcbuf[:, :])
            s_kd = p.step("gpsimd", kdma, n_dma=2, deps=[s_ropek], tag=f"A{c}_kdma")
            last_dma_q = s_kd

            def kvld(e, p_, cc=c):
                for t_ in range(2):
                    p_.dma(e, out=kT[t_][:, cc * CH:(cc + 1) * CH],
                           in_=kv_h[cc * 256 + t_ * 128: cc * 256 + (t_ + 1) * 128, :])
            s_kvld[c] = p.step("sync", kvld, n_dma=2, deps=[s_kd], tag=f"A{c}_kvld")

            for sv in range(2):
                def v_pe(e, p_, sv=sv):
                    for j in range(2):
                        s_ = sv * 2 + j
                        ins = e.matmul(P2[:, 0:256] if j == 0 else P3[:, 0:256],
                                       t1k0[0:32, s_ * 128:(s_ + 1) * 128],
                                       wkvbv_s[:, :],
                                       start=True, stop=True)
                    return ins
                s_vp = p.step("tensor", v_pe, deps=[s_st1e, s_ropek if sv == 0 else None,
                                                    s_va_ev.get((c, sv * 2 - 1)),
                                                    s_va_ev.get((c - 1, 3)) if sv == 0 else None],
                              tag=f"A{c}_v{sv}")

                for j in range(2):
                    def v_ev(e, p_, sv=sv, j=j, cc=c):
                        s_ = sv * 2 + j
                        tile = cc * 4 + s_
                        dst = vbuf[:, tile * VW1: tile * VW1 + VW1]
                        return e.tensor_copy(
                            dst.rearrange("p (h w) -> p h w", w=65)[:, :, 0:64],
                            (P2 if j == 0 else P3)[:, 0:256].rearrange("p (h w) -> p h w", w=64))
                    s_va_ev[(c, sv * 2 + j)] = p.step("vector", v_ev,
                                                      deps=[s_vp, s_const], tag=f"A{c}_vev{sv}{j}")

        s_a_done = s_va_ev[(NCHUNK - 1, 3)]
        if upto == "A":
            p.wait_only("gpsimd", [s_qdma[NCHUNK - 1], s_xcopy], tag="final")
            p.wait_only("sync", [s_kvld[NCHUNK - 1]], tag="final2")
            with nc.Block() as block:
                p.emit(block)
            return nc

        st = P0
        pv = [P2, P3]
        s_div = {}
        s_ao_dma = [None] * NCHUNK
        s_c_done = [None] * NCHUNK
        st_steps = []
        exp_steps = []
        pv_steps = []
        s_cev_last = [None] * NCHUNK

        for qc in range(NCHUNK):
            NKTq = 4 * qc + 4

            def qld(e, p_, qc=qc):
                p_.dma(e, out=qcb[:, :].rearrange("p (a n) -> p a n", a=NDT),
                       in_=qT_h[qc * D:(qc + 1) * D, :].rearrange("(a p) n -> p a n", p=128))
            s_qld = p.step("sync", qld, n_dma=1, tag=f"B{qc}_qld",
                           deps=[s_qdma[qc], s_kvld[NCHUNK - 1],
                                 s_qdma[NCHUNK - 1] if qc == 0 else st_steps[-1]])

            for hp in range(8):
                he, ho = L1_PAIRS[hp]
                pend_pv = []
                s_e = None
                for g in range(NKTq):
                    G = len(st_steps)
                    sthalf = st[:, (g % 2) * 1024:(g % 2) * 1024 + 1024]

                    def st_mm(e, p_, hp=hp, g=g, sthalf=sthalf):
                        for pos, hh in enumerate(L1_PAIRS[hp]):
                            sl = L1_SLOT[hh]
                            gkv = hh // 4
                            ins = e.matmul(
                                sthalf[:, pos * 512:pos * 512 + 512],
                                kT[gkv // 2][(gkv % 2) * 64:(gkv % 2) * 64 + 64,
                                             g * 128:(g + 1) * 128],
                                qcb[(sl % 2) * 64:(sl % 2) * 64 + 64,
                                    (sl // 2) * CH:(sl // 2 + 1) * CH],
                                start=True, stop=True)
                        return ins
                    s_stm = p.step("tensor", st_mm,
                                   deps=[s_qld, s_a_done,
                                         exp_steps[G - 2] if len(exp_steps) >= 2 else None,
                                         s_div.get((qc, hp - 1)) if g == 0 else None,
                                         s_div.get((qc - 1, 7)) if (g == 0 and hp == 0) else None,
                                         s_cev_last[qc - 1] if (g == 0 and hp == 0 and qc) else None],
                                   tag=f"B1{qc}_{hp}_st{g}")
                    st_steps.append(s_stm)

                    def expf(e, p_, g=g, G=G, sthalf=sthalf):
                        return e.activation(ptb[G % 4][:, :], sthalf, AF.Exp, scale=1.0 / 8)
                    s_e = p.step("scalar", expf,
                                 deps=[s_stm, pv_steps[G - 4] if G >= 4 else None],
                                 tag=f"B1{qc}_{hp}_e{g}")
                    exp_steps.append(s_e)

                    delta = g - 4 * qc
                    s_mask = None
                    if 0 <= delta <= 3:
                        def maskf(e, p_, g=g, G=G, delta=delta):
                            return e.tensor_tensor(ptb[G % 4][:, :], ptb[G % 4][:, :],
                                                   cm_s[:, delta * 1024:(delta + 1) * 1024],
                                                   ALU.mult)
                        s_mask = p.step("vector", maskf, deps=[s_e], tag=f"B1{qc}_{hp}_m{g}")
                        exp_steps[-1] = s_mask

                    def pv_mm(e, p_, hp=hp, g=g, G=G, NKTq=NKTq):
                        for pos, hh in enumerate(L1_PAIRS[hp]):
                            gkv = hh // 4
                            for sub in range(4):
                                ins = e.matmul(
                                    pv[pos][:, sub * 65:(sub + 1) * 65],
                                    ptb[G % 4][:, pos * 512 + sub * 128:
                                               pos * 512 + sub * 128 + 128],
                                    vbuf[:, g * VW1 + gkv * 65: g * VW1 + gkv * 65 + 65],
                                    start=(g == 0 and sub == 0), stop=(g == NKTq - 1 and sub == 3))
                        return ins
                    pend_pv.append((pv_mm, f"B1{qc}_{hp}_pv{g}"))
                    if len(pend_pv) == 3:
                        fn0, tag0 = pend_pv.pop(0)
                        pv_steps.append(p.step("tensor", fn0, deps=[exp_steps[-2]], tag=tag0))

                def echo(e, p_):
                    return e.activation(sqp[0][0:1, 0:1], ptb[1][0:1, 1023:1024], AF.Exp)
                s_echo = p.step("scalar", echo, deps=[], tag=f"B1{qc}_{hp}_echo")
                for fn0, tag0 in pend_pv:
                    pv_steps.append(p.step("tensor", fn0, deps=[s_echo], tag=tag0))
                pend_pv = []

                def rec_act(e, p_):
                    for b in range(2):
                        e.activation(recb[:, 8 + b * 4:12 + b * 4].rearrange("p (s w) -> p s w", w=1),
                                     pv[b][:, 0:260].rearrange("p (s w) -> p s w", w=65)[:, :, 64:65],
                                     AF.Ln)
                    return e.activation(recb[:, 0:8], recb[:, 8:16], AF.Exp, scale=-1.0)
                s_rec = p.step("scalar", rec_act, deps=[pv_steps[-1]], tag=f"B1{qc}_{hp}_rec")

                def div(e, p_, hp=hp):
                    for pos, hh in enumerate(L1_PAIRS[hp]):
                        for sub in range(4):
                            i = pos * 4 + sub
                            ins = e.tensor_scalar(
                                anb[:, sub * D + hh * 64: sub * D + hh * 64 + 64],
                                pv[pos][:, sub * 65:sub * 65 + 64],
                                recb[:, i:i + 1], None, ALU.mult)
                    return ins
                s_div[(qc, hp)] = p.step(
                    "vector", div,
                    deps=[s_rec, s_ao_dma[qc - 1] if qc >= 1 else None],
                    tag=f"B1{qc}_{hp}_div")

            def ao_dma(e, p_, qc=qc):
                p_.dma(e, out=ao_h[qc * CH:(qc + 1) * CH, :].rearrange("(s p) d -> p s d", p=128),
                       in_=anb[:, :].rearrange("p (s d) -> p s d", s=4))
            s_ao_dma[qc] = p.step("gpsimd", ao_dma, n_dma=1,
                                  deps=[s_div[(qc, hp)] for hp in range(8)], tag=f"B1{qc}_ao")

            if upto == "B":
                s_cev_last[qc] = s_ao_dma[qc]
                s_c_done[qc] = s_ao_dma[qc]
                continue

            def cldT(e, p_, qc=qc):
                for dt in range(NDT):
                    p_.dma_t(e, out=xnb[:, dt * CH:(dt + 1) * CH],
                             in_=ao_h[qc * CH:(qc + 1) * CH, dt * 128:(dt + 1) * 128])
            s_cldT = p.step("sync", cldT, n_dma=NDT,
                            deps=[s_ao_dma[qc],
                                  s_c_done[qc - 1] if qc else s_st1,
                                  None if qc else s_xn[NCHUNK - 1]],
                            tag=f"C1{qc}_ldT")

            s_cev = None
            s_cout = None
            for dt in range(NDT):
                def c_pe(e, p_, dt=dt):
                    yo = P4[:, :] if dt % 2 == 0 else P5[:, :]
                    for ktd in range(NDT):
                        ins = e.matmul(yo, wout_s[ktd][:, dt * 128:(dt + 1) * 128],
                                       xnb[:, ktd * CH:(ktd + 1) * CH],
                                       start=(ktd == 0), stop=(ktd == NDT - 1))
                    return ins
                dep_stats = prev["statm"] if (qc == 0 and dt < 2) else None
                dep_war = s_cev if dt >= 2 else None
                s_cpe = p.step("tensor", c_pe, deps=[s_cldT, dep_stats, dep_war],
                               tag=f"C1{qc}_pe{dt}")

                def c_ev(e, p_, dt=dt):
                    yo = P4[:, :] if dt % 2 == 0 else P5[:, :]
                    return e.tensor_scalar(resf[:, :], yo, bout_s[:, dt:dt + 1], None, ALU.add)
                s_cev = p.step("vector", c_ev, deps=[s_cpe, s_cout], tag=f"C1{qc}_ev{dt}")

                def c_out(e, p_, dt=dt, qc=qc):
                    p_.dma(e, out=x2T[dt * 128:(dt + 1) * 128, qc * CH:(qc + 1) * CH],
                           in_=resf[:, :], accum_op=ALU.add)
                s_cout = p.step("gpsimd", c_out, n_dma=1, deps=[s_cev, s_xcopy],
                                tag=f"C1{qc}_out{dt}")
            s_cev_last[qc] = s_cev
            s_c_done[qc] = s_cout

        p.wait_only("gpsimd", [s_c_done[NCHUNK - 1], s_xcopy], tag="final")
        if fctx is not None:
            p.fence()
            return nc
        with nc.Block() as block:
            p.emit(block)
    return nc


def host_l1_attn_inputs(x_unused_count, Wqa, Wqb, Wkva, Wkvb, Wout, bout, xs):
    import ml_dtypes
    evens = [h * 64 + 2 * i for h in range(16) for i in range(32)]
    odds = [h * 64 + 2 * i + 1 for h in range(16) for i in range(32)]
    Wqb_r = np.ascontiguousarray(np.asarray(Wqb, np.float32)[:, evens + odds])
    k_evens = [g * 128 + 4 * i for g in range(4) for i in range(32)]
    k_odds = [g * 128 + 4 * i + 2 for g in range(4) for i in range(32)]
    v_cols = [g * 128 + 2 * d_ + 1 for g in range(4) for d_ in range(64)]
    Wkvb = np.asarray(Wkvb, np.float32)
    Wkvb_k = np.ascontiguousarray(Wkvb[:, k_evens + k_odds])
    Wkvb_v = np.ascontiguousarray(Wkvb[:, v_cols])
    inv_freq = (1.0 / 10000.0 ** (np.arange(0, 64, 2, dtype=np.float32) / 64)).astype(np.float32)
    t = np.arange(T, dtype=np.float32)
    freqs = np.outer(t, inv_freq).astype(np.float32)
    cosT = np.ascontiguousarray(np.tile(np.cos(freqs).T, (4, 1)).astype(np.float32))
    sinT = np.ascontiguousarray(np.tile(np.sin(freqs).T, (4, 1)).astype(np.float32))
    k_i = np.arange(128)[:, None]
    q_i = np.arange(512)[None, :]
    cm = np.zeros((128, 4 * 1024), np.float32)
    for dlt in range(4):
        m = (k_i <= q_i - 128 * dlt).astype(np.float32)
        cm[:, dlt * 1024:dlt * 1024 + 512] = m
        cm[:, dlt * 1024 + 512:(dlt + 1) * 1024] = m
    cm = cm.astype(ml_dtypes.bfloat16)
    bout_t = np.ascontiguousarray(np.asarray(bout, np.float32).reshape(NDT, 128).T)
    maps = []
    for c in range(NC):
        maps.append({
            "xT": np.ascontiguousarray(np.asarray(xs[c], np.float32).T),
            "Wqa": np.asarray(Wqa, np.float32), "Wqb": Wqb_r,
            "Wkva": np.asarray(Wkva, np.float32), "Wkvb_k": Wkvb_k, "Wkvb_v": Wkvb_v,
            "Wout": np.asarray(Wout, np.float32), "bout_t": bout_t,
            "cosT": cosT, "sinT": sinT, "cmask": cm,
        })
    return maps
def build_l1_ffn(dbg=False, fctx=None):
    nc = fctx["nc"] if fctx else bass.Bass("TRN2", target_bir_lowering=False, debug=False, detect_race_conditions=False)
    xT = fctx["xin"] if fctx else nc.dram_tensor("xT", [D, T], F32, kind="ExternalInput").ap()
    lng_t = nc.dram_tensor("lng_t", [128, NDT], F32, kind="ExternalInput").ap()
    lnb_t = nc.dram_tensor("lnb_t", [128, NDT], F32, kind="ExternalInput").ap()
    bng_t = nc.dram_tensor("bng_t", [128, NDT], F32, kind="ExternalInput").ap()
    bnb_t = nc.dram_tensor("bnb_t", [128, NDT], F32, kind="ExternalInput").ap()
    bnm_t = nc.dram_tensor("bnm_t", [128, NDT], F32, kind="ExternalInput").ap()
    bnv_t = nc.dram_tensor("bnv_t", [128, NDT], F32, kind="ExternalInput").ap()
    Wr = nc.dram_tensor("Wr", [D, 8], F32, kind="ExternalInput").ap()
    br_row = nc.dram_tensor("br_row", [1, 8], F32, kind="ExternalInput").ap()
    Wdn = nc.dram_tensor("Wdn", [8 * D, 64], F32, kind="ExternalInput").ap()
    bdn = nc.dram_tensor("bdn", [8, 64], F32, kind="ExternalInput").ap()
    Wup = nc.dram_tensor("Wup", [8 * 64, D], F32, kind="ExternalInput").ap()
    bup = nc.dram_tensor("bup", [8, D], F32, kind="ExternalInput").ap()
    x4T = fctx["xout"] if fctx else nc.dram_tensor("x4T", [D, T], F32, kind="ExternalOutput").ap()
    z_h = nc.dram_tensor("z_h", [NCHUNK * D, CH], BF16).ap()
    oh_d = nc.dram_tensor("oh_d", [8, 1], F32).ap()
    rdbg = nc.dram_tensor("rdbg", [128, 32], F32, kind="ExternalOutput" if dbg else "Internal").ap()

    with contextlib.ExitStack() as ctx:
        pref = fctx["tag"] if fctx else ""
        sb = lambda name, p_, f, dt=BF16: ctx.enter_context(nc.sbuf_tensor(pref + name, [p_, f], dt))
        c_onesf = sb("c_onesf", 128, 1, F32)
        c_onesrf = sb("c_onesrf", 1, 128, F32)
        c_zer = sb("c_zer", 64, CH)
        lng_s = sb("lng_s", 128, NDT, F32); lnb_s = sb("lnb_s", 128, NDT, F32)
        bng_s = sb("bng_s", 128, NDT, F32); bnb_s = sb("bnb_s", 128, NDT, F32)
        bnm_s = sb("bnm_s", 128, NDT, F32); bnv_s = sb("bnv_s", 128, NDT, F32)
        wr_s = sb("wr_s", 128, NDT * 8, F32)
        br_s = sb("br_s", 1, 8, F32)
        wdn_s = sb("wdn_s", 128, 8 * NDT * 64)
        bdn_s = sb("bdn_s", 8, 64, F32)
        wup_s = sb("wup_s", 64, 8 * D)
        bup_s = sb("bup_s", 8, D, F32)
        sxf = sb("sxf", 128, NDT * CH, F32)
        xnf = sb("xnf", 128, NDT * CH, F32)
        sqf = [sb(f"sqf{i}", 128, CH, F32) for i in range(2)]
        zbf = sb("zbf", 128, NDT * CH)
        tmpf = sb("tmpf", 128, CH, F32)
        mrow = sb("mrow", 1, CH, F32)
        rrow = sb("rrow", 1, 2 * CH, F32)
        zsum = sb("zsum", 128, NDT, F32)
        zsum_all = sb("zsum_all", 128, NDT * NCHUNK, F32)
        zbn = sb("zbn", 128, NDT, F32)
        l8f = sb("l8f", 1, 24, F32)
        oh_row = sb("oh_row", 1, 8, F32)
        ohT = sb("ohT", 8, 1, F32)
        wsel = [sb(f"wsel{i}", 128, NDT * 64) for i in range(2)]
        wusel = [sb(f"wusel{i}", 64, D) for i in range(2)]
        hbf = sb("hbf", 64, CH)
        resf = tmpf

        P0 = ctx.enter_context(nc.psum_tensor(pref + "P0", [128, 2048], F32))
        P2 = ctx.enter_context(nc.psum_tensor(pref + "P2", [128, 512], F32))
        P3 = ctx.enter_context(nc.psum_tensor(pref + "P3", [128, 512], F32))
        P4 = ctx.enter_context(nc.psum_tensor(pref + "P4", [128, 512], F32))
        P5 = ctx.enter_context(nc.psum_tensor(pref + "P5", [128, 512], F32))

        p = fctx["p"] if fctx else P(nc, ctx)

        def ld_w(e, p_):
            for dst, srcw in ((lng_s, lng_t), (lnb_s, lnb_t), (bng_s, bng_t),
                              (bnb_s, bnb_t), (bnm_s, bnm_t), (bnv_s, bnv_t)):
                p_.dma(e, out=dst[:, :], in_=srcw)
            p_.dma(e, out=wr_s[:, :].rearrange("p (a n) -> p a n", a=NDT),
                   in_=Wr.rearrange("(a p) n -> p a n", p=128))
            p_.dma(e, out=br_s[0:1, :], in_=br_row)
            p_.dma(e, out=wdn_s[:, :].rearrange("p (a n) -> p a n", a=8 * NDT),
                   in_=Wdn.rearrange("(a p) n -> p a n", p=128))
            p_.dma(e, out=bdn_s[:, :], in_=bdn)
            p_.dma(e, out=wup_s[:, :].rearrange("p (a n) -> p a n", a=8),
                   in_=Wup.rearrange("(a p) n -> p a n", p=64))
            p_.dma(e, out=bup_s[:, :], in_=bup)
        s_wld = p.step("gpsimd", ld_w, n_dma=12, tag="wld")

        def consts(e, p_):
            e.memset(c_onesf[:, :], 1.0)
            e.memset(c_onesrf[0:1, :], 1.0)
            e.memset(c_zer[:, :], 0.0)
            return e.memset(zsum[:, :], 0.0)
        s_const = p.step("vector", consts, tag="consts")

        def xcopy(e, p_):
            p_.dma(e, out=x4T, in_=xT)
        s_xcopy = p.step("sync", xcopy, n_dma=1, tag="xcopy")

        bca_m = P0[:, 0:CH]
        bca_r = P0[:, CH:2 * CH]

        s_zdma = [None] * NCHUNK
        prev = dict(statm=None, bca=None, stats_pe=None, xn=None, z=None)
        s_sqs = {}
        s_pes = {}

        def emit_stats(tagp, src_buf, extra_dep=None):
            for dt in range(NDT):
                def sq(e, p_, dt=dt, src_buf=src_buf):
                    return e.tensor_tensor(sqf[dt % 2][:, :], src_buf[:, dt * CH:(dt + 1) * CH],
                                           src_buf[:, dt * CH:(dt + 1) * CH], ALU.mult)
                s_sq = p.step("vector", sq, tag=f"{tagp}_sq{dt}",
                              deps=[extra_dep, s_pes.get((tagp, dt - 2)),
                                    prev["stats_pe"] if dt < 2 else None])
                s_sqs[(tagp, dt)] = s_sq

                def pe2(e, p_, dt=dt, src_buf=src_buf):
                    e.matmul(P4[0:1, :], c_onesf[:, :], src_buf[:, dt * CH:(dt + 1) * CH],
                             start=(dt == 0), stop=(dt == NDT - 1))
                    return e.matmul(P5[0:1, :], c_onesf[:, :], sqf[dt % 2][:, :],
                                    start=(dt == 0), stop=(dt == NDT - 1))
                s_pes[(tagp, dt)] = p.step("tensor", pe2, tag=f"{tagp}_pe{dt}",
                                           deps=[s_sq, s_const,
                                                 prev["statm"] if dt == 0 else None])
            s_stats = s_pes[(tagp, NDT - 1)]
            prev["stats_pe"] = s_stats

            def statm1(e, p_):
                e.tensor_scalar(mrow[0:1, :], P4[0:1, :], 1.0 / D, None, ALU.mult)
                e.tensor_scalar(rrow[0:1, 0:CH], P5[0:1, :], 1.0 / D, None, ALU.mult)
                e.tensor_tensor(rrow[0:1, CH:], mrow[0:1, :], mrow[0:1, :], ALU.mult)
                e.tensor_tensor(rrow[0:1, 0:CH], rrow[0:1, 0:CH], rrow[0:1, CH:], ALU.subtract)
                return e.tensor_scalar(rrow[0:1, 0:CH], rrow[0:1, 0:CH], 1e-5, None, ALU.add)
            s_sm = p.step("vector", statm1, deps=[s_stats, prev["bca"]], tag=f"{tagp}_sm")

            def sqrtf(e, p_):
                return e.activation(rrow[0:1, CH:], rrow[0:1, 0:CH], AF.Sqrt)
            s_sq2 = p.step("scalar", sqrtf, deps=[s_sm], tag=f"{tagp}_sqrt")

            def recipf(e, p_):
                return e.reciprocal(rrow[0:1, 0:CH], rrow[0:1, CH:])
            s_rc = p.step("vector", recipf, deps=[s_sq2], tag=f"{tagp}_rec")
            prev["statm"] = s_rc

            def bcaf(e, p_):
                e.matmul(bca_m, c_onesrf[0:1, :], mrow[0:1, :], start=True, stop=True)
                return e.matmul(bca_r, c_onesrf[0:1, :], rrow[0:1, 0:CH], start=True, stop=True)
            s_b = p.step("tensor", bcaf, deps=[s_rc, prev["xn"], prev["z"]], tag=f"{tagp}_bca")
            prev["bca"] = s_b
            return s_b

        for c in range(NCHUNK):
            def ldx(e, p_, cc=c):
                p_.dma(e, out=sxf[:, :].rearrange("p (a n) -> p a n", a=NDT),
                       in_=xT[:, cc * CH:(cc + 1) * CH].rearrange("(a p) n -> p a n", p=128))
            s_ldx = p.step("sync", ldx, n_dma=1, tag=f"F{c}_ldx",
                           deps=[s_sqs.get((f"F{c-1}s1", NDT - 1)), prev["stats_pe"],
                                 prev["xn"]])

            s_b1 = emit_stats(f"F{c}s1", sxf, extra_dep=s_ldx)

            def xn(e, p_):
                for dt in range(NDT):
                    e.tensor_tensor(tmpf[:, :], sxf[:, dt * CH:(dt + 1) * CH], bca_m, ALU.subtract)
                    ins = e.tensor_tensor(xnf[:, dt * CH:(dt + 1) * CH], tmpf[:, :], bca_r, ALU.mult)
                return ins
            s_xn = p.step("vector", xn, deps=[s_b1, prev["z"]], tag=f"F{c}_xn")
            prev["xn"] = s_xn

            s_b2 = emit_stats(f"F{c}s2", xnf, extra_dep=s_xn)

            def zf(e, p_, cc=c):
                def red(dt):
                    return e.tensor_reduce(
                        zsum_all[:, dt * NCHUNK + cc:dt * NCHUNK + cc + 1],
                        zbf[:, dt * CH:(dt + 1) * CH], AX.X, ALU.add)
                for dt in range(NDT):
                    e.tensor_tensor(tmpf[:, :], xnf[:, dt * CH:(dt + 1) * CH], bca_m, ALU.subtract)
                    e.tensor_tensor(tmpf[:, :], tmpf[:, :], bca_r, ALU.mult)
                    e.tensor_scalar(zbf[:, dt * CH:(dt + 1) * CH], tmpf[:, :],
                                    lng_s[:, dt:dt + 1], lnb_s[:, dt:dt + 1],
                                    ALU.mult, ALU.add)
                    if dt > 0:
                        red(dt - 1)
                return red(NDT - 1)
            s_z = p.step("vector", zf, deps=[s_b2, s_zdma[c - 1] if c else None, s_wld],
                         tag=f"F{c}_z")
            prev["z"] = s_z

            def zdma(e, p_, cc=c):
                p_.dma(e, out=z_h[cc * D:(cc + 1) * D, :].rearrange("(a p) n -> p a n", p=128),
                       in_=zbf[:, :].rearrange("p (a n) -> p a n", a=NDT))
            s_zdma[c] = p.step("gpsimd", zdma, n_dma=1, deps=[s_z], tag=f"F{c}_zdma")

        zsum2 = sb("zsum2", 128, NDT, F32)
        tmpq = sb("tmpq", 128, 3 * NDT, F32)
        padb = [sb(f"padb{i}", 128, 1024) for i in range(2)]

        def padv(e):
            return e.tensor_copy(padb[0][:, :], padb[1][:, :])

        def zbn1(e, p_):
            e.tensor_reduce(zsum[:, :], zsum_all[:, :].rearrange("p (a n) -> p a n", n=NCHUNK),
                            AX.X, ALU.add)
            e.tensor_scalar(tmpq[:, 0:NDT], bnv_s[:, :], 1e-5, None, ALU.add)
            return padv(e)
        s_z1 = p.step("vector", zbn1, deps=[prev["z"], prev["statm"]], tag="zbn1")

        def zbn2(e, p_):
            e.copy(zsum2[:, :], zsum[:, :])
            e.activation(tmpq[:, NDT:2 * NDT], tmpq[:, 0:NDT], AF.Ln)
            return e.activation(tmpq[:, 2 * NDT:3 * NDT], tmpq[:, NDT:2 * NDT],
                                AF.Exp, scale=-0.5)
        s_z2 = p.step("scalar", zbn2, deps=[s_z1], tag="zbn2")

        zbn2b = sb("zbn2b", 128, NDT, F32)

        def zbn5(e, p_):
            e.tensor_scalar(zbn[:, :], zsum2[:, :], 1.0 / T, None, ALU.mult)
            padv(e)
            e.tensor_tensor(zbn2b[:, :], zbn[:, :], bnm_s[:, :], ALU.subtract)
            padv(e)
            e.tensor_tensor(zbn[:, :], zbn2b[:, :], bng_s[:, :], ALU.mult)
            padv(e)
            e.tensor_tensor(zbn2b[:, :], zbn[:, :], tmpq[:, 2 * NDT:3 * NDT], ALU.mult)
            padv(e)
            e.tensor_tensor(zbn[:, :], zbn2b[:, :], bnb_s[:, :], ALU.add)
            return padv(e)
        s_z3 = p.step("vector", zbn5, deps=[s_z2], tag="zbn5")

        def l8_pe(e, p_):
            for dt in range(NDT):
                e.matmul(P0[0:1, 1024:1032], zbn[:, dt:dt + 1], wr_s[:, dt * 8:(dt + 1) * 8],
                         start=(dt == 0), stop=False)
            return e.matmul(P0[0:1, 1024:1032], c_onesf[0:1, 0:1], br_s[0:1, :],
                            start=False, stop=True)
        s_l8 = p.step("tensor", l8_pe, deps=[s_z3, prev["bca"], s_wld], tag="l8")

        def l8m(e, p_):
            e.tensor_copy(l8f[0:1, 0:8], P0[0:1, 1024:1032])
            e.tensor_reduce(l8f[0:1, 8:9], P0[0:1, 1024:1032], AX.X, ALU.max)
            return padv(e)
        s_l8m = p.step("vector", l8m, deps=[s_l8], tag="l8m")

        def mxcp(e, p_):
            return e.copy(l8f[0:1, 20:21], l8f[0:1, 8:9])
        s_mx = p.step("scalar", mxcp, deps=[s_l8m], tag="mxcp")

        def oh(e, p_):
            e.tensor_tensor(oh_row[0:1, :], l8f[0:1, 0:8],
                            l8f[0:1, 20:21].broadcast_to((1, 8)), ALU.is_ge)
            e.tensor_tensor(l8f[0:1, 9:17], l8f[0:1, 0:8],
                            l8f[0:1, 20:21].broadcast_to((1, 8)), ALU.subtract)
            return padv(e)
        s_oh = p.step("vector", oh, deps=[s_mx], tag="oh")

        def ohdma1(e, p_):
            p_.dma(e, out=oh_d, in_=oh_row[0:1, :])
        s_ohd1 = p.step("gpsimd", ohdma1, n_dma=1, deps=[s_oh], tag="ohd1")

        def ohdma2(e, p_):
            p_.dma(e, out=ohT[:, :], in_=oh_d)
        s_ohd2 = p.step("gpsimd", ohdma2, n_dma=1, deps=[s_ohd1], tag="ohd2")

        def gexp(e, p_):
            return e.activation(l8f[0:1, 9:17], l8f[0:1, 9:17], AF.Exp)
        s_ge = p.step("scalar", gexp, deps=[s_oh], tag="gexp")

        def gsum(e, p_):
            e.tensor_reduce(l8f[0:1, 17:18], l8f[0:1, 9:17], AX.X, ALU.add)
            return padv(e)
        s_gs = p.step("vector", gsum, deps=[s_ge], tag="gsum")

        def grec(e, p_):
            e.activation(l8f[0:1, 18:19], l8f[0:1, 17:18], AF.Ln)
            return e.activation(l8f[0:1, 19:20], l8f[0:1, 18:19], AF.Exp, scale=-1.0)
        s_gr = p.step("scalar", grec, deps=[s_gs], tag="grec")

        def sel_pe(e, p_):
            e.matmul(P0[:, 1024:1032], c_onesrf[0:1, :], oh_row[0:1, :], start=True, stop=True)
            e.matmul(P0[0:64, 1537:1538], bdn_s[:, :], ohT[:, :], start=True, stop=True)
            for dt in range(NDT):
                e.matmul(P2[:, 8 + dt:9 + dt], bup_s[:, dt * 128:(dt + 1) * 128], ohT[:, :],
                         start=(dt == 0), stop=False)
            return e.matmul(P2[:, 16:17], c_onesrf[0:1, :], l8f[0:1, 19:20],
                            start=False, stop=True)
        s_sel = p.step("tensor", sel_pe, deps=[s_oh, s_gr, s_ohd2], tag="sel_pe")

        selc_s = sb("selc_s", 128, 10, F32)

        def sel_ev(e, p_):
            e.copy(selc_s[:, 0:9], P2[:, 8:17])
            return e.copy(selc_s[0:64, 9:10], P0[0:64, 1537:1538])
        s_selev = p.step("scalar", sel_ev, deps=[s_sel], tag="sel_ev")

        if dbg:
            def dmpr(e, p_):
                e.tensor_copy(sqf[0][0:1, 0:21], l8f[0:1, 0:21])
                e.tensor_copy(sqf[0][:, 21:22], zsum2[:, 0:1])
                e.tensor_copy(sqf[0][:, 22:23], tmpq[:, 0:1])
                e.tensor_copy(sqf[0][:, 23:24], tmpq[:, 2 * NDT:2 * NDT + 1])
                e.tensor_copy(sqf[0][0:1, 24:32], oh_row[0:1, :])
                e.tensor_copy(sqf[0][:, 30:31], zbn[:, 0:1])
                return e.tensor_copy(sqf[0][:, 31:32], zsum[:, 0:1])
            s_dmp = p.step("vector", dmpr, deps=[s_gr, s_oh, s_z3], tag="dmpr")
            def dmpd(e, p_):
                p_.dma(e, out=rdbg[:, :], in_=sqf[0][:, 0:32])
            s_dmpd = p.step("gpsimd", dmpd, n_dma=1, deps=[s_dmp], tag="dmpd")

        s_ws = None
        for ei in range(8):
            def wsel_acc(e, p_, ei=ei):
                oc = P0[:, 1024 + ei:1025 + ei]
                if ei == 0:
                    e.tensor_scalar(wsel[0][:, :], wdn_s[:, ei * 512:(ei + 1) * 512],
                                    oc, None, ALU.mult)
                    return e.tensor_scalar(wusel[0][:, :], wup_s[:, ei * D:(ei + 1) * D],
                                           oc[0:64, :], None, ALU.mult)
                src, dst = wsel[(ei - 1) % 2], wsel[ei % 2]
                usrc, udst = wusel[(ei - 1) % 2], wusel[ei % 2]
                e.scalar_tensor_tensor(dst[:, :], wdn_s[:, ei * 512:(ei + 1) * 512],
                                       oc, src[:, :], ALU.mult, ALU.add)
                return e.scalar_tensor_tensor(udst[:, :], wup_s[:, ei * D:(ei + 1) * D],
                                              oc[0:64, :], usrc[:, :], ALU.mult, ALU.add)
            s_ws = p.step("vector", wsel_acc, deps=[s_sel if ei == 0 else s_ws],
                          tag=f"wsel{ei}")
        wfin = wsel[7 % 2]
        ufin = wusel[7 % 2]

        s_cout = None
        s_hev = None
        s_yev = None
        for c in range(NCHUNK):
            def ldz(e, p_, cc=c):
                p_.dma(e, out=zbf[:, :].rearrange("p (a n) -> p a n", a=NDT),
                       in_=z_h[cc * D:(cc + 1) * D, :].rearrange("(a p) n -> p a n", p=128))
            s_ldz = p.step("sync", ldz, n_dma=1, tag=f"G{c}_ldz",
                           deps=[s_zdma[c], s_zdma[NCHUNK - 1] if c == 0 else s_hev])

            def h_pe(e, p_):
                for dt in range(NDT):
                    ins = e.matmul(P3[0:64, :], wfin[:, dt * 64:(dt + 1) * 64],
                                   zbf[:, dt * CH:(dt + 1) * CH],
                                   start=(dt == 0), stop=(dt == NDT - 1))
                return ins
            s_hpe = p.step("tensor", h_pe, deps=[s_ldz, s_ws, s_hev], tag=f"G{c}_hpe")

            def h_ev(e, p_):
                return e.scalar_tensor_tensor(hbf[:, :], P3[0:64, :], selc_s[0:64, 9:10],
                                              c_zer[:, :], ALU.add, ALU.max)
            s_hev = p.step("vector", h_ev, deps=[s_hpe, s_yev, s_selev], tag=f"G{c}_hev")

            for dt in range(NDT):
                def y_pe(e, p_, dt=dt):
                    yo = P4[:, :] if dt % 2 == 0 else P5[:, :]
                    return e.matmul(yo, ufin[:, dt * 128:(dt + 1) * 128], hbf[:, :],
                                    start=True, stop=True)
                dep_war = s_yev if dt >= 2 else (prev["statm"] if c == 0 else None)
                s_ype = p.step("tensor", y_pe, deps=[s_hev, dep_war], tag=f"G{c}_ype{dt}")

                def y_ev(e, p_, dt=dt):
                    yo = P4[:, :] if dt % 2 == 0 else P5[:, :]
                    return e.tensor_scalar(resf[:, :], yo, selc_s[:, dt:dt + 1],
                                           selc_s[:, 8:9], ALU.add, ALU.mult)
                s_yev = p.step("vector", y_ev, deps=[s_ype, s_cout, s_selev], tag=f"G{c}_yev{dt}")

                def c_out(e, p_, dt=dt, cc=c):
                    p_.dma(e, out=x4T[dt * 128:(dt + 1) * 128, cc * CH:(cc + 1) * CH],
                           in_=resf[:, :], accum_op=ALU.add)
                s_cout = p.step("gpsimd", c_out, n_dma=1, deps=[s_yev, s_xcopy],
                                tag=f"G{c}_out{dt}")

        p.wait_only("gpsimd", [s_cout, s_xcopy], tag="final")
        if fctx is not None:
            p.fence()
            return nc
        with nc.Block() as block:
            p.emit(block)
    return nc


def host_l1_ffn_inputs(ln_g, ln_b, bn_g, bn_b, bn_mean, bn_var, Wr, br,
                       W_down, b_down, W_up, b_up, xs):
    def col(v):
        return np.ascontiguousarray(np.asarray(v, np.float32).reshape(NDT, 128).T)
    maps = []
    for c in range(NC):
        maps.append({
            "xT": np.ascontiguousarray(np.asarray(xs[c], np.float32).T),
            "lng_t": col(ln_g), "lnb_t": col(ln_b), "bng_t": col(bn_g),
            "bnb_t": col(bn_b), "bnm_t": col(bn_mean), "bnv_t": col(bn_var),
            "Wr": np.asarray(Wr, np.float32),
            "br_row": np.asarray(br, np.float32).reshape(1, 8),
            "Wdn": np.asarray(W_down, np.float32).reshape(8 * D, 64),
            "bdn": np.asarray(b_down, np.float32),
            "Wup": np.asarray(W_up, np.float32).reshape(8 * 64, D),
            "bup": np.asarray(b_up, np.float32),
        })
    return maps
NEXP = 8
CAPE = 1024
NGRP = 32
INVB = float(1 << 23)


def build_l0_moe(dbg=False, upto="all", fctx=None):
    nc = fctx["nc"] if fctx else bass.Bass("TRN2", target_bir_lowering=False, debug=False, detect_race_conditions=False)
    xT = fctx["xin"] if fctx else nc.dram_tensor("xT", [D, T], F32, kind="ExternalInput").ap()
    rmsw_t = nc.dram_tensor("rmsw_t", [128, NDT], F32, kind="ExternalInput").ap()
    gwt = nc.dram_tensor("gwt", [D, 64], F32, kind="ExternalInput").ap()
    w13s = nc.dram_tensor("w13s", [NEXP * D, D], F32, kind="ExternalInput").ap()
    w2s = nc.dram_tensor("w2s", [NEXP * 512, D], F32, kind="ExternalInput").ap()
    swg = nc.dram_tensor("swg", [D, 512], F32, kind="ExternalInput").ap()
    swu = nc.dram_tensor("swu", [D, 512], F32, kind="ExternalInput").ap()
    swd = nc.dram_tensor("swd", [512, D], F32, kind="ExternalInput").ap()
    identb = nc.dram_tensor("identb", [128, 128], BF16, kind="ExternalInput").ap()
    iotabig = nc.dram_tensor("iotabig", [128, 512], F32, kind="ExternalInput").ap()
    tokv = nc.dram_tensor("tokv", [128, 256], F32, kind="ExternalInput").ap()
    idx8q = nc.dram_tensor("idx8q", [128, 1], I32, kind="ExternalInput").ap()
    idxrt = nc.dram_tensor("idxrt", [128, 32], I32, kind="ExternalInput").ap()
    core8 = nc.dram_tensor("core8", [128, 1], F32, kind="ExternalInput").ap()
    identf = nc.dram_tensor("identf", [128, 128], F32, kind="ExternalInput").ap()
    x2T = fctx["xout"] if fctx else nc.dram_tensor("x2T", [D, T], F32, kind="ExternalOutput").ap()
    if dbg:
        rdbg = nc.dram_tensor("rdbg", [128, 96], F32, kind="ExternalOutput").ap()
    h_h = nc.dram_tensor("h_h", [NCHUNK * D, CH], F32).ap()
    sh_h = nc.dram_tensor("sh_h", [NCHUNK * D, CH], BF16).ap()
    xnn_b = nc.dram_tensor("xnn_b", [T, D], BF16)
    xn_full = nc.dram_tensor("xn_full", [NC * T, D], BF16)
    lg_b = nc.dram_tensor("lg_b", [T, 64], F32)
    lg_full = nc.dram_tensor("lg_full", [NC * T, 64], F32)
    rt2 = nc.dram_tensor("rt2", [NC * T, 8], F32,
                         kind="ExternalOutput" if dbg else "Internal").ap()
    mo_h = nc.dram_tensor("mo_h", [T, D], BF16).ap()
    table = nc.dram_tensor("table", [NEXP * CAPE, 1], F32).ap()
    yb_b = nc.dram_tensor("yb_b", [NEXP * CAPE, D], BF16)
    yb_fullA = nc.dram_tensor("yb_fullA", [NC * NEXP // 2 * CAPE, D], BF16)
    yb_fullB = nc.dram_tensor("yb_fullB", [NC * NEXP // 2 * CAPE, D], BF16)
    if dbg:
        ydbgA = nc.dram_tensor("ydbgA", [NC * NEXP // 2 * CAPE, D], BF16, kind="ExternalOutput").ap()
        ydbgB = nc.dram_tensor("ydbgB", [NC * NEXP // 2 * CAPE, D], BF16, kind="ExternalOutput").ap()

    with contextlib.ExitStack() as ctx:
        pref = fctx["tag"] if fctx else ""
        sb = lambda name, p_, f, dt=BF16: ctx.enter_context(nc.sbuf_tensor(pref + name, [p_, f], dt))
        c_ones = sb("c_ones", 128, 1)
        c_onesrf = sb("c_onesrf", 1, 128, F32)
        c_identb = sb("c_identb", 128, 128)
        c_identf = sb("c_identf", 128, 128, F32)
        c_iotab = sb("c_iotab", 128, 512, F32)
        c_tokv = sb("c_tokv", 128, 256, F32)
        c_idx8q = sb("c_idx8q", 128, 1, I32)
        c_idxrt = sb("c_idxrt", 128, 32, I32)
        c_core8 = sb("c_core8", 128, 1, F32)
        c_zer64 = sb("c_zer64", 64, 1024)
        rmsw_s = sb("rmsw_s", 128, NDT, F32)
        gwt_s = sb("gwt_s", 128, NDT * 64, F32)
        swg_s = [sb(f"swg{i}", 128, 512) for i in range(NDT)]
        swu_s = [sb(f"swu{i}", 128, 512) for i in range(NDT)]
        swd_s = [sb(f"swd{i}", 128, D) for i in range(4)]
        padb = [sb(f"padb{i}", 128, 1024) for i in range(2)]
        dum = sb("dum", 128, 16)
        sxb = sb("sxb", 128, NDT * CH)
        sqp = [sb(f"sqp{i}", 128, CH) for i in range(2)]
        hF = sb("hF", 128, NDT * CH, F32)
        xnp = [sb(f"xnp{i}", 128, CH, F32) for i in range(2)]
        xnb = sb("xnb", 128, NDT * CH)
        tmpf = sb("tmpf", 128, CH, F32)
        mrow = sb("mrow", 1, CH, F32)
        rrow = sb("rrow", 1, 2 * CH, F32)
        lgev = sb("lgev", 128, 256, F32)
        gsb = sb("gsb", 128, 4 * CH)
        hub = sb("hub", 128, 4 * CH)
        shp = [sb(f"shp{i}", 128, CH) for i in range(2)]
        xtn = sb("xtn", 128, D)
        lgb2 = [sb(f"lgb{i}", 128, 512, F32) for i in range(2)]
        lgb = lgb2[0]
        eq1 = sb("eq1", 128, 512, F32)
        eq2 = sb("eq2", 128, 512, F32)
        uw = sb("uw", 128, 512, F32)
        Ab = sb("Ab", 128, 512)
        AT = sb("AT", 64, 1024)
        CT = sb("CT", 64, 1024, F32)
        Cx = sb("Cx", 128, 512, F32)
        carry = sb("carry", 64, 2, F32)
        m1B = sb("m1B", 128, 256, F32)
        m2B = sb("m2B", 128, 256, F32)
        e1B = sb("e1B", 128, 256, F32)
        e2B = sb("e2B", 128, 256, F32)
        p1B = sb("p1B", 128, 256, F32)
        p2B = sb("p2B", 128, 256, F32)
        wA = sb("wA", 128, 512, F32)
        gA = sb("gA", 128, 512, F32)
        gB = sb("gB", 128, 512, F32)
        pxi = sb("pxi", 128, 512, I32)
        pxi2 = sb("pxi2", 128, 512, I32)
        rtpack = sb("rtpack", 128, 2048, F32)
        ix1 = sb("ix1", 128, 512, I32)
        ix2 = sb("ix2", 128, 512, I32)
        zc = sb("zc", 128, 512, F32)
        zrow = sb("zrow", 128, 16, F32)
        tblf = sb("tblf", 128, 64, F32)
        tbli = sb("tbli", 128, 64, I32)
        xbn = sb("xbn", 128, D)
        xbT = sb("xbT", 128, NDT * D)
        w13b = sb("w13b", 128, NDT * D)
        w2b = xnb
        hTb = sb("hTb", 128, 4 * D)
        ybT = sb("ybT", 128, NDT * D)
        ybn = sb("ybn", 128, D)
        r1b = sxb[:, CH * 2:CH * 4]
        r2b = sxb[:, CH * 4:CH * 6]
        rtc = sb("rtc", 128, 32, F32)
        rti = sb("rti", 128, 16, I32)
        mot4 = xnb
        mdt = sb("mdt", 128, CH)
        x1c = hF[:, 0:2 * CH]
        hc = hF[:, 2 * CH:4 * CH]
        shc = sxb[:, 0:2 * CH]

        P0 = ctx.enter_context(nc.psum_tensor(pref + "P0", [128, 1024], F32))
        P2 = ctx.enter_context(nc.psum_tensor(pref + "P2", [128, 512], F32))
        P3 = ctx.enter_context(nc.psum_tensor(pref + "P3", [128, 512], F32))
        P4 = ctx.enter_context(nc.psum_tensor(pref + "P4", [128, 512], F32))
        P5 = ctx.enter_context(nc.psum_tensor(pref + "P5", [128, 512], F32))
        PT0 = ctx.enter_context(nc.psum_tensor(pref + "PT0", [128, 1024], BF16))
        PT1 = ctx.enter_context(nc.psum_tensor(pref + "PT1", [128, 1024], BF16))

        cc1 = ctx.enter_context(nc.semaphore("cc1"))
        cc2 = ctx.enter_context(nc.semaphore("cc2"))
        cc3 = ctx.enter_context(nc.semaphore("cc3"))
        cc4 = ctx.enter_context(nc.semaphore("cc4"))

        p = fctx["p"] if fctx else P(nc, ctx)
        regs = {}

        def padv(e):
            return e.tensor_copy(padb[0][:, :], padb[1][:, :])

        def ld_w(e, p_):
            p_.dma(e, out=rmsw_s[:, :], in_=rmsw_t)
            p_.dma(e, out=c_identb[:, :], in_=identb)
            p_.dma(e, out=c_identf[:, :], in_=identf)
            p_.dma(e, out=c_iotab[:, :], in_=iotabig)
            p_.dma(e, out=c_tokv[:, :], in_=tokv)
            p_.dma(e, out=c_idx8q[:, :], in_=idx8q)
            p_.dma(e, out=c_idxrt[:, :], in_=idxrt)
            p_.dma(e, out=c_core8[:, :], in_=core8)
            p_.dma(e, out=gwt_s[:, :].rearrange("p (a n) -> p a n", a=NDT),
                   in_=gwt.rearrange("(a p) n -> p a n", p=128))
            for i in range(NDT):
                p_.dma(e, out=swg_s[i][:, :], in_=swg[i * 128:(i + 1) * 128, :])
                p_.dma(e, out=swu_s[i][:, :], in_=swu[i * 128:(i + 1) * 128, :])
            for i in range(4):
                p_.dma(e, out=swd_s[i][:, :], in_=swd[i * 128:(i + 1) * 128, :])
        s_wld = p.step("gpsimd", ld_w, n_dma=9 + 2 * NDT + 4, tag="wld")

        def mkregs(e, p_):
            regs["r32k"] = e.to_reg(NC * T - 1)
            regs["r64k"] = e.to_reg(64 * CAPE - 1)
            regs["r8k"] = e.to_reg(NEXP * CAPE - 1)
            return e.memset(dum[:, :], 0.0)
        s_regs = p.step("gpsimd", mkregs, tag="regs")

        def consts(e, p_):
            e.memset(c_ones[:, :], 1.0)
            e.memset(c_onesrf[0:1, :], 1.0)
            e.memset(c_zer64[:, :], 0.0)
            e.memset(carry[:, :], 0.0)
            e.memset(padb[1][:, :], 0.0)
            return e.memset(zc[:, :], 0.0)
        s_const = p.step("vector", consts, tag="consts")

        def tblzero(e, p_):
            p_.dma(e, out=table.rearrange("(q a) b -> q a b", q=128).rearrange("q a b -> q (a b)"),
                   in_=zc[:, 0:NEXP * CAPE // 128])
        s_tblz = p.step("gpsimd", tblzero, n_dma=1, deps=[s_const], tag="tblz")

        bca_m = P0[:, 0:CH]
        bca_r = P0[:, CH:2 * CH]
        prev = dict(statm=None, stats_pe=None)
        s_sqs, s_pes, s_hdma, s_shdma = {}, {}, {}, {}
        s_xnl = [None] * NCHUNK
        s_lgdma = [None] * NCHUNK
        s_hlast = [None] * NCHUNK
        s_lge_prev = None
        s_sh_prev = None
        s_ptchain = None
        s_xtnd_prev = None
        s_bca_prev = None
        s_lgdma_last = None
        s_xtnd_all = []

        for c in range(NCHUNK):
            def ldxb(e, p_, cc=c):
                p_.dma(e, out=sxb[:, :].rearrange("p (a n) -> p a n", a=NDT),
                       in_=xT[:, cc * CH:(cc + 1) * CH].rearrange("(a p) n -> p a n", p=128))
            s_ldxb = p.step("gpsimd", ldxb, n_dma=1, tag=f"M1{c}_ldxb",
                            deps=[s_hlast[c - 1] if c else None, prev["stats_pe"]])

            for dt in range(NDT):
                def sq(e, p_, dt=dt):
                    return e.tensor_tensor(sqp[dt % 2][:, :], sxb[:, dt * CH:(dt + 1) * CH],
                                           sxb[:, dt * CH:(dt + 1) * CH], ALU.mult)
                s_sqs[(c, dt)] = p.step("vector", sq, tag=f"M1{c}_sq{dt}",
                                        deps=[s_ldxb, s_pes.get((c, dt - 2)),
                                              prev["stats_pe"] if dt < 2 else None])

                def pe2(e, p_, dt=dt):
                    e.matmul(P4[0:1, :], c_ones[:, :], sxb[:, dt * CH:(dt + 1) * CH],
                             start=(dt == 0), stop=(dt == NDT - 1))
                    return e.matmul(P5[0:1, :], c_ones[:, :], sqp[dt % 2][:, :],
                                    start=(dt == 0), stop=(dt == NDT - 1))
                s_pes[(c, dt)] = p.step("tensor", pe2, tag=f"M1{c}_pe{dt}",
                                        deps=[s_sqs[(c, dt)], s_const,
                                              prev["statm"] if dt == 0 else None])
            s_stats_pe = s_pes[(c, NDT - 1)]
            prev["stats_pe"] = s_stats_pe

            def statm1(e, p_):
                e.tensor_scalar(mrow[0:1, :], P4[0:1, :], 1.0 / D, None, ALU.mult)
                e.tensor_scalar(rrow[0:1, 0:CH], P5[0:1, :], 1.0 / D, None, ALU.mult)
                e.tensor_tensor(rrow[0:1, CH:], mrow[0:1, :], mrow[0:1, :], ALU.mult)
                e.tensor_tensor(rrow[0:1, 0:CH], rrow[0:1, 0:CH], rrow[0:1, CH:], ALU.subtract)
                return e.tensor_scalar(rrow[0:1, 0:CH], rrow[0:1, 0:CH], 1e-5, None, ALU.add)
            s_sm1 = p.step("vector", statm1, deps=[s_stats_pe, s_bca_prev], tag=f"M1{c}_sm1")

            def sqrtf(e, p_):
                return e.activation(rrow[0:1, CH:], rrow[0:1, 0:CH], AF.Sqrt)
            s_sq2 = p.step("scalar", sqrtf, deps=[s_sm1], tag=f"M1{c}_sqrt")

            def recipf(e, p_):
                return e.reciprocal(rrow[0:1, 0:CH], rrow[0:1, CH:])
            s_rc = p.step("vector", recipf, deps=[s_sq2], tag=f"M1{c}_rec")
            prev["statm"] = s_rc

            def bcaf(e, p_):
                e.matmul(bca_m, c_onesrf[0:1, :], mrow[0:1, :], start=True, stop=True)
                return e.matmul(bca_r, c_onesrf[0:1, :], rrow[0:1, 0:CH], start=True, stop=True)
            s_bca = p.step("tensor", bcaf, deps=[s_rc, s_sh_prev], tag=f"M1{c}_bca")
            s_bca_prev = s_bca

            s_h = None
            for dt in range(NDT):
                def hstep(e, p_, dt=dt):
                    e.tensor_tensor(tmpf[:, :], sxb[:, dt * CH:(dt + 1) * CH], bca_m, ALU.subtract)
                    e.tensor_tensor(hF[:, dt * CH:(dt + 1) * CH], tmpf[:, :], bca_r, ALU.mult)
                    return e.tensor_tensor(sqp[dt % 2][:, :], hF[:, dt * CH:(dt + 1) * CH],
                                           hF[:, dt * CH:(dt + 1) * CH], ALU.mult)
                s_h = p.step("vector", hstep, tag=f"M1{c}_h{dt}",
                             deps=[s_bca, s_pes.get((c, "r", dt - 2)),
                                   s_hdma.get((c - 1, dt))])

                def hdma(e, p_, dt=dt, cc=c):
                    p_.dma(e, out=h_h[cc * D + dt * 128: cc * D + (dt + 1) * 128, :],
                           in_=hF[:, dt * CH:(dt + 1) * CH])
                s_hdma[(c, dt)] = p.step("gpsimd", hdma, n_dma=1, deps=[s_h], tag=f"M1{c}_hd{dt}")

                def rpe(e, p_, dt=dt):
                    return e.matmul(P5[0:1, :], c_ones[:, :], sqp[dt % 2][:, :],
                                    start=(dt == 0), stop=(dt == NDT - 1))
                s_pes[(c, "r", dt)] = p.step("tensor", rpe, deps=[s_h], tag=f"M1{c}_rpe{dt}")
            s_hlast[c] = s_h

            def rmsrow(e, p_):
                return e.tensor_scalar(rrow[0:1, 0:CH], P5[0:1, :], 1.0 / D, 1e-6,
                                       ALU.mult, ALU.add)
            s_rr = p.step("vector", rmsrow, deps=[s_pes[(c, "r", NDT - 1)]], tag=f"M1{c}_rr")
            prev["statm"] = s_rr

            def rmsact(e, p_):
                e.activation(rrow[0:1, CH:], rrow[0:1, 0:CH], AF.Ln)
                return e.activation(rrow[0:1, 0:CH], rrow[0:1, CH:], AF.Exp, scale=-0.5)
            s_ra = p.step("scalar", rmsact, deps=[s_rr], tag=f"M1{c}_ra")

            def r2bca(e, p_):
                return e.matmul(P4[:, :], c_onesrf[0:1, :], rrow[0:1, 0:CH],
                                start=True, stop=True)
            s_r2b = p.step("tensor", r2bca, deps=[s_ra, s_sm1], tag=f"M1{c}_r2b")

            s_lgpe = {}
            s_xc = None
            for dt in range(NDT):
                def xns(e, p_, dt=dt):
                    e.tensor_tensor(xnp[dt % 2][:, :], hF[:, dt * CH:(dt + 1) * CH],
                                    P4[:, :], ALU.mult)
                    return e.tensor_scalar(xnp[dt % 2][:, :], xnp[dt % 2][:, :],
                                           rmsw_s[:, dt:dt + 1], None, ALU.mult)
                s_x = p.step("vector", xns, deps=[s_r2b, s_lgpe.get(dt - 2)], tag=f"M1{c}_xn{dt}")

                def xnc(e, p_, dt=dt):
                    return e.tensor_copy(xnb[:, dt * CH:(dt + 1) * CH], xnp[dt % 2][:, :])
                s_xc = p.step("vector", xnc, deps=[s_x], tag=f"M1{c}_xnc{dt}")

                def lgpe(e, p_, dt=dt):
                    for sub in range(4):
                        ins = e.matmul(P2[:, sub * 64:(sub + 1) * 64],
                                       xnp[dt % 2][:, sub * 128:(sub + 1) * 128],
                                       gwt_s[:, dt * 64:(dt + 1) * 64],
                                       start=(dt == 0 and sub == 0),
                                       stop=(dt == NDT - 1 and sub == 3))
                    return ins
                s_lgpe[dt] = p.step("tensor", lgpe, tag=f"M1{c}_lg{dt}",
                                    deps=[s_x, s_wld, s_lge_prev if dt < 2 else None])
            s_xnl[c] = s_xc

            def lgevf(e, p_):
                return e.tensor_copy(lgev[:, :], P2[:, 0:256])
            s_lge = p.step("vector", lgevf, deps=[s_lgpe[NDT - 1], s_lgdma_last], tag=f"M1{c}_lgev")
            s_lge_prev = s_lge

            def lgdma(e, p_, cc=c):
                p_.dma(e, out=lg_b.ap()[cc * CH:(cc + 1) * CH, :].rearrange("(s q) e -> q s e", q=128),
                       in_=lgev[:, :].rearrange("q (s e) -> q s e", e=64))
            s_lgdma_last = p.step("gpsimd", lgdma, n_dma=1, deps=[s_lge], tag=f"M1{c}_lgd")
            s_lgdma[c] = s_lgdma_last

            s_uev = {}
            for m in range(4):
                def gu_pe(e, p_, m=m):
                    for kt in range(NDT):
                        e.matmul(P0[:, 0:CH], swg_s[kt][:, m * 128:(m + 1) * 128],
                                 xnb[:, kt * CH:(kt + 1) * CH],
                                 start=(kt == 0), stop=(kt == NDT - 1))
                    for kt in range(NDT):
                        ins = e.matmul(P0[:, CH:2 * CH], swu_s[kt][:, m * 128:(m + 1) * 128],
                                       xnb[:, kt * CH:(kt + 1) * CH],
                                       start=(kt == 0), stop=(kt == NDT - 1))
                    return ins
                s_gu = p.step("tensor", gu_pe, tag=f"M1{c}_gu{m}",
                              deps=[s_xnl[c], s_uev.get(m - 1)])

                def gsilu(e, p_, m=m):
                    return e.activation(gsb[:, m * CH:(m + 1) * CH], P0[:, 0:CH], AF.Silu)
                s_gs = p.step("scalar", gsilu, deps=[s_gu], tag=f"M1{c}_gs{m}")

                def uevf(e, p_, m=m):
                    return e.tensor_tensor(hub[:, m * CH:(m + 1) * CH],
                                           gsb[:, m * CH:(m + 1) * CH],
                                           P0[:, CH:2 * CH], ALU.mult)
                s_uev[m] = p.step("vector", uevf, deps=[s_gu, s_gs], tag=f"M1{c}_ue{m}")

            s_she_hist = {}
            for mo in range(NDT):
                def sh_pe(e, p_, mo=mo):
                    PS = P3 if mo % 2 == 0 else P4
                    for kt in range(4):
                        ins = e.matmul(PS[:, :], swd_s[kt][:, mo * 128:(mo + 1) * 128],
                                       hub[:, kt * CH:(kt + 1) * CH],
                                       start=(kt == 0), stop=(kt == 3))
                    return ins
                s_shp = p.step("tensor", sh_pe, tag=f"M1{c}_shp{mo}",
                               deps=[s_uev[3], s_she_hist.get(mo - 2),
                                     s_sh_prev if mo == 0 else None])

                def sh_ev(e, p_, mo=mo):
                    PS = P3 if mo % 2 == 0 else P4
                    return e.tensor_copy(shp[mo % 2][:, :], PS[:, :])
                s_she = p.step("vector", sh_ev, deps=[s_shp, s_shdma.get((c, mo - 2))],
                               tag=f"M1{c}_she{mo}")
                s_she_hist[mo] = s_she
                s_sh_prev = s_she

                def sh_dma(e, p_, mo=mo, cc=c):
                    p_.dma(e, out=sh_h[cc * D + mo * 128: cc * D + (mo + 1) * 128, :],
                           in_=shp[mo % 2][:, :])
                s_shdma[(c, mo)] = p.step("gpsimd", sh_dma, n_dma=1, deps=[s_she],
                                          tag=f"M1{c}_shd{mo}")

            for sub in range(4):
                s_evl = None
                for dt2 in range(0, NDT, 2):
                    def trx(e, p_, sub=sub, dt2=dt2):
                        e.transpose(PT0[:, 0:128],
                                    xnb[:, dt2 * CH + sub * 128: dt2 * CH + sub * 128 + 128],
                                    c_identb[:, :])
                        return e.transpose(PT1[:, 0:128],
                                           xnb[:, (dt2 + 1) * CH + sub * 128: (dt2 + 1) * CH + sub * 128 + 128],
                                           c_identb[:, :])
                    s_tr = p.step("tensor", trx, tag=f"M2{c}_t{sub}_{dt2}",
                                  deps=[s_xnl[c], s_ptchain])

                    def trev(e, p_, dt2=dt2):
                        e.tensor_copy(xtn[:, dt2 * 128:(dt2 + 1) * 128], PT0[:, 0:128])
                        return e.tensor_copy(xtn[:, (dt2 + 1) * 128:(dt2 + 2) * 128], PT1[:, 0:128])
                    s_evl = p.step("vector", trev, tag=f"M2{c}_e{sub}_{dt2}",
                                   deps=[s_tr, s_xtnd_prev if dt2 == 0 else None])
                    s_ptchain = s_evl

                def xtdma(e, p_, sub=sub, cc=c):
                    p_.dma(e, out=xnn_b.ap()[cc * CH + sub * 128: cc * CH + (sub + 1) * 128, :],
                           in_=xtn[:, :])
                s_xtnd_prev = p.step("gpsimd", xtdma, n_dma=1, deps=[s_evl], tag=f"M2{c}_d{sub}")
                s_xtnd_all.append(s_xtnd_prev)

        def agi(e, p_):
            e.collective_compute("AllGather", ALU.bypass,
                                 replica_groups=[list(range(NC))],
                                 ins=[lg_b.ap()], outs=[lg_full.ap()]).then_inc(cc2)
            e.collective_compute("AllGather", ALU.bypass,
                                 replica_groups=[list(range(NC))],
                                 ins=[xnn_b.ap()], outs=[xn_full.ap()]).then_inc(cc1)
            return e.memset(dum[:, :], 1.0)
        s_agi = p.step("gpsimd", agi, deps=[s_xtnd_all[-1], s_lgdma[NCHUNK - 1]], tag="agi")
        s_ag_xn = (cc1, 1)
        s_ag_lg = (cc2, 1)

        def _early_exit():
            def dummy_out(e, p_):
                for cc in range(NCHUNK):
                    for dt in range(NDT):
                        p_.dma(e, out=x2T[dt * 128:(dt + 1) * 128, cc * CH:(cc + 1) * CH],
                               in_=zc[:, :])
            s_do = p.step("sync", dummy_out, n_dma=64, deps=[s_const], tag="dummy_out")
            p.wait_only("sync", [s_do, (p.dsem["gpsimd"], p.dma_val["gpsimd"])], tag="fin")
            with nc.Block() as block:
                p.emit(block)

        if upto == "M2":
            def agwait(e, p_):
                e.wait_ge(cc1, 1)
                return e.memset(dum[:, :], 3.0)
            p.step("gpsimd", agwait, deps=[s_ag_lg], tag="agw")
            _early_exit()
            return nc

        s_pos_prev = None
        s_eq1_prev = None
        s_eq2_prev = None
        s_ccp_prev = None
        s_pos_hist = {}
        s_tA_hist = {}
        s_evl2_prev = None
        for g in range(NGRP):
            lgb = lgb2[g % 2]

            def lgld(e, p_, g=g, lgb=lgb):
                p_.dma(e, out=lgb[:, :].rearrange("q (a n) -> q a n", n=64),
                       in_=lg_full.ap()[g * 1024:(g + 1) * 1024, :].rearrange("(a q) n -> q a n", q=128))
            s_lgld = p.step("sync", lgld, n_dma=1, tag=f"M3{g}_ld",
                            deps=[s_ag_lg, s_pos_hist.get(g - 1)])

            def r_m1(e, p_, g=g, lgb=lgb):
                e.tensor_reduce(m1B[:, g * 8:(g + 1) * 8],
                                lgb[:, :].rearrange("q (a n) -> q a n", n=64), AX.X, ALU.max)
                return padv(e)
            s_m1 = p.step("vector", r_m1, deps=[s_lgld], tag=f"M3{g}_m1")

            def c_m1(e, p_, g=g):
                return e.copy(zrow[:, 0:8], m1B[:, g * 8:(g + 1) * 8])
            s_cm1 = p.step("scalar", c_m1, deps=[s_m1, s_eq1_prev], tag=f"M3{g}_cm1")

            def f_eq1(e, p_, g=g, lgb=lgb):
                for t_ in range(8):
                    e.tensor_scalar(eq1[:, t_ * 64:(t_ + 1) * 64], lgb[:, t_ * 64:(t_ + 1) * 64],
                                    zrow[:, t_:t_ + 1], None, ALU.is_ge)
                e.scalar_tensor_tensor(lgb[:, :], eq1[:, :], -1e30, lgb[:, :],
                                       ALU.mult, ALU.add)
                e.tensor_reduce(m2B[:, g * 8:(g + 1) * 8],
                                lgb[:, :].rearrange("q (a n) -> q a n", n=64), AX.X, ALU.max)
                return padv(e)
            s_eq1 = p.step("vector", f_eq1, deps=[s_cm1], tag=f"M3{g}_eq1")
            s_eq1_prev = s_eq1

            def c_m2(e, p_, g=g):
                return e.copy(zrow[:, 8:16], m2B[:, g * 8:(g + 1) * 8])
            s_cm2 = p.step("scalar", c_m2, deps=[s_eq1, s_eq2_prev], tag=f"M3{g}_cm2")

            def f_eq2(e, p_, g=g, lgb=lgb):
                for t_ in range(8):
                    e.tensor_scalar(eq2[:, t_ * 64:(t_ + 1) * 64], lgb[:, t_ * 64:(t_ + 1) * 64],
                                    zrow[:, 8 + t_:9 + t_], None, ALU.is_ge)
                e.tensor_tensor(Ab[:, :], eq1[:, :], eq2[:, :], ALU.add)
                e.scalar_tensor_tensor(uw[:, :], eq1[:, :], -float(1 << 20), c_iotab[:, :],
                                       ALU.mult, ALU.add)
                e.tensor_reduce(e1B[:, g * 8:(g + 1) * 8],
                                uw[:, :].rearrange("q (a n) -> q a n", n=64), AX.X, ALU.min)
                e.scalar_tensor_tensor(uw[:, :], eq2[:, :], -float(1 << 20), c_iotab[:, :],
                                       ALU.mult, ALU.add)
                e.tensor_reduce(e2B[:, g * 8:(g + 1) * 8],
                                uw[:, :].rearrange("q (a n) -> q a n", n=64), AX.X, ALU.min)
                return padv(e)
            s_eq2 = p.step("vector", f_eq2, deps=[s_cm2, s_tA_hist.get(g - 1)],
                           tag=f"M3{g}_eq2")
            s_eq2_prev = s_eq2

            s_evl = None
            for t2 in range(0, 8, 2):
                def trA(e, p_, t2=t2):
                    e.transpose(PT0[0:64, 0:128], Ab[:, t2 * 64:(t2 + 1) * 64], c_identb[:, :])
                    return e.transpose(PT1[0:64, 0:128], Ab[:, (t2 + 1) * 64:(t2 + 2) * 64],
                                       c_identb[:, :])
                s_tA = p.step("tensor", trA, deps=[s_eq2, s_ptchain], tag=f"M3{g}_tA{t2}")
                s_tA_hist[g] = s_tA

                def trAe(e, p_, t2=t2):
                    e.tensor_copy(AT[:, t2 * 128:(t2 + 1) * 128], PT0[0:64, 0:128])
                    return e.tensor_copy(AT[:, (t2 + 1) * 128:(t2 + 2) * 128], PT1[0:64, 0:128])
                s_evl = p.step("vector", trAe, deps=[s_tA], tag=f"M3{g}_tAe{t2}")
                s_ptchain = s_evl

            def scan(e, p_):
                e.tensor_tensor_scan(CT[:, :], AT[:, :], c_zer64[:, :], carry[:, 0:1],
                                     ALU.add, ALU.add)
                padv(e)
                e.tensor_copy(carry[:, 1:2], CT[:, 1023:1024])
                padv(e)
                e.tensor_tensor(CT[:, :], CT[:, :], AT[:, :], ALU.subtract)
                return padv(e)
            s_scan = p.step("vector", scan, deps=[s_evl, s_ccp_prev], tag=f"M3{g}_scan")

            def carrycp(e, p_):
                return e.copy(carry[:, 0:1], carry[:, 1:2])
            s_ccp_prev = p.step("scalar", carrycp, deps=[s_scan], tag=f"M3{g}_ccp")

            s_evl2 = s_evl2_prev
            for t2 in range(0, 8, 2):
                def trC(e, p_, t2=t2):
                    e.transpose(P2[0:128, 0:64], CT[0:64, t2 * 128:(t2 + 1) * 128],
                                c_identf[0:64, 0:64])
                    return e.transpose(P3[0:128, 0:64], CT[0:64, (t2 + 1) * 128:(t2 + 2) * 128],
                                       c_identf[0:64, 0:64])
                s_tC = p.step("tensor", trC, deps=[s_scan, s_evl2], tag=f"M3{g}_tC{t2}")

                def trCe(e, p_, t2=t2):
                    e.tensor_copy(Cx[:, t2 * 64:(t2 + 1) * 64], P2[0:128, 0:64])
                    return e.tensor_copy(Cx[:, (t2 + 1) * 64:(t2 + 2) * 64], P3[0:128, 0:64])
                s_evl2 = p.step("vector", trCe, deps=[s_tC], tag=f"M3{g}_tCe{t2}")

            def posf(e, p_, g=g):
                e.tensor_tensor(uw[:, :], Cx[:, :], eq1[:, :], ALU.mult)
                e.tensor_reduce(p1B[:, g * 8:(g + 1) * 8],
                                uw[:, :].rearrange("q (a n) -> q a n", n=64), AX.X, ALU.add)
                e.tensor_tensor(uw[:, :], Cx[:, :], eq2[:, :], ALU.mult)
                e.tensor_reduce(p2B[:, g * 8:(g + 1) * 8],
                                uw[:, :].rearrange("q (a n) -> q a n", n=64), AX.X, ALU.add)
                return padv(e)
            s_pos_prev = p.step("vector", posf, deps=[s_evl2], tag=f"M3{g}_pos")
            s_pos_hist[g] = s_pos_prev
            s_evl2_prev = s_evl2

        def fin1(e, p_):
            e.tensor_tensor(gA[:, 0:256], m1B[:, :], m2B[:, :], ALU.subtract)
            return padv(e)
        s_f1 = p.step("vector", fin1, deps=[s_pos_prev], tag="fin1")

        def fin2(e, p_):
            e.activation(wA[:, 0:256], gA[:, 0:256], AF.Sigmoid)
            return e.activation(wA[:, 256:512], gA[:, 0:256], AF.Sigmoid, scale=-1.0)
        s_f2 = p.step("scalar", fin2, deps=[s_f1], tag="fin2")

        def fin3(e, p_):
            e.tensor_scalar(gA[:, 0:256], e1B[:, :], 1024.0, None, ALU.mult)
            e.tensor_scalar(gA[:, 256:512], e2B[:, :], 1024.0, None, ALU.mult)
            padv(e)
            e.tensor_tensor(gA[:, 0:256], gA[:, 0:256], p1B[:, :], ALU.add)
            e.tensor_tensor(gA[:, 256:512], gA[:, 256:512], p2B[:, :], ALU.add)
            padv(e)
            e.tensor_scalar(gB[:, 0:256], p1B[:, :], 1023.5, None, ALU.is_gt)
            e.tensor_scalar(gB[:, 256:512], p2B[:, :], 1023.5, None, ALU.is_gt)
            padv(e)
            e.scalar_tensor_tensor(gB[:, :], gB[:, :], INVB, gA[:, :], ALU.mult, ALU.add)
            return padv(e)
        s_f3 = p.step("vector", fin3, deps=[s_f2], tag="fin3")

        def fin4(e, p_):
            e.tensor_copy(eq1[:, 0:256], e1B[:, :])
            e.tensor_copy(eq1[:, 256:512], e2B[:, :])
            e.tensor_copy(eq2[:, 0:256], p1B[:, :])
            e.tensor_copy(eq2[:, 256:512], p2B[:, :])
            padv(e)
            e.tensor_scalar(uw[:, :], eq1[:, :], c_core8[:, 0:1], None, ALU.subtract)
            padv(e)
            e.tensor_scalar(Cx[:, :], uw[:, :], -0.5, None, ALU.is_lt)
            e.tensor_scalar(lgb[:, :], uw[:, :], 7.5, None, ALU.is_gt)
            padv(e)
            e.tensor_tensor(Cx[:, :], Cx[:, :], lgb[:, :], ALU.add)
            e.tensor_scalar(lgb[:, :], eq2[:, :], 1023.5, None, ALU.is_gt)
            padv(e)
            e.tensor_tensor(Cx[:, :], Cx[:, :], lgb[:, :], ALU.add)
            padv(e)
            e.tensor_copy(pxi[:, :], eq2[:, :])
            e.tensor_copy(pxi2[:, :], eq2[:, :])
            padv(e)
            e.tensor_scalar(pxi[:, :], pxi[:, :], 127, None, ALU.bitwise_and)
            e.tensor_scalar(pxi2[:, :], pxi2[:, :], 7, None, ALU.logical_shift_right)
            padv(e)
            e.tensor_scalar(pxi[:, :], pxi[:, :], 6, None, ALU.logical_shift_left)
            padv(e)
            e.tensor_tensor(pxi[:, :], pxi[:, :], pxi2[:, :], ALU.add)
            e.tensor_copy(pxi2[:, :], uw[:, :])
            padv(e)
            e.tensor_scalar(pxi2[:, :], pxi2[:, :], 3, None, ALU.arith_shift_left)
            padv(e)
            e.tensor_tensor(pxi[:, :], pxi[:, :], pxi2[:, :], ALU.add)
            e.tensor_copy(pxi2[:, :], Cx[:, :])
            padv(e)
            e.tensor_scalar(pxi2[:, :], pxi2[:, :], 20, None, ALU.logical_shift_left)
            padv(e)
            e.tensor_tensor(pxi[:, :], pxi[:, :], pxi2[:, :], ALU.add)
            return padv(e)
        s_f4 = p.step("vector", fin4, deps=[s_f3], tag="fin4")

        def fin4b(e, p_):
            e.tensor_copy(ix1[:, :], gB[:, :])
            padv(e)
            e.tensor_scalar(pxi2[:, :], ix1[:, :], 12, None, ALU.logical_shift_right)
            padv(e)
            e.tensor_scalar(pxi2[:, :], pxi2[:, :], 1, None, ALU.bitwise_and)
            e.tensor_scalar(ix2[:, :], ix1[:, :], 13, None, ALU.logical_shift_right)
            padv(e)
            e.tensor_scalar(pxi2[:, :], pxi2[:, :], 12, None, ALU.logical_shift_left)
            e.tensor_scalar(ix2[:, :], ix2[:, :], 12, None, ALU.logical_shift_left)
            padv(e)
            e.tensor_tensor(ix1[:, :], ix1[:, :], ix2[:, :], ALU.subtract)
            padv(e)
            e.tensor_tensor(ix1[:, :], ix1[:, :], pxi2[:, :], ALU.subtract)
            e.tensor_scalar(pxi2[:, :], pxi2[:, :], 11, None, ALU.logical_shift_left)
            padv(e)
            e.tensor_tensor(ix2[:, :], ix1[:, :], pxi2[:, :], ALU.add)
            e.tensor_scalar(ix1[:, :], ix1[:, :], 1 << 23, None, ALU.add)
            padv(e)
            e.tensor_tensor(ix1[:, :], ix1[:, :], pxi2[:, :], ALU.subtract)
            return padv(e)
        s_f4b = p.step("vector", fin4b, deps=[s_f4], tag="fin4b")

        def fin5(e, p_):
            rp = rtpack[:, :].rearrange("q (t f) -> q t f", f=8)
            one = lambda v: v.rearrange("q (t o) -> q t o", o=1)
            e.tensor_copy(rp[:, :, 0:1], one(ix2[:, 0:256]))
            e.tensor_copy(rp[:, :, 1:2], one(ix1[:, 0:256]))
            e.tensor_copy(rp[:, :, 2:3], one(ix2[:, 256:512]))
            e.tensor_copy(rp[:, :, 3:4], one(ix1[:, 256:512]))
            e.tensor_copy(rp[:, :, 4:5], one(wA[:, 0:256]))
            e.tensor_copy(rp[:, :, 5:6], one(wA[:, 256:512]))
            e.tensor_copy(rp[:, :, 6:7], one(wA[:, 0:256]))
            e.tensor_copy(rp[:, :, 7:8], one(wA[:, 256:512]))
            return padv(e)
        s_f5 = p.step("vector", fin5, deps=[s_f4b], tag="fin5")

        def rtdma(e, p_):
            p_.dma(e, out=rt2.rearrange("(q t) f -> q (t f)", q=128),
                   in_=rtpack[:, :])
        s_rtd = p.step("gpsimd", rtdma, n_dma=1, deps=[s_f5], tag="rtdma")

        def scatter_all(e, p_):
            for t_ in range(256):
                for ch_ in range(2):
                    e.indirect_dma_start(
                        out=table,
                        out_offset=IndirectOffsetOnAxis(pxi[:, ch_ * 256 + t_: ch_ * 256 + t_ + 1], 0),
                        in_=c_tokv[:, t_:t_ + 1], in_offset=None,
                        bounds_check=regs["r8k"], oob_is_err=False,
                    ).then_inc(p_.dsem["gpsimd"], 16)
                    p_._dma_seen += 1
        s_sc = p.step("gpsimd", scatter_all, n_dma=512, deps=[s_f4, s_tblz, s_regs], tag="scat")

        def tblgather(e, p_):
            e.indirect_dma_start(
                out=tblf[:, :], out_offset=None,
                in_=table.rearrange("(x o) b -> x (o b)", o=64),
                in_offset=IndirectOffsetOnAxis(c_idx8q[:, :], 0),
            ).then_inc(p_.dsem["gpsimd"], 16)
            p_._dma_seen += 1
        s_tg = p.step("gpsimd", tblgather, n_dma=1, deps=[s_sc], tag="tblg")

        def tokidx(e, p_):
            e.tensor_scalar(tblf[:, :], tblf[:, :], -1.0, None, ALU.add)
            padv(e)
            e.tensor_scalar(uw[:, 0:64], tblf[:, :], -0.5, None, ALU.is_lt)
            padv(e)
            e.scalar_tensor_tensor(tblf[:, :], uw[:, 0:64], 1e9, tblf[:, :], ALU.mult, ALU.add)
            padv(e)
            e.tensor_copy(tbli[:, :], tblf[:, :])
            return padv(e)
        s_ti = p.step("vector", tokidx, deps=[s_tg], tag="toki")

        if upto == "M4":
            p.wait_only("vector", [s_ti], tag="tiw")
            _early_exit()
            return nc

        if dbg:
            def dbg2(e, p_):
                p_.dma(e, out=rdbg[:, 0:8], in_=m1B[:, 0:8])
                p_.dma(e, out=rdbg[:, 8:16], in_=m2B[:, 0:8])
                p_.dma(e, out=rdbg[:, 16:24], in_=e1B[:, 0:8])
                p_.dma(e, out=rdbg[:, 24:32], in_=e2B[:, 0:8])
                p_.dma(e, out=rdbg[:, 32:40], in_=p1B[:, 0:8])
                p_.dma(e, out=rdbg[:, 40:48], in_=p2B[:, 0:8])
                p_.dma(e, out=rdbg[:, 48:56], in_=wA[:, 0:8])
                p_.dma(e, out=rdbg[:, 56:64], in_=gB[:, 0:8])
                p_.dma(e, out=rdbg[:, 64:96], in_=tblf[:, 0:32])
            p.step("sync", dbg2, n_dma=9, deps=[s_ti, s_f5], tag="dbgd")

        s_dn_last = None
        s_gu_last = None
        s_trx_last = None
        s_ytr_last = None
        s_ybdma_last = None
        for el in range(NEXP):
            def wline(e, p_, el=el):
                p_.dma(e, out=w13b[:, :].rearrange("q (a n) -> q a n", a=NDT),
                       in_=w13s[el * D:(el + 1) * D, :].rearrange("(a q) n -> q a n", q=128))
                p_.dma(e, out=w2b[:, :].rearrange("q (a n) -> q a n", a=4),
                       in_=w2s[el * 512:(el + 1) * 512, :].rearrange("(a q) n -> q a n", q=128))
            s_wl = p.step("gpsimd", wline, n_dma=2, tag=f"M5{el}_w",
                          deps=[s_dn_last])

            s_txe_last = None
            for sl in range(8):
                def xgat(e, p_, el=el, sl=sl):
                    e.memset(xbn[:, :], 0.0)
                    e.indirect_dma_start(
                        out=xbn[:, :], out_offset=None,
                        in_=xn_full.ap(),
                        in_offset=IndirectOffsetOnAxis(tbli[:, el * 8 + sl: el * 8 + sl + 1], 0),
                        bounds_check=regs["r32k"], oob_is_err=False,
                    ).then_inc(p_.dsem["gpsimd"], 16)
                    p_._dma_seen += 1
                s_xg = p.step("gpsimd", xgat, n_dma=1, tag=f"M5{el}_xg{sl}",
                              deps=[s_ag_xn, s_ti, s_regs, s_trx_last])

                for dt2 in range(0, NDT, 2):
                    def trxb(e, p_, dt2=dt2):
                        e.transpose(PT0[:, 0:128], xbn[:, dt2 * 128:(dt2 + 1) * 128],
                                    c_identb[:, :])
                        return e.transpose(PT1[:, 0:128], xbn[:, (dt2 + 1) * 128:(dt2 + 2) * 128],
                                           c_identb[:, :])
                    s_txb = p.step("tensor", trxb, tag=f"M5{el}_tx{sl}_{dt2}",
                                   deps=[s_xg, s_ptchain,
                                         s_gu_last if (sl == 0 and dt2 == 0) else None])
                    s_trx_last = s_txb

                    def trxbe(e, p_, sl=sl, dt2=dt2):
                        e.tensor_copy(xbT[:, dt2 * D + sl * 128: dt2 * D + sl * 128 + 128],
                                      PT0[:, 0:128])
                        return e.tensor_copy(xbT[:, (dt2 + 1) * D + sl * 128: (dt2 + 1) * D + sl * 128 + 128],
                                             PT1[:, 0:128])
                    s_txe_last = p.step("vector", trxbe, deps=[s_txb], tag=f"M5{el}_te{sl}_{dt2}")
                    s_ptchain = s_txe_last

            s_hm_prev = None
            s_silu_prev = None
            for i2 in range(8):
                m, th = divmod(i2, 2)

                def gu2(e, p_, m=m, th=th):
                    for kt in range(NDT):
                        e.matmul(P2[:, :], w13b[:, kt * D + m * 128: kt * D + m * 128 + 128],
                                 xbT[:, kt * D + th * 512: kt * D + th * 512 + 512],
                                 start=(kt == 0), stop=(kt == NDT - 1))
                    for kt in range(NDT):
                        ins = e.matmul(P3[:, :], w13b[:, kt * D + 512 + m * 128: kt * D + 512 + m * 128 + 128],
                                       xbT[:, kt * D + th * 512: kt * D + th * 512 + 512],
                                       start=(kt == 0), stop=(kt == NDT - 1))
                    return ins
                s_g2 = p.step("tensor", gu2, tag=f"M5{el}_gu{i2}",
                              deps=[s_txe_last, s_wl, s_silu_prev, s_hm_prev])
                s_gu_last = s_g2

                def gsilu2(e, p_, i2=i2):
                    return e.activation(shp[i2 % 2][:, :], P2[:, :], AF.Silu)
                s_silu_prev = p.step("scalar", gsilu2, deps=[s_g2], tag=f"M5{el}_sl{i2}")

                def hmul(e, p_, m=m, th=th, i2=i2):
                    return e.tensor_tensor(hTb[:, m * D + th * 512: m * D + th * 512 + 512],
                                           shp[i2 % 2][:, :], P3[:, :], ALU.mult)
                s_hm_prev = p.step("vector", hmul, tag=f"M5{el}_hm{i2}",
                                   deps=[s_g2, s_silu_prev,
                                         s_dn_last if i2 == 0 else None])

            s_ybev_prev = {}
            s_ybev_last = None
            for i2 in range(16):
                m, th = divmod(i2, 2)

                def dn2(e, p_, m=m, th=th, i2=i2):
                    PS = P4 if i2 % 2 == 0 else P5
                    for kt in range(4):
                        ins = e.matmul(PS[:, :], w2b[:, kt * D + m * 128: kt * D + m * 128 + 128],
                                       hTb[:, kt * D + th * 512: kt * D + th * 512 + 512],
                                       start=(kt == 0), stop=(kt == 3))
                    return ins
                s_dn = p.step("tensor", dn2, tag=f"M5{el}_dn{i2}",
                              deps=[s_hm_prev, s_wl, s_ybev_prev.get(i2 - 2)])
                s_dn_last = s_dn

                def ybev(e, p_, m=m, th=th, i2=i2):
                    PS = P4 if i2 % 2 == 0 else P5
                    return e.tensor_copy(ybT[:, m * D + th * 512: m * D + th * 512 + 512], PS[:, :])
                s_ybev_prev[i2] = p.step("vector", ybev, tag=f"M5{el}_ye{i2}",
                                         deps=[s_dn, s_ytr_last if i2 == 0 else None])
                s_ybev_last = s_ybev_prev[i2]

            for st in range(8):
                s_evl3 = None
                for dt2 in range(0, NDT, 2):
                    def ytr(e, p_, st=st, dt2=dt2):
                        e.transpose(PT0[:, 0:128], ybT[:, dt2 * D + st * 128: dt2 * D + st * 128 + 128],
                                    c_identb[:, :])
                        return e.transpose(PT1[:, 0:128],
                                           ybT[:, (dt2 + 1) * D + st * 128: (dt2 + 1) * D + st * 128 + 128],
                                           c_identb[:, :])
                    s_yt = p.step("tensor", ytr, tag=f"M5{el}_yt{st}_{dt2}",
                                  deps=[s_ybev_last, s_ptchain])
                    s_ytr_last = s_yt

                    def ytre(e, p_, dt2=dt2):
                        e.tensor_copy(ybn[:, dt2 * 128:(dt2 + 1) * 128], PT0[:, 0:128])
                        return e.tensor_copy(ybn[:, (dt2 + 1) * 128:(dt2 + 2) * 128], PT1[:, 0:128])
                    s_evl3 = p.step("vector", ytre, tag=f"M5{el}_yv{st}_{dt2}",
                                    deps=[s_yt, s_ybdma_last if dt2 == 0 else None])
                    s_ptchain = s_evl3

                def ybdma(e, p_, el=el, st=st):
                    p_.dma(e, out=yb_b.ap()[el * CAPE + st * 128: el * CAPE + (st + 1) * 128, :],
                           in_=ybn[:, :])
                s_ybdma_last = p.step("gpsimd", ybdma, n_dma=1, deps=[s_evl3], tag=f"M5{el}_yd{st}")

            if el == NEXP // 2 - 1 and st == 7:
                def agi2a(e, p_):
                    e.collective_compute(
                        "AllGather", ALU.bypass,
                        replica_groups=[list(range(NC))],
                        ins=[yb_b.ap()[0:NEXP // 2 * CAPE, :]],
                        outs=[yb_fullA.ap()],
                    ).then_inc(cc3)
                    return e.memset(dum[:, :], 2.0)
                p.step("gpsimd", agi2a, deps=[s_ybdma_last], tag="agi2a")

        def agi2(e, p_):
            e.collective_compute(
                "AllGather", ALU.bypass,
                replica_groups=[list(range(NC))],
                ins=[yb_b.ap()[NEXP // 2 * CAPE:, :]],
                outs=[yb_fullB.ap()],
            ).then_inc(cc4)
            return e.memset(dum[:, :], 3.0)
        s_agi2 = p.step("gpsimd", agi2, deps=[s_ybdma_last], tag="agi2")
        s_ag_yb = (cc4, 1)
        s_ag_yb1 = (cc3, 1)
        if dbg:
            def ydbg(e, p_):
                p_.dma(e, out=ydbgA, in_=yb_fullA.ap())
                p_.dma(e, out=ydbgB, in_=yb_fullB.ap())
            p.step("gpsimd", ydbg, n_dma=2, deps=[s_ag_yb1, s_ag_yb], tag="ydbg")

        if upto == "M6":
            def agwait2(e, p_):
                e.wait_ge(cc3, 1)
                return e.memset(dum[:, :], 4.0)
            p.step("gpsimd", agwait2, tag="agw2")
            _early_exit()
            return nc

        s_comb_prev = None
        s_add_hist = {}
        s_out_hist = {}
        s_modma = {}
        for c in range(NCHUNK):
            def rtg(e, p_, cc=c):
                for t_ in range(4):
                    e.indirect_dma_start(
                        out=rtc[:, t_ * 8:(t_ + 1) * 8], out_offset=None,
                        in_=rt2,
                        in_offset=IndirectOffsetOnAxis(c_idxrt[:, cc * 4 + t_: cc * 4 + t_ + 1], 0),
                    ).then_inc(p_.dsem["gpsimd"], 16)
                    p_._dma_seen += 1
            s_rtg = p.step("gpsimd", rtg, n_dma=4, tag=f"M7{c}_rtg",
                           deps=[s_rtd, s_comb_prev])

            def rticv(e, p_):
                e.tensor_copy(rti[:, :].rearrange("q (t f) -> q t f", f=4),
                              rtc[:, :].rearrange("q (t f) -> q t f", f=8)[:, :, 0:4])
                return padv(e)
            s_rti = p.step("vector", rticv, deps=[s_rtg], tag=f"M7{c}_rti")

            for t_ in range(4):
                def ygat(e, p_, t_=t_):
                    e.memset(r1b[:, :], 0.0)
                    e.memset(r2b[:, :], 0.0)
                    for dst, f0 in ((r1b, 0), (r2b, 2)):
                        for buf, f in ((yb_fullA, f0), (yb_fullB, f0 + 1)):
                            e.indirect_dma_start(
                                out=dst[:, :], out_offset=None,
                                in_=buf.ap(),
                                in_offset=IndirectOffsetOnAxis(rti[:, t_ * 4 + f: t_ * 4 + f + 1], 0),
                                bounds_check=regs["r32k"], oob_is_err=False,
                            ).then_inc(p_.dsem["gpsimd"], 16)
                            p_._dma_seen += 1
                s_yg = p.step("gpsimd", ygat, n_dma=4, tag=f"M7{c}_yg{t_}",
                              deps=[s_ag_yb1, s_ag_yb, s_rti, s_comb_prev])

                def comb(e, p_, t_=t_):
                    e.tensor_scalar(mot4[:, t_ * D:(t_ + 1) * D], r1b[:, :],
                                    rtc[:, t_ * 8 + 4: t_ * 8 + 5], None, ALU.mult)
                    e.scalar_tensor_tensor(mot4[:, t_ * D:(t_ + 1) * D], r2b[:, :],
                                           rtc[:, t_ * 8 + 5: t_ * 8 + 6],
                                           mot4[:, t_ * D:(t_ + 1) * D], ALU.mult, ALU.add)
                    return padv(e)
                s_comb_prev = p.step("vector", comb, tag=f"M7{c}_cb{t_}",
                                     deps=[s_yg, s_modma.get((c - 1, t_))])

                def modma(e, p_, cc=c, t_=t_):
                    p_.dma(e, out=mo_h[cc * CH + t_ * 128: cc * CH + (t_ + 1) * 128, :],
                           in_=mot4[:, t_ * D:(t_ + 1) * D])
                s_modma[(c, t_)] = p.step("gpsimd", modma, n_dma=1, deps=[s_comb_prev],
                                          tag=f"M7{c}_md{t_}")

            if upto == "M7":
                continue
            for dt in range(NDT):
                def ldx(e, p_, cc=c, dt=dt):
                    hh = (dt % 2) * CH
                    p_.dma(e, out=x1c[:, hh:hh + CH], in_=xT[dt * 128:(dt + 1) * 128,
                                                             cc * CH:(cc + 1) * CH])
                    p_.dma(e, out=hc[:, hh:hh + CH], in_=h_h[cc * D + dt * 128: cc * D + (dt + 1) * 128, :])
                    p_.dma(e, out=shc[:, hh:hh + CH], in_=sh_h[cc * D + dt * 128: cc * D + (dt + 1) * 128, :])
                s_ldx = p.step("sync", ldx, n_dma=3, tag=f"M8{c}_ld{dt}",
                               deps=[s_add_hist.get((c, dt - 2)), s_hdma[(c, dt)], s_shdma[(c, dt)]])

                def mld(e, p_, cc=c, dt=dt):
                    p_.dma_t(e, out=mdt[:, 0:CH],
                             in_=mo_h[cc * CH:(cc + 1) * CH, dt * 128:(dt + 1) * 128])
                s_mld = p.step("sync", mld, n_dma=1, tag=f"M8{c}_mld{dt}",
                               deps=[s_modma[(c, 3)], s_add_hist.get((c, dt - 1))])

                def addf(e, p_, dt=dt):
                    hh = (dt % 2) * CH
                    outb = tmpf if dt % 2 == 0 else lgb
                    e.tensor_tensor(xnp[0][:, :], x1c[:, hh:hh + CH], hc[:, hh:hh + CH], ALU.add)
                    e.tensor_tensor(xnp[1][:, :], shc[:, hh:hh + CH], mdt[:, 0:CH], ALU.add)
                    return e.tensor_tensor(outb[:, 0:CH], xnp[0][:, :], xnp[1][:, :], ALU.add)
                s_add_hist[(c, dt)] = p.step("vector", addf, tag=f"M8{c}_add{dt}",
                                             deps=[s_ldx, s_mld, s_out_hist.get((c, dt - 2)),
                                                   s_out_hist.get((c - 1, dt + 6))])

                def outd(e, p_, cc=c, dt=dt):
                    outb = tmpf if dt % 2 == 0 else lgb
                    p_.dma(e, out=x2T[dt * 128:(dt + 1) * 128, cc * CH:(cc + 1) * CH],
                           in_=outb[:, 0:CH])
                s_out_hist[(c, dt)] = p.step("sync", outd, n_dma=1,
                                             deps=[s_add_hist[(c, dt)]], tag=f"M8{c}_o{dt}")

        if upto == "M7":
            p.wait_only("vector", [s_comb_prev], tag="cbw")
            _early_exit()
            return nc

        p.wait_only("sync", [s_out_hist[(NCHUNK - 1, NDT - 1)],
                             (p.dsem["gpsimd"], p.dma_val["gpsimd"])], tag="fin")
        if fctx is not None:
            p.fence()
            return nc
        with nc.Block() as block:
            p.emit(block)
    return nc


def host_l0_moe_inputs(x, rms_w, gate_w, w13, w2, sw_gate, sw_up, sw_down):
    import ml_dtypes
    ident_bf = np.eye(128, dtype=np.float32).astype(ml_dtypes.bfloat16)
    ident_f = np.eye(128, dtype=np.float32)
    rmsw_t = np.ascontiguousarray(np.asarray(rms_w, np.float32).reshape(NDT, 128).T)
    gwt = np.ascontiguousarray(np.asarray(gate_w, np.float32).T)
    iota = (np.arange(64, dtype=np.float32) + float(1 << 20))
    iotabig = np.ascontiguousarray(np.tile(iota, (128, 8)).astype(np.float32))
    tokv = (np.arange(256, dtype=np.float32)[None, :] * 128.0
            + np.arange(128, dtype=np.float32)[:, None] + 1.0).astype(np.float32)
    pp = np.arange(128, dtype=np.int64)
    maps = []
    for c in range(NC):
        idx8q = pp.astype(np.int32).reshape(128, 1)
        core8 = np.full((128, 1), 8.0 * c, np.float32)
        idxrt = (pp[:, None] * 256
                 + (c * 32 + np.arange(32, dtype=np.int64))[None, :]).astype(np.int32)
        maps.append({
            "xT": np.ascontiguousarray(np.asarray(x[c], np.float32).T),
            "rmsw_t": rmsw_t, "gwt": gwt,
            "w13s": np.ascontiguousarray(np.asarray(w13, np.float32)[8 * c:8 * c + 8]
                                         .reshape(NEXP * D, D)),
            "w2s": np.ascontiguousarray(np.asarray(w2, np.float32)[8 * c:8 * c + 8]
                                        .reshape(NEXP * 512, D)),
            "swg": np.asarray(sw_gate, np.float32), "swu": np.asarray(sw_up, np.float32),
            "swd": np.asarray(sw_down, np.float32),
            "identb": ident_bf, "identf": ident_f,
            "iotabig": iotabig, "tokv": tokv,
            "idx8q": idx8q, "idxrt": idxrt, "core8": core8,
        })
    return maps


# version 05
# speedup vs baseline: 1.1551x; 1.1551x over previous
import contextlib
import numpy as np

import concourse.bass as bass
import concourse.mybir as mybir
from concourse.bass_utils import run_bass_kernel_spmd
from concourse.bass_core_rust import IndirectOffsetOnAxis

F32 = mybir.dt.float32
BF16 = mybir.dt.bfloat16
I32 = mybir.dt.int32
AF = mybir.ActivationFunctionType
ALU = mybir.AluOpType
AX = mybir.AxisListType

D = 1024
T = 4096
NC = 8
CH = 512
NCHUNK = T // CH
NDT = D // 128
NKT = T // 128
NH = 16
HD = 64
VW = 65 * NH


class P:
    ENGS = ("tensor", "vector", "scalar", "gpsimd", "sync")

    def __init__(self, nc, ctx):
        self.nc = nc
        self.steps = []
        self.clk = {e: ctx.enter_context(nc.semaphore(f"clk_{e}")) for e in self.ENGS}
        self.dsem = {e: ctx.enter_context(nc.semaphore(f"dma_{e}")) for e in self.ENGS}
        self.clk_val = {e: 0 for e in self.ENGS}
        self.dma_val = {e: 0 for e in self.ENGS}
        self.waited = {e: {} for e in self.ENGS}
        self._emitting = None
        self._dma_seen = 0
        self._fence_deps = {}

    def fence(self):
        refs = ([(self.clk[e], self.clk_val[e]) for e in self.ENGS if self.clk_val[e]]
                + [(self.dsem[e], self.dma_val[e]) for e in self.ENGS if self.dma_val[e]])
        for e in self.ENGS:
            self._fence_deps[e] = refs

    def _mk_waits(self, eng, deps):
        waits = []
        for d in deps:
            if d is None:
                continue
            sem, val = d
            if val <= self.waited[eng].get(id(sem), 0):
                continue
            self.waited[eng][id(sem)] = val
            waits.append((sem, val))
        return waits

    def step(self, eng, fn, deps=(), n_dma=0, tag=""):
        deps = list(deps) + self._fence_deps.pop(eng, [])
        waits = self._mk_waits(eng, deps)
        if n_dma > 0:
            self.dma_val[eng] += 16 * n_dma
            ref = (self.dsem[eng], self.dma_val[eng])
        else:
            self.clk_val[eng] += 1
            ref = (self.clk[eng], self.clk_val[eng])
        self.steps.append(dict(eng=eng, fn=fn, waits=waits, n_dma=n_dma, tag=tag))
        return ref

    def wait_only(self, eng, deps, tag=""):
        deps = list(deps) + self._fence_deps.pop(eng, [])
        waits = self._mk_waits(eng, deps)
        self.steps.append(dict(eng=eng, fn=None, waits=waits, n_dma=0, tag=tag))

    def dma(self, e, **kw):
        ins = e.dma_start(**kw)
        ins.then_inc(self.dsem[self._emitting], 16)
        self._dma_seen += 1
        return ins

    def dma_t(self, e, out, in_):
        ins = e.dma_start_transpose(out=out, in_=in_)
        ins.then_inc(self.dsem[self._emitting], 16)
        self._dma_seen += 1
        return ins

    def emit(self, block):
        def make(eng_name):
            def run(e):
                for s in self.steps:
                    if s["eng"] != eng_name:
                        continue
                    for sem, val in s["waits"]:
                        e.wait_ge(sem, val)
                    if s["fn"] is None:
                        continue
                    self._emitting = eng_name
                    self._dma_seen = 0
                    last = s["fn"](e, self)
                    assert self._dma_seen == s["n_dma"], (
                        f"step {s['tag']}: declared {s['n_dma']} dma, emitted {self._dma_seen}")
                    if s["n_dma"] == 0:
                        assert last is not None, f"step {s['tag']} returned no instruction"
                        last.then_inc(self.clk[eng_name], 1)
            return run
        block.tensor(make("tensor"))
        block.vector(make("vector"))
        block.scalar(make("scalar"))
        block.gpsimd(make("gpsimd"))
        block.sync(make("sync"))


def build_l0_attn(upto="C", dbg=False, fctx=None):
    nc = fctx["nc"] if fctx else bass.Bass("TRN2", target_bir_lowering=False, debug=False, detect_race_conditions=False)
    xT = fctx["xin"] if fctx else nc.dram_tensor("xT", [D, T], F32, kind="ExternalInput").ap()
    Wk = nc.dram_tensor("Wk", [D, D], F32, kind="ExternalInput").ap()
    Wq = nc.dram_tensor("Wq", [D, D], F32, kind="ExternalInput").ap()
    Wo = nc.dram_tensor("Wo", [D, D], F32, kind="ExternalInput").ap()
    bo_t = nc.dram_tensor("bo_t", [128, NDT], F32, kind="ExternalInput").ap()
    ident = nc.dram_tensor("ident", [128, 128], BF16, kind="ExternalInput").ap()
    x1T = fctx["xout"] if fctx else nc.dram_tensor("x1T", [D, T], F32, kind="ExternalOutput").ap()
    xn_h = nc.dram_tensor("xn_h", [NCHUNK * D, CH], BF16).ap()
    qT_h = nc.dram_tensor("qT_h", [NCHUNK * D, CH], BF16).ap()
    ao_h = nc.dram_tensor("ao_h", [T, D], BF16, kind="ExternalOutput" if dbg else "Internal").ap()
    st_dbg = nc.dram_tensor("st_dbg", [128, 2048], F32, kind="ExternalOutput" if dbg else "Internal").ap()
    xn_dbg = nc.dram_tensor("xn_dbg", [NCHUNK * D, CH], BF16, kind="ExternalOutput" if dbg else "Internal").ap()

    with contextlib.ExitStack() as ctx:
        pref = fctx["tag"] if fctx else ""
        sb = lambda name, p_, f, dt=BF16: ctx.enter_context(nc.sbuf_tensor(pref + name, [p_, f], dt))
        c_ident = sb("c_ident", 128, 96)
        c_ones = sb("c_ones", 128, 1)
        c_onesrf = sb("c_onesrf", 1, 128, F32)
        wk_s = [sb(f"wk{i}", 128, D) for i in range(NDT)]
        wq_s = [sb(f"wq{i}", 128, D) for i in range(NDT)]
        bo_s = sb("bo_s", 128, NDT, F32)
        kT = [sb(f"kT{i}", 128, T) for i in range(NDT)]
        vbuf = sb("vbuf", 128, NKT * VW)
        sxb = sb("sxb", 128, NDT * CH)
        sqp = [sb(f"sqp{i}", 128, CH) for i in range(2)]
        xnb = sb("xnb", 128, NDT * CH)
        tmpf = sb("tmpf", 128, CH, F32)
        mrow = sb("mrow", 1, CH, F32)
        rrow = sb("rrow", 1, 2 * CH, F32)
        qcb = sb("qcb", 128, NDT * CH)
        ptb = [sb(f"ptb{i}", 128, 1024) for i in range(4)]
        anb = sxb
        recb = sb("recb", 128, 16, F32)
        resf = tmpf

        P0 = ctx.enter_context(nc.psum_tensor(pref + "P0", [128, 2048], F32))
        P2 = ctx.enter_context(nc.psum_tensor(pref + "P2", [128, 512], F32))
        P3 = ctx.enter_context(nc.psum_tensor(pref + "P3", [128, 512], F32))
        P4 = ctx.enter_context(nc.psum_tensor(pref + "P4", [128, 512], F32))
        P5 = ctx.enter_context(nc.psum_tensor(pref + "P5", [128, 512], F32))

        p = fctx["p"] if fctx else P(nc, ctx)

        def ld_w(e, p_):
            p_.dma(e, out=bo_s[:, :], in_=bo_t)
            p_.dma(e, out=c_ident[:, 0:64], in_=ident[:, 0:64])
            for i in range(NDT):
                p_.dma(e, out=wk_s[i][:, :], in_=Wk[i * 128:(i + 1) * 128, :])
                p_.dma(e, out=wq_s[i][:, :], in_=Wq[i * 128:(i + 1) * 128, :])
        s_wld = p.step("gpsimd", ld_w, n_dma=2 + 2 * NDT, tag="wld")

        def consts(e, p_):
            e.memset(c_ones[:, :], 1.0)
            e.memset(c_onesrf[0:1, :], 1.0)
            return e.memset(vbuf[:, :], 1.0)
        s_const = p.step("vector", consts, tag="consts")

        def init_ptb(e, p_):
            e.memset(ptb[0][:, :], 0.0)
            e.memset(ptb[1][:, :], 0.0)
            e.memset(ptb[2][:, :], 0.0)
            return e.memset(ptb[3][:, :], 0.0)
        s_iptb = p.step("vector", init_ptb, deps=[], tag="init_ptb")

        def warm_exp(e, p_):
            return e.activation(recb[:, 0:1], c_ones[:, 0:1], AF.Exp)
        s_warm = p.step("scalar", warm_exp, deps=[s_const], tag="warm_exp")

        def xcopy(e, p_):
            p_.dma(e, out=x1T, in_=xT)
        s_xcopy = p.step("sync", xcopy, n_dma=1, tag="xcopy")

        bca_m = P0[:, 0:CH]
        bca_r = P0[:, CH:2 * CH]
        kq_rot = [P0[:, 1024:1536], P0[:, 1536:2048], P2[:, :], P3[:, :]]
        s_xn = [None] * NCHUNK
        s_kq_ev = {}
        s_kq_pe_last = None
        s_qdma = [None] * NCHUNK
        s_xndma = [None] * NCHUNK
        prev = dict(statm=None, bca=None, stats_pe=None, sq=None)
        s_sqs = {}
        s_pes = {}

        for c in range(NCHUNK):
            def ldxb(e, p_, cc=c):
                p_.dma(e, out=sxb[:, :].rearrange("p (a n) -> p a n", a=NDT),
                       in_=xT[:, cc * CH:(cc + 1) * CH].rearrange("(a p) n -> p a n", p=128))
            s_ldxb = p.step("gpsimd", ldxb, n_dma=1, tag=f"A{c}_ldxb",
                            deps=[s_sqs.get((c - 1, NDT - 1)), prev["stats_pe"],
                                  s_xn[c - 1] if c else None])

            for dt in range(NDT):
                def sq(e, p_, dt=dt):
                    return e.tensor_tensor(sqp[dt % 2][:, :], sxb[:, dt * CH:(dt + 1) * CH],
                                           sxb[:, dt * CH:(dt + 1) * CH], ALU.mult)
                s_sqs[(c, dt)] = p.step("vector", sq, tag=f"A{c}_sq{dt}",
                                        deps=[s_ldxb, s_pes.get((c, dt - 2)),
                                              prev["stats_pe"] if dt < 2 else None])

                def pe2(e, p_, dt=dt):
                    e.matmul(P4[0:1, :], c_ones[:, :], sxb[:, dt * CH:(dt + 1) * CH],
                             start=(dt == 0), stop=(dt == NDT - 1))
                    return e.matmul(P5[0:1, :], c_ones[:, :], sqp[dt % 2][:, :],
                                    start=(dt == 0), stop=(dt == NDT - 1))
                s_pes[(c, dt)] = p.step("tensor", pe2, tag=f"A{c}_pe{dt}",
                                        deps=[s_sqs[(c, dt)], s_const,
                                              prev["statm"] if dt == 0 else None])
            s_stats_pe = s_pes[(c, NDT - 1)]
            prev["stats_pe"] = s_stats_pe

            def statm1(e, p_):
                e.tensor_scalar(mrow[0:1, :], P4[0:1, :], 1.0 / D, None, ALU.mult)
                e.tensor_scalar(rrow[0:1, 0:CH], P5[0:1, :], 1.0 / D, None, ALU.mult)
                e.tensor_tensor(rrow[0:1, CH:], mrow[0:1, :], mrow[0:1, :], ALU.mult)
                e.tensor_tensor(rrow[0:1, 0:CH], rrow[0:1, 0:CH], rrow[0:1, CH:], ALU.subtract)
                return e.tensor_scalar(rrow[0:1, 0:CH], rrow[0:1, 0:CH], 1e-5, None, ALU.add)
            s_statm1 = p.step("vector", statm1, deps=[s_stats_pe, prev["bca"]], tag=f"A{c}_sm1")

            def sqrt(e, p_):
                return e.activation(rrow[0:1, CH:], rrow[0:1, 0:CH], AF.Sqrt)
            s_sqrt = p.step("scalar", sqrt, deps=[s_statm1], tag=f"A{c}_sqrt")

            def recip(e, p_):
                return e.reciprocal(rrow[0:1, 0:CH], rrow[0:1, CH:])
            s_recip = p.step("vector", recip, deps=[s_sqrt], tag=f"A{c}_rec")
            prev["statm"] = s_recip

            def bca(e, p_):
                e.matmul(bca_m, c_onesrf[0:1, :], mrow[0:1, :], start=True, stop=True)
                return e.matmul(bca_r, c_onesrf[0:1, :], rrow[0:1, 0:CH], start=True, stop=True)
            s_bca = p.step("tensor", bca, deps=[s_recip, s_xn[c - 1] if c else None],
                           tag=f"A{c}_bca")
            prev["bca"] = s_bca

            def xn(e, p_):
                for dt in range(NDT):
                    e.tensor_tensor(tmpf[:, :], sxb[:, dt * CH:(dt + 1) * CH], bca_m, ALU.subtract)
                    ins = e.tensor_tensor(xnb[:, dt * CH:(dt + 1) * CH], tmpf[:, :], bca_r, ALU.mult)
                return ins
            s_xn[c] = p.step("vector", xn,
                             deps=[s_bca, s_xndma[c - 1] if c else None,
                                   s_kq_ev.get((c - 1, 2, 1))], tag=f"A{c}_xn")

            def xndma(e, p_, cc=c):
                p_.dma(e, out=xn_h[cc * D:(cc + 1) * D, :].rearrange("(a p) n -> p a n", p=128),
                       in_=xnb[:, :].rearrange("p (a n) -> p a n", a=NDT))
                if dbg:
                    p_.dma(e, out=xn_dbg[cc * D:(cc + 1) * D, :].rearrange("(a p) n -> p a n", p=128),
                           in_=xnb[:, :].rearrange("p (a n) -> p a n", a=NDT))
            s_xndma[c] = p.step("gpsimd", xndma, n_dma=2 if dbg else 1, deps=[s_xn[c]], tag=f"A{c}_xndma")

            for half, tag in ((0, "k"), (1, "q"), (2, "v")):
                for grp in range(2):
                    def kq_pe(e, p_, half=half, grp=grp, cc=c):
                        for j in range(4):
                            mt = grp * 4 + j
                            for ktd in range(NDT):
                                if half == 0 or half == 1:
                                    wgt = wk_s if half == 0 else wq_s
                                    ins = e.matmul(kq_rot[j],
                                                   wgt[ktd][:, mt * 128:(mt + 1) * 128],
                                                   xnb[:, ktd * CH:(ktd + 1) * CH],
                                                   start=(ktd == 0), stop=(ktd == NDT - 1))
                                else:
                                    s_, dh = mt % 4, mt // 4
                                    ins = e.matmul(kq_rot[j],
                                                   xnb[:, ktd * CH + s_ * 128:
                                                       ktd * CH + s_ * 128 + 128],
                                                   wk_s[ktd][:, dh * 512:(dh + 1) * 512],
                                                   start=(ktd == 0), stop=(ktd == NDT - 1))
                        return ins
                    dep_ev = (s_kq_ev.get((c, half, grp - 1)) if grp else
                              (s_kq_ev.get((c, half - 1, 1)) if half else
                               s_kq_ev.get((c - 1, 2, 1))))
                    s_pe = p.step("tensor", kq_pe, deps=[s_xn[c], dep_ev, s_wld],
                                  tag=f"A{c}_{tag}pe{grp}")
                    if half == 2 and grp == 1:
                        s_kq_pe_last = s_pe

                    def kq_ev(e, p_, half=half, grp=grp, cc=c):
                        for j in range(4):
                            mt = grp * 4 + j
                            if half == 0:
                                ins = e.tensor_copy(kT[mt][:, cc * CH:(cc + 1) * CH], kq_rot[j])
                            elif half == 1:
                                ins = e.tensor_copy(qcb[:, mt * CH:(mt + 1) * CH], kq_rot[j])
                            else:
                                s_, dh = mt % 4, mt // 4
                                tile = cc * 4 + s_
                                dst = vbuf[:, tile * VW + dh * 8 * 65:
                                           tile * VW + dh * 8 * 65 + 520]
                                ins = e.tensor_copy(
                                    dst.rearrange("p (h w) -> p h w", w=65)[:, :, 0:64],
                                    kq_rot[j][:, :].rearrange("p (h w) -> p h w", w=64))
                        return ins
                    dep_q = s_qdma[c - 1] if (half == 1 and grp == 0 and c) else None
                    dep_v = s_const if half == 2 else None
                    s_kq_ev[(c, half, grp)] = p.step("vector", kq_ev, deps=[s_pe, dep_q, dep_v],
                                                     tag=f"A{c}_{tag}ev{grp}")

            def qdma(e, p_, cc=c):
                p_.dma(e, out=qT_h[cc * D:(cc + 1) * D, :].rearrange("(a p) n -> p a n", p=128),
                       in_=qcb[:, :].rearrange("p (a n) -> p a n", a=NDT))
            s_qdma[c] = p.step("gpsimd", qdma, n_dma=1,
                               deps=[s_kq_ev[(c, 1, 0)], s_kq_ev[(c, 1, 1)]], tag=f"A{c}_qdma")

        s_a_done = s_kq_ev[(NCHUNK - 1, 2, 1)]

        def ld_wo(e, p_):
            for i in range(NDT):
                p_.dma(e, out=wk_s[i][:, :], in_=Wo[i * 128:(i + 1) * 128, :])
        s_wold = p.step("gpsimd", ld_wo, n_dma=NDT, deps=[s_kq_pe_last, s_a_done], tag="wold")

        if upto == "A":
            p.wait_only("gpsimd", [s_qdma[NCHUNK - 1], s_xndma[NCHUNK - 1], s_xcopy], tag="final")
            with nc.Block() as block:
                p.emit(block)
            return nc

        def ld_wo(e, p_):
            for i in range(NDT):
                p_.dma(e, out=wk_s[i][:, :], in_=Wo[i * 128:(i + 1) * 128, :])
        s_wold = p.step("gpsimd", ld_wo, n_dma=NDT, deps=[s_kq_pe_last, s_a_done], tag="wold")
        s_a2_done = s_a_done

        st = P0
        pv = [P2, P3]
        s_div = {}
        s_ao_dma = [None] * NCHUNK
        s_c_done = [None] * NCHUNK
        st_steps = []
        exp_steps = []
        pv_steps = []
        s_cev_last = [None] * NCHUNK

        for qc in range(NCHUNK):
            def qld(e, p_, qc=qc):
                p_.dma(e, out=qcb[:, :].rearrange("p (a n) -> p a n", a=NDT),
                       in_=qT_h[qc * D:(qc + 1) * D, :].rearrange("(a p) n -> p a n", p=128))
            s_qld = p.step("sync", qld, n_dma=1, tag=f"B{qc}_qld",
                           deps=[s_qdma[qc],
                                 s_qdma[NCHUNK - 1] if qc == 0 else st_steps[-1]])

            for hp in range(8):
                pend_pv = []
                for g in range(32):
                    G = (qc * 8 + hp) * 32 + g
                    sthalf = st[:, (g % 2) * 1024:(g % 2) * 1024 + 1024]

                    def st_mm(e, p_, hp=hp, g=g, sthalf=sthalf):
                        for hh in (2 * hp, 2 * hp + 1):
                            ins = e.matmul(
                                sthalf[:, (hh % 2) * 512:(hh % 2) * 512 + 512],
                                kT[hh // 2][(hh % 2) * 64:(hh % 2) * 64 + 64,
                                            g * 128:(g + 1) * 128],
                                qcb[(hh % 2) * 64:(hh % 2) * 64 + 64,
                                    (hh // 2) * CH:(hh // 2 + 1) * CH],
                                start=True, stop=True)
                        return ins
                    s_stm = p.step("tensor", st_mm,
                                   deps=[s_qld, s_a2_done,
                                         exp_steps[G - 2] if G >= 2 else None,
                                         s_div.get((qc, hp - 1)) if g == 0 else None,
                                         s_div.get((qc - 1, 7)) if (g == 0 and hp == 0) else None,
                                         s_cev_last[qc - 1] if (g == 0 and hp == 0 and qc) else None],
                                   tag=f"B{qc}_{hp}_st{g}")
                    st_steps.append(s_stm)

                    def expf(e, p_, g=g, sthalf=sthalf):
                        return e.activation(ptb[g % 4][:, :], sthalf, AF.Exp, scale=1.0 / 32)
                    s_e = p.step("scalar", expf,
                                 deps=[s_stm, pv_steps[G - 4] if G >= 4 else None],
                                 tag=f"B{qc}_{hp}_e{g}")
                    exp_steps.append(s_e)
                    if False and dbg and qc == 0 and hp == 0 and g == 2:
                        def dbg_ev(e, p_, g=g):
                            e.tensor_copy(sxb[:, 0:1024], ptb[g % 2][:, :])
                            return e.tensor_copy(sxb[:, 1024:2048],
                                                 st[:, (g % 2) * 1024:(g % 2) * 1024 + 1024])
                        s_dbg = p.step("vector", dbg_ev, deps=[s_e], tag="dbg_ev")
                        def dbg_dma(e, p_):
                            p_.dma(e, out=st_dbg[0:128, :], in_=sxb[:, 0:2048])
                        s_dbgd = p.step("gpsimd", dbg_dma, n_dma=1, deps=[s_dbg], tag="dbg_dma")
                        exp_steps[-1] = s_dbgd

                    def pv_mm(e, p_, hp=hp, g=g):
                        for hh in (2 * hp, 2 * hp + 1):
                            for sub in range(4):
                                ins = e.matmul(
                                    pv[hh % 2][:, sub * 65:(sub + 1) * 65],
                                    ptb[g % 4][:, (hh % 2) * 512 + sub * 128:
                                               (hh % 2) * 512 + sub * 128 + 128],
                                    vbuf[:, g * VW + hh * 65: g * VW + hh * 65 + 65],
                                    start=(g == 0 and sub == 0),
                                    stop=(g == 31 and sub == 3))
                        return ins
                    pend_pv.append((pv_mm, f"B{qc}_{hp}_pv{g}"))
                    if len(pend_pv) == 3:
                        fn0, tag0 = pend_pv.pop(0)
                        pv_steps.append(p.step("tensor", fn0,
                                               deps=[exp_steps[G - 1]], tag=tag0))

                def echo(e, p_, hp=hp):
                    return e.activation(sqp[0][0:1, 0:1], ptb[1][0:1, 1023:1024], AF.Exp)
                s_echo = p.step("scalar", echo, deps=[], tag=f"B{qc}_{hp}_echo")
                for fn0, tag0 in pend_pv:
                    pv_steps.append(p.step("tensor", fn0, deps=[s_echo], tag=tag0))
                pend_pv = []

                if dbg and qc == 0 and hp == 0:
                    def dbg_pv(e, p_):
                        e.tensor_copy(sxb[:, 0:512].rearrange("p (a b) -> p a b", a=1), pv[0][:, 0:512].rearrange("p (a b) -> p a b", a=1))
                        return e.tensor_copy(sxb[:, 512:1024], pv[1][:, 0:512])
                    s_dpv = p.step("vector", dbg_pv, deps=[pv_steps[-1]], tag="dbg_pv")
                    def dbg_pvd(e, p_):
                        p_.dma(e, out=st_dbg[0:128, 0:1024], in_=sxb[:, 0:1024])
                    s_dpvd = p.step("gpsimd", dbg_pvd, n_dma=1, deps=[s_dpv], tag="dbg_pvd")
                    pv_steps[-1] = s_dpvd

                def rec_act(e, p_):
                    for b in range(2):
                        e.activation(recb[:, 8 + b * 4:12 + b * 4].rearrange("p (s w) -> p s w", w=1),
                                     pv[b][:, 0:260].rearrange("p (s w) -> p s w", w=65)[:, :, 64:65],
                                     AF.Ln)
                    return e.activation(recb[:, 0:8], recb[:, 8:16], AF.Exp, scale=-1.0)
                s_rec = p.step("scalar", rec_act, deps=[pv_steps[-1]], tag=f"B{qc}_{hp}_rec")

                def div(e, p_, hp=hp):
                    for hh in (2 * hp, 2 * hp + 1):
                        for sub in range(4):
                            i = (hh % 2) * 4 + sub
                            ins = e.tensor_scalar(
                                anb[:, sub * D + hh * 64: sub * D + hh * 64 + 64],
                                pv[hh % 2][:, sub * 65:sub * 65 + 64],
                                recb[:, i:i + 1], None, ALU.mult)
                    return ins
                s_div[(qc, hp)] = p.step(
                    "vector", div,
                    deps=[s_rec, s_ao_dma[qc - 1] if qc >= 1 else None],
                    tag=f"B{qc}_{hp}_div")

            def ao_dma(e, p_, qc=qc):
                p_.dma(e, out=ao_h[qc * CH:(qc + 1) * CH, :].rearrange("(s p) d -> p s d", p=128),
                       in_=anb[:, :].rearrange("p (s d) -> p s d", s=4))
            s_ao_dma[qc] = p.step("gpsimd", ao_dma, n_dma=1,
                                  deps=[s_div[(qc, hp)] for hp in range(8)], tag=f"B{qc}_ao")

            if upto == "B":
                s_cev_last[qc] = s_ao_dma[qc]
                s_c_done[qc] = s_ao_dma[qc]
                continue
            def cldT(e, p_, qc=qc):
                for dt in range(NDT):
                    p_.dma_t(e, out=xnb[:, dt * CH:(dt + 1) * CH],
                             in_=ao_h[qc * CH:(qc + 1) * CH, dt * 128:(dt + 1) * 128])
            s_cldT = p.step("sync", cldT, n_dma=NDT,
                            deps=[s_ao_dma[qc],
                                  s_c_done[qc - 1] if qc else s_a_done,
                                  None if qc else s_xndma[NCHUNK - 1]],
                            tag=f"C{qc}_ldT")

            s_cev = None
            s_cout = None
            for dt in range(NDT):
                def c_pe(e, p_, dt=dt):
                    yo = P4[:, :] if dt % 2 == 0 else P5[:, :]
                    for ktd in range(NDT):
                        ins = e.matmul(yo, wk_s[ktd][:, dt * 128:(dt + 1) * 128],
                                       xnb[:, ktd * CH:(ktd + 1) * CH],
                                       start=(ktd == 0), stop=(ktd == NDT - 1))
                    return ins
                dep_stats = prev["statm"] if (qc == 0 and dt < 2) else None
                dep_war = s_cev if dt >= 2 else None
                s_cpe = p.step("tensor", c_pe, deps=[s_cldT, dep_stats, dep_war, s_wold],
                               tag=f"C{qc}_pe{dt}")

                def c_ev(e, p_, dt=dt):
                    yo = P4[:, :] if dt % 2 == 0 else P5[:, :]
                    return e.tensor_scalar(resf[:, :], yo, bo_s[:, dt:dt + 1], None, ALU.add)
                s_cev = p.step("vector", c_ev, deps=[s_cpe, s_cout], tag=f"C{qc}_ev{dt}")

                def c_out(e, p_, dt=dt, qc=qc):
                    p_.dma(e, out=x1T[dt * 128:(dt + 1) * 128, qc * CH:(qc + 1) * CH],
                           in_=resf[:, :], accum_op=ALU.add)
                s_cout = p.step("gpsimd", c_out, n_dma=1, deps=[s_cev, s_xcopy],
                                tag=f"C{qc}_out{dt}")
            s_cev_last[qc] = s_cev
            s_c_done[qc] = s_cout

        p.wait_only("gpsimd", [s_c_done[NCHUNK - 1], s_xcopy], tag="final")
        if fctx is not None:
            p.fence()
            return nc
        with nc.Block() as block:
            p.emit(block)
    return nc


def host_l0_attn_inputs(x, Wk, Wq, Wo, bo):
    import ml_dtypes
    ident_bf = np.eye(128, dtype=np.float32).astype(ml_dtypes.bfloat16)
    bo_t = np.ascontiguousarray(np.asarray(bo, np.float32).reshape(NDT, 128).T)
    maps = []
    for c in range(NC):
        maps.append({
            "xT": np.ascontiguousarray(np.asarray(x[c], np.float32).T),
            "Wk": np.asarray(Wk, np.float32), "Wq": np.asarray(Wq, np.float32),
            "Wo": np.asarray(Wo, np.float32),
            "bo_t": bo_t, "ident": ident_bf,
        })
    return maps




def build_fused():
    nc = bass.Bass("TRN2", target_bir_lowering=False, debug=False, detect_race_conditions=False)
    xin = nc.dram_tensor("xT", [D, T], F32, kind="ExternalInput").ap()
    x1 = nc.dram_tensor("x1b", [D, T], F32).ap()
    x2 = nc.dram_tensor("x2b", [D, T], F32).ap()
    x3 = nc.dram_tensor("x3b", [D, T], F32).ap()
    xout = nc.dram_tensor("x4T", [D, T], F32, kind="ExternalOutput").ap()
    with contextlib.ExitStack() as gctx:
        p = P(nc, gctx)
        build_l0_attn("C", fctx=dict(nc=nc, p=p, xin=xin, xout=x1, tag="a_"))
        build_l0_moe(fctx=dict(nc=nc, p=p, xin=x1, xout=x2, tag="m_"))
        build_l1_attn("C", fctx=dict(nc=nc, p=p, xin=x2, xout=x3, tag="b_"))
        build_l1_ffn(fctx=dict(nc=nc, p=p, xin=x3, xout=xout, tag="f_"))
        p.wait_only("sync", [(p.dsem[e], p.dma_val[e]) for e in P.ENGS if p.dma_val[e]]
                    + [(p.clk[e], p.clk_val[e]) for e in P.ENGS if p.clk_val[e]],
                    tag="final")
        with nc.Block() as block:
            p.emit(block)
    return nc


def host_fused_inputs(inputs):
    x = np.asarray(inputs["L0_attn_x"], np.float32)
    m1 = host_l0_attn_inputs(x, inputs["L0_attn_Wk"], inputs["L0_attn_Wq"],
                             inputs["L0_attn_Wo"], inputs["L0_attn_bo"])
    dummy = [np.zeros((1, 1), np.float32)] * NC
    m2 = host_l0_moe_inputs(dummy, inputs["L0_ffn_rms_w"], inputs["L0_ffn_gate_w"],
                            inputs["L0_ffn_w13"], inputs["L0_ffn_w2"],
                            inputs["L0_ffn_sw_gate"], inputs["L0_ffn_sw_up"],
                            inputs["L0_ffn_sw_down"])
    m3 = host_l1_attn_inputs(None, inputs["L1_attn_Wqa"], inputs["L1_attn_Wqb"],
                             inputs["L1_attn_Wkva"], inputs["L1_attn_Wkvb"],
                             inputs["L1_attn_Wout"], inputs["L1_attn_bout"], xs=dummy)
    m4 = host_l1_ffn_inputs(inputs["L1_ffn_ln_g"], inputs["L1_ffn_ln_b"],
                            inputs["L1_ffn_bn_g"], inputs["L1_ffn_bn_b"],
                            inputs["L1_ffn_bn_mean"], inputs["L1_ffn_bn_var"],
                            inputs["L1_ffn_Wr"], inputs["L1_ffn_br"],
                            inputs["L1_ffn_W_down"], inputs["L1_ffn_b_down"],
                            inputs["L1_ffn_W_up"], inputs["L1_ffn_b_up"], xs=dummy)
    maps = []
    for c in range(NC):
        for mm in (m2, m3, m4):
            mm[c].pop("xT", None)
        maps.append({**m1[c], **m2[c], **m3[c], **m4[c]})
    return maps


_prog_cache = {}


def _get_prog(name, builder):
    if name not in _prog_cache:
        _prog_cache[name] = builder()
    return _prog_cache[name]


def kernel(**inputs):
    nc = _get_prog("fused", build_fused)
    maps = host_fused_inputs(inputs)
    res = run_bass_kernel_spmd(nc, maps, core_ids=list(range(NC)))
    return np.ascontiguousarray(
        np.stack([res.results[c]["x4T"].T for c in range(NC)]).astype(np.float32))


def kernel_multilaunch(**inputs):
    x = np.asarray(inputs["L0_attn_x"], np.float32)

    nc1 = _get_prog("l0a", lambda: build_l0_attn("C"))
    maps1 = host_l0_attn_inputs(x, inputs["L0_attn_Wk"], inputs["L0_attn_Wq"],
                                inputs["L0_attn_Wo"], inputs["L0_attn_bo"])
    r1 = run_bass_kernel_spmd(nc1, maps1, core_ids=list(range(NC)))
    x1T = [r1.results[c]["x1T"] for c in range(NC)]

    nc2 = _get_prog("l0m", lambda: build_l0_moe())
    maps2 = host_l0_moe_inputs([a.T for a in x1T], inputs["L0_ffn_rms_w"],
                               inputs["L0_ffn_gate_w"], inputs["L0_ffn_w13"],
                               inputs["L0_ffn_w2"], inputs["L0_ffn_sw_gate"],
                               inputs["L0_ffn_sw_up"], inputs["L0_ffn_sw_down"])
    r2 = run_bass_kernel_spmd(nc2, maps2, core_ids=list(range(NC)))
    x2T = [r2.results[c]["x2T"] for c in range(NC)]

    nc3 = _get_prog("l1a", lambda: build_l1_attn("C"))
    maps3 = host_l1_attn_inputs(None, inputs["L1_attn_Wqa"], inputs["L1_attn_Wqb"],
                                inputs["L1_attn_Wkva"], inputs["L1_attn_Wkvb"],
                                inputs["L1_attn_Wout"], inputs["L1_attn_bout"],
                                xs=[a.T for a in x2T])
    r3 = run_bass_kernel_spmd(nc3, maps3, core_ids=list(range(NC)))
    x3T = [r3.results[c]["x2T"] for c in range(NC)]

    nc4 = _get_prog("l1f", lambda: build_l1_ffn())
    maps4 = host_l1_ffn_inputs(inputs["L1_ffn_ln_g"], inputs["L1_ffn_ln_b"],
                               inputs["L1_ffn_bn_g"], inputs["L1_ffn_bn_b"],
                               inputs["L1_ffn_bn_mean"], inputs["L1_ffn_bn_var"],
                               inputs["L1_ffn_Wr"], inputs["L1_ffn_br"],
                               inputs["L1_ffn_W_down"], inputs["L1_ffn_b_down"],
                               inputs["L1_ffn_W_up"], inputs["L1_ffn_b_up"],
                               xs=[a.T for a in x3T])
    r4 = run_bass_kernel_spmd(nc4, maps4, core_ids=list(range(NC)))
    return np.ascontiguousarray(
        np.stack([r4.results[c]["x4T"].T for c in range(NC)]).astype(np.float32))
L1_SLOT = {}
for _h in range(16):
    _g = _h // 4
    _idx = _h % 4 + (4 if _h >= 8 else 0)
    L1_SLOT[_h] = 2 * _idx + (_g % 2)
L1_PAIRS = [([0, 1, 2, 3, 8, 9, 10, 11][j], [4, 5, 6, 7, 12, 13, 14, 15][j])
            for j in range(8)]
VW1 = 65 * 4


def build_l1_attn(upto="C", fctx=None):
    nc = fctx["nc"] if fctx else bass.Bass("TRN2", target_bir_lowering=False, debug=False, detect_race_conditions=False)
    xT = fctx["xin"] if fctx else nc.dram_tensor("xT", [D, T], F32, kind="ExternalInput").ap()
    Wqa = nc.dram_tensor("Wqa", [D, 64], F32, kind="ExternalInput").ap()
    Wqb = nc.dram_tensor("Wqb", [64, D], F32, kind="ExternalInput").ap()
    Wkva = nc.dram_tensor("Wkva", [D, 32], F32, kind="ExternalInput").ap()
    Wkvb_k = nc.dram_tensor("Wkvb_k", [32, 256], F32, kind="ExternalInput").ap()
    Wkvb_v = nc.dram_tensor("Wkvb_v", [32, 256], F32, kind="ExternalInput").ap()
    Wout = nc.dram_tensor("Wout", [D, D], F32, kind="ExternalInput").ap()
    bout_t = nc.dram_tensor("bout_t", [128, NDT], F32, kind="ExternalInput").ap()
    cosT = nc.dram_tensor("cosT", [128, T], F32, kind="ExternalInput").ap()
    sinT = nc.dram_tensor("sinT", [128, T], F32, kind="ExternalInput").ap()
    cmask = nc.dram_tensor("cmask", [128, 4 * 1024], BF16, kind="ExternalInput").ap()
    x2T = fctx["xout"] if fctx else nc.dram_tensor("x2T", [D, T], F32, kind="ExternalOutput").ap()
    qT_h = nc.dram_tensor("qT_h1", [NCHUNK * D, CH], BF16).ap()
    kv_h = nc.dram_tensor("kv_h1", [NCHUNK * 256, CH], BF16).ap()
    ao_h = nc.dram_tensor("ao_h1", [T, D], BF16).ap()

    with contextlib.ExitStack() as ctx:
        pref = fctx["tag"] if fctx else ""
        sb = lambda name, p_, f, dt=BF16: ctx.enter_context(nc.sbuf_tensor(pref + name, [p_, f], dt))
        c_ones = sb("c_ones", 128, 1)
        c_onesrf = sb("c_onesrf", 1, 128, F32)
        wqa_s = [sb(f"wqa{i}", 128, 64) for i in range(NDT)]
        wqb_s = sb("wqb_s", 64, D)
        wkva_s = [sb(f"wkva{i}", 128, 32) for i in range(NDT)]
        wkvbk_s = sb("wkvbk_s", 32, 256)
        wkvbv_s = sb("wkvbv_s", 32, 256)
        wout_s = [sb(f"wout{i}", 128, D) for i in range(NDT)]
        bout_s = sb("bout_s", 128, NDT, F32)
        cos_s = sb("cos_s", 128, T, F32)
        sin_s = sb("sin_s", 128, T, F32)
        cm_s = sb("cm_s", 128, 4 * 1024)
        kT = [sb(f"kTl{i}", 128, T) for i in range(2)]
        vbuf = sb("vbufl", 128, NKT * VW1)
        sxb = sb("sxb", 128, NDT * CH)
        sqp = [sb(f"sqp{i}", 128, CH) for i in range(2)]
        xnb = sb("xnb", 128, NDT * CH)
        tmpf = sb("tmpf", 128, CH, F32)
        t2f = sb("t2f", 128, CH, F32)
        mrow = sb("mrow", 1, CH, F32)
        rrow = sb("rrow", 1, 2 * CH, F32)
        t1b = sb("t1b", 64, CH)
        t1k0 = sb("t1k0", 32, CH)
        qre = sb("qre", 128, CH)
        qro = sb("qro", 128, CH)
        qcb = sb("qcb", 128, NDT * CH)
        ptb = [sb(f"ptb{i}", 128, 1024) for i in range(4)]
        anb = sxb
        recb = sb("recb", 128, 16, F32)
        resf = tmpf

        P0 = ctx.enter_context(nc.psum_tensor(pref + "P0", [128, 2048], F32))
        P2 = ctx.enter_context(nc.psum_tensor(pref + "P2", [128, 512], F32))
        P3 = ctx.enter_context(nc.psum_tensor(pref + "P3", [128, 512], F32))
        P4 = ctx.enter_context(nc.psum_tensor(pref + "P4", [128, 512], F32))
        P5 = ctx.enter_context(nc.psum_tensor(pref + "P5", [128, 512], F32))

        p = fctx["p"] if fctx else P(nc, ctx)

        def ld_w(e, p_):
            p_.dma(e, out=bout_s[:, :], in_=bout_t)
            p_.dma(e, out=wqb_s[:, :], in_=Wqb)
            p_.dma(e, out=wkvbk_s[:, :], in_=Wkvb_k)
            p_.dma(e, out=wkvbv_s[:, :], in_=Wkvb_v)
            p_.dma(e, out=cm_s[:, :], in_=cmask)
            for i in range(NDT):
                p_.dma(e, out=wqa_s[i][:, :], in_=Wqa[i * 128:(i + 1) * 128, :])
                p_.dma(e, out=wkva_s[i][:, :], in_=Wkva[i * 128:(i + 1) * 128, :])
                p_.dma(e, out=wout_s[i][:, :], in_=Wout[i * 128:(i + 1) * 128, :])
        s_wld = p.step("gpsimd", ld_w, n_dma=5 + 3 * NDT, tag="wld")

        def ld_rope(e, p_):
            p_.dma(e, out=cos_s[:, :], in_=cosT)
            p_.dma(e, out=sin_s[:, :], in_=sinT)
        s_rope = p.step("sync", ld_rope, n_dma=2, tag="ldrope")

        def consts(e, p_):
            e.memset(c_ones[:, :], 1.0)
            e.memset(c_onesrf[0:1, :], 1.0)
            return e.memset(vbuf[:, :], 1.0)
        s_const = p.step("vector", consts, tag="consts")

        def xcopy(e, p_):
            p_.dma(e, out=x2T, in_=xT)
        s_xcopy = p.step("sync", xcopy, n_dma=1, tag="xcopy")

        bca_m = P0[:, 0:CH]
        bca_r = P0[:, CH:2 * CH]
        s_xn = [None] * NCHUNK
        s_qdma = [None] * NCHUNK
        s_kvld = [None] * NCHUNK
        prev = dict(statm=None, bca=None, stats_pe=None)
        s_sqs = {}
        s_pes = {}
        s_va_ev = {}
        last_dma_q = None

        for c in range(NCHUNK):
            def ldxb(e, p_, cc=c):
                p_.dma(e, out=sxb[:, :].rearrange("p (a n) -> p a n", a=NDT),
                       in_=xT[:, cc * CH:(cc + 1) * CH].rearrange("(a p) n -> p a n", p=128))
            s_ldxb = p.step("gpsimd", ldxb, n_dma=1, tag=f"A{c}_ldxb",
                            deps=[s_sqs.get((c - 1, NDT - 1)), prev["stats_pe"],
                                  s_xn[c - 1] if c else None])

            for dt in range(NDT):
                def sq(e, p_, dt=dt):
                    return e.tensor_tensor(sqp[dt % 2][:, :], sxb[:, dt * CH:(dt + 1) * CH],
                                           sxb[:, dt * CH:(dt + 1) * CH], ALU.mult)
                s_sqs[(c, dt)] = p.step("vector", sq, tag=f"A{c}_sq{dt}",
                                        deps=[s_ldxb, s_pes.get((c, dt - 2)),
                                              prev["stats_pe"] if dt < 2 else None])

                def pe2(e, p_, dt=dt):
                    e.matmul(P4[0:1, :], c_ones[:, :], sxb[:, dt * CH:(dt + 1) * CH],
                             start=(dt == 0), stop=(dt == NDT - 1))
                    return e.matmul(P5[0:1, :], c_ones[:, :], sqp[dt % 2][:, :],
                                    start=(dt == 0), stop=(dt == NDT - 1))
                s_pes[(c, dt)] = p.step("tensor", pe2, tag=f"A{c}_pe{dt}",
                                        deps=[s_sqs[(c, dt)], s_const,
                                              prev["statm"] if dt == 0 else None])
            s_stats_pe = s_pes[(c, NDT - 1)]
            prev["stats_pe"] = s_stats_pe

            def statm1(e, p_):
                e.tensor_scalar(mrow[0:1, :], P4[0:1, :], 1.0 / D, None, ALU.mult)
                e.tensor_scalar(rrow[0:1, 0:CH], P5[0:1, :], 1.0 / D, None, ALU.mult)
                e.tensor_tensor(rrow[0:1, CH:], mrow[0:1, :], mrow[0:1, :], ALU.mult)
                e.tensor_tensor(rrow[0:1, 0:CH], rrow[0:1, 0:CH], rrow[0:1, CH:], ALU.subtract)
                return e.tensor_scalar(rrow[0:1, 0:CH], rrow[0:1, 0:CH], 1e-5, None, ALU.add)
            s_statm1 = p.step("vector", statm1, deps=[s_stats_pe, prev["bca"]], tag=f"A{c}_sm1")

            def sqrt(e, p_):
                return e.activation(rrow[0:1, CH:], rrow[0:1, 0:CH], AF.Sqrt)
            s_sqrt = p.step("scalar", sqrt, deps=[s_statm1], tag=f"A{c}_sqrt")

            def recip(e, p_):
                return e.reciprocal(rrow[0:1, 0:CH], rrow[0:1, CH:])
            s_recip = p.step("vector", recip, deps=[s_sqrt], tag=f"A{c}_rec")
            prev["statm"] = s_recip

            def bca(e, p_):
                e.matmul(bca_m, c_onesrf[0:1, :], mrow[0:1, :], start=True, stop=True)
                return e.matmul(bca_r, c_onesrf[0:1, :], rrow[0:1, 0:CH], start=True, stop=True)
            s_bca = p.step("tensor", bca, deps=[s_recip, s_xn[c - 1] if c else None],
                           tag=f"A{c}_bca")
            prev["bca"] = s_bca

            def xn(e, p_):
                for dt in range(NDT):
                    e.tensor_tensor(tmpf[:, :], sxb[:, dt * CH:(dt + 1) * CH], bca_m, ALU.subtract)
                    ins = e.tensor_tensor(xnb[:, dt * CH:(dt + 1) * CH], tmpf[:, :], bca_r, ALU.mult)
                return ins
            s_xn[c] = p.step("vector", xn, deps=[s_bca, s_va_ev.get((c - 1, 3))],
                             tag=f"A{c}_xn")

            def st1_pe(e, p_):
                for ktd in range(NDT):
                    e.matmul(P0[0:64, 1024:1536], wqa_s[ktd][:, :],
                             xnb[:, ktd * CH:(ktd + 1) * CH],
                             start=(ktd == 0), stop=(ktd == NDT - 1))
                for ktd in range(NDT):
                    ins = e.matmul(P0[0:32, 1536:2048], wkva_s[ktd][:, :],
                                   xnb[:, ktd * CH:(ktd + 1) * CH],
                                   start=(ktd == 0), stop=(ktd == NDT - 1))
                return ins
            s_st1 = p.step("tensor", st1_pe, deps=[s_xn[c], s_wld,
                                                   s_va_ev.get((c - 1, 3))], tag=f"A{c}_st1")

            def st1_ev(e, p_):
                e.tensor_copy(t1b[0:64, :], P0[0:64, 1024:1536])
                return e.tensor_copy(t1k0[0:32, :], P0[0:32, 1536:2048])
            s_st1e = p.step("vector", st1_ev, deps=[s_st1], tag=f"A{c}_st1e")

            s_rope_prev = None
            for m in range(4):
                def q2_pe(e, p_, m=m):
                    e.matmul(P2[:, :], wqb_s[:, m * 128:(m + 1) * 128], t1b[0:64, :],
                             start=True, stop=True)
                    return e.matmul(P3[:, :], wqb_s[:, 512 + m * 128: 512 + (m + 1) * 128],
                                    t1b[0:64, :], start=True, stop=True)
                s_q2 = p.step("tensor", q2_pe, deps=[s_st1e, s_rope_prev], tag=f"A{c}_q2_{m}")

                def ropeq(e, p_, m=m, cc=c):
                    cslice = cos_s[:, cc * CH:(cc + 1) * CH]
                    sslice = sin_s[:, cc * CH:(cc + 1) * CH]
                    e.tensor_tensor(tmpf[:, :], P2[:, :], cslice, ALU.mult)
                    e.tensor_tensor(t2f[:, :], P3[:, :], sslice, ALU.mult)
                    e.tensor_tensor(qre[:, :], tmpf[:, :], t2f[:, :], ALU.subtract)
                    e.tensor_tensor(tmpf[:, :], P2[:, :], sslice, ALU.mult)
                    e.tensor_tensor(t2f[:, :], P3[:, :], cslice, ALU.mult)
                    return e.tensor_tensor(qro[:, :], tmpf[:, :], t2f[:, :], ALU.add)
                s_ropeq = p.step("vector", ropeq, deps=[s_q2, last_dma_q], tag=f"A{c}_ropeq{m}")
                s_rope_prev = s_ropeq

                def qdma(e, p_, m=m, cc=c):
                    base = (m % 2) + 8 * (m // 2)
                    region = qT_h[cc * D:(cc + 1) * D, :]
                    rr2 = region.rearrange("(sp two half i) n -> half two sp i n",
                                           two=2, half=2, i=32)
                    so = 4 if base >= 8 else 0
                    for half, srcbuf in ((0, qre), (1, qro)):
                        dst = rr2[half, base % 2, so:so + 4, :, :]
                        p_.dma(e, out=dst, in_=srcbuf[:, :])
                s_qd = p.step("gpsimd", qdma, n_dma=2, deps=[s_ropeq], tag=f"A{c}_qdma{m}")
                last_dma_q = s_qd
            s_qdma[c] = last_dma_q

            def k2_pe(e, p_):
                e.matmul(P2[:, :], wkvbk_s[:, 0:128], t1k0[0:32, :], start=True, stop=True)
                return e.matmul(P3[:, :], wkvbk_s[:, 128:256], t1k0[0:32, :], start=True, stop=True)
            s_k2 = p.step("tensor", k2_pe, deps=[s_st1e, s_rope_prev], tag=f"A{c}_k2")

            def ropek(e, p_, cc=c):
                cslice = cos_s[:, cc * CH:(cc + 1) * CH]
                sslice = sin_s[:, cc * CH:(cc + 1) * CH]
                e.tensor_tensor(tmpf[:, :], P2[:, :], cslice, ALU.mult)
                e.tensor_tensor(t2f[:, :], P3[:, :], sslice, ALU.mult)
                e.tensor_tensor(qre[:, :], tmpf[:, :], t2f[:, :], ALU.subtract)
                e.tensor_tensor(tmpf[:, :], P2[:, :], sslice, ALU.mult)
                e.tensor_tensor(t2f[:, :], P3[:, :], cslice, ALU.mult)
                return e.tensor_tensor(qro[:, :], tmpf[:, :], t2f[:, :], ALU.add)
            s_ropek = p.step("vector", ropek, deps=[s_k2, last_dma_q], tag=f"A{c}_ropek")

            def kdma(e, p_, cc=c):
                region = kv_h[cc * 256:(cc + 1) * 256, :]
                rr = region.rearrange("(g half i) n -> half g i n", half=2, i=32)
                for half, srcbuf in ((0, qre), (1, qro)):
                    p_.dma(e, out=rr[half, :, :, :], in_=srcbuf[:, :])
            s_kd = p.step("gpsimd", kdma, n_dma=2, deps=[s_ropek], tag=f"A{c}_kdma")
            last_dma_q = s_kd

            def kvld(e, p_, cc=c):
                for t_ in range(2):
                    p_.dma(e, out=kT[t_][:, cc * CH:(cc + 1) * CH],
                           in_=kv_h[cc * 256 + t_ * 128: cc * 256 + (t_ + 1) * 128, :])
            s_kvld[c] = p.step("sync", kvld, n_dma=2, deps=[s_kd], tag=f"A{c}_kvld")

            for sv in range(2):
                def v_pe(e, p_, sv=sv):
                    for j in range(2):
                        s_ = sv * 2 + j
                        ins = e.matmul(P2[:, 0:256] if j == 0 else P3[:, 0:256],
                                       t1k0[0:32, s_ * 128:(s_ + 1) * 128],
                                       wkvbv_s[:, :],
                                       start=True, stop=True)
                    return ins
                s_vp = p.step("tensor", v_pe, deps=[s_st1e, s_ropek if sv == 0 else None,
                                                    s_va_ev.get((c, sv * 2 - 1)),
                                                    s_va_ev.get((c - 1, 3)) if sv == 0 else None],
                              tag=f"A{c}_v{sv}")

                for j in range(2):
                    def v_ev(e, p_, sv=sv, j=j, cc=c):
                        s_ = sv * 2 + j
                        tile = cc * 4 + s_
                        dst = vbuf[:, tile * VW1: tile * VW1 + VW1]
                        return e.tensor_copy(
                            dst.rearrange("p (h w) -> p h w", w=65)[:, :, 0:64],
                            (P2 if j == 0 else P3)[:, 0:256].rearrange("p (h w) -> p h w", w=64))
                    s_va_ev[(c, sv * 2 + j)] = p.step("vector", v_ev,
                                                      deps=[s_vp, s_const], tag=f"A{c}_vev{sv}{j}")

        s_a_done = s_va_ev[(NCHUNK - 1, 3)]
        if upto == "A":
            p.wait_only("gpsimd", [s_qdma[NCHUNK - 1], s_xcopy], tag="final")
            p.wait_only("sync", [s_kvld[NCHUNK - 1]], tag="final2")
            with nc.Block() as block:
                p.emit(block)
            return nc

        st = P0
        pv = [P2, P3]
        s_div = {}
        s_ao_dma = [None] * NCHUNK
        s_c_done = [None] * NCHUNK
        st_steps = []
        exp_steps = []
        pv_steps = []
        s_cev_last = [None] * NCHUNK

        for qc in range(NCHUNK):
            NKTq = 4 * qc + 4

            def qld(e, p_, qc=qc):
                p_.dma(e, out=qcb[:, :].rearrange("p (a n) -> p a n", a=NDT),
                       in_=qT_h[qc * D:(qc + 1) * D, :].rearrange("(a p) n -> p a n", p=128))
            s_qld = p.step("sync", qld, n_dma=1, tag=f"B{qc}_qld",
                           deps=[s_qdma[qc], s_kvld[NCHUNK - 1],
                                 s_qdma[NCHUNK - 1] if qc == 0 else st_steps[-1]])

            for hp in range(8):
                he, ho = L1_PAIRS[hp]
                pend_pv = []
                s_e = None
                for g in range(NKTq):
                    G = len(st_steps)
                    sthalf = st[:, (g % 2) * 1024:(g % 2) * 1024 + 1024]

                    def st_mm(e, p_, hp=hp, g=g, sthalf=sthalf):
                        for pos, hh in enumerate(L1_PAIRS[hp]):
                            sl = L1_SLOT[hh]
                            gkv = hh // 4
                            ins = e.matmul(
                                sthalf[:, pos * 512:pos * 512 + 512],
                                kT[gkv // 2][(gkv % 2) * 64:(gkv % 2) * 64 + 64,
                                             g * 128:(g + 1) * 128],
                                qcb[(sl % 2) * 64:(sl % 2) * 64 + 64,
                                    (sl // 2) * CH:(sl // 2 + 1) * CH],
                                start=True, stop=True)
                        return ins
                    s_stm = p.step("tensor", st_mm,
                                   deps=[s_qld, s_a_done,
                                         exp_steps[G - 2] if len(exp_steps) >= 2 else None,
                                         s_div.get((qc, hp - 1)) if g == 0 else None,
                                         s_div.get((qc - 1, 7)) if (g == 0 and hp == 0) else None,
                                         s_cev_last[qc - 1] if (g == 0 and hp == 0 and qc) else None],
                                   tag=f"B1{qc}_{hp}_st{g}")
                    st_steps.append(s_stm)

                    def expf(e, p_, g=g, G=G, sthalf=sthalf):
                        return e.activation(ptb[G % 4][:, :], sthalf, AF.Exp, scale=1.0 / 8)
                    s_e = p.step("scalar", expf,
                                 deps=[s_stm, pv_steps[G - 4] if G >= 4 else None],
                                 tag=f"B1{qc}_{hp}_e{g}")
                    exp_steps.append(s_e)

                    delta = g - 4 * qc
                    s_mask = None
                    if 0 <= delta <= 3:
                        def maskf(e, p_, g=g, G=G, delta=delta):
                            return e.tensor_tensor(ptb[G % 4][:, :], ptb[G % 4][:, :],
                                                   cm_s[:, delta * 1024:(delta + 1) * 1024],
                                                   ALU.mult)
                        s_mask = p.step("vector", maskf, deps=[s_e], tag=f"B1{qc}_{hp}_m{g}")
                        exp_steps[-1] = s_mask

                    def pv_mm(e, p_, hp=hp, g=g, G=G, NKTq=NKTq):
                        for pos, hh in enumerate(L1_PAIRS[hp]):
                            gkv = hh // 4
                            for sub in range(4):
                                ins = e.matmul(
                                    pv[pos][:, sub * 65:(sub + 1) * 65],
                                    ptb[G % 4][:, pos * 512 + sub * 128:
                                               pos * 512 + sub * 128 + 128],
                                    vbuf[:, g * VW1 + gkv * 65: g * VW1 + gkv * 65 + 65],
                                    start=(g == 0 and sub == 0), stop=(g == NKTq - 1 and sub == 3))
                        return ins
                    pend_pv.append((pv_mm, f"B1{qc}_{hp}_pv{g}"))
                    if len(pend_pv) == 3:
                        fn0, tag0 = pend_pv.pop(0)
                        pv_steps.append(p.step("tensor", fn0, deps=[exp_steps[-2]], tag=tag0))

                def echo(e, p_):
                    return e.activation(sqp[0][0:1, 0:1], ptb[1][0:1, 1023:1024], AF.Exp)
                s_echo = p.step("scalar", echo, deps=[], tag=f"B1{qc}_{hp}_echo")
                for fn0, tag0 in pend_pv:
                    pv_steps.append(p.step("tensor", fn0, deps=[s_echo], tag=tag0))
                pend_pv = []

                def rec_act(e, p_):
                    for b in range(2):
                        e.activation(recb[:, 8 + b * 4:12 + b * 4].rearrange("p (s w) -> p s w", w=1),
                                     pv[b][:, 0:260].rearrange("p (s w) -> p s w", w=65)[:, :, 64:65],
                                     AF.Ln)
                    return e.activation(recb[:, 0:8], recb[:, 8:16], AF.Exp, scale=-1.0)
                s_rec = p.step("scalar", rec_act, deps=[pv_steps[-1]], tag=f"B1{qc}_{hp}_rec")

                def div(e, p_, hp=hp):
                    for pos, hh in enumerate(L1_PAIRS[hp]):
                        for sub in range(4):
                            i = pos * 4 + sub
                            ins = e.tensor_scalar(
                                anb[:, sub * D + hh * 64: sub * D + hh * 64 + 64],
                                pv[pos][:, sub * 65:sub * 65 + 64],
                                recb[:, i:i + 1], None, ALU.mult)
                    return ins
                s_div[(qc, hp)] = p.step(
                    "vector", div,
                    deps=[s_rec, s_ao_dma[qc - 1] if qc >= 1 else None],
                    tag=f"B1{qc}_{hp}_div")

            def ao_dma(e, p_, qc=qc):
                p_.dma(e, out=ao_h[qc * CH:(qc + 1) * CH, :].rearrange("(s p) d -> p s d", p=128),
                       in_=anb[:, :].rearrange("p (s d) -> p s d", s=4))
            s_ao_dma[qc] = p.step("gpsimd", ao_dma, n_dma=1,
                                  deps=[s_div[(qc, hp)] for hp in range(8)], tag=f"B1{qc}_ao")

            if upto == "B":
                s_cev_last[qc] = s_ao_dma[qc]
                s_c_done[qc] = s_ao_dma[qc]
                continue

            def cldT(e, p_, qc=qc):
                for dt in range(NDT):
                    p_.dma_t(e, out=xnb[:, dt * CH:(dt + 1) * CH],
                             in_=ao_h[qc * CH:(qc + 1) * CH, dt * 128:(dt + 1) * 128])
            s_cldT = p.step("sync", cldT, n_dma=NDT,
                            deps=[s_ao_dma[qc],
                                  s_c_done[qc - 1] if qc else s_st1,
                                  None if qc else s_xn[NCHUNK - 1]],
                            tag=f"C1{qc}_ldT")

            s_cev = None
            s_cout = None
            for dt in range(NDT):
                def c_pe(e, p_, dt=dt):
                    yo = P4[:, :] if dt % 2 == 0 else P5[:, :]
                    for ktd in range(NDT):
                        ins = e.matmul(yo, wout_s[ktd][:, dt * 128:(dt + 1) * 128],
                                       xnb[:, ktd * CH:(ktd + 1) * CH],
                                       start=(ktd == 0), stop=(ktd == NDT - 1))
                    return ins
                dep_stats = prev["statm"] if (qc == 0 and dt < 2) else None
                dep_war = s_cev if dt >= 2 else None
                s_cpe = p.step("tensor", c_pe, deps=[s_cldT, dep_stats, dep_war],
                               tag=f"C1{qc}_pe{dt}")

                def c_ev(e, p_, dt=dt):
                    yo = P4[:, :] if dt % 2 == 0 else P5[:, :]
                    return e.tensor_scalar(resf[:, :], yo, bout_s[:, dt:dt + 1], None, ALU.add)
                s_cev = p.step("vector", c_ev, deps=[s_cpe, s_cout], tag=f"C1{qc}_ev{dt}")

                def c_out(e, p_, dt=dt, qc=qc):
                    p_.dma(e, out=x2T[dt * 128:(dt + 1) * 128, qc * CH:(qc + 1) * CH],
                           in_=resf[:, :], accum_op=ALU.add)
                s_cout = p.step("gpsimd", c_out, n_dma=1, deps=[s_cev, s_xcopy],
                                tag=f"C1{qc}_out{dt}")
            s_cev_last[qc] = s_cev
            s_c_done[qc] = s_cout

        p.wait_only("gpsimd", [s_c_done[NCHUNK - 1], s_xcopy], tag="final")
        if fctx is not None:
            p.fence()
            return nc
        with nc.Block() as block:
            p.emit(block)
    return nc


def host_l1_attn_inputs(x_unused_count, Wqa, Wqb, Wkva, Wkvb, Wout, bout, xs):
    import ml_dtypes
    evens = [h * 64 + 2 * i for h in range(16) for i in range(32)]
    odds = [h * 64 + 2 * i + 1 for h in range(16) for i in range(32)]
    Wqb_r = np.ascontiguousarray(np.asarray(Wqb, np.float32)[:, evens + odds])
    k_evens = [g * 128 + 4 * i for g in range(4) for i in range(32)]
    k_odds = [g * 128 + 4 * i + 2 for g in range(4) for i in range(32)]
    v_cols = [g * 128 + 2 * d_ + 1 for g in range(4) for d_ in range(64)]
    Wkvb = np.asarray(Wkvb, np.float32)
    Wkvb_k = np.ascontiguousarray(Wkvb[:, k_evens + k_odds])
    Wkvb_v = np.ascontiguousarray(Wkvb[:, v_cols])
    inv_freq = (1.0 / 10000.0 ** (np.arange(0, 64, 2, dtype=np.float32) / 64)).astype(np.float32)
    t = np.arange(T, dtype=np.float32)
    freqs = np.outer(t, inv_freq).astype(np.float32)
    cosT = np.ascontiguousarray(np.tile(np.cos(freqs).T, (4, 1)).astype(np.float32))
    sinT = np.ascontiguousarray(np.tile(np.sin(freqs).T, (4, 1)).astype(np.float32))
    k_i = np.arange(128)[:, None]
    q_i = np.arange(512)[None, :]
    cm = np.zeros((128, 4 * 1024), np.float32)
    for dlt in range(4):
        m = (k_i <= q_i - 128 * dlt).astype(np.float32)
        cm[:, dlt * 1024:dlt * 1024 + 512] = m
        cm[:, dlt * 1024 + 512:(dlt + 1) * 1024] = m
    cm = cm.astype(ml_dtypes.bfloat16)
    bout_t = np.ascontiguousarray(np.asarray(bout, np.float32).reshape(NDT, 128).T)
    maps = []
    for c in range(NC):
        maps.append({
            "xT": np.ascontiguousarray(np.asarray(xs[c], np.float32).T),
            "Wqa": np.asarray(Wqa, np.float32), "Wqb": Wqb_r,
            "Wkva": np.asarray(Wkva, np.float32), "Wkvb_k": Wkvb_k, "Wkvb_v": Wkvb_v,
            "Wout": np.asarray(Wout, np.float32), "bout_t": bout_t,
            "cosT": cosT, "sinT": sinT, "cmask": cm,
        })
    return maps
def build_l1_ffn(dbg=False, fctx=None):
    nc = fctx["nc"] if fctx else bass.Bass("TRN2", target_bir_lowering=False, debug=False, detect_race_conditions=False)
    xT = fctx["xin"] if fctx else nc.dram_tensor("xT", [D, T], F32, kind="ExternalInput").ap()
    lng_t = nc.dram_tensor("lng_t", [128, NDT], F32, kind="ExternalInput").ap()
    lnb_t = nc.dram_tensor("lnb_t", [128, NDT], F32, kind="ExternalInput").ap()
    bng_t = nc.dram_tensor("bng_t", [128, NDT], F32, kind="ExternalInput").ap()
    bnb_t = nc.dram_tensor("bnb_t", [128, NDT], F32, kind="ExternalInput").ap()
    bnm_t = nc.dram_tensor("bnm_t", [128, NDT], F32, kind="ExternalInput").ap()
    bnv_t = nc.dram_tensor("bnv_t", [128, NDT], F32, kind="ExternalInput").ap()
    Wr = nc.dram_tensor("Wr", [D, 8], F32, kind="ExternalInput").ap()
    br_row = nc.dram_tensor("br_row", [1, 8], F32, kind="ExternalInput").ap()
    Wdn = nc.dram_tensor("Wdn", [8 * D, 64], F32, kind="ExternalInput").ap()
    bdn = nc.dram_tensor("bdn", [8, 64], F32, kind="ExternalInput").ap()
    Wup = nc.dram_tensor("Wup", [8 * 64, D], F32, kind="ExternalInput").ap()
    bup = nc.dram_tensor("bup", [8, D], F32, kind="ExternalInput").ap()
    x4T = fctx["xout"] if fctx else nc.dram_tensor("x4T", [D, T], F32, kind="ExternalOutput").ap()
    z_h = nc.dram_tensor("z_h", [NCHUNK * D, CH], BF16).ap()
    oh_d = nc.dram_tensor("oh_d", [8, 1], F32).ap()
    rdbg = nc.dram_tensor("rdbg", [128, 32], F32, kind="ExternalOutput" if dbg else "Internal").ap()

    with contextlib.ExitStack() as ctx:
        pref = fctx["tag"] if fctx else ""
        sb = lambda name, p_, f, dt=BF16: ctx.enter_context(nc.sbuf_tensor(pref + name, [p_, f], dt))
        c_onesf = sb("c_onesf", 128, 1, F32)
        c_onesrf = sb("c_onesrf", 1, 128, F32)
        c_zer = sb("c_zer", 64, CH)
        lng_s = sb("lng_s", 128, NDT, F32); lnb_s = sb("lnb_s", 128, NDT, F32)
        bng_s = sb("bng_s", 128, NDT, F32); bnb_s = sb("bnb_s", 128, NDT, F32)
        bnm_s = sb("bnm_s", 128, NDT, F32); bnv_s = sb("bnv_s", 128, NDT, F32)
        wr_s = sb("wr_s", 128, NDT * 8, F32)
        br_s = sb("br_s", 1, 8, F32)
        wdn_s = sb("wdn_s", 128, 8 * NDT * 64)
        bdn_s = sb("bdn_s", 8, 64, F32)
        wup_s = sb("wup_s", 64, 8 * D)
        bup_s = sb("bup_s", 8, D, F32)
        sxf = sb("sxf", 128, NDT * CH, F32)
        xnf = sb("xnf", 128, NDT * CH, F32)
        sqf = [sb(f"sqf{i}", 128, CH, F32) for i in range(2)]
        zbf = sb("zbf", 128, NDT * CH)
        tmpf = sb("tmpf", 128, CH, F32)
        mrow = sb("mrow", 1, CH, F32)
        rrow = sb("rrow", 1, 2 * CH, F32)
        zsum = sb("zsum", 128, NDT, F32)
        zsum_all = sb("zsum_all", 128, NDT * NCHUNK, F32)
        zbn = sb("zbn", 128, NDT, F32)
        l8f = sb("l8f", 1, 24, F32)
        oh_row = sb("oh_row", 1, 8, F32)
        ohT = sb("ohT", 8, 1, F32)
        wsel = [sb(f"wsel{i}", 128, NDT * 64) for i in range(2)]
        wusel = [sb(f"wusel{i}", 64, D) for i in range(2)]
        hbf = sb("hbf", 64, CH)
        resf = tmpf

        P0 = ctx.enter_context(nc.psum_tensor(pref + "P0", [128, 2048], F32))
        P2 = ctx.enter_context(nc.psum_tensor(pref + "P2", [128, 512], F32))
        P3 = ctx.enter_context(nc.psum_tensor(pref + "P3", [128, 512], F32))
        P4 = ctx.enter_context(nc.psum_tensor(pref + "P4", [128, 512], F32))
        P5 = ctx.enter_context(nc.psum_tensor(pref + "P5", [128, 512], F32))

        p = fctx["p"] if fctx else P(nc, ctx)

        def ld_w(e, p_):
            for dst, srcw in ((lng_s, lng_t), (lnb_s, lnb_t), (bng_s, bng_t),
                              (bnb_s, bnb_t), (bnm_s, bnm_t), (bnv_s, bnv_t)):
                p_.dma(e, out=dst[:, :], in_=srcw)
            p_.dma(e, out=wr_s[:, :].rearrange("p (a n) -> p a n", a=NDT),
                   in_=Wr.rearrange("(a p) n -> p a n", p=128))
            p_.dma(e, out=br_s[0:1, :], in_=br_row)
            p_.dma(e, out=wdn_s[:, :].rearrange("p (a n) -> p a n", a=8 * NDT),
                   in_=Wdn.rearrange("(a p) n -> p a n", p=128))
            p_.dma(e, out=bdn_s[:, :], in_=bdn)
            p_.dma(e, out=wup_s[:, :].rearrange("p (a n) -> p a n", a=8),
                   in_=Wup.rearrange("(a p) n -> p a n", p=64))
            p_.dma(e, out=bup_s[:, :], in_=bup)
        s_wld = p.step("gpsimd", ld_w, n_dma=12, tag="wld")

        def consts(e, p_):
            e.memset(c_onesf[:, :], 1.0)
            e.memset(c_onesrf[0:1, :], 1.0)
            e.memset(c_zer[:, :], 0.0)
            return e.memset(zsum[:, :], 0.0)
        s_const = p.step("vector", consts, tag="consts")

        def xcopy(e, p_):
            p_.dma(e, out=x4T, in_=xT)
        s_xcopy = p.step("sync", xcopy, n_dma=1, tag="xcopy")

        bca_m = P0[:, 0:CH]
        bca_r = P0[:, CH:2 * CH]

        s_zdma = [None] * NCHUNK
        prev = dict(statm=None, bca=None, stats_pe=None, xn=None, z=None)
        s_sqs = {}
        s_pes = {}

        def emit_stats(tagp, src_buf, extra_dep=None):
            for dt in range(NDT):
                def sq(e, p_, dt=dt, src_buf=src_buf):
                    return e.tensor_tensor(sqf[dt % 2][:, :], src_buf[:, dt * CH:(dt + 1) * CH],
                                           src_buf[:, dt * CH:(dt + 1) * CH], ALU.mult)
                s_sq = p.step("vector", sq, tag=f"{tagp}_sq{dt}",
                              deps=[extra_dep, s_pes.get((tagp, dt - 2)),
                                    prev["stats_pe"] if dt < 2 else None])
                s_sqs[(tagp, dt)] = s_sq

                def pe2(e, p_, dt=dt, src_buf=src_buf):
                    e.matmul(P4[0:1, :], c_onesf[:, :], src_buf[:, dt * CH:(dt + 1) * CH],
                             start=(dt == 0), stop=(dt == NDT - 1))
                    return e.matmul(P5[0:1, :], c_onesf[:, :], sqf[dt % 2][:, :],
                                    start=(dt == 0), stop=(dt == NDT - 1))
                s_pes[(tagp, dt)] = p.step("tensor", pe2, tag=f"{tagp}_pe{dt}",
                                           deps=[s_sq, s_const,
                                                 prev["statm"] if dt == 0 else None])
            s_stats = s_pes[(tagp, NDT - 1)]
            prev["stats_pe"] = s_stats

            def statm1(e, p_):
                e.tensor_scalar(mrow[0:1, :], P4[0:1, :], 1.0 / D, None, ALU.mult)
                e.tensor_scalar(rrow[0:1, 0:CH], P5[0:1, :], 1.0 / D, None, ALU.mult)
                e.tensor_tensor(rrow[0:1, CH:], mrow[0:1, :], mrow[0:1, :], ALU.mult)
                e.tensor_tensor(rrow[0:1, 0:CH], rrow[0:1, 0:CH], rrow[0:1, CH:], ALU.subtract)
                return e.tensor_scalar(rrow[0:1, 0:CH], rrow[0:1, 0:CH], 1e-5, None, ALU.add)
            s_sm = p.step("vector", statm1, deps=[s_stats, prev["bca"]], tag=f"{tagp}_sm")

            def sqrtf(e, p_):
                return e.activation(rrow[0:1, CH:], rrow[0:1, 0:CH], AF.Sqrt)
            s_sq2 = p.step("scalar", sqrtf, deps=[s_sm], tag=f"{tagp}_sqrt")

            def recipf(e, p_):
                return e.reciprocal(rrow[0:1, 0:CH], rrow[0:1, CH:])
            s_rc = p.step("vector", recipf, deps=[s_sq2], tag=f"{tagp}_rec")
            prev["statm"] = s_rc

            def bcaf(e, p_):
                e.matmul(bca_m, c_onesrf[0:1, :], mrow[0:1, :], start=True, stop=True)
                return e.matmul(bca_r, c_onesrf[0:1, :], rrow[0:1, 0:CH], start=True, stop=True)
            s_b = p.step("tensor", bcaf, deps=[s_rc, prev["xn"], prev["z"]], tag=f"{tagp}_bca")
            prev["bca"] = s_b
            return s_b

        for c in range(NCHUNK):
            def ldx(e, p_, cc=c):
                p_.dma(e, out=sxf[:, :].rearrange("p (a n) -> p a n", a=NDT),
                       in_=xT[:, cc * CH:(cc + 1) * CH].rearrange("(a p) n -> p a n", p=128))
            s_ldx = p.step("sync", ldx, n_dma=1, tag=f"F{c}_ldx",
                           deps=[s_sqs.get((f"F{c-1}s1", NDT - 1)), prev["stats_pe"],
                                 prev["xn"]])

            s_b1 = emit_stats(f"F{c}s1", sxf, extra_dep=s_ldx)

            def xn(e, p_):
                for dt in range(NDT):
                    e.tensor_tensor(tmpf[:, :], sxf[:, dt * CH:(dt + 1) * CH], bca_m, ALU.subtract)
                    ins = e.tensor_tensor(xnf[:, dt * CH:(dt + 1) * CH], tmpf[:, :], bca_r, ALU.mult)
                return ins
            s_xn = p.step("vector", xn, deps=[s_b1, prev["z"]], tag=f"F{c}_xn")
            prev["xn"] = s_xn

            s_b2 = emit_stats(f"F{c}s2", xnf, extra_dep=s_xn)

            def zf(e, p_, cc=c):
                def red(dt):
                    return e.tensor_reduce(
                        zsum_all[:, dt * NCHUNK + cc:dt * NCHUNK + cc + 1],
                        zbf[:, dt * CH:(dt + 1) * CH], AX.X, ALU.add)
                for dt in range(NDT):
                    e.tensor_tensor(tmpf[:, :], xnf[:, dt * CH:(dt + 1) * CH], bca_m, ALU.subtract)
                    e.tensor_tensor(tmpf[:, :], tmpf[:, :], bca_r, ALU.mult)
                    e.tensor_scalar(zbf[:, dt * CH:(dt + 1) * CH], tmpf[:, :],
                                    lng_s[:, dt:dt + 1], lnb_s[:, dt:dt + 1],
                                    ALU.mult, ALU.add)
                    if dt > 0:
                        red(dt - 1)
                return red(NDT - 1)
            s_z = p.step("vector", zf, deps=[s_b2, s_zdma[c - 1] if c else None, s_wld],
                         tag=f"F{c}_z")
            prev["z"] = s_z

            def zdma(e, p_, cc=c):
                p_.dma(e, out=z_h[cc * D:(cc + 1) * D, :].rearrange("(a p) n -> p a n", p=128),
                       in_=zbf[:, :].rearrange("p (a n) -> p a n", a=NDT))
            s_zdma[c] = p.step("gpsimd", zdma, n_dma=1, deps=[s_z], tag=f"F{c}_zdma")

        zsum2 = sb("zsum2", 128, NDT, F32)
        tmpq = sb("tmpq", 128, 3 * NDT, F32)
        padb = [sb(f"padb{i}", 128, 1024) for i in range(2)]

        def padv(e):
            return e.tensor_copy(padb[0][:, :], padb[1][:, :])

        def zbn1(e, p_):
            e.tensor_reduce(zsum[:, :], zsum_all[:, :].rearrange("p (a n) -> p a n", n=NCHUNK),
                            AX.X, ALU.add)
            e.tensor_scalar(tmpq[:, 0:NDT], bnv_s[:, :], 1e-5, None, ALU.add)
            return padv(e)
        s_z1 = p.step("vector", zbn1, deps=[prev["z"], prev["statm"]], tag="zbn1")

        def zbn2(e, p_):
            e.copy(zsum2[:, :], zsum[:, :])
            e.activation(tmpq[:, NDT:2 * NDT], tmpq[:, 0:NDT], AF.Ln)
            return e.activation(tmpq[:, 2 * NDT:3 * NDT], tmpq[:, NDT:2 * NDT],
                                AF.Exp, scale=-0.5)
        s_z2 = p.step("scalar", zbn2, deps=[s_z1], tag="zbn2")

        zbn2b = sb("zbn2b", 128, NDT, F32)

        def zbn5(e, p_):
            e.tensor_scalar(zbn[:, :], zsum2[:, :], 1.0 / T, None, ALU.mult)
            padv(e)
            e.tensor_tensor(zbn2b[:, :], zbn[:, :], bnm_s[:, :], ALU.subtract)
            padv(e)
            e.tensor_tensor(zbn[:, :], zbn2b[:, :], bng_s[:, :], ALU.mult)
            padv(e)
            e.tensor_tensor(zbn2b[:, :], zbn[:, :], tmpq[:, 2 * NDT:3 * NDT], ALU.mult)
            padv(e)
            e.tensor_tensor(zbn[:, :], zbn2b[:, :], bnb_s[:, :], ALU.add)
            return padv(e)
        s_z3 = p.step("vector", zbn5, deps=[s_z2], tag="zbn5")

        def l8_pe(e, p_):
            for dt in range(NDT):
                e.matmul(P0[0:1, 1024:1032], zbn[:, dt:dt + 1], wr_s[:, dt * 8:(dt + 1) * 8],
                         start=(dt == 0), stop=False)
            return e.matmul(P0[0:1, 1024:1032], c_onesf[0:1, 0:1], br_s[0:1, :],
                            start=False, stop=True)
        s_l8 = p.step("tensor", l8_pe, deps=[s_z3, prev["bca"], s_wld], tag="l8")

        def l8m(e, p_):
            e.tensor_copy(l8f[0:1, 0:8], P0[0:1, 1024:1032])
            e.tensor_reduce(l8f[0:1, 8:9], P0[0:1, 1024:1032], AX.X, ALU.max)
            return padv(e)
        s_l8m = p.step("vector", l8m, deps=[s_l8], tag="l8m")

        def mxcp(e, p_):
            return e.copy(l8f[0:1, 20:21], l8f[0:1, 8:9])
        s_mx = p.step("scalar", mxcp, deps=[s_l8m], tag="mxcp")

        def oh(e, p_):
            e.tensor_tensor(oh_row[0:1, :], l8f[0:1, 0:8],
                            l8f[0:1, 20:21].broadcast_to((1, 8)), ALU.is_ge)
            e.tensor_tensor(l8f[0:1, 9:17], l8f[0:1, 0:8],
                            l8f[0:1, 20:21].broadcast_to((1, 8)), ALU.subtract)
            return padv(e)
        s_oh = p.step("vector", oh, deps=[s_mx], tag="oh")

        def ohdma1(e, p_):
            p_.dma(e, out=oh_d, in_=oh_row[0:1, :])
        s_ohd1 = p.step("gpsimd", ohdma1, n_dma=1, deps=[s_oh], tag="ohd1")

        def ohdma2(e, p_):
            p_.dma(e, out=ohT[:, :], in_=oh_d)
        s_ohd2 = p.step("gpsimd", ohdma2, n_dma=1, deps=[s_ohd1], tag="ohd2")

        def gexp(e, p_):
            return e.activation(l8f[0:1, 9:17], l8f[0:1, 9:17], AF.Exp)
        s_ge = p.step("scalar", gexp, deps=[s_oh], tag="gexp")

        def gsum(e, p_):
            e.tensor_reduce(l8f[0:1, 17:18], l8f[0:1, 9:17], AX.X, ALU.add)
            return padv(e)
        s_gs = p.step("vector", gsum, deps=[s_ge], tag="gsum")

        def grec(e, p_):
            e.activation(l8f[0:1, 18:19], l8f[0:1, 17:18], AF.Ln)
            return e.activation(l8f[0:1, 19:20], l8f[0:1, 18:19], AF.Exp, scale=-1.0)
        s_gr = p.step("scalar", grec, deps=[s_gs], tag="grec")

        def sel_pe(e, p_):
            e.matmul(P0[:, 1024:1032], c_onesrf[0:1, :], oh_row[0:1, :], start=True, stop=True)
            e.matmul(P0[0:64, 1537:1538], bdn_s[:, :], ohT[:, :], start=True, stop=True)
            for dt in range(NDT):
                e.matmul(P2[:, 8 + dt:9 + dt], bup_s[:, dt * 128:(dt + 1) * 128], ohT[:, :],
                         start=(dt == 0), stop=False)
            return e.matmul(P2[:, 16:17], c_onesrf[0:1, :], l8f[0:1, 19:20],
                            start=False, stop=True)
        s_sel = p.step("tensor", sel_pe, deps=[s_oh, s_gr, s_ohd2], tag="sel_pe")

        selc_s = sb("selc_s", 128, 10, F32)

        def sel_ev(e, p_):
            e.copy(selc_s[:, 0:9], P2[:, 8:17])
            return e.copy(selc_s[0:64, 9:10], P0[0:64, 1537:1538])
        s_selev = p.step("scalar", sel_ev, deps=[s_sel], tag="sel_ev")

        if dbg:
            def dmpr(e, p_):
                e.tensor_copy(sqf[0][0:1, 0:21], l8f[0:1, 0:21])
                e.tensor_copy(sqf[0][:, 21:22], zsum2[:, 0:1])
                e.tensor_copy(sqf[0][:, 22:23], tmpq[:, 0:1])
                e.tensor_copy(sqf[0][:, 23:24], tmpq[:, 2 * NDT:2 * NDT + 1])
                e.tensor_copy(sqf[0][0:1, 24:32], oh_row[0:1, :])
                e.tensor_copy(sqf[0][:, 30:31], zbn[:, 0:1])
                return e.tensor_copy(sqf[0][:, 31:32], zsum[:, 0:1])
            s_dmp = p.step("vector", dmpr, deps=[s_gr, s_oh, s_z3], tag="dmpr")
            def dmpd(e, p_):
                p_.dma(e, out=rdbg[:, :], in_=sqf[0][:, 0:32])
            s_dmpd = p.step("gpsimd", dmpd, n_dma=1, deps=[s_dmp], tag="dmpd")

        s_ws = None
        for ei in range(8):
            def wsel_acc(e, p_, ei=ei):
                oc = P0[:, 1024 + ei:1025 + ei]
                if ei == 0:
                    e.tensor_scalar(wsel[0][:, :], wdn_s[:, ei * 512:(ei + 1) * 512],
                                    oc, None, ALU.mult)
                    return e.tensor_scalar(wusel[0][:, :], wup_s[:, ei * D:(ei + 1) * D],
                                           oc[0:64, :], None, ALU.mult)
                src, dst = wsel[(ei - 1) % 2], wsel[ei % 2]
                usrc, udst = wusel[(ei - 1) % 2], wusel[ei % 2]
                e.scalar_tensor_tensor(dst[:, :], wdn_s[:, ei * 512:(ei + 1) * 512],
                                       oc, src[:, :], ALU.mult, ALU.add)
                return e.scalar_tensor_tensor(udst[:, :], wup_s[:, ei * D:(ei + 1) * D],
                                              oc[0:64, :], usrc[:, :], ALU.mult, ALU.add)
            s_ws = p.step("vector", wsel_acc, deps=[s_sel if ei == 0 else s_ws],
                          tag=f"wsel{ei}")
        wfin = wsel[7 % 2]
        ufin = wusel[7 % 2]

        s_cout = None
        s_hev = None
        s_yev = None
        for c in range(NCHUNK):
            def ldz(e, p_, cc=c):
                p_.dma(e, out=zbf[:, :].rearrange("p (a n) -> p a n", a=NDT),
                       in_=z_h[cc * D:(cc + 1) * D, :].rearrange("(a p) n -> p a n", p=128))
            s_ldz = p.step("sync", ldz, n_dma=1, tag=f"G{c}_ldz",
                           deps=[s_zdma[c], s_zdma[NCHUNK - 1] if c == 0 else s_hev])

            def h_pe(e, p_):
                for dt in range(NDT):
                    ins = e.matmul(P3[0:64, :], wfin[:, dt * 64:(dt + 1) * 64],
                                   zbf[:, dt * CH:(dt + 1) * CH],
                                   start=(dt == 0), stop=(dt == NDT - 1))
                return ins
            s_hpe = p.step("tensor", h_pe, deps=[s_ldz, s_ws, s_hev], tag=f"G{c}_hpe")

            def h_ev(e, p_):
                return e.scalar_tensor_tensor(hbf[:, :], P3[0:64, :], selc_s[0:64, 9:10],
                                              c_zer[:, :], ALU.add, ALU.max)
            s_hev = p.step("vector", h_ev, deps=[s_hpe, s_yev, s_selev], tag=f"G{c}_hev")

            for dt in range(NDT):
                def y_pe(e, p_, dt=dt):
                    yo = P4[:, :] if dt % 2 == 0 else P5[:, :]
                    return e.matmul(yo, ufin[:, dt * 128:(dt + 1) * 128], hbf[:, :],
                                    start=True, stop=True)
                dep_war = s_yev if dt >= 2 else (prev["statm"] if c == 0 else None)
                s_ype = p.step("tensor", y_pe, deps=[s_hev, dep_war], tag=f"G{c}_ype{dt}")

                def y_ev(e, p_, dt=dt):
                    yo = P4[:, :] if dt % 2 == 0 else P5[:, :]
                    return e.tensor_scalar(resf[:, :], yo, selc_s[:, dt:dt + 1],
                                           selc_s[:, 8:9], ALU.add, ALU.mult)
                s_yev = p.step("vector", y_ev, deps=[s_ype, s_cout, s_selev], tag=f"G{c}_yev{dt}")

                def c_out(e, p_, dt=dt, cc=c):
                    p_.dma(e, out=x4T[dt * 128:(dt + 1) * 128, cc * CH:(cc + 1) * CH],
                           in_=resf[:, :], accum_op=ALU.add)
                s_cout = p.step("gpsimd", c_out, n_dma=1, deps=[s_yev, s_xcopy],
                                tag=f"G{c}_out{dt}")

        p.wait_only("gpsimd", [s_cout, s_xcopy], tag="final")
        if fctx is not None:
            p.fence()
            return nc
        with nc.Block() as block:
            p.emit(block)
    return nc


def host_l1_ffn_inputs(ln_g, ln_b, bn_g, bn_b, bn_mean, bn_var, Wr, br,
                       W_down, b_down, W_up, b_up, xs):
    def col(v):
        return np.ascontiguousarray(np.asarray(v, np.float32).reshape(NDT, 128).T)
    maps = []
    for c in range(NC):
        maps.append({
            "xT": np.ascontiguousarray(np.asarray(xs[c], np.float32).T),
            "lng_t": col(ln_g), "lnb_t": col(ln_b), "bng_t": col(bn_g),
            "bnb_t": col(bn_b), "bnm_t": col(bn_mean), "bnv_t": col(bn_var),
            "Wr": np.asarray(Wr, np.float32),
            "br_row": np.asarray(br, np.float32).reshape(1, 8),
            "Wdn": np.asarray(W_down, np.float32).reshape(8 * D, 64),
            "bdn": np.asarray(b_down, np.float32),
            "Wup": np.asarray(W_up, np.float32).reshape(8 * 64, D),
            "bup": np.asarray(b_up, np.float32),
        })
    return maps
NEXP = 8
CAPE = 1024
NGRP = 32
INVB = float(1 << 23)


def build_l0_moe(dbg=False, upto="all", fctx=None):
    nc = fctx["nc"] if fctx else bass.Bass("TRN2", target_bir_lowering=False, debug=False, detect_race_conditions=False)
    xT = fctx["xin"] if fctx else nc.dram_tensor("xT", [D, T], F32, kind="ExternalInput").ap()
    rmsw_t = nc.dram_tensor("rmsw_t", [128, NDT], F32, kind="ExternalInput").ap()
    gwt = nc.dram_tensor("gwt", [D, 64], F32, kind="ExternalInput").ap()
    w13s = nc.dram_tensor("w13s", [NEXP * D, D], F32, kind="ExternalInput").ap()
    w2s = nc.dram_tensor("w2s", [NEXP * 512, D], F32, kind="ExternalInput").ap()
    swg = nc.dram_tensor("swg", [D, 512], F32, kind="ExternalInput").ap()
    swu = nc.dram_tensor("swu", [D, 512], F32, kind="ExternalInput").ap()
    swd = nc.dram_tensor("swd", [512, D], F32, kind="ExternalInput").ap()
    identb = nc.dram_tensor("identb", [128, 128], BF16, kind="ExternalInput").ap()
    iotabig = nc.dram_tensor("iotabig", [128, 512], F32, kind="ExternalInput").ap()
    tokv = nc.dram_tensor("tokv", [128, 256], F32, kind="ExternalInput").ap()
    idx8q = nc.dram_tensor("idx8q", [128, 1], I32, kind="ExternalInput").ap()
    idxrt = nc.dram_tensor("idxrt", [128, 32], I32, kind="ExternalInput").ap()
    core8 = nc.dram_tensor("core8", [128, 1], F32, kind="ExternalInput").ap()
    identf = nc.dram_tensor("identf", [128, 128], F32, kind="ExternalInput").ap()
    x2T = fctx["xout"] if fctx else nc.dram_tensor("x2T", [D, T], F32, kind="ExternalOutput").ap()
    if dbg:
        rdbg = nc.dram_tensor("rdbg", [128, 96], F32, kind="ExternalOutput").ap()
    h_h = nc.dram_tensor("h_h", [NCHUNK * D, CH], F32).ap()
    sh_h = nc.dram_tensor("sh_h", [NCHUNK * D, CH], BF16).ap()
    xnn_b = nc.dram_tensor("xnn_b", [T, D], BF16)
    xn_full = nc.dram_tensor("xn_full", [NC * T, D], BF16)
    lg_b = nc.dram_tensor("lg_b", [T, 64], F32)
    lg_full = nc.dram_tensor("lg_full", [NC * T, 64], F32)
    rt2 = nc.dram_tensor("rt2", [NC * T, 8], F32,
                         kind="ExternalOutput" if dbg else "Internal").ap()
    mo_h = nc.dram_tensor("mo_h", [T, D], BF16).ap()
    table = nc.dram_tensor("table", [NEXP * CAPE, 1], F32).ap()
    yb_b = nc.dram_tensor("yb_b", [NEXP * CAPE, D], BF16)
    yb_fullA = nc.dram_tensor("yb_fullA", [NC * NEXP // 2 * CAPE, D], BF16)
    yb_fullB = nc.dram_tensor("yb_fullB", [NC * NEXP // 2 * CAPE, D], BF16)
    if dbg:
        ydbgA = nc.dram_tensor("ydbgA", [NC * NEXP // 2 * CAPE, D], BF16, kind="ExternalOutput").ap()
        ydbgB = nc.dram_tensor("ydbgB", [NC * NEXP // 2 * CAPE, D], BF16, kind="ExternalOutput").ap()

    with contextlib.ExitStack() as ctx:
        pref = fctx["tag"] if fctx else ""
        sb = lambda name, p_, f, dt=BF16: ctx.enter_context(nc.sbuf_tensor(pref + name, [p_, f], dt))
        c_ones = sb("c_ones", 128, 1)
        c_onesrf = sb("c_onesrf", 1, 128, F32)
        c_identb = sb("c_identb", 128, 128)
        c_identf = sb("c_identf", 128, 128, F32)
        c_iotab = sb("c_iotab", 128, 512, F32)
        c_tokv = sb("c_tokv", 128, 256, F32)
        c_idx8q = sb("c_idx8q", 128, 1, I32)
        c_idxrt = sb("c_idxrt", 128, 32, I32)
        c_core8 = sb("c_core8", 128, 1, F32)
        c_zer64 = sb("c_zer64", 64, 1024)
        rmsw_s = sb("rmsw_s", 128, NDT, F32)
        gwt_s = sb("gwt_s", 128, NDT * 64, F32)
        swg_s = [sb(f"swg{i}", 128, 512) for i in range(NDT)]
        swu_s = [sb(f"swu{i}", 128, 512) for i in range(NDT)]
        swd_s = [sb(f"swd{i}", 128, D) for i in range(4)]
        padb = [sb(f"padb{i}", 128, 1024) for i in range(2)]
        dum = sb("dum", 128, 16)
        sxb = sb("sxb", 128, NDT * CH)
        sqp = [sb(f"sqp{i}", 128, CH) for i in range(2)]
        hF = sb("hF", 128, NDT * CH, F32)
        xnp = [sb(f"xnp{i}", 128, CH, F32) for i in range(2)]
        xnb = sb("xnb", 128, NDT * CH)
        tmpf = sb("tmpf", 128, CH, F32)
        mrow = sb("mrow", 1, CH, F32)
        rrow = sb("rrow", 1, 2 * CH, F32)
        lgev = sb("lgev", 128, 256, F32)
        gsb = sb("gsb", 128, 4 * CH)
        hub = sb("hub", 128, 4 * CH)
        shp = [sb(f"shp{i}", 128, CH) for i in range(2)]
        xtn = sb("xtn", 128, D)
        lgb2 = [sb(f"lgb{i}", 128, 512, F32) for i in range(2)]
        lgb = lgb2[0]
        eq1 = sb("eq1", 128, 512, F32)
        eq2 = sb("eq2", 128, 512, F32)
        uw = sb("uw", 128, 512, F32)
        Ab = sb("Ab", 128, 512)
        AT = sb("AT", 64, 1024)
        CT = sb("CT", 64, 1024, F32)
        Cx = sb("Cx", 128, 512, F32)
        carry = sb("carry", 64, 2, F32)
        m1B = sb("m1B", 128, 256, F32)
        m2B = sb("m2B", 128, 256, F32)
        e1B = sb("e1B", 128, 256, F32)
        e2B = sb("e2B", 128, 256, F32)
        p1B = sb("p1B", 128, 256, F32)
        p2B = sb("p2B", 128, 256, F32)
        wA = sb("wA", 128, 512, F32)
        gA = sb("gA", 128, 512, F32)
        gB = sb("gB", 128, 512, F32)
        pxi = sb("pxi", 128, 512, I32)
        pxi2 = sb("pxi2", 128, 512, I32)
        rtpack = sb("rtpack", 128, 2048, F32)
        ix1 = sb("ix1", 128, 512, I32)
        ix2 = sb("ix2", 128, 512, I32)
        zc = sb("zc", 128, 512, F32)
        zrow = sb("zrow", 128, 16, F32)
        tblf = sb("tblf", 128, 64, F32)
        tbli = sb("tbli", 128, 64, I32)
        xbn = sb("xbn", 128, D)
        xbT = sb("xbT", 128, NDT * D)
        w13b = sb("w13b", 128, NDT * D)
        w2b = xnb
        hTb = sb("hTb", 128, 4 * D)
        ybT = sb("ybT", 128, NDT * D)
        ybn = sb("ybn", 128, D)
        r1b = sxb[:, CH * 2:CH * 4]
        r2b = sxb[:, CH * 4:CH * 6]
        rtc = sb("rtc", 128, 32, F32)
        rti = sb("rti", 128, 16, I32)
        mot4 = xnb
        mdt = sb("mdt", 128, 2 * CH)
        x1c = hF[:, 0:2 * CH]
        hc = hF[:, 2 * CH:4 * CH]
        shc = sxb[:, 0:2 * CH]

        P2 = ctx.enter_context(nc.psum_tensor(pref + "P2", [128, 512], F32))
        P3 = ctx.enter_context(nc.psum_tensor(pref + "P3", [128, 512], F32))
        P4 = ctx.enter_context(nc.psum_tensor(pref + "P4", [128, 512], F32))
        P5 = ctx.enter_context(nc.psum_tensor(pref + "P5", [128, 512], F32))
        PT0 = ctx.enter_context(nc.psum_tensor(pref + "PT0", [128, 1024], BF16))
        PT1 = ctx.enter_context(nc.psum_tensor(pref + "PT1", [128, 1024], BF16))
        _p0cm = nc.psum_tensor(pref + "P0", [128, 1024], F32)
        P0 = _p0cm.__enter__()

        cc1 = ctx.enter_context(nc.semaphore("cc1"))
        cc2 = ctx.enter_context(nc.semaphore("cc2"))
        cc3 = ctx.enter_context(nc.semaphore("cc3"))
        cc4 = ctx.enter_context(nc.semaphore("cc4"))

        p = fctx["p"] if fctx else P(nc, ctx)
        regs = {}

        def padv(e):
            return e.tensor_copy(padb[0][:, :], padb[1][:, :])

        def ld_w(e, p_):
            p_.dma(e, out=rmsw_s[:, :], in_=rmsw_t)
            p_.dma(e, out=c_identb[:, :], in_=identb)
            p_.dma(e, out=c_identf[:, :], in_=identf)
            p_.dma(e, out=c_iotab[:, :], in_=iotabig)
            p_.dma(e, out=c_tokv[:, :], in_=tokv)
            p_.dma(e, out=c_idx8q[:, :], in_=idx8q)
            p_.dma(e, out=c_idxrt[:, :], in_=idxrt)
            p_.dma(e, out=c_core8[:, :], in_=core8)
            p_.dma(e, out=gwt_s[:, :].rearrange("p (a n) -> p a n", a=NDT),
                   in_=gwt.rearrange("(a p) n -> p a n", p=128))
            for i in range(NDT):
                p_.dma(e, out=swg_s[i][:, :], in_=swg[i * 128:(i + 1) * 128, :])
                p_.dma(e, out=swu_s[i][:, :], in_=swu[i * 128:(i + 1) * 128, :])
            for i in range(4):
                p_.dma(e, out=swd_s[i][:, :], in_=swd[i * 128:(i + 1) * 128, :])
        s_wld = p.step("gpsimd", ld_w, n_dma=9 + 2 * NDT + 4, tag="wld")

        def mkregs(e, p_):
            regs["r32k"] = e.to_reg(NC * T - 1)
            regs["r64k"] = e.to_reg(64 * CAPE - 1)
            regs["r8k"] = e.to_reg(NEXP * CAPE - 1)
            return e.memset(dum[:, :], 0.0)
        s_regs = p.step("gpsimd", mkregs, tag="regs")

        def consts(e, p_):
            e.memset(c_ones[:, :], 1.0)
            e.memset(c_onesrf[0:1, :], 1.0)
            e.memset(c_zer64[:, :], 0.0)
            e.memset(carry[:, :], 0.0)
            e.memset(padb[1][:, :], 0.0)
            return e.memset(zc[:, :], 0.0)
        s_const = p.step("vector", consts, tag="consts")

        def tblzero(e, p_):
            p_.dma(e, out=table.rearrange("(q a) b -> q a b", q=128).rearrange("q a b -> q (a b)"),
                   in_=zc[:, 0:NEXP * CAPE // 128])
        s_tblz = p.step("gpsimd", tblzero, n_dma=1, deps=[s_const], tag="tblz")

        bca_m = P0[:, 0:CH]
        bca_r = P0[:, CH:2 * CH]
        prev = dict(statm=None, stats_pe=None)
        s_sqs, s_pes, s_hdma, s_shdma = {}, {}, {}, {}
        s_xnl = [None] * NCHUNK
        s_lgdma = [None] * NCHUNK
        s_hlast = [None] * NCHUNK
        s_lge_prev = None
        s_sh_prev = None
        s_ptchain = None
        s_xtnd_prev = None
        s_bca_prev = None
        s_lgdma_last = None
        s_xtnd_all = []

        for c in range(NCHUNK):
            def ldxb(e, p_, cc=c):
                p_.dma(e, out=sxb[:, :].rearrange("p (a n) -> p a n", a=NDT),
                       in_=xT[:, cc * CH:(cc + 1) * CH].rearrange("(a p) n -> p a n", p=128))
            s_ldxb = p.step("gpsimd", ldxb, n_dma=1, tag=f"M1{c}_ldxb",
                            deps=[s_hlast[c - 1] if c else None, prev["stats_pe"]])

            for dt in range(NDT):
                def sq(e, p_, dt=dt):
                    return e.tensor_tensor(sqp[dt % 2][:, :], sxb[:, dt * CH:(dt + 1) * CH],
                                           sxb[:, dt * CH:(dt + 1) * CH], ALU.mult)
                s_sqs[(c, dt)] = p.step("vector", sq, tag=f"M1{c}_sq{dt}",
                                        deps=[s_ldxb, s_pes.get((c, dt - 2)),
                                              prev["stats_pe"] if dt < 2 else None])

                def pe2(e, p_, dt=dt):
                    e.matmul(P4[0:1, :], c_ones[:, :], sxb[:, dt * CH:(dt + 1) * CH],
                             start=(dt == 0), stop=(dt == NDT - 1))
                    return e.matmul(P5[0:1, :], c_ones[:, :], sqp[dt % 2][:, :],
                                    start=(dt == 0), stop=(dt == NDT - 1))
                s_pes[(c, dt)] = p.step("tensor", pe2, tag=f"M1{c}_pe{dt}",
                                        deps=[s_sqs[(c, dt)], s_const,
                                              prev["statm"] if dt == 0 else None])
            s_stats_pe = s_pes[(c, NDT - 1)]
            prev["stats_pe"] = s_stats_pe

            def statm1(e, p_):
                e.tensor_scalar(mrow[0:1, :], P4[0:1, :], 1.0 / D, None, ALU.mult)
                e.tensor_scalar(rrow[0:1, 0:CH], P5[0:1, :], 1.0 / D, None, ALU.mult)
                e.tensor_tensor(rrow[0:1, CH:], mrow[0:1, :], mrow[0:1, :], ALU.mult)
                e.tensor_tensor(rrow[0:1, 0:CH], rrow[0:1, 0:CH], rrow[0:1, CH:], ALU.subtract)
                return e.tensor_scalar(rrow[0:1, 0:CH], rrow[0:1, 0:CH], 1e-5, None, ALU.add)
            s_sm1 = p.step("vector", statm1, deps=[s_stats_pe, s_bca_prev], tag=f"M1{c}_sm1")

            def sqrtf(e, p_):
                return e.activation(rrow[0:1, CH:], rrow[0:1, 0:CH], AF.Sqrt)
            s_sq2 = p.step("scalar", sqrtf, deps=[s_sm1], tag=f"M1{c}_sqrt")

            def recipf(e, p_):
                return e.reciprocal(rrow[0:1, 0:CH], rrow[0:1, CH:])
            s_rc = p.step("vector", recipf, deps=[s_sq2], tag=f"M1{c}_rec")
            prev["statm"] = s_rc

            def bcaf(e, p_):
                e.matmul(bca_m, c_onesrf[0:1, :], mrow[0:1, :], start=True, stop=True)
                return e.matmul(bca_r, c_onesrf[0:1, :], rrow[0:1, 0:CH], start=True, stop=True)
            s_bca = p.step("tensor", bcaf, deps=[s_rc, s_sh_prev], tag=f"M1{c}_bca")
            s_bca_prev = s_bca

            s_h = None
            for dt in range(NDT):
                def hstep(e, p_, dt=dt):
                    e.tensor_tensor(tmpf[:, :], sxb[:, dt * CH:(dt + 1) * CH], bca_m, ALU.subtract)
                    e.tensor_tensor(hF[:, dt * CH:(dt + 1) * CH], tmpf[:, :], bca_r, ALU.mult)
                    return e.tensor_tensor(sqp[dt % 2][:, :], hF[:, dt * CH:(dt + 1) * CH],
                                           hF[:, dt * CH:(dt + 1) * CH], ALU.mult)
                s_h = p.step("vector", hstep, tag=f"M1{c}_h{dt}",
                             deps=[s_bca, s_pes.get((c, "r", dt - 2)),
                                   s_hdma.get((c - 1, dt))])

                def hdma(e, p_, dt=dt, cc=c):
                    p_.dma(e, out=h_h[cc * D + dt * 128: cc * D + (dt + 1) * 128, :],
                           in_=hF[:, dt * CH:(dt + 1) * CH])
                s_hdma[(c, dt)] = p.step("gpsimd", hdma, n_dma=1, deps=[s_h], tag=f"M1{c}_hd{dt}")

                def rpe(e, p_, dt=dt):
                    return e.matmul(P5[0:1, :], c_ones[:, :], sqp[dt % 2][:, :],
                                    start=(dt == 0), stop=(dt == NDT - 1))
                s_pes[(c, "r", dt)] = p.step("tensor", rpe, deps=[s_h], tag=f"M1{c}_rpe{dt}")
            s_hlast[c] = s_h

            def rmsrow(e, p_):
                return e.tensor_scalar(rrow[0:1, 0:CH], P5[0:1, :], 1.0 / D, 1e-6,
                                       ALU.mult, ALU.add)
            s_rr = p.step("vector", rmsrow, deps=[s_pes[(c, "r", NDT - 1)]], tag=f"M1{c}_rr")
            prev["statm"] = s_rr

            def rmsact(e, p_):
                e.activation(rrow[0:1, CH:], rrow[0:1, 0:CH], AF.Ln)
                return e.activation(rrow[0:1, 0:CH], rrow[0:1, CH:], AF.Exp, scale=-0.5)
            s_ra = p.step("scalar", rmsact, deps=[s_rr], tag=f"M1{c}_ra")

            def r2bca(e, p_):
                return e.matmul(P4[:, :], c_onesrf[0:1, :], rrow[0:1, 0:CH],
                                start=True, stop=True)
            s_r2b = p.step("tensor", r2bca, deps=[s_ra, s_sm1], tag=f"M1{c}_r2b")

            s_lgpe = {}
            s_xc = None
            for dt in range(NDT):
                def xns(e, p_, dt=dt):
                    e.tensor_tensor(xnp[dt % 2][:, :], hF[:, dt * CH:(dt + 1) * CH],
                                    P4[:, :], ALU.mult)
                    return e.tensor_scalar(xnp[dt % 2][:, :], xnp[dt % 2][:, :],
                                           rmsw_s[:, dt:dt + 1], None, ALU.mult)
                s_x = p.step("vector", xns, deps=[s_r2b, s_lgpe.get(dt - 2)], tag=f"M1{c}_xn{dt}")

                def xnc(e, p_, dt=dt):
                    return e.tensor_copy(xnb[:, dt * CH:(dt + 1) * CH], xnp[dt % 2][:, :])
                s_xc = p.step("vector", xnc, deps=[s_x], tag=f"M1{c}_xnc{dt}")

                def lgpe(e, p_, dt=dt):
                    for sub in range(4):
                        ins = e.matmul(P2[:, sub * 64:(sub + 1) * 64],
                                       xnp[dt % 2][:, sub * 128:(sub + 1) * 128],
                                       gwt_s[:, dt * 64:(dt + 1) * 64],
                                       start=(dt == 0 and sub == 0),
                                       stop=(dt == NDT - 1 and sub == 3))
                    return ins
                s_lgpe[dt] = p.step("tensor", lgpe, tag=f"M1{c}_lg{dt}",
                                    deps=[s_x, s_wld, s_lge_prev if dt < 2 else None])
            s_xnl[c] = s_xc

            def lgevf(e, p_):
                return e.tensor_copy(lgev[:, :], P2[:, 0:256])
            s_lge = p.step("vector", lgevf, deps=[s_lgpe[NDT - 1], s_lgdma_last], tag=f"M1{c}_lgev")
            s_lge_prev = s_lge

            def lgdma(e, p_, cc=c):
                p_.dma(e, out=lg_b.ap()[cc * CH:(cc + 1) * CH, :].rearrange("(s q) e -> q s e", q=128),
                       in_=lgev[:, :].rearrange("q (s e) -> q s e", e=64))
            s_lgdma_last = p.step("gpsimd", lgdma, n_dma=1, deps=[s_lge], tag=f"M1{c}_lgd")
            s_lgdma[c] = s_lgdma_last

            s_uev = {}
            for m in range(4):
                def gu_pe(e, p_, m=m):
                    for kt in range(NDT):
                        e.matmul(P0[:, 0:CH], swg_s[kt][:, m * 128:(m + 1) * 128],
                                 xnb[:, kt * CH:(kt + 1) * CH],
                                 start=(kt == 0), stop=(kt == NDT - 1))
                    for kt in range(NDT):
                        ins = e.matmul(P0[:, CH:2 * CH], swu_s[kt][:, m * 128:(m + 1) * 128],
                                       xnb[:, kt * CH:(kt + 1) * CH],
                                       start=(kt == 0), stop=(kt == NDT - 1))
                    return ins
                s_gu = p.step("tensor", gu_pe, tag=f"M1{c}_gu{m}",
                              deps=[s_xnl[c], s_uev.get(m - 1)])

                def gsilu(e, p_, m=m):
                    return e.activation(gsb[:, m * CH:(m + 1) * CH], P0[:, 0:CH], AF.Silu)
                s_gs = p.step("scalar", gsilu, deps=[s_gu], tag=f"M1{c}_gs{m}")

                def uevf(e, p_, m=m):
                    return e.tensor_tensor(hub[:, m * CH:(m + 1) * CH],
                                           gsb[:, m * CH:(m + 1) * CH],
                                           P0[:, CH:2 * CH], ALU.mult)
                s_uev[m] = p.step("vector", uevf, deps=[s_gu, s_gs], tag=f"M1{c}_ue{m}")

            s_she_hist = {}
            for mo in range(NDT):
                def sh_pe(e, p_, mo=mo):
                    PS = P3 if mo % 2 == 0 else P4
                    for kt in range(4):
                        ins = e.matmul(PS[:, :], swd_s[kt][:, mo * 128:(mo + 1) * 128],
                                       hub[:, kt * CH:(kt + 1) * CH],
                                       start=(kt == 0), stop=(kt == 3))
                    return ins
                s_shp = p.step("tensor", sh_pe, tag=f"M1{c}_shp{mo}",
                               deps=[s_uev[3], s_she_hist.get(mo - 2),
                                     s_sh_prev if mo == 0 else None])

                def sh_ev(e, p_, mo=mo):
                    PS = P3 if mo % 2 == 0 else P4
                    return e.tensor_copy(shp[mo % 2][:, :], PS[:, :])
                s_she = p.step("vector", sh_ev, deps=[s_shp, s_shdma.get((c, mo - 2))],
                               tag=f"M1{c}_she{mo}")
                s_she_hist[mo] = s_she
                s_sh_prev = s_she

                def sh_dma(e, p_, mo=mo, cc=c):
                    p_.dma(e, out=sh_h[cc * D + mo * 128: cc * D + (mo + 1) * 128, :],
                           in_=shp[mo % 2][:, :])
                s_shdma[(c, mo)] = p.step("gpsimd", sh_dma, n_dma=1, deps=[s_she],
                                          tag=f"M1{c}_shd{mo}")

            for sub in range(4):
                s_evl = None
                for dt2 in range(0, NDT, 2):
                    def trx(e, p_, sub=sub, dt2=dt2):
                        e.transpose(PT0[:, 0:128],
                                    xnb[:, dt2 * CH + sub * 128: dt2 * CH + sub * 128 + 128],
                                    c_identb[:, :])
                        return e.transpose(PT1[:, 0:128],
                                           xnb[:, (dt2 + 1) * CH + sub * 128: (dt2 + 1) * CH + sub * 128 + 128],
                                           c_identb[:, :])
                    s_tr = p.step("tensor", trx, tag=f"M2{c}_t{sub}_{dt2}",
                                  deps=[s_xnl[c], s_ptchain])

                    def trev(e, p_, dt2=dt2):
                        e.tensor_copy(xtn[:, dt2 * 128:(dt2 + 1) * 128], PT0[:, 0:128])
                        return e.tensor_copy(xtn[:, (dt2 + 1) * 128:(dt2 + 2) * 128], PT1[:, 0:128])
                    s_evl = p.step("vector", trev, tag=f"M2{c}_e{sub}_{dt2}",
                                   deps=[s_tr, s_xtnd_prev if dt2 == 0 else None])
                    s_ptchain = s_evl

                def xtdma(e, p_, sub=sub, cc=c):
                    p_.dma(e, out=xnn_b.ap()[cc * CH + sub * 128: cc * CH + (sub + 1) * 128, :],
                           in_=xtn[:, :])
                s_xtnd_prev = p.step("gpsimd", xtdma, n_dma=1, deps=[s_evl], tag=f"M2{c}_d{sub}")
                s_xtnd_all.append(s_xtnd_prev)

        _p0cm.__exit__(None, None, None)
        PT2 = ctx.enter_context(nc.psum_tensor(pref + "PT2", [128, 1024], BF16))
        PT3 = ctx.enter_context(nc.psum_tensor(pref + "PT3", [128, 1024], BF16))
        PTS = [PT0, PT1, PT2, PT3]
        ptq = {"k": 0, "ev": [None, None]}

        def pt_next():
            b0 = PTS[ptq["k"] % 4]
            b1 = PTS[(ptq["k"] + 1) % 4]
            dep = ptq["ev"][0]
            ptq["k"] += 2
            return b0, b1, dep

        def pt_done(ev):
            ptq["ev"] = [ptq["ev"][1], ev]

        def agi(e, p_):
            e.collective_compute("AllGather", ALU.bypass,
                                 replica_groups=[list(range(NC))],
                                 ins=[lg_b.ap()], outs=[lg_full.ap()]).then_inc(cc2)
            e.collective_compute("AllGather", ALU.bypass,
                                 replica_groups=[list(range(NC))],
                                 ins=[xnn_b.ap()], outs=[xn_full.ap()]).then_inc(cc1)
            return e.memset(dum[:, :], 1.0)
        s_agi = p.step("gpsimd", agi, deps=[s_xtnd_all[-1], s_lgdma[NCHUNK - 1]], tag="agi")
        s_ag_xn = (cc1, 1)
        s_ag_lg = (cc2, 1)

        def _early_exit():
            def dummy_out(e, p_):
                for cc in range(NCHUNK):
                    for dt in range(NDT):
                        p_.dma(e, out=x2T[dt * 128:(dt + 1) * 128, cc * CH:(cc + 1) * CH],
                               in_=zc[:, :])
            s_do = p.step("sync", dummy_out, n_dma=64, deps=[s_const], tag="dummy_out")
            p.wait_only("sync", [s_do, (p.dsem["gpsimd"], p.dma_val["gpsimd"])], tag="fin")
            with nc.Block() as block:
                p.emit(block)

        if upto == "M2":
            def agwait(e, p_):
                e.wait_ge(cc1, 1)
                return e.memset(dum[:, :], 3.0)
            p.step("gpsimd", agwait, deps=[s_ag_lg], tag="agw")
            _early_exit()
            return nc

        s_pos_prev = None
        s_eq1_prev = None
        s_eq2_prev = None
        s_ccp_prev = None
        s_pos_hist = {}
        s_tA_hist = {}
        s_evl2_prev = None
        for g in range(NGRP):
            lgb = lgb2[g % 2]

            def lgld(e, p_, g=g, lgb=lgb):
                p_.dma(e, out=lgb[:, :].rearrange("q (a n) -> q a n", n=64),
                       in_=lg_full.ap()[g * 1024:(g + 1) * 1024, :].rearrange("(a q) n -> q a n", q=128))
            s_lgld = p.step("sync", lgld, n_dma=1, tag=f"M3{g}_ld",
                            deps=[s_ag_lg, s_pos_hist.get(g - 1)])

            def r_m1(e, p_, g=g, lgb=lgb):
                e.tensor_reduce(m1B[:, g * 8:(g + 1) * 8],
                                lgb[:, :].rearrange("q (a n) -> q a n", n=64), AX.X, ALU.max)
                return padv(e)
            s_m1 = p.step("vector", r_m1, deps=[s_lgld], tag=f"M3{g}_m1")

            def c_m1(e, p_, g=g):
                return e.copy(zrow[:, 0:8], m1B[:, g * 8:(g + 1) * 8])
            s_cm1 = p.step("scalar", c_m1, deps=[s_m1, s_eq1_prev], tag=f"M3{g}_cm1")

            def f_eq1(e, p_, g=g, lgb=lgb):
                for t_ in range(8):
                    e.tensor_scalar(eq1[:, t_ * 64:(t_ + 1) * 64], lgb[:, t_ * 64:(t_ + 1) * 64],
                                    zrow[:, t_:t_ + 1], None, ALU.is_ge)
                e.scalar_tensor_tensor(lgb[:, :], eq1[:, :], -1e30, lgb[:, :],
                                       ALU.mult, ALU.add)
                e.tensor_reduce(m2B[:, g * 8:(g + 1) * 8],
                                lgb[:, :].rearrange("q (a n) -> q a n", n=64), AX.X, ALU.max)
                return padv(e)
            s_eq1 = p.step("vector", f_eq1, deps=[s_cm1], tag=f"M3{g}_eq1")
            s_eq1_prev = s_eq1

            def c_m2(e, p_, g=g):
                return e.copy(zrow[:, 8:16], m2B[:, g * 8:(g + 1) * 8])
            s_cm2 = p.step("scalar", c_m2, deps=[s_eq1, s_eq2_prev], tag=f"M3{g}_cm2")

            def f_eq2(e, p_, g=g, lgb=lgb):
                for t_ in range(8):
                    e.tensor_scalar(eq2[:, t_ * 64:(t_ + 1) * 64], lgb[:, t_ * 64:(t_ + 1) * 64],
                                    zrow[:, 8 + t_:9 + t_], None, ALU.is_ge)
                e.tensor_tensor(Ab[:, :], eq1[:, :], eq2[:, :], ALU.add)
                e.scalar_tensor_tensor(uw[:, :], eq1[:, :], -float(1 << 20), c_iotab[:, :],
                                       ALU.mult, ALU.add)
                e.tensor_reduce(e1B[:, g * 8:(g + 1) * 8],
                                uw[:, :].rearrange("q (a n) -> q a n", n=64), AX.X, ALU.min)
                e.scalar_tensor_tensor(uw[:, :], eq2[:, :], -float(1 << 20), c_iotab[:, :],
                                       ALU.mult, ALU.add)
                e.tensor_reduce(e2B[:, g * 8:(g + 1) * 8],
                                uw[:, :].rearrange("q (a n) -> q a n", n=64), AX.X, ALU.min)
                return padv(e)
            s_eq2 = p.step("vector", f_eq2, deps=[s_cm2, s_tA_hist.get(g - 1)],
                           tag=f"M3{g}_eq2")
            s_eq2_prev = s_eq2

            s_evl = None
            for t2 in range(0, 8, 2):
                ba, bb, pdep = pt_next()

                def trA(e, p_, t2=t2, ba=ba, bb=bb):
                    e.transpose(ba[0:64, 0:128], Ab[:, t2 * 64:(t2 + 1) * 64], c_identb[:, :])
                    return e.transpose(bb[0:64, 0:128], Ab[:, (t2 + 1) * 64:(t2 + 2) * 64],
                                       c_identb[:, :])
                s_tA = p.step("tensor", trA, deps=[s_eq2, pdep], tag=f"M3{g}_tA{t2}")
                s_tA_hist[g] = s_tA

                def trAe(e, p_, t2=t2, ba=ba, bb=bb):
                    e.tensor_copy(AT[:, t2 * 128:(t2 + 1) * 128], ba[0:64, 0:128])
                    return e.tensor_copy(AT[:, (t2 + 1) * 128:(t2 + 2) * 128], bb[0:64, 0:128])
                s_evl = p.step("vector", trAe, deps=[s_tA], tag=f"M3{g}_tAe{t2}")
                pt_done(s_evl)

            def scan(e, p_):
                e.tensor_tensor_scan(CT[:, :], AT[:, :], c_zer64[:, :], carry[:, 0:1],
                                     ALU.add, ALU.add)
                padv(e)
                e.tensor_copy(carry[:, 1:2], CT[:, 1023:1024])
                padv(e)
                e.tensor_tensor(CT[:, :], CT[:, :], AT[:, :], ALU.subtract)
                return padv(e)
            s_scan = p.step("vector", scan, deps=[s_evl, s_ccp_prev], tag=f"M3{g}_scan")

            def carrycp(e, p_):
                return e.copy(carry[:, 0:1], carry[:, 1:2])
            s_ccp_prev = p.step("scalar", carrycp, deps=[s_scan], tag=f"M3{g}_ccp")

            s_evl2 = s_evl2_prev
            for t2 in range(0, 8, 2):
                def trC(e, p_, t2=t2):
                    e.transpose(P2[0:128, 0:64], CT[0:64, t2 * 128:(t2 + 1) * 128],
                                c_identf[0:64, 0:64])
                    return e.transpose(P3[0:128, 0:64], CT[0:64, (t2 + 1) * 128:(t2 + 2) * 128],
                                       c_identf[0:64, 0:64])
                s_tC = p.step("tensor", trC, deps=[s_scan, s_evl2], tag=f"M3{g}_tC{t2}")

                def trCe(e, p_, t2=t2):
                    e.tensor_copy(Cx[:, t2 * 64:(t2 + 1) * 64], P2[0:128, 0:64])
                    return e.tensor_copy(Cx[:, (t2 + 1) * 64:(t2 + 2) * 64], P3[0:128, 0:64])
                s_evl2 = p.step("vector", trCe, deps=[s_tC], tag=f"M3{g}_tCe{t2}")

            def posf(e, p_, g=g):
                e.tensor_tensor(uw[:, :], Cx[:, :], eq1[:, :], ALU.mult)
                e.tensor_reduce(p1B[:, g * 8:(g + 1) * 8],
                                uw[:, :].rearrange("q (a n) -> q a n", n=64), AX.X, ALU.add)
                e.tensor_tensor(uw[:, :], Cx[:, :], eq2[:, :], ALU.mult)
                e.tensor_reduce(p2B[:, g * 8:(g + 1) * 8],
                                uw[:, :].rearrange("q (a n) -> q a n", n=64), AX.X, ALU.add)
                return padv(e)
            s_pos_prev = p.step("vector", posf, deps=[s_evl2], tag=f"M3{g}_pos")
            s_pos_hist[g] = s_pos_prev
            s_evl2_prev = s_evl2

        def fin1(e, p_):
            e.tensor_tensor(gA[:, 0:256], m1B[:, :], m2B[:, :], ALU.subtract)
            return padv(e)
        s_f1 = p.step("vector", fin1, deps=[s_pos_prev], tag="fin1")

        def fin2(e, p_):
            e.activation(wA[:, 0:256], gA[:, 0:256], AF.Sigmoid)
            return e.activation(wA[:, 256:512], gA[:, 0:256], AF.Sigmoid, scale=-1.0)
        s_f2 = p.step("scalar", fin2, deps=[s_f1], tag="fin2")

        def fin3(e, p_):
            e.tensor_scalar(gA[:, 0:256], e1B[:, :], 1024.0, None, ALU.mult)
            e.tensor_scalar(gA[:, 256:512], e2B[:, :], 1024.0, None, ALU.mult)
            padv(e)
            e.tensor_tensor(gA[:, 0:256], gA[:, 0:256], p1B[:, :], ALU.add)
            e.tensor_tensor(gA[:, 256:512], gA[:, 256:512], p2B[:, :], ALU.add)
            padv(e)
            e.tensor_scalar(gB[:, 0:256], p1B[:, :], 1023.5, None, ALU.is_gt)
            e.tensor_scalar(gB[:, 256:512], p2B[:, :], 1023.5, None, ALU.is_gt)
            padv(e)
            e.scalar_tensor_tensor(gB[:, :], gB[:, :], INVB, gA[:, :], ALU.mult, ALU.add)
            return padv(e)
        s_f3 = p.step("vector", fin3, deps=[s_f2], tag="fin3")

        def fin4(e, p_):
            e.tensor_copy(eq1[:, 0:256], e1B[:, :])
            e.tensor_copy(eq1[:, 256:512], e2B[:, :])
            e.tensor_copy(eq2[:, 0:256], p1B[:, :])
            e.tensor_copy(eq2[:, 256:512], p2B[:, :])
            padv(e)
            e.tensor_scalar(uw[:, :], eq1[:, :], c_core8[:, 0:1], None, ALU.subtract)
            padv(e)
            e.tensor_scalar(Cx[:, :], uw[:, :], -0.5, None, ALU.is_lt)
            e.tensor_scalar(lgb[:, :], uw[:, :], 7.5, None, ALU.is_gt)
            padv(e)
            e.tensor_tensor(Cx[:, :], Cx[:, :], lgb[:, :], ALU.add)
            e.tensor_scalar(lgb[:, :], eq2[:, :], 1023.5, None, ALU.is_gt)
            padv(e)
            e.tensor_tensor(Cx[:, :], Cx[:, :], lgb[:, :], ALU.add)
            padv(e)
            e.tensor_copy(pxi[:, :], eq2[:, :])
            e.tensor_copy(pxi2[:, :], eq2[:, :])
            padv(e)
            e.tensor_scalar(pxi[:, :], pxi[:, :], 127, None, ALU.bitwise_and)
            e.tensor_scalar(pxi2[:, :], pxi2[:, :], 7, None, ALU.logical_shift_right)
            padv(e)
            e.tensor_scalar(pxi[:, :], pxi[:, :], 6, None, ALU.logical_shift_left)
            padv(e)
            e.tensor_tensor(pxi[:, :], pxi[:, :], pxi2[:, :], ALU.add)
            e.tensor_copy(pxi2[:, :], uw[:, :])
            padv(e)
            e.tensor_scalar(pxi2[:, :], pxi2[:, :], 3, None, ALU.arith_shift_left)
            padv(e)
            e.tensor_tensor(pxi[:, :], pxi[:, :], pxi2[:, :], ALU.add)
            e.tensor_copy(pxi2[:, :], Cx[:, :])
            padv(e)
            e.tensor_scalar(pxi2[:, :], pxi2[:, :], 20, None, ALU.logical_shift_left)
            padv(e)
            e.tensor_tensor(pxi[:, :], pxi[:, :], pxi2[:, :], ALU.add)
            return padv(e)
        s_f4 = p.step("vector", fin4, deps=[s_f3], tag="fin4")

        def fin4b(e, p_):
            e.tensor_copy(ix1[:, :], gB[:, :])
            padv(e)
            e.tensor_scalar(pxi2[:, :], ix1[:, :], 12, None, ALU.logical_shift_right)
            padv(e)
            e.tensor_scalar(pxi2[:, :], pxi2[:, :], 1, None, ALU.bitwise_and)
            e.tensor_scalar(ix2[:, :], ix1[:, :], 13, None, ALU.logical_shift_right)
            padv(e)
            e.tensor_scalar(pxi2[:, :], pxi2[:, :], 12, None, ALU.logical_shift_left)
            e.tensor_scalar(ix2[:, :], ix2[:, :], 12, None, ALU.logical_shift_left)
            padv(e)
            e.tensor_tensor(ix1[:, :], ix1[:, :], ix2[:, :], ALU.subtract)
            padv(e)
            e.tensor_tensor(ix1[:, :], ix1[:, :], pxi2[:, :], ALU.subtract)
            e.tensor_scalar(pxi2[:, :], pxi2[:, :], 11, None, ALU.logical_shift_left)
            padv(e)
            e.tensor_tensor(ix2[:, :], ix1[:, :], pxi2[:, :], ALU.add)
            e.tensor_scalar(ix1[:, :], ix1[:, :], 1 << 23, None, ALU.add)
            padv(e)
            e.tensor_tensor(ix1[:, :], ix1[:, :], pxi2[:, :], ALU.subtract)
            return padv(e)
        s_f4b = p.step("vector", fin4b, deps=[s_f4], tag="fin4b")

        def fin5(e, p_):
            rp = rtpack[:, :].rearrange("q (t f) -> q t f", f=8)
            one = lambda v: v.rearrange("q (t o) -> q t o", o=1)
            e.tensor_copy(rp[:, :, 0:1], one(ix2[:, 0:256]))
            e.tensor_copy(rp[:, :, 1:2], one(ix1[:, 0:256]))
            e.tensor_copy(rp[:, :, 2:3], one(ix2[:, 256:512]))
            e.tensor_copy(rp[:, :, 3:4], one(ix1[:, 256:512]))
            e.tensor_copy(rp[:, :, 4:5], one(wA[:, 0:256]))
            e.tensor_copy(rp[:, :, 5:6], one(wA[:, 256:512]))
            e.tensor_copy(rp[:, :, 6:7], one(wA[:, 0:256]))
            e.tensor_copy(rp[:, :, 7:8], one(wA[:, 256:512]))
            return padv(e)
        s_f5 = p.step("vector", fin5, deps=[s_f4b], tag="fin5")

        def rtdma(e, p_):
            p_.dma(e, out=rt2.rearrange("(q t) f -> q (t f)", q=128),
                   in_=rtpack[:, :])
        s_rtd = p.step("gpsimd", rtdma, n_dma=1, deps=[s_f5], tag="rtdma")

        def scatter_all(e, p_):
            for t_ in range(256):
                for ch_ in range(2):
                    e.indirect_dma_start(
                        out=table,
                        out_offset=IndirectOffsetOnAxis(pxi[:, ch_ * 256 + t_: ch_ * 256 + t_ + 1], 0),
                        in_=c_tokv[:, t_:t_ + 1], in_offset=None,
                        bounds_check=regs["r8k"], oob_is_err=False,
                    ).then_inc(p_.dsem["gpsimd"], 16)
                    p_._dma_seen += 1
        s_sc = p.step("gpsimd", scatter_all, n_dma=512, deps=[s_f4, s_tblz, s_regs], tag="scat")

        def tblgather(e, p_):
            e.indirect_dma_start(
                out=tblf[:, :], out_offset=None,
                in_=table.rearrange("(x o) b -> x (o b)", o=64),
                in_offset=IndirectOffsetOnAxis(c_idx8q[:, :], 0),
            ).then_inc(p_.dsem["gpsimd"], 16)
            p_._dma_seen += 1
        s_tg = p.step("gpsimd", tblgather, n_dma=1, deps=[s_sc], tag="tblg")

        def tokidx(e, p_):
            e.tensor_scalar(tblf[:, :], tblf[:, :], -1.0, None, ALU.add)
            padv(e)
            e.tensor_scalar(uw[:, 0:64], tblf[:, :], -0.5, None, ALU.is_lt)
            padv(e)
            e.scalar_tensor_tensor(tblf[:, :], uw[:, 0:64], 1e9, tblf[:, :], ALU.mult, ALU.add)
            padv(e)
            e.tensor_copy(tbli[:, :], tblf[:, :])
            return padv(e)
        s_ti = p.step("vector", tokidx, deps=[s_tg], tag="toki")

        if upto == "M4":
            p.wait_only("vector", [s_ti], tag="tiw")
            _early_exit()
            return nc

        if dbg:
            def dbg2(e, p_):
                p_.dma(e, out=rdbg[:, 0:8], in_=m1B[:, 0:8])
                p_.dma(e, out=rdbg[:, 8:16], in_=m2B[:, 0:8])
                p_.dma(e, out=rdbg[:, 16:24], in_=e1B[:, 0:8])
                p_.dma(e, out=rdbg[:, 24:32], in_=e2B[:, 0:8])
                p_.dma(e, out=rdbg[:, 32:40], in_=p1B[:, 0:8])
                p_.dma(e, out=rdbg[:, 40:48], in_=p2B[:, 0:8])
                p_.dma(e, out=rdbg[:, 48:56], in_=wA[:, 0:8])
                p_.dma(e, out=rdbg[:, 56:64], in_=gB[:, 0:8])
                p_.dma(e, out=rdbg[:, 64:96], in_=tblf[:, 0:32])
            p.step("sync", dbg2, n_dma=9, deps=[s_ti, s_f5], tag="dbgd")

        s_dn_last = None
        s_gu_last = None
        s_trx_last = None
        s_ytr_last = None
        s_ybdma_last = None
        for el in range(NEXP):
            def wline(e, p_, el=el):
                p_.dma(e, out=w13b[:, :].rearrange("q (a n) -> q a n", a=NDT),
                       in_=w13s[el * D:(el + 1) * D, :].rearrange("(a q) n -> q a n", q=128))
                p_.dma(e, out=w2b[:, :].rearrange("q (a n) -> q a n", a=4),
                       in_=w2s[el * 512:(el + 1) * 512, :].rearrange("(a q) n -> q a n", q=128))
            s_wl = p.step("gpsimd", wline, n_dma=2, tag=f"M5{el}_w",
                          deps=[s_dn_last])

            s_txe_last = None
            for sl in range(8):
                def xgat(e, p_, el=el, sl=sl):
                    e.memset(xbn[:, :], 0.0)
                    e.indirect_dma_start(
                        out=xbn[:, :], out_offset=None,
                        in_=xn_full.ap(),
                        in_offset=IndirectOffsetOnAxis(tbli[:, el * 8 + sl: el * 8 + sl + 1], 0),
                        bounds_check=regs["r32k"], oob_is_err=False,
                    ).then_inc(p_.dsem["gpsimd"], 16)
                    p_._dma_seen += 1
                s_xg = p.step("gpsimd", xgat, n_dma=1, tag=f"M5{el}_xg{sl}",
                              deps=[s_ag_xn, s_ti, s_regs, s_trx_last])

                for dt2 in range(0, NDT, 2):
                    ba, bb, pdep = pt_next()

                    def trxb(e, p_, dt2=dt2, ba=ba, bb=bb):
                        e.transpose(ba[:, 0:128], xbn[:, dt2 * 128:(dt2 + 1) * 128],
                                    c_identb[:, :])
                        return e.transpose(bb[:, 0:128], xbn[:, (dt2 + 1) * 128:(dt2 + 2) * 128],
                                           c_identb[:, :])
                    s_txb = p.step("tensor", trxb, tag=f"M5{el}_tx{sl}_{dt2}",
                                   deps=[s_xg, pdep,
                                         s_gu_last if (sl == 0 and dt2 == 0) else None])
                    s_trx_last = s_txb

                    def trxbe(e, p_, sl=sl, dt2=dt2, ba=ba, bb=bb):
                        e.tensor_copy(xbT[:, dt2 * D + sl * 128: dt2 * D + sl * 128 + 128],
                                      ba[:, 0:128])
                        return e.tensor_copy(xbT[:, (dt2 + 1) * D + sl * 128: (dt2 + 1) * D + sl * 128 + 128],
                                             bb[:, 0:128])
                    s_txe_last = p.step("vector", trxbe, deps=[s_txb], tag=f"M5{el}_te{sl}_{dt2}")
                    pt_done(s_txe_last)

            s_hm_prev = None
            s_silu_prev = None
            for i2 in range(8):
                m, th = divmod(i2, 2)

                def gu2(e, p_, m=m, th=th):
                    for kt in range(NDT):
                        e.matmul(P2[:, :], w13b[:, kt * D + m * 128: kt * D + m * 128 + 128],
                                 xbT[:, kt * D + th * 512: kt * D + th * 512 + 512],
                                 start=(kt == 0), stop=(kt == NDT - 1))
                    for kt in range(NDT):
                        ins = e.matmul(P3[:, :], w13b[:, kt * D + 512 + m * 128: kt * D + 512 + m * 128 + 128],
                                       xbT[:, kt * D + th * 512: kt * D + th * 512 + 512],
                                       start=(kt == 0), stop=(kt == NDT - 1))
                    return ins
                s_g2 = p.step("tensor", gu2, tag=f"M5{el}_gu{i2}",
                              deps=[s_txe_last, s_wl, s_silu_prev, s_hm_prev])
                s_gu_last = s_g2

                def gsilu2(e, p_, i2=i2):
                    return e.activation(shp[i2 % 2][:, :], P2[:, :], AF.Silu)
                s_silu_prev = p.step("scalar", gsilu2, deps=[s_g2], tag=f"M5{el}_sl{i2}")

                def hmul(e, p_, m=m, th=th, i2=i2):
                    return e.tensor_tensor(hTb[:, m * D + th * 512: m * D + th * 512 + 512],
                                           shp[i2 % 2][:, :], P3[:, :], ALU.mult)
                s_hm_prev = p.step("vector", hmul, tag=f"M5{el}_hm{i2}",
                                   deps=[s_g2, s_silu_prev,
                                         s_dn_last if i2 == 0 else None])

            s_ybev_prev = {}
            s_ybev_last = None
            for i2 in range(16):
                m, th = divmod(i2, 2)

                def dn2(e, p_, m=m, th=th, i2=i2):
                    PS = P4 if i2 % 2 == 0 else P5
                    for kt in range(4):
                        ins = e.matmul(PS[:, :], w2b[:, kt * D + m * 128: kt * D + m * 128 + 128],
                                       hTb[:, kt * D + th * 512: kt * D + th * 512 + 512],
                                       start=(kt == 0), stop=(kt == 3))
                    return ins
                s_dn = p.step("tensor", dn2, tag=f"M5{el}_dn{i2}",
                              deps=[s_hm_prev, s_wl, s_ybev_prev.get(i2 - 2)])
                s_dn_last = s_dn

                def ybev(e, p_, m=m, th=th, i2=i2):
                    PS = P4 if i2 % 2 == 0 else P5
                    return e.tensor_copy(ybT[:, m * D + th * 512: m * D + th * 512 + 512], PS[:, :])
                s_ybev_prev[i2] = p.step("vector", ybev, tag=f"M5{el}_ye{i2}",
                                         deps=[s_dn, s_ytr_last if i2 == 0 else None])
                s_ybev_last = s_ybev_prev[i2]

            for st in range(8):
                s_evl3 = None
                for dt2 in range(0, NDT, 2):
                    ba, bb, pdep = pt_next()

                    def ytr(e, p_, st=st, dt2=dt2, ba=ba, bb=bb):
                        e.transpose(ba[:, 0:128], ybT[:, dt2 * D + st * 128: dt2 * D + st * 128 + 128],
                                    c_identb[:, :])
                        return e.transpose(bb[:, 0:128],
                                           ybT[:, (dt2 + 1) * D + st * 128: (dt2 + 1) * D + st * 128 + 128],
                                           c_identb[:, :])
                    s_yt = p.step("tensor", ytr, tag=f"M5{el}_yt{st}_{dt2}",
                                  deps=[s_ybev_last, pdep])
                    s_ytr_last = s_yt

                    def ytre(e, p_, dt2=dt2, ba=ba, bb=bb):
                        e.tensor_copy(ybn[:, dt2 * 128:(dt2 + 1) * 128], ba[:, 0:128])
                        return e.tensor_copy(ybn[:, (dt2 + 1) * 128:(dt2 + 2) * 128], bb[:, 0:128])
                    s_evl3 = p.step("vector", ytre, tag=f"M5{el}_yv{st}_{dt2}",
                                    deps=[s_yt, s_ybdma_last if dt2 == 0 else None])
                    pt_done(s_evl3)

                def ybdma(e, p_, el=el, st=st):
                    p_.dma(e, out=yb_b.ap()[el * CAPE + st * 128: el * CAPE + (st + 1) * 128, :],
                           in_=ybn[:, :])
                s_ybdma_last = p.step("gpsimd", ybdma, n_dma=1, deps=[s_evl3], tag=f"M5{el}_yd{st}")

            if el == NEXP // 2 - 1 and st == 7:
                def agi2a(e, p_):
                    e.collective_compute(
                        "AllGather", ALU.bypass,
                        replica_groups=[list(range(NC))],
                        ins=[yb_b.ap()[0:NEXP // 2 * CAPE, :]],
                        outs=[yb_fullA.ap()],
                    ).then_inc(cc3)
                    return e.memset(dum[:, :], 2.0)
                p.step("gpsimd", agi2a, deps=[s_ybdma_last], tag="agi2a")

        def agi2(e, p_):
            e.collective_compute(
                "AllGather", ALU.bypass,
                replica_groups=[list(range(NC))],
                ins=[yb_b.ap()[NEXP // 2 * CAPE:, :]],
                outs=[yb_fullB.ap()],
            ).then_inc(cc4)
            return e.memset(dum[:, :], 3.0)
        s_agi2 = p.step("gpsimd", agi2, deps=[s_ybdma_last], tag="agi2")
        s_ag_yb = (cc4, 1)
        s_ag_yb1 = (cc3, 1)
        if dbg:
            def ydbg(e, p_):
                p_.dma(e, out=ydbgA, in_=yb_fullA.ap())
                p_.dma(e, out=ydbgB, in_=yb_fullB.ap())
            p.step("gpsimd", ydbg, n_dma=2, deps=[s_ag_yb1, s_ag_yb], tag="ydbg")

        if upto == "M6":
            def agwait2(e, p_):
                e.wait_ge(cc3, 1)
                return e.memset(dum[:, :], 4.0)
            p.step("gpsimd", agwait2, tag="agw2")
            _early_exit()
            return nc

        s_comb_prev = None
        s_add_hist = {}
        s_out_hist = {}
        s_modma = {}
        for c in range(NCHUNK):
            def rtg(e, p_, cc=c):
                for t_ in range(4):
                    e.indirect_dma_start(
                        out=rtc[:, t_ * 8:(t_ + 1) * 8], out_offset=None,
                        in_=rt2,
                        in_offset=IndirectOffsetOnAxis(c_idxrt[:, cc * 4 + t_: cc * 4 + t_ + 1], 0),
                    ).then_inc(p_.dsem["gpsimd"], 16)
                    p_._dma_seen += 1
            s_rtg = p.step("gpsimd", rtg, n_dma=4, tag=f"M7{c}_rtg",
                           deps=[s_rtd, s_comb_prev])

            def rticv(e, p_):
                e.tensor_copy(rti[:, :].rearrange("q (t f) -> q t f", f=4),
                              rtc[:, :].rearrange("q (t f) -> q t f", f=8)[:, :, 0:4])
                return padv(e)
            s_rti = p.step("vector", rticv, deps=[s_rtg], tag=f"M7{c}_rti")

            for t_ in range(4):
                def ygat(e, p_, t_=t_):
                    e.memset(r1b[:, :], 0.0)
                    e.memset(r2b[:, :], 0.0)
                    for dst, f0 in ((r1b, 0), (r2b, 2)):
                        for buf, f in ((yb_fullA, f0), (yb_fullB, f0 + 1)):
                            e.indirect_dma_start(
                                out=dst[:, :], out_offset=None,
                                in_=buf.ap(),
                                in_offset=IndirectOffsetOnAxis(rti[:, t_ * 4 + f: t_ * 4 + f + 1], 0),
                                bounds_check=regs["r32k"], oob_is_err=False,
                            ).then_inc(p_.dsem["gpsimd"], 16)
                            p_._dma_seen += 1
                s_yg = p.step("gpsimd", ygat, n_dma=4, tag=f"M7{c}_yg{t_}",
                              deps=[s_ag_yb1, s_ag_yb, s_rti, s_comb_prev])

                def comb(e, p_, t_=t_):
                    e.tensor_scalar(mot4[:, t_ * D:(t_ + 1) * D], r1b[:, :],
                                    rtc[:, t_ * 8 + 4: t_ * 8 + 5], None, ALU.mult)
                    e.scalar_tensor_tensor(mot4[:, t_ * D:(t_ + 1) * D], r2b[:, :],
                                           rtc[:, t_ * 8 + 5: t_ * 8 + 6],
                                           mot4[:, t_ * D:(t_ + 1) * D], ALU.mult, ALU.add)
                    return padv(e)
                s_comb_prev = p.step("vector", comb, tag=f"M7{c}_cb{t_}",
                                     deps=[s_yg, s_modma.get((c - 1, t_))])

                def modma(e, p_, cc=c, t_=t_):
                    p_.dma(e, out=mo_h[cc * CH + t_ * 128: cc * CH + (t_ + 1) * 128, :],
                           in_=mot4[:, t_ * D:(t_ + 1) * D])
                s_modma[(c, t_)] = p.step("gpsimd", modma, n_dma=1, deps=[s_comb_prev],
                                          tag=f"M7{c}_md{t_}")

            if upto == "M7":
                continue
            for dt in range(NDT):
                def ldx(e, p_, cc=c, dt=dt):
                    hh = (dt % 2) * CH
                    p_.dma(e, out=x1c[:, hh:hh + CH], in_=xT[dt * 128:(dt + 1) * 128,
                                                             cc * CH:(cc + 1) * CH])
                    p_.dma(e, out=hc[:, hh:hh + CH], in_=h_h[cc * D + dt * 128: cc * D + (dt + 1) * 128, :])
                    p_.dma(e, out=shc[:, hh:hh + CH], in_=sh_h[cc * D + dt * 128: cc * D + (dt + 1) * 128, :])
                s_ldx = p.step("sync", ldx, n_dma=3, tag=f"M8{c}_ld{dt}",
                               deps=[s_add_hist.get((c, dt - 2)), s_hdma[(c, dt)], s_shdma[(c, dt)]])

                def mld(e, p_, cc=c, dt=dt):
                    p_.dma_t(e, out=mdt[:, (dt % 2) * CH:(dt % 2) * CH + CH],
                             in_=mo_h[cc * CH:(cc + 1) * CH, dt * 128:(dt + 1) * 128])
                s_mld = p.step("sync", mld, n_dma=1, tag=f"M8{c}_mld{dt}",
                               deps=[s_modma[(c, 3)], s_add_hist.get((c, dt - 2))])

                def addf(e, p_, dt=dt):
                    hh = (dt % 2) * CH
                    outb = tmpf if dt % 2 == 0 else lgb
                    e.tensor_tensor(xnp[0][:, :], x1c[:, hh:hh + CH], hc[:, hh:hh + CH], ALU.add)
                    e.tensor_tensor(xnp[1][:, :], shc[:, hh:hh + CH],
                                    mdt[:, hh:hh + CH], ALU.add)
                    return e.tensor_tensor(outb[:, 0:CH], xnp[0][:, :], xnp[1][:, :], ALU.add)
                s_add_hist[(c, dt)] = p.step("vector", addf, tag=f"M8{c}_add{dt}",
                                             deps=[s_ldx, s_mld, s_out_hist.get((c, dt - 2)),
                                                   s_out_hist.get((c - 1, dt + 6))])

                def outd(e, p_, cc=c, dt=dt):
                    outb = tmpf if dt % 2 == 0 else lgb
                    p_.dma(e, out=x2T[dt * 128:(dt + 1) * 128, cc * CH:(cc + 1) * CH],
                           in_=outb[:, 0:CH])
                s_out_hist[(c, dt)] = p.step("sync", outd, n_dma=1,
                                             deps=[s_add_hist[(c, dt)]], tag=f"M8{c}_o{dt}")

        if upto == "M7":
            p.wait_only("vector", [s_comb_prev], tag="cbw")
            _early_exit()
            return nc

        p.wait_only("sync", [s_out_hist[(NCHUNK - 1, NDT - 1)],
                             (p.dsem["gpsimd"], p.dma_val["gpsimd"])], tag="fin")
        if fctx is not None:
            p.fence()
            return nc
        with nc.Block() as block:
            p.emit(block)
    return nc


def host_l0_moe_inputs(x, rms_w, gate_w, w13, w2, sw_gate, sw_up, sw_down):
    import ml_dtypes
    ident_bf = np.eye(128, dtype=np.float32).astype(ml_dtypes.bfloat16)
    ident_f = np.eye(128, dtype=np.float32)
    rmsw_t = np.ascontiguousarray(np.asarray(rms_w, np.float32).reshape(NDT, 128).T)
    gwt = np.ascontiguousarray(np.asarray(gate_w, np.float32).T)
    iota = (np.arange(64, dtype=np.float32) + float(1 << 20))
    iotabig = np.ascontiguousarray(np.tile(iota, (128, 8)).astype(np.float32))
    tokv = (np.arange(256, dtype=np.float32)[None, :] * 128.0
            + np.arange(128, dtype=np.float32)[:, None] + 1.0).astype(np.float32)
    pp = np.arange(128, dtype=np.int64)
    maps = []
    for c in range(NC):
        idx8q = pp.astype(np.int32).reshape(128, 1)
        core8 = np.full((128, 1), 8.0 * c, np.float32)
        idxrt = (pp[:, None] * 256
                 + (c * 32 + np.arange(32, dtype=np.int64))[None, :]).astype(np.int32)
        maps.append({
            "xT": np.ascontiguousarray(np.asarray(x[c], np.float32).T),
            "rmsw_t": rmsw_t, "gwt": gwt,
            "w13s": np.ascontiguousarray(np.asarray(w13, np.float32)[8 * c:8 * c + 8]
                                         .reshape(NEXP * D, D)),
            "w2s": np.ascontiguousarray(np.asarray(w2, np.float32)[8 * c:8 * c + 8]
                                        .reshape(NEXP * 512, D)),
            "swg": np.asarray(sw_gate, np.float32), "swu": np.asarray(sw_up, np.float32),
            "swd": np.asarray(sw_down, np.float32),
            "identb": ident_bf, "identf": ident_f,
            "iotabig": iotabig, "tokv": tokv,
            "idx8q": idx8q, "idxrt": idxrt, "core8": core8,
        })
    return maps


# version 06
# speedup vs baseline: 1.2066x; 1.0445x over previous
import contextlib
import numpy as np

import concourse.bass as bass
import concourse.mybir as mybir
from concourse.bass_utils import run_bass_kernel_spmd
from concourse.bass_core_rust import IndirectOffsetOnAxis

F32 = mybir.dt.float32
BF16 = mybir.dt.bfloat16
I32 = mybir.dt.int32
AF = mybir.ActivationFunctionType
ALU = mybir.AluOpType
AX = mybir.AxisListType

D = 1024
T = 4096
NC = 8
CH = 512
NCHUNK = T // CH
NDT = D // 128
NKT = T // 128
NH = 16
HD = 64
VW = 65 * NH


class P:
    ENGS = ("tensor", "vector", "scalar", "gpsimd", "sync")

    def __init__(self, nc, ctx):
        self.nc = nc
        self.steps = []
        self.clk = {e: ctx.enter_context(nc.semaphore(f"clk_{e}")) for e in self.ENGS}
        self.dsem = {e: ctx.enter_context(nc.semaphore(f"dma_{e}")) for e in self.ENGS}
        self.clk_val = {e: 0 for e in self.ENGS}
        self.dma_val = {e: 0 for e in self.ENGS}
        self.waited = {e: {} for e in self.ENGS}
        self._emitting = None
        self._dma_seen = 0
        self._fence_deps = {}

    def fence(self):
        refs = ([(self.clk[e], self.clk_val[e]) for e in self.ENGS if self.clk_val[e]]
                + [(self.dsem[e], self.dma_val[e]) for e in self.ENGS if self.dma_val[e]])
        for e in self.ENGS:
            self._fence_deps[e] = refs

    def _mk_waits(self, eng, deps):
        waits = []
        for d in deps:
            if d is None:
                continue
            sem, val = d
            if val <= self.waited[eng].get(id(sem), 0):
                continue
            self.waited[eng][id(sem)] = val
            waits.append((sem, val))
        return waits

    def step(self, eng, fn, deps=(), n_dma=0, tag=""):
        deps = list(deps) + self._fence_deps.pop(eng, [])
        waits = self._mk_waits(eng, deps)
        if n_dma > 0:
            self.dma_val[eng] += 16 * n_dma
            ref = (self.dsem[eng], self.dma_val[eng])
        else:
            self.clk_val[eng] += 1
            ref = (self.clk[eng], self.clk_val[eng])
        self.steps.append(dict(eng=eng, fn=fn, waits=waits, n_dma=n_dma, tag=tag))
        return ref

    def wait_only(self, eng, deps, tag=""):
        deps = list(deps) + self._fence_deps.pop(eng, [])
        waits = self._mk_waits(eng, deps)
        self.steps.append(dict(eng=eng, fn=None, waits=waits, n_dma=0, tag=tag))

    def dma(self, e, **kw):
        ins = e.dma_start(**kw)
        ins.then_inc(self.dsem[self._emitting], 16)
        self._dma_seen += 1
        return ins

    def dma_t(self, e, out, in_):
        ins = e.dma_start_transpose(out=out, in_=in_)
        ins.then_inc(self.dsem[self._emitting], 16)
        self._dma_seen += 1
        return ins

    def emit(self, block):
        def make(eng_name):
            def run(e):
                for s in self.steps:
                    if s["eng"] != eng_name:
                        continue
                    for sem, val in s["waits"]:
                        e.wait_ge(sem, val)
                    if s["fn"] is None:
                        continue
                    self._emitting = eng_name
                    self._dma_seen = 0
                    last = s["fn"](e, self)
                    assert self._dma_seen == s["n_dma"], (
                        f"step {s['tag']}: declared {s['n_dma']} dma, emitted {self._dma_seen}")
                    if s["n_dma"] == 0:
                        assert last is not None, f"step {s['tag']} returned no instruction"
                        last.then_inc(self.clk[eng_name], 1)
            return run
        block.tensor(make("tensor"))
        block.vector(make("vector"))
        block.scalar(make("scalar"))
        block.gpsimd(make("gpsimd"))
        block.sync(make("sync"))


def build_l0_attn(upto="C", dbg=False, fctx=None):
    nc = fctx["nc"] if fctx else bass.Bass("TRN2", target_bir_lowering=False, debug=False, detect_race_conditions=False)
    xT = fctx["xin"] if fctx else nc.dram_tensor("xT", [D, T], F32, kind="ExternalInput").ap()
    Wk = nc.dram_tensor("Wk", [D, D], F32, kind="ExternalInput").ap()
    Wq = nc.dram_tensor("Wq", [D, D], F32, kind="ExternalInput").ap()
    Wo = nc.dram_tensor("Wo", [D, D], F32, kind="ExternalInput").ap()
    bo_t = nc.dram_tensor("bo_t", [128, NDT], F32, kind="ExternalInput").ap()
    ident = nc.dram_tensor("ident", [128, 128], BF16, kind="ExternalInput").ap()
    x1T = fctx["xout"] if fctx else nc.dram_tensor("x1T", [D, T], F32, kind="ExternalOutput").ap()
    xn_h = nc.dram_tensor("xn_h", [NCHUNK * D, CH], BF16).ap()
    qT_h = nc.dram_tensor("qT_h", [NCHUNK * D, CH], BF16).ap()
    ao_h = nc.dram_tensor("ao_h", [T, D], BF16, kind="ExternalOutput" if dbg else "Internal").ap()
    st_dbg = nc.dram_tensor("st_dbg", [128, 2048], F32, kind="ExternalOutput" if dbg else "Internal").ap()
    xn_dbg = nc.dram_tensor("xn_dbg", [NCHUNK * D, CH], BF16, kind="ExternalOutput" if dbg else "Internal").ap()

    with contextlib.ExitStack() as ctx:
        pref = fctx["tag"] if fctx else ""
        sb = lambda name, p_, f, dt=BF16: ctx.enter_context(nc.sbuf_tensor(pref + name, [p_, f], dt))
        c_ident = sb("c_ident", 128, 96)
        c_ones = sb("c_ones", 128, 1)
        c_onesrf = sb("c_onesrf", 1, 128, F32)
        wk_s = [sb(f"wk{i}", 128, D) for i in range(NDT)]
        wq_s = [sb(f"wq{i}", 128, D) for i in range(NDT)]
        bo_s = sb("bo_s", 128, NDT, F32)
        kT = [sb(f"kT{i}", 128, T) for i in range(NDT)]
        vbuf = sb("vbuf", 128, NKT * VW)
        sxb = sb("sxb", 128, NDT * CH)
        sqp = [sb(f"sqp{i}", 128, CH) for i in range(2)]
        xnb = sb("xnb", 128, NDT * CH)
        tmpf = sb("tmpf", 128, CH, F32)
        mrow = sb("mrow", 1, CH, F32)
        rrow = sb("rrow", 1, 2 * CH, F32)
        qcb = sb("qcb", 128, NDT * CH)
        ptb = [sb(f"ptb{i}", 128, 1024) for i in range(4)]
        anb = sxb
        recb = sb("recb", 128, 16, F32)
        resf = tmpf

        P0 = ctx.enter_context(nc.psum_tensor(pref + "P0", [128, 2048], F32))
        P2 = ctx.enter_context(nc.psum_tensor(pref + "P2", [128, 512], F32))
        P3 = ctx.enter_context(nc.psum_tensor(pref + "P3", [128, 512], F32))
        P4 = ctx.enter_context(nc.psum_tensor(pref + "P4", [128, 512], F32))
        P5 = ctx.enter_context(nc.psum_tensor(pref + "P5", [128, 512], F32))

        p = fctx["p"] if fctx else P(nc, ctx)

        def ld_w(e, p_):
            p_.dma(e, out=bo_s[:, :], in_=bo_t)
            p_.dma(e, out=c_ident[:, 0:64], in_=ident[:, 0:64])
            for i in range(NDT):
                p_.dma(e, out=wk_s[i][:, :], in_=Wk[i * 128:(i + 1) * 128, :])
                p_.dma(e, out=wq_s[i][:, :], in_=Wq[i * 128:(i + 1) * 128, :])
        s_wld = p.step("gpsimd", ld_w, n_dma=2 + 2 * NDT, tag="wld")

        def consts(e, p_):
            e.memset(c_ones[:, :], 1.0)
            e.memset(c_onesrf[0:1, :], 1.0)
            return e.memset(vbuf[:, :], 1.0)
        s_const = p.step("vector", consts, tag="consts")

        def init_ptb(e, p_):
            e.memset(ptb[0][:, :], 0.0)
            e.memset(ptb[1][:, :], 0.0)
            e.memset(ptb[2][:, :], 0.0)
            return e.memset(ptb[3][:, :], 0.0)
        s_iptb = p.step("vector", init_ptb, deps=[], tag="init_ptb")

        def warm_exp(e, p_):
            return e.activation(recb[:, 0:1], c_ones[:, 0:1], AF.Exp)
        s_warm = p.step("scalar", warm_exp, deps=[s_const], tag="warm_exp")

        def xcopy(e, p_):
            p_.dma(e, out=x1T, in_=xT)
        s_xcopy = p.step("sync", xcopy, n_dma=1, tag="xcopy")

        bca_m = P0[:, 0:CH]
        bca_r = P0[:, CH:2 * CH]
        kq_rot = [P0[:, 1024:1536], P0[:, 1536:2048], P2[:, :], P3[:, :]]
        s_xn = [None] * NCHUNK
        s_kq_ev = {}
        s_kq_pe_last = None
        s_qdma = [None] * NCHUNK
        s_xndma = [None] * NCHUNK
        prev = dict(statm=None, bca=None, stats_pe=None, sq=None)
        s_sqs = {}
        s_pes = {}

        for c in range(NCHUNK):
            def ldxb(e, p_, cc=c):
                p_.dma(e, out=sxb[:, :].rearrange("p (a n) -> p a n", a=NDT),
                       in_=xT[:, cc * CH:(cc + 1) * CH].rearrange("(a p) n -> p a n", p=128))
            s_ldxb = p.step("gpsimd", ldxb, n_dma=1, tag=f"A{c}_ldxb",
                            deps=[s_sqs.get((c - 1, NDT - 1)), prev["stats_pe"],
                                  s_xn[c - 1] if c else None])

            for dt in range(NDT):
                def sq(e, p_, dt=dt):
                    return e.tensor_tensor(sqp[dt % 2][:, :], sxb[:, dt * CH:(dt + 1) * CH],
                                           sxb[:, dt * CH:(dt + 1) * CH], ALU.mult)
                s_sqs[(c, dt)] = p.step("vector", sq, tag=f"A{c}_sq{dt}",
                                        deps=[s_ldxb, s_pes.get((c, dt - 2)),
                                              prev["stats_pe"] if dt < 2 else None])

                def pe2(e, p_, dt=dt):
                    e.matmul(P4[0:1, :], c_ones[:, :], sxb[:, dt * CH:(dt + 1) * CH],
                             start=(dt == 0), stop=(dt == NDT - 1))
                    return e.matmul(P5[0:1, :], c_ones[:, :], sqp[dt % 2][:, :],
                                    start=(dt == 0), stop=(dt == NDT - 1))
                s_pes[(c, dt)] = p.step("tensor", pe2, tag=f"A{c}_pe{dt}",
                                        deps=[s_sqs[(c, dt)], s_const,
                                              prev["statm"] if dt == 0 else None])
            s_stats_pe = s_pes[(c, NDT - 1)]
            prev["stats_pe"] = s_stats_pe

            def statm1(e, p_):
                e.tensor_scalar(mrow[0:1, :], P4[0:1, :], 1.0 / D, None, ALU.mult)
                e.tensor_scalar(rrow[0:1, 0:CH], P5[0:1, :], 1.0 / D, None, ALU.mult)
                e.tensor_tensor(rrow[0:1, CH:], mrow[0:1, :], mrow[0:1, :], ALU.mult)
                e.tensor_tensor(rrow[0:1, 0:CH], rrow[0:1, 0:CH], rrow[0:1, CH:], ALU.subtract)
                return e.tensor_scalar(rrow[0:1, 0:CH], rrow[0:1, 0:CH], 1e-5, None, ALU.add)
            s_statm1 = p.step("vector", statm1, deps=[s_stats_pe, prev["bca"]], tag=f"A{c}_sm1")

            def sqrt(e, p_):
                return e.activation(rrow[0:1, CH:], rrow[0:1, 0:CH], AF.Sqrt)
            s_sqrt = p.step("scalar", sqrt, deps=[s_statm1], tag=f"A{c}_sqrt")

            def recip(e, p_):
                return e.reciprocal(rrow[0:1, 0:CH], rrow[0:1, CH:])
            s_recip = p.step("vector", recip, deps=[s_sqrt], tag=f"A{c}_rec")
            prev["statm"] = s_recip

            def bca(e, p_):
                e.matmul(bca_m, c_onesrf[0:1, :], mrow[0:1, :], start=True, stop=True)
                return e.matmul(bca_r, c_onesrf[0:1, :], rrow[0:1, 0:CH], start=True, stop=True)
            s_bca = p.step("tensor", bca, deps=[s_recip, s_xn[c - 1] if c else None],
                           tag=f"A{c}_bca")
            prev["bca"] = s_bca

            def xn(e, p_):
                for dt in range(NDT):
                    e.tensor_tensor(tmpf[:, :], sxb[:, dt * CH:(dt + 1) * CH], bca_m, ALU.subtract)
                    ins = e.tensor_tensor(xnb[:, dt * CH:(dt + 1) * CH], tmpf[:, :], bca_r, ALU.mult)
                return ins
            s_xn[c] = p.step("vector", xn,
                             deps=[s_bca, s_xndma[c - 1] if c else None,
                                   s_kq_ev.get((c - 1, 2, 1))], tag=f"A{c}_xn")

            def xndma(e, p_, cc=c):
                p_.dma(e, out=xn_h[cc * D:(cc + 1) * D, :].rearrange("(a p) n -> p a n", p=128),
                       in_=xnb[:, :].rearrange("p (a n) -> p a n", a=NDT))
                if dbg:
                    p_.dma(e, out=xn_dbg[cc * D:(cc + 1) * D, :].rearrange("(a p) n -> p a n", p=128),
                           in_=xnb[:, :].rearrange("p (a n) -> p a n", a=NDT))
            s_xndma[c] = p.step("gpsimd", xndma, n_dma=2 if dbg else 1, deps=[s_xn[c]], tag=f"A{c}_xndma")

            for half, tag in ((0, "k"), (1, "q"), (2, "v")):
                for grp in range(2):
                    def kq_pe(e, p_, half=half, grp=grp, cc=c):
                        for j in range(4):
                            mt = grp * 4 + j
                            for ktd in range(NDT):
                                if half == 0 or half == 1:
                                    wgt = wk_s if half == 0 else wq_s
                                    ins = e.matmul(kq_rot[j],
                                                   wgt[ktd][:, mt * 128:(mt + 1) * 128],
                                                   xnb[:, ktd * CH:(ktd + 1) * CH],
                                                   start=(ktd == 0), stop=(ktd == NDT - 1))
                                else:
                                    s_, dh = mt % 4, mt // 4
                                    ins = e.matmul(kq_rot[j],
                                                   xnb[:, ktd * CH + s_ * 128:
                                                       ktd * CH + s_ * 128 + 128],
                                                   wk_s[ktd][:, dh * 512:(dh + 1) * 512],
                                                   start=(ktd == 0), stop=(ktd == NDT - 1))
                        return ins
                    dep_ev = (s_kq_ev.get((c, half, grp - 1)) if grp else
                              (s_kq_ev.get((c, half - 1, 1)) if half else
                               s_kq_ev.get((c - 1, 2, 1))))
                    s_pe = p.step("tensor", kq_pe, deps=[s_xn[c], dep_ev, s_wld],
                                  tag=f"A{c}_{tag}pe{grp}")
                    if half == 2 and grp == 1:
                        s_kq_pe_last = s_pe

                    def kq_ev(e, p_, half=half, grp=grp, cc=c):
                        for j in range(4):
                            mt = grp * 4 + j
                            if half == 0:
                                ins = e.tensor_copy(kT[mt][:, cc * CH:(cc + 1) * CH], kq_rot[j])
                            elif half == 1:
                                ins = e.tensor_copy(qcb[:, mt * CH:(mt + 1) * CH], kq_rot[j])
                            else:
                                s_, dh = mt % 4, mt // 4
                                tile = cc * 4 + s_
                                dst = vbuf[:, tile * VW + dh * 8 * 65:
                                           tile * VW + dh * 8 * 65 + 520]
                                ins = e.tensor_copy(
                                    dst.rearrange("p (h w) -> p h w", w=65)[:, :, 0:64],
                                    kq_rot[j][:, :].rearrange("p (h w) -> p h w", w=64))
                        return ins
                    dep_q = s_qdma[c - 1] if (half == 1 and grp == 0 and c) else None
                    dep_v = s_const if half == 2 else None
                    s_kq_ev[(c, half, grp)] = p.step("vector", kq_ev, deps=[s_pe, dep_q, dep_v],
                                                     tag=f"A{c}_{tag}ev{grp}")

            def qdma(e, p_, cc=c):
                p_.dma(e, out=qT_h[cc * D:(cc + 1) * D, :].rearrange("(a p) n -> p a n", p=128),
                       in_=qcb[:, :].rearrange("p (a n) -> p a n", a=NDT))
            s_qdma[c] = p.step("gpsimd", qdma, n_dma=1,
                               deps=[s_kq_ev[(c, 1, 0)], s_kq_ev[(c, 1, 1)]], tag=f"A{c}_qdma")

        s_a_done = s_kq_ev[(NCHUNK - 1, 2, 1)]

        def ld_wo(e, p_):
            for i in range(NDT):
                p_.dma(e, out=wk_s[i][:, :], in_=Wo[i * 128:(i + 1) * 128, :])
        s_wold = p.step("gpsimd", ld_wo, n_dma=NDT, deps=[s_kq_pe_last, s_a_done], tag="wold")

        if upto == "A":
            p.wait_only("gpsimd", [s_qdma[NCHUNK - 1], s_xndma[NCHUNK - 1], s_xcopy], tag="final")
            with nc.Block() as block:
                p.emit(block)
            return nc

        def ld_wo(e, p_):
            for i in range(NDT):
                p_.dma(e, out=wk_s[i][:, :], in_=Wo[i * 128:(i + 1) * 128, :])
        s_wold = p.step("gpsimd", ld_wo, n_dma=NDT, deps=[s_kq_pe_last, s_a_done], tag="wold")
        s_a2_done = s_a_done

        st = P0
        pv = [P2, P3]
        s_div = {}
        s_ao_dma = [None] * NCHUNK
        s_c_done = [None] * NCHUNK
        st_steps = []
        exp_steps = []
        pv_steps = []
        s_cev_last = [None] * NCHUNK

        for qc in range(NCHUNK):
            def qld(e, p_, qc=qc):
                p_.dma(e, out=qcb[:, :].rearrange("p (a n) -> p a n", a=NDT),
                       in_=qT_h[qc * D:(qc + 1) * D, :].rearrange("(a p) n -> p a n", p=128))
            s_qld = p.step("sync", qld, n_dma=1, tag=f"B{qc}_qld",
                           deps=[s_qdma[qc],
                                 s_qdma[NCHUNK - 1] if qc == 0 else st_steps[-1]])

            for hp in range(8):
                pend_pv = []
                for g in range(32):
                    G = (qc * 8 + hp) * 32 + g
                    sthalf = st[:, (g % 2) * 1024:(g % 2) * 1024 + 1024]

                    def st_mm(e, p_, hp=hp, g=g, sthalf=sthalf):
                        for hh in (2 * hp, 2 * hp + 1):
                            ins = e.matmul(
                                sthalf[:, (hh % 2) * 512:(hh % 2) * 512 + 512],
                                kT[hh // 2][(hh % 2) * 64:(hh % 2) * 64 + 64,
                                            g * 128:(g + 1) * 128],
                                qcb[(hh % 2) * 64:(hh % 2) * 64 + 64,
                                    (hh // 2) * CH:(hh // 2 + 1) * CH],
                                start=True, stop=True)
                        return ins
                    s_stm = p.step("tensor", st_mm,
                                   deps=[s_qld, s_a2_done,
                                         exp_steps[G - 2] if G >= 2 else None,
                                         s_div.get((qc, hp - 1)) if g == 0 else None,
                                         s_div.get((qc - 1, 7)) if (g == 0 and hp == 0) else None,
                                         s_cev_last[qc - 1] if (g == 0 and hp == 0 and qc) else None],
                                   tag=f"B{qc}_{hp}_st{g}")
                    st_steps.append(s_stm)

                    def expf(e, p_, g=g, sthalf=sthalf):
                        return e.activation(ptb[g % 4][:, :], sthalf, AF.Exp, scale=1.0 / 32)
                    s_e = p.step("scalar", expf,
                                 deps=[s_stm, pv_steps[G - 4] if G >= 4 else None],
                                 tag=f"B{qc}_{hp}_e{g}")
                    exp_steps.append(s_e)
                    if False and dbg and qc == 0 and hp == 0 and g == 2:
                        def dbg_ev(e, p_, g=g):
                            e.tensor_copy(sxb[:, 0:1024], ptb[g % 2][:, :])
                            return e.tensor_copy(sxb[:, 1024:2048],
                                                 st[:, (g % 2) * 1024:(g % 2) * 1024 + 1024])
                        s_dbg = p.step("vector", dbg_ev, deps=[s_e], tag="dbg_ev")
                        def dbg_dma(e, p_):
                            p_.dma(e, out=st_dbg[0:128, :], in_=sxb[:, 0:2048])
                        s_dbgd = p.step("gpsimd", dbg_dma, n_dma=1, deps=[s_dbg], tag="dbg_dma")
                        exp_steps[-1] = s_dbgd

                    def pv_mm(e, p_, hp=hp, g=g):
                        for hh in (2 * hp, 2 * hp + 1):
                            for sub in range(4):
                                ins = e.matmul(
                                    pv[hh % 2][:, sub * 65:(sub + 1) * 65],
                                    ptb[g % 4][:, (hh % 2) * 512 + sub * 128:
                                               (hh % 2) * 512 + sub * 128 + 128],
                                    vbuf[:, g * VW + hh * 65: g * VW + hh * 65 + 65],
                                    start=(g == 0 and sub == 0),
                                    stop=(g == 31 and sub == 3))
                        return ins
                    pend_pv.append((pv_mm, f"B{qc}_{hp}_pv{g}"))
                    if len(pend_pv) == 4:
                        fn0, tag0 = pend_pv.pop(0)
                        pv_steps.append(p.step("tensor", fn0,
                                               deps=[exp_steps[G - 2]], tag=tag0))

                def echo(e, p_, hp=hp):
                    return e.activation(sqp[0][0:1, 0:1], ptb[1][0:1, 1023:1024], AF.Exp)
                s_echo = p.step("scalar", echo, deps=[], tag=f"B{qc}_{hp}_echo")
                for fn0, tag0 in pend_pv:
                    pv_steps.append(p.step("tensor", fn0, deps=[s_echo], tag=tag0))
                pend_pv = []

                if dbg and qc == 0 and hp == 0:
                    def dbg_pv(e, p_):
                        e.tensor_copy(sxb[:, 0:512].rearrange("p (a b) -> p a b", a=1), pv[0][:, 0:512].rearrange("p (a b) -> p a b", a=1))
                        return e.tensor_copy(sxb[:, 512:1024], pv[1][:, 0:512])
                    s_dpv = p.step("vector", dbg_pv, deps=[pv_steps[-1]], tag="dbg_pv")
                    def dbg_pvd(e, p_):
                        p_.dma(e, out=st_dbg[0:128, 0:1024], in_=sxb[:, 0:1024])
                    s_dpvd = p.step("gpsimd", dbg_pvd, n_dma=1, deps=[s_dpv], tag="dbg_pvd")
                    pv_steps[-1] = s_dpvd

                def rec_act(e, p_):
                    for b in range(2):
                        e.activation(recb[:, 8 + b * 4:12 + b * 4].rearrange("p (s w) -> p s w", w=1),
                                     pv[b][:, 0:260].rearrange("p (s w) -> p s w", w=65)[:, :, 64:65],
                                     AF.Ln)
                    return e.activation(recb[:, 0:8], recb[:, 8:16], AF.Exp, scale=-1.0)
                s_rec = p.step("scalar", rec_act, deps=[pv_steps[-1]], tag=f"B{qc}_{hp}_rec")

                def div(e, p_, hp=hp):
                    for hh in (2 * hp, 2 * hp + 1):
                        for sub in range(4):
                            i = (hh % 2) * 4 + sub
                            ins = e.tensor_scalar(
                                anb[:, sub * D + hh * 64: sub * D + hh * 64 + 64],
                                pv[hh % 2][:, sub * 65:sub * 65 + 64],
                                recb[:, i:i + 1], None, ALU.mult)
                    return ins
                s_div[(qc, hp)] = p.step(
                    "vector", div,
                    deps=[s_rec, s_ao_dma[qc - 1] if qc >= 1 else None],
                    tag=f"B{qc}_{hp}_div")

            def ao_dma(e, p_, qc=qc):
                p_.dma(e, out=ao_h[qc * CH:(qc + 1) * CH, :].rearrange("(s p) d -> p s d", p=128),
                       in_=anb[:, :].rearrange("p (s d) -> p s d", s=4))
            s_ao_dma[qc] = p.step("gpsimd", ao_dma, n_dma=1,
                                  deps=[s_div[(qc, hp)] for hp in range(8)], tag=f"B{qc}_ao")

            if upto == "B":
                s_cev_last[qc] = s_ao_dma[qc]
                s_c_done[qc] = s_ao_dma[qc]
                continue
            def cldT(e, p_, qc=qc):
                for dt in range(NDT):
                    p_.dma_t(e, out=xnb[:, dt * CH:(dt + 1) * CH],
                             in_=ao_h[qc * CH:(qc + 1) * CH, dt * 128:(dt + 1) * 128])
            s_cldT = p.step("sync", cldT, n_dma=NDT,
                            deps=[s_ao_dma[qc],
                                  s_c_done[qc - 1] if qc else s_a_done,
                                  None if qc else s_xndma[NCHUNK - 1]],
                            tag=f"C{qc}_ldT")

            s_cev = None
            s_cout = None
            for dt in range(NDT):
                def c_pe(e, p_, dt=dt):
                    yo = P4[:, :] if dt % 2 == 0 else P5[:, :]
                    for ktd in range(NDT):
                        ins = e.matmul(yo, wk_s[ktd][:, dt * 128:(dt + 1) * 128],
                                       xnb[:, ktd * CH:(ktd + 1) * CH],
                                       start=(ktd == 0), stop=(ktd == NDT - 1))
                    return ins
                dep_stats = prev["statm"] if (qc == 0 and dt < 2) else None
                dep_war = s_cev if dt >= 2 else None
                s_cpe = p.step("tensor", c_pe, deps=[s_cldT, dep_stats, dep_war, s_wold],
                               tag=f"C{qc}_pe{dt}")

                def c_ev(e, p_, dt=dt):
                    yo = P4[:, :] if dt % 2 == 0 else P5[:, :]
                    return e.tensor_scalar(resf[:, :], yo, bo_s[:, dt:dt + 1], None, ALU.add)
                s_cev = p.step("vector", c_ev, deps=[s_cpe, s_cout], tag=f"C{qc}_ev{dt}")

                def c_out(e, p_, dt=dt, qc=qc):
                    p_.dma(e, out=x1T[dt * 128:(dt + 1) * 128, qc * CH:(qc + 1) * CH],
                           in_=resf[:, :], accum_op=ALU.add)
                s_cout = p.step("gpsimd", c_out, n_dma=1, deps=[s_cev, s_xcopy],
                                tag=f"C{qc}_out{dt}")
            s_cev_last[qc] = s_cev
            s_c_done[qc] = s_cout

        p.wait_only("gpsimd", [s_c_done[NCHUNK - 1], s_xcopy], tag="final")
        if fctx is not None:
            p.fence()
            return nc
        with nc.Block() as block:
            p.emit(block)
    return nc


def host_l0_attn_inputs(x, Wk, Wq, Wo, bo):
    import ml_dtypes
    ident_bf = np.eye(128, dtype=np.float32).astype(ml_dtypes.bfloat16)
    bo_t = np.ascontiguousarray(np.asarray(bo, np.float32).reshape(NDT, 128).T)
    maps = []
    for c in range(NC):
        maps.append({
            "xT": np.ascontiguousarray(np.asarray(x[c], np.float32).T),
            "Wk": np.asarray(Wk, np.float32), "Wq": np.asarray(Wq, np.float32),
            "Wo": np.asarray(Wo, np.float32),
            "bo_t": bo_t, "ident": ident_bf,
        })
    return maps




def build_fused():
    nc = bass.Bass("TRN2", target_bir_lowering=False, debug=False, detect_race_conditions=False)
    xin = nc.dram_tensor("xT", [D, T], F32, kind="ExternalInput").ap()
    x1 = nc.dram_tensor("x1b", [D, T], F32).ap()
    x2 = nc.dram_tensor("x2b", [D, T], F32).ap()
    x3 = nc.dram_tensor("x3b", [D, T], F32).ap()
    xout = nc.dram_tensor("x4T", [D, T], F32, kind="ExternalOutput").ap()
    with contextlib.ExitStack() as gctx:
        p = P(nc, gctx)
        build_l0_attn("C", fctx=dict(nc=nc, p=p, xin=xin, xout=x1, tag="a_"))
        build_l0_moe(fctx=dict(nc=nc, p=p, xin=x1, xout=x2, tag="m_"))
        build_l1_attn("C", fctx=dict(nc=nc, p=p, xin=x2, xout=x3, tag="b_"))
        build_l1_ffn(fctx=dict(nc=nc, p=p, xin=x3, xout=xout, tag="f_"))
        p.wait_only("sync", [(p.dsem[e], p.dma_val[e]) for e in P.ENGS if p.dma_val[e]]
                    + [(p.clk[e], p.clk_val[e]) for e in P.ENGS if p.clk_val[e]],
                    tag="final")
        with nc.Block() as block:
            p.emit(block)
    return nc


def host_fused_inputs(inputs):
    x = np.asarray(inputs["L0_attn_x"], np.float32)
    m1 = host_l0_attn_inputs(x, inputs["L0_attn_Wk"], inputs["L0_attn_Wq"],
                             inputs["L0_attn_Wo"], inputs["L0_attn_bo"])
    dummy = [np.zeros((1, 1), np.float32)] * NC
    m2 = host_l0_moe_inputs(dummy, inputs["L0_ffn_rms_w"], inputs["L0_ffn_gate_w"],
                            inputs["L0_ffn_w13"], inputs["L0_ffn_w2"],
                            inputs["L0_ffn_sw_gate"], inputs["L0_ffn_sw_up"],
                            inputs["L0_ffn_sw_down"])
    m3 = host_l1_attn_inputs(None, inputs["L1_attn_Wqa"], inputs["L1_attn_Wqb"],
                             inputs["L1_attn_Wkva"], inputs["L1_attn_Wkvb"],
                             inputs["L1_attn_Wout"], inputs["L1_attn_bout"], xs=dummy)
    m4 = host_l1_ffn_inputs(inputs["L1_ffn_ln_g"], inputs["L1_ffn_ln_b"],
                            inputs["L1_ffn_bn_g"], inputs["L1_ffn_bn_b"],
                            inputs["L1_ffn_bn_mean"], inputs["L1_ffn_bn_var"],
                            inputs["L1_ffn_Wr"], inputs["L1_ffn_br"],
                            inputs["L1_ffn_W_down"], inputs["L1_ffn_b_down"],
                            inputs["L1_ffn_W_up"], inputs["L1_ffn_b_up"], xs=dummy)
    maps = []
    for c in range(NC):
        for mm in (m2, m3, m4):
            mm[c].pop("xT", None)
        maps.append({**m1[c], **m2[c], **m3[c], **m4[c]})
    return maps


_prog_cache = {}


def _get_prog(name, builder):
    if name not in _prog_cache:
        _prog_cache[name] = builder()
    return _prog_cache[name]


def kernel(**inputs):
    nc = _get_prog("fused", build_fused)
    maps = host_fused_inputs(inputs)
    res = run_bass_kernel_spmd(nc, maps, core_ids=list(range(NC)))
    return np.ascontiguousarray(
        np.stack([res.results[c]["x4T"].T for c in range(NC)]).astype(np.float32))


def kernel_multilaunch(**inputs):
    x = np.asarray(inputs["L0_attn_x"], np.float32)

    nc1 = _get_prog("l0a", lambda: build_l0_attn("C"))
    maps1 = host_l0_attn_inputs(x, inputs["L0_attn_Wk"], inputs["L0_attn_Wq"],
                                inputs["L0_attn_Wo"], inputs["L0_attn_bo"])
    r1 = run_bass_kernel_spmd(nc1, maps1, core_ids=list(range(NC)))
    x1T = [r1.results[c]["x1T"] for c in range(NC)]

    nc2 = _get_prog("l0m", lambda: build_l0_moe())
    maps2 = host_l0_moe_inputs([a.T for a in x1T], inputs["L0_ffn_rms_w"],
                               inputs["L0_ffn_gate_w"], inputs["L0_ffn_w13"],
                               inputs["L0_ffn_w2"], inputs["L0_ffn_sw_gate"],
                               inputs["L0_ffn_sw_up"], inputs["L0_ffn_sw_down"])
    r2 = run_bass_kernel_spmd(nc2, maps2, core_ids=list(range(NC)))
    x2T = [r2.results[c]["x2T"] for c in range(NC)]

    nc3 = _get_prog("l1a", lambda: build_l1_attn("C"))
    maps3 = host_l1_attn_inputs(None, inputs["L1_attn_Wqa"], inputs["L1_attn_Wqb"],
                                inputs["L1_attn_Wkva"], inputs["L1_attn_Wkvb"],
                                inputs["L1_attn_Wout"], inputs["L1_attn_bout"],
                                xs=[a.T for a in x2T])
    r3 = run_bass_kernel_spmd(nc3, maps3, core_ids=list(range(NC)))
    x3T = [r3.results[c]["x2T"] for c in range(NC)]

    nc4 = _get_prog("l1f", lambda: build_l1_ffn())
    maps4 = host_l1_ffn_inputs(inputs["L1_ffn_ln_g"], inputs["L1_ffn_ln_b"],
                               inputs["L1_ffn_bn_g"], inputs["L1_ffn_bn_b"],
                               inputs["L1_ffn_bn_mean"], inputs["L1_ffn_bn_var"],
                               inputs["L1_ffn_Wr"], inputs["L1_ffn_br"],
                               inputs["L1_ffn_W_down"], inputs["L1_ffn_b_down"],
                               inputs["L1_ffn_W_up"], inputs["L1_ffn_b_up"],
                               xs=[a.T for a in x3T])
    r4 = run_bass_kernel_spmd(nc4, maps4, core_ids=list(range(NC)))
    return np.ascontiguousarray(
        np.stack([r4.results[c]["x4T"].T for c in range(NC)]).astype(np.float32))
L1_SLOT = {}
for _h in range(16):
    _g = _h // 4
    _idx = _h % 4 + (4 if _h >= 8 else 0)
    L1_SLOT[_h] = 2 * _idx + (_g % 2)
L1_PAIRS = [([0, 1, 2, 3, 8, 9, 10, 11][j], [4, 5, 6, 7, 12, 13, 14, 15][j])
            for j in range(8)]
VW1 = 65 * 4


def build_l1_attn(upto="C", fctx=None):
    nc = fctx["nc"] if fctx else bass.Bass("TRN2", target_bir_lowering=False, debug=False, detect_race_conditions=False)
    xT = fctx["xin"] if fctx else nc.dram_tensor("xT", [D, T], F32, kind="ExternalInput").ap()
    Wqa = nc.dram_tensor("Wqa", [D, 64], F32, kind="ExternalInput").ap()
    Wqb = nc.dram_tensor("Wqb", [64, D], F32, kind="ExternalInput").ap()
    Wkva = nc.dram_tensor("Wkva", [D, 32], F32, kind="ExternalInput").ap()
    Wkvb_k = nc.dram_tensor("Wkvb_k", [32, 256], F32, kind="ExternalInput").ap()
    Wkvb_v = nc.dram_tensor("Wkvb_v", [32, 256], F32, kind="ExternalInput").ap()
    Wout = nc.dram_tensor("Wout", [D, D], F32, kind="ExternalInput").ap()
    bout_t = nc.dram_tensor("bout_t", [128, NDT], F32, kind="ExternalInput").ap()
    cosT = nc.dram_tensor("cosT", [128, T], F32, kind="ExternalInput").ap()
    sinT = nc.dram_tensor("sinT", [128, T], F32, kind="ExternalInput").ap()
    cmask = nc.dram_tensor("cmask", [128, 4 * 1024], BF16, kind="ExternalInput").ap()
    x2T = fctx["xout"] if fctx else nc.dram_tensor("x2T", [D, T], F32, kind="ExternalOutput").ap()
    qT_h = nc.dram_tensor("qT_h1", [NCHUNK * D, CH], BF16).ap()
    kv_h = nc.dram_tensor("kv_h1", [NCHUNK * 256, CH], BF16).ap()
    ao_h = nc.dram_tensor("ao_h1", [T, D], BF16).ap()

    with contextlib.ExitStack() as ctx:
        pref = fctx["tag"] if fctx else ""
        sb = lambda name, p_, f, dt=BF16: ctx.enter_context(nc.sbuf_tensor(pref + name, [p_, f], dt))
        c_ones = sb("c_ones", 128, 1)
        c_onesrf = sb("c_onesrf", 1, 128, F32)
        wqa_s = [sb(f"wqa{i}", 128, 64) for i in range(NDT)]
        wqb_s = sb("wqb_s", 64, D)
        wkva_s = [sb(f"wkva{i}", 128, 32) for i in range(NDT)]
        wkvbk_s = sb("wkvbk_s", 32, 256)
        wkvbv_s = sb("wkvbv_s", 32, 256)
        wout_s = [sb(f"wout{i}", 128, D) for i in range(NDT)]
        bout_s = sb("bout_s", 128, NDT, F32)
        cos_s = sb("cos_s", 128, T, F32)
        sin_s = sb("sin_s", 128, T, F32)
        cm_s = sb("cm_s", 128, 4 * 1024)
        kT = [sb(f"kTl{i}", 128, T) for i in range(2)]
        vbuf = sb("vbufl", 128, NKT * VW1)
        sxb = sb("sxb", 128, NDT * CH)
        sqp = [sb(f"sqp{i}", 128, CH) for i in range(2)]
        xnb = sb("xnb", 128, NDT * CH)
        tmpf = sb("tmpf", 128, CH, F32)
        t2f = sb("t2f", 128, CH, F32)
        mrow = sb("mrow", 1, CH, F32)
        rrow = sb("rrow", 1, 2 * CH, F32)
        t1b = sb("t1b", 64, CH)
        t1k0 = sb("t1k0", 32, CH)
        qre = sb("qre", 128, CH)
        qro = sb("qro", 128, CH)
        qcb = sb("qcb", 128, NDT * CH)
        ptb = [sb(f"ptb{i}", 128, 1024) for i in range(4)]
        anb = sxb
        recb = sb("recb", 128, 16, F32)
        resf = tmpf

        P0 = ctx.enter_context(nc.psum_tensor(pref + "P0", [128, 2048], F32))
        P2 = ctx.enter_context(nc.psum_tensor(pref + "P2", [128, 512], F32))
        P3 = ctx.enter_context(nc.psum_tensor(pref + "P3", [128, 512], F32))
        P4 = ctx.enter_context(nc.psum_tensor(pref + "P4", [128, 512], F32))
        P5 = ctx.enter_context(nc.psum_tensor(pref + "P5", [128, 512], F32))

        p = fctx["p"] if fctx else P(nc, ctx)

        def ld_w(e, p_):
            p_.dma(e, out=bout_s[:, :], in_=bout_t)
            p_.dma(e, out=wqb_s[:, :], in_=Wqb)
            p_.dma(e, out=wkvbk_s[:, :], in_=Wkvb_k)
            p_.dma(e, out=wkvbv_s[:, :], in_=Wkvb_v)
            p_.dma(e, out=cm_s[:, :], in_=cmask)
            for i in range(NDT):
                p_.dma(e, out=wqa_s[i][:, :], in_=Wqa[i * 128:(i + 1) * 128, :])
                p_.dma(e, out=wkva_s[i][:, :], in_=Wkva[i * 128:(i + 1) * 128, :])
                p_.dma(e, out=wout_s[i][:, :], in_=Wout[i * 128:(i + 1) * 128, :])
        s_wld = p.step("gpsimd", ld_w, n_dma=5 + 3 * NDT, tag="wld")

        def ld_rope(e, p_):
            p_.dma(e, out=cos_s[:, :], in_=cosT)
            p_.dma(e, out=sin_s[:, :], in_=sinT)
        s_rope = p.step("sync", ld_rope, n_dma=2, tag="ldrope")

        def consts(e, p_):
            e.memset(c_ones[:, :], 1.0)
            e.memset(c_onesrf[0:1, :], 1.0)
            return e.memset(vbuf[:, :], 1.0)
        s_const = p.step("vector", consts, tag="consts")

        def xcopy(e, p_):
            p_.dma(e, out=x2T, in_=xT)
        s_xcopy = p.step("sync", xcopy, n_dma=1, tag="xcopy")

        bca_m = P0[:, 0:CH]
        bca_r = P0[:, CH:2 * CH]
        s_xn = [None] * NCHUNK
        s_qdma = [None] * NCHUNK
        s_kvld = [None] * NCHUNK
        prev = dict(statm=None, bca=None, stats_pe=None)
        s_sqs = {}
        s_pes = {}
        s_va_ev = {}
        last_dma_q = None

        for c in range(NCHUNK):
            def ldxb(e, p_, cc=c):
                p_.dma(e, out=sxb[:, :].rearrange("p (a n) -> p a n", a=NDT),
                       in_=xT[:, cc * CH:(cc + 1) * CH].rearrange("(a p) n -> p a n", p=128))
            s_ldxb = p.step("gpsimd", ldxb, n_dma=1, tag=f"A{c}_ldxb",
                            deps=[s_sqs.get((c - 1, NDT - 1)), prev["stats_pe"],
                                  s_xn[c - 1] if c else None])

            for dt in range(NDT):
                def sq(e, p_, dt=dt):
                    return e.tensor_tensor(sqp[dt % 2][:, :], sxb[:, dt * CH:(dt + 1) * CH],
                                           sxb[:, dt * CH:(dt + 1) * CH], ALU.mult)
                s_sqs[(c, dt)] = p.step("vector", sq, tag=f"A{c}_sq{dt}",
                                        deps=[s_ldxb, s_pes.get((c, dt - 2)),
                                              prev["stats_pe"] if dt < 2 else None])

                def pe2(e, p_, dt=dt):
                    e.matmul(P4[0:1, :], c_ones[:, :], sxb[:, dt * CH:(dt + 1) * CH],
                             start=(dt == 0), stop=(dt == NDT - 1))
                    return e.matmul(P5[0:1, :], c_ones[:, :], sqp[dt % 2][:, :],
                                    start=(dt == 0), stop=(dt == NDT - 1))
                s_pes[(c, dt)] = p.step("tensor", pe2, tag=f"A{c}_pe{dt}",
                                        deps=[s_sqs[(c, dt)], s_const,
                                              prev["statm"] if dt == 0 else None])
            s_stats_pe = s_pes[(c, NDT - 1)]
            prev["stats_pe"] = s_stats_pe

            def statm1(e, p_):
                e.tensor_scalar(mrow[0:1, :], P4[0:1, :], 1.0 / D, None, ALU.mult)
                e.tensor_scalar(rrow[0:1, 0:CH], P5[0:1, :], 1.0 / D, None, ALU.mult)
                e.tensor_tensor(rrow[0:1, CH:], mrow[0:1, :], mrow[0:1, :], ALU.mult)
                e.tensor_tensor(rrow[0:1, 0:CH], rrow[0:1, 0:CH], rrow[0:1, CH:], ALU.subtract)
                return e.tensor_scalar(rrow[0:1, 0:CH], rrow[0:1, 0:CH], 1e-5, None, ALU.add)
            s_statm1 = p.step("vector", statm1, deps=[s_stats_pe, prev["bca"]], tag=f"A{c}_sm1")

            def sqrt(e, p_):
                return e.activation(rrow[0:1, CH:], rrow[0:1, 0:CH], AF.Sqrt)
            s_sqrt = p.step("scalar", sqrt, deps=[s_statm1], tag=f"A{c}_sqrt")

            def recip(e, p_):
                return e.reciprocal(rrow[0:1, 0:CH], rrow[0:1, CH:])
            s_recip = p.step("vector", recip, deps=[s_sqrt], tag=f"A{c}_rec")
            prev["statm"] = s_recip

            def bca(e, p_):
                e.matmul(bca_m, c_onesrf[0:1, :], mrow[0:1, :], start=True, stop=True)
                return e.matmul(bca_r, c_onesrf[0:1, :], rrow[0:1, 0:CH], start=True, stop=True)
            s_bca = p.step("tensor", bca, deps=[s_recip, s_xn[c - 1] if c else None],
                           tag=f"A{c}_bca")
            prev["bca"] = s_bca

            def xn(e, p_):
                for dt in range(NDT):
                    e.tensor_tensor(tmpf[:, :], sxb[:, dt * CH:(dt + 1) * CH], bca_m, ALU.subtract)
                    ins = e.tensor_tensor(xnb[:, dt * CH:(dt + 1) * CH], tmpf[:, :], bca_r, ALU.mult)
                return ins
            s_xn[c] = p.step("vector", xn, deps=[s_bca, s_va_ev.get((c - 1, 3))],
                             tag=f"A{c}_xn")

            def st1_pe(e, p_):
                for ktd in range(NDT):
                    e.matmul(P0[0:64, 1024:1536], wqa_s[ktd][:, :],
                             xnb[:, ktd * CH:(ktd + 1) * CH],
                             start=(ktd == 0), stop=(ktd == NDT - 1))
                for ktd in range(NDT):
                    ins = e.matmul(P0[0:32, 1536:2048], wkva_s[ktd][:, :],
                                   xnb[:, ktd * CH:(ktd + 1) * CH],
                                   start=(ktd == 0), stop=(ktd == NDT - 1))
                return ins
            s_st1 = p.step("tensor", st1_pe, deps=[s_xn[c], s_wld,
                                                   s_va_ev.get((c - 1, 3))], tag=f"A{c}_st1")

            def st1_ev(e, p_):
                e.tensor_copy(t1b[0:64, :], P0[0:64, 1024:1536])
                return e.tensor_copy(t1k0[0:32, :], P0[0:32, 1536:2048])
            s_st1e = p.step("vector", st1_ev, deps=[s_st1], tag=f"A{c}_st1e")

            s_rope_prev = None
            for m in range(4):
                def q2_pe(e, p_, m=m):
                    e.matmul(P2[:, :], wqb_s[:, m * 128:(m + 1) * 128], t1b[0:64, :],
                             start=True, stop=True)
                    return e.matmul(P3[:, :], wqb_s[:, 512 + m * 128: 512 + (m + 1) * 128],
                                    t1b[0:64, :], start=True, stop=True)
                s_q2 = p.step("tensor", q2_pe, deps=[s_st1e, s_rope_prev], tag=f"A{c}_q2_{m}")

                def ropeq(e, p_, m=m, cc=c):
                    cslice = cos_s[:, cc * CH:(cc + 1) * CH]
                    sslice = sin_s[:, cc * CH:(cc + 1) * CH]
                    e.tensor_tensor(tmpf[:, :], P2[:, :], cslice, ALU.mult)
                    e.tensor_tensor(t2f[:, :], P3[:, :], sslice, ALU.mult)
                    e.tensor_tensor(qre[:, :], tmpf[:, :], t2f[:, :], ALU.subtract)
                    e.tensor_tensor(tmpf[:, :], P2[:, :], sslice, ALU.mult)
                    e.tensor_tensor(t2f[:, :], P3[:, :], cslice, ALU.mult)
                    return e.tensor_tensor(qro[:, :], tmpf[:, :], t2f[:, :], ALU.add)
                s_ropeq = p.step("vector", ropeq, deps=[s_q2, last_dma_q], tag=f"A{c}_ropeq{m}")
                s_rope_prev = s_ropeq

                def qdma(e, p_, m=m, cc=c):
                    base = (m % 2) + 8 * (m // 2)
                    region = qT_h[cc * D:(cc + 1) * D, :]
                    rr2 = region.rearrange("(sp two half i) n -> half two sp i n",
                                           two=2, half=2, i=32)
                    so = 4 if base >= 8 else 0
                    for half, srcbuf in ((0, qre), (1, qro)):
                        dst = rr2[half, base % 2, so:so + 4, :, :]
                        p_.dma(e, out=dst, in_=srcbuf[:, :])
                s_qd = p.step("gpsimd", qdma, n_dma=2, deps=[s_ropeq], tag=f"A{c}_qdma{m}")
                last_dma_q = s_qd
            s_qdma[c] = last_dma_q

            def k2_pe(e, p_):
                e.matmul(P2[:, :], wkvbk_s[:, 0:128], t1k0[0:32, :], start=True, stop=True)
                return e.matmul(P3[:, :], wkvbk_s[:, 128:256], t1k0[0:32, :], start=True, stop=True)
            s_k2 = p.step("tensor", k2_pe, deps=[s_st1e, s_rope_prev], tag=f"A{c}_k2")

            def ropek(e, p_, cc=c):
                cslice = cos_s[:, cc * CH:(cc + 1) * CH]
                sslice = sin_s[:, cc * CH:(cc + 1) * CH]
                e.tensor_tensor(tmpf[:, :], P2[:, :], cslice, ALU.mult)
                e.tensor_tensor(t2f[:, :], P3[:, :], sslice, ALU.mult)
                e.tensor_tensor(qre[:, :], tmpf[:, :], t2f[:, :], ALU.subtract)
                e.tensor_tensor(tmpf[:, :], P2[:, :], sslice, ALU.mult)
                e.tensor_tensor(t2f[:, :], P3[:, :], cslice, ALU.mult)
                return e.tensor_tensor(qro[:, :], tmpf[:, :], t2f[:, :], ALU.add)
            s_ropek = p.step("vector", ropek, deps=[s_k2, last_dma_q], tag=f"A{c}_ropek")

            def kdma(e, p_, cc=c):
                region = kv_h[cc * 256:(cc + 1) * 256, :]
                rr = region.rearrange("(g half i) n -> half g i n", half=2, i=32)
                for half, srcbuf in ((0, qre), (1, qro)):
                    p_.dma(e, out=rr[half, :, :, :], in_=srcbuf[:, :])
            s_kd = p.step("gpsimd", kdma, n_dma=2, deps=[s_ropek], tag=f"A{c}_kdma")
            last_dma_q = s_kd

            def kvld(e, p_, cc=c):
                for t_ in range(2):
                    p_.dma(e, out=kT[t_][:, cc * CH:(cc + 1) * CH],
                           in_=kv_h[cc * 256 + t_ * 128: cc * 256 + (t_ + 1) * 128, :])
            s_kvld[c] = p.step("sync", kvld, n_dma=2, deps=[s_kd], tag=f"A{c}_kvld")

            for sv in range(2):
                def v_pe(e, p_, sv=sv):
                    for j in range(2):
                        s_ = sv * 2 + j
                        ins = e.matmul(P2[:, 0:256] if j == 0 else P3[:, 0:256],
                                       t1k0[0:32, s_ * 128:(s_ + 1) * 128],
                                       wkvbv_s[:, :],
                                       start=True, stop=True)
                    return ins
                s_vp = p.step("tensor", v_pe, deps=[s_st1e, s_ropek if sv == 0 else None,
                                                    s_va_ev.get((c, sv * 2 - 1)),
                                                    s_va_ev.get((c - 1, 3)) if sv == 0 else None],
                              tag=f"A{c}_v{sv}")

                for j in range(2):
                    def v_ev(e, p_, sv=sv, j=j, cc=c):
                        s_ = sv * 2 + j
                        tile = cc * 4 + s_
                        dst = vbuf[:, tile * VW1: tile * VW1 + VW1]
                        return e.tensor_copy(
                            dst.rearrange("p (h w) -> p h w", w=65)[:, :, 0:64],
                            (P2 if j == 0 else P3)[:, 0:256].rearrange("p (h w) -> p h w", w=64))
                    s_va_ev[(c, sv * 2 + j)] = p.step("vector", v_ev,
                                                      deps=[s_vp, s_const], tag=f"A{c}_vev{sv}{j}")

        s_a_done = s_va_ev[(NCHUNK - 1, 3)]
        if upto == "A":
            p.wait_only("gpsimd", [s_qdma[NCHUNK - 1], s_xcopy], tag="final")
            p.wait_only("sync", [s_kvld[NCHUNK - 1]], tag="final2")
            with nc.Block() as block:
                p.emit(block)
            return nc

        st = P0
        pv = [P2, P3]
        s_div = {}
        s_ao_dma = [None] * NCHUNK
        s_c_done = [None] * NCHUNK
        st_steps = []
        exp_steps = []
        pv_steps = []
        s_cev_last = [None] * NCHUNK

        for qc in range(NCHUNK):
            NKTq = 4 * qc + 4

            def qld(e, p_, qc=qc):
                p_.dma(e, out=qcb[:, :].rearrange("p (a n) -> p a n", a=NDT),
                       in_=qT_h[qc * D:(qc + 1) * D, :].rearrange("(a p) n -> p a n", p=128))
            s_qld = p.step("sync", qld, n_dma=1, tag=f"B{qc}_qld",
                           deps=[s_qdma[qc], s_kvld[NCHUNK - 1],
                                 s_qdma[NCHUNK - 1] if qc == 0 else st_steps[-1]])

            for hp in range(8):
                he, ho = L1_PAIRS[hp]
                pend_pv = []
                s_e = None
                for g in range(NKTq):
                    G = len(st_steps)
                    sthalf = st[:, (g % 2) * 1024:(g % 2) * 1024 + 1024]

                    def st_mm(e, p_, hp=hp, g=g, sthalf=sthalf):
                        for pos, hh in enumerate(L1_PAIRS[hp]):
                            sl = L1_SLOT[hh]
                            gkv = hh // 4
                            ins = e.matmul(
                                sthalf[:, pos * 512:pos * 512 + 512],
                                kT[gkv // 2][(gkv % 2) * 64:(gkv % 2) * 64 + 64,
                                             g * 128:(g + 1) * 128],
                                qcb[(sl % 2) * 64:(sl % 2) * 64 + 64,
                                    (sl // 2) * CH:(sl // 2 + 1) * CH],
                                start=True, stop=True)
                        return ins
                    s_stm = p.step("tensor", st_mm,
                                   deps=[s_qld, s_a_done,
                                         exp_steps[G - 2] if len(exp_steps) >= 2 else None,
                                         s_div.get((qc, hp - 1)) if g == 0 else None,
                                         s_div.get((qc - 1, 7)) if (g == 0 and hp == 0) else None,
                                         s_cev_last[qc - 1] if (g == 0 and hp == 0 and qc) else None],
                                   tag=f"B1{qc}_{hp}_st{g}")
                    st_steps.append(s_stm)

                    def expf(e, p_, g=g, G=G, sthalf=sthalf):
                        return e.activation(ptb[G % 4][:, :], sthalf, AF.Exp, scale=1.0 / 8)
                    s_e = p.step("scalar", expf,
                                 deps=[s_stm, pv_steps[G - 4] if G >= 4 else None],
                                 tag=f"B1{qc}_{hp}_e{g}")
                    exp_steps.append(s_e)

                    delta = g - 4 * qc
                    s_mask = None
                    if 0 <= delta <= 3:
                        def maskf(e, p_, g=g, G=G, delta=delta):
                            return e.tensor_tensor(ptb[G % 4][:, :], ptb[G % 4][:, :],
                                                   cm_s[:, delta * 1024:(delta + 1) * 1024],
                                                   ALU.mult)
                        s_mask = p.step("vector", maskf, deps=[s_e], tag=f"B1{qc}_{hp}_m{g}")
                        exp_steps[-1] = s_mask

                    def pv_mm(e, p_, hp=hp, g=g, G=G, NKTq=NKTq):
                        for pos, hh in enumerate(L1_PAIRS[hp]):
                            gkv = hh // 4
                            for sub in range(4):
                                ins = e.matmul(
                                    pv[pos][:, sub * 65:(sub + 1) * 65],
                                    ptb[G % 4][:, pos * 512 + sub * 128:
                                               pos * 512 + sub * 128 + 128],
                                    vbuf[:, g * VW1 + gkv * 65: g * VW1 + gkv * 65 + 65],
                                    start=(g == 0 and sub == 0), stop=(g == NKTq - 1 and sub == 3))
                        return ins
                    pend_pv.append((pv_mm, f"B1{qc}_{hp}_pv{g}"))
                    if len(pend_pv) == 4:
                        fn0, tag0 = pend_pv.pop(0)
                        pv_steps.append(p.step("tensor", fn0, deps=[exp_steps[-3]], tag=tag0))

                def echo(e, p_):
                    return e.activation(sqp[0][0:1, 0:1], ptb[1][0:1, 1023:1024], AF.Exp)
                s_echo = p.step("scalar", echo, deps=[], tag=f"B1{qc}_{hp}_echo")
                for fn0, tag0 in pend_pv:
                    pv_steps.append(p.step("tensor", fn0, deps=[s_echo], tag=tag0))
                pend_pv = []

                def rec_act(e, p_):
                    for b in range(2):
                        e.activation(recb[:, 8 + b * 4:12 + b * 4].rearrange("p (s w) -> p s w", w=1),
                                     pv[b][:, 0:260].rearrange("p (s w) -> p s w", w=65)[:, :, 64:65],
                                     AF.Ln)
                    return e.activation(recb[:, 0:8], recb[:, 8:16], AF.Exp, scale=-1.0)
                s_rec = p.step("scalar", rec_act, deps=[pv_steps[-1]], tag=f"B1{qc}_{hp}_rec")

                def div(e, p_, hp=hp):
                    for pos, hh in enumerate(L1_PAIRS[hp]):
                        for sub in range(4):
                            i = pos * 4 + sub
                            ins = e.tensor_scalar(
                                anb[:, sub * D + hh * 64: sub * D + hh * 64 + 64],
                                pv[pos][:, sub * 65:sub * 65 + 64],
                                recb[:, i:i + 1], None, ALU.mult)
                    return ins
                s_div[(qc, hp)] = p.step(
                    "vector", div,
                    deps=[s_rec, s_ao_dma[qc - 1] if qc >= 1 else None],
                    tag=f"B1{qc}_{hp}_div")

            def ao_dma(e, p_, qc=qc):
                p_.dma(e, out=ao_h[qc * CH:(qc + 1) * CH, :].rearrange("(s p) d -> p s d", p=128),
                       in_=anb[:, :].rearrange("p (s d) -> p s d", s=4))
            s_ao_dma[qc] = p.step("gpsimd", ao_dma, n_dma=1,
                                  deps=[s_div[(qc, hp)] for hp in range(8)], tag=f"B1{qc}_ao")

            if upto == "B":
                s_cev_last[qc] = s_ao_dma[qc]
                s_c_done[qc] = s_ao_dma[qc]
                continue

            def cldT(e, p_, qc=qc):
                for dt in range(NDT):
                    p_.dma_t(e, out=xnb[:, dt * CH:(dt + 1) * CH],
                             in_=ao_h[qc * CH:(qc + 1) * CH, dt * 128:(dt + 1) * 128])
            s_cldT = p.step("sync", cldT, n_dma=NDT,
                            deps=[s_ao_dma[qc],
                                  s_c_done[qc - 1] if qc else s_st1,
                                  None if qc else s_xn[NCHUNK - 1]],
                            tag=f"C1{qc}_ldT")

            s_cev = None
            s_cout = None
            for dt in range(NDT):
                def c_pe(e, p_, dt=dt):
                    yo = P4[:, :] if dt % 2 == 0 else P5[:, :]
                    for ktd in range(NDT):
                        ins = e.matmul(yo, wout_s[ktd][:, dt * 128:(dt + 1) * 128],
                                       xnb[:, ktd * CH:(ktd + 1) * CH],
                                       start=(ktd == 0), stop=(ktd == NDT - 1))
                    return ins
                dep_stats = prev["statm"] if (qc == 0 and dt < 2) else None
                dep_war = s_cev if dt >= 2 else None
                s_cpe = p.step("tensor", c_pe, deps=[s_cldT, dep_stats, dep_war],
                               tag=f"C1{qc}_pe{dt}")

                def c_ev(e, p_, dt=dt):
                    yo = P4[:, :] if dt % 2 == 0 else P5[:, :]
                    return e.tensor_scalar(resf[:, :], yo, bout_s[:, dt:dt + 1], None, ALU.add)
                s_cev = p.step("vector", c_ev, deps=[s_cpe, s_cout], tag=f"C1{qc}_ev{dt}")

                def c_out(e, p_, dt=dt, qc=qc):
                    p_.dma(e, out=x2T[dt * 128:(dt + 1) * 128, qc * CH:(qc + 1) * CH],
                           in_=resf[:, :], accum_op=ALU.add)
                s_cout = p.step("gpsimd", c_out, n_dma=1, deps=[s_cev, s_xcopy],
                                tag=f"C1{qc}_out{dt}")
            s_cev_last[qc] = s_cev
            s_c_done[qc] = s_cout

        p.wait_only("gpsimd", [s_c_done[NCHUNK - 1], s_xcopy], tag="final")
        if fctx is not None:
            p.fence()
            return nc
        with nc.Block() as block:
            p.emit(block)
    return nc


def host_l1_attn_inputs(x_unused_count, Wqa, Wqb, Wkva, Wkvb, Wout, bout, xs):
    import ml_dtypes
    evens = [h * 64 + 2 * i for h in range(16) for i in range(32)]
    odds = [h * 64 + 2 * i + 1 for h in range(16) for i in range(32)]
    Wqb_r = np.ascontiguousarray(np.asarray(Wqb, np.float32)[:, evens + odds])
    k_evens = [g * 128 + 4 * i for g in range(4) for i in range(32)]
    k_odds = [g * 128 + 4 * i + 2 for g in range(4) for i in range(32)]
    v_cols = [g * 128 + 2 * d_ + 1 for g in range(4) for d_ in range(64)]
    Wkvb = np.asarray(Wkvb, np.float32)
    Wkvb_k = np.ascontiguousarray(Wkvb[:, k_evens + k_odds])
    Wkvb_v = np.ascontiguousarray(Wkvb[:, v_cols])
    inv_freq = (1.0 / 10000.0 ** (np.arange(0, 64, 2, dtype=np.float32) / 64)).astype(np.float32)
    t = np.arange(T, dtype=np.float32)
    freqs = np.outer(t, inv_freq).astype(np.float32)
    cosT = np.ascontiguousarray(np.tile(np.cos(freqs).T, (4, 1)).astype(np.float32))
    sinT = np.ascontiguousarray(np.tile(np.sin(freqs).T, (4, 1)).astype(np.float32))
    k_i = np.arange(128)[:, None]
    q_i = np.arange(512)[None, :]
    cm = np.zeros((128, 4 * 1024), np.float32)
    for dlt in range(4):
        m = (k_i <= q_i - 128 * dlt).astype(np.float32)
        cm[:, dlt * 1024:dlt * 1024 + 512] = m
        cm[:, dlt * 1024 + 512:(dlt + 1) * 1024] = m
    cm = cm.astype(ml_dtypes.bfloat16)
    bout_t = np.ascontiguousarray(np.asarray(bout, np.float32).reshape(NDT, 128).T)
    maps = []
    for c in range(NC):
        maps.append({
            "xT": np.ascontiguousarray(np.asarray(xs[c], np.float32).T),
            "Wqa": np.asarray(Wqa, np.float32), "Wqb": Wqb_r,
            "Wkva": np.asarray(Wkva, np.float32), "Wkvb_k": Wkvb_k, "Wkvb_v": Wkvb_v,
            "Wout": np.asarray(Wout, np.float32), "bout_t": bout_t,
            "cosT": cosT, "sinT": sinT, "cmask": cm,
        })
    return maps
def build_l1_ffn(dbg=False, fctx=None):
    nc = fctx["nc"] if fctx else bass.Bass("TRN2", target_bir_lowering=False, debug=False, detect_race_conditions=False)
    xT = fctx["xin"] if fctx else nc.dram_tensor("xT", [D, T], F32, kind="ExternalInput").ap()
    lng_t = nc.dram_tensor("lng_t", [128, NDT], F32, kind="ExternalInput").ap()
    lnb_t = nc.dram_tensor("lnb_t", [128, NDT], F32, kind="ExternalInput").ap()
    bng_t = nc.dram_tensor("bng_t", [128, NDT], F32, kind="ExternalInput").ap()
    bnb_t = nc.dram_tensor("bnb_t", [128, NDT], F32, kind="ExternalInput").ap()
    bnm_t = nc.dram_tensor("bnm_t", [128, NDT], F32, kind="ExternalInput").ap()
    bnv_t = nc.dram_tensor("bnv_t", [128, NDT], F32, kind="ExternalInput").ap()
    Wr = nc.dram_tensor("Wr", [D, 8], F32, kind="ExternalInput").ap()
    br_row = nc.dram_tensor("br_row", [1, 8], F32, kind="ExternalInput").ap()
    Wdn = nc.dram_tensor("Wdn", [8 * D, 64], F32, kind="ExternalInput").ap()
    bdn = nc.dram_tensor("bdn", [8, 64], F32, kind="ExternalInput").ap()
    Wup = nc.dram_tensor("Wup", [8 * 64, D], F32, kind="ExternalInput").ap()
    bup = nc.dram_tensor("bup", [8, D], F32, kind="ExternalInput").ap()
    x4T = fctx["xout"] if fctx else nc.dram_tensor("x4T", [D, T], F32, kind="ExternalOutput").ap()
    z_h = nc.dram_tensor("z_h", [NCHUNK * D, CH], BF16).ap()
    oh_d = nc.dram_tensor("oh_d", [8, 1], F32).ap()
    rdbg = nc.dram_tensor("rdbg", [128, 32], F32, kind="ExternalOutput" if dbg else "Internal").ap()

    with contextlib.ExitStack() as ctx:
        pref = fctx["tag"] if fctx else ""
        sb = lambda name, p_, f, dt=BF16: ctx.enter_context(nc.sbuf_tensor(pref + name, [p_, f], dt))
        c_onesf = sb("c_onesf", 128, 1, F32)
        c_onesrf = sb("c_onesrf", 1, 128, F32)
        c_zer = sb("c_zer", 64, CH)
        lng_s = sb("lng_s", 128, NDT, F32); lnb_s = sb("lnb_s", 128, NDT, F32)
        bng_s = sb("bng_s", 128, NDT, F32); bnb_s = sb("bnb_s", 128, NDT, F32)
        bnm_s = sb("bnm_s", 128, NDT, F32); bnv_s = sb("bnv_s", 128, NDT, F32)
        wr_s = sb("wr_s", 128, NDT * 8, F32)
        br_s = sb("br_s", 1, 8, F32)
        wdn_s = sb("wdn_s", 128, 8 * NDT * 64)
        bdn_s = sb("bdn_s", 8, 64, F32)
        wup_s = sb("wup_s", 64, 8 * D)
        bup_s = sb("bup_s", 8, D, F32)
        sxf = sb("sxf", 128, NDT * CH, F32)
        xnf = sb("xnf", 128, NDT * CH, F32)
        sqf = [sb(f"sqf{i}", 128, CH, F32) for i in range(2)]
        zbf = sb("zbf", 128, NDT * CH)
        tmpf = sb("tmpf", 128, CH, F32)
        mrow = sb("mrow", 1, CH, F32)
        rrow = sb("rrow", 1, 2 * CH, F32)
        zsum = sb("zsum", 128, NDT, F32)
        zsum_all = sb("zsum_all", 128, NDT * NCHUNK, F32)
        zbn = sb("zbn", 128, NDT, F32)
        l8f = sb("l8f", 1, 24, F32)
        oh_row = sb("oh_row", 1, 8, F32)
        ohT = sb("ohT", 8, 1, F32)
        wsel = [sb(f"wsel{i}", 128, NDT * 64) for i in range(2)]
        wusel = [sb(f"wusel{i}", 64, D) for i in range(2)]
        hbf = sb("hbf", 64, CH)
        resf = tmpf

        P0 = ctx.enter_context(nc.psum_tensor(pref + "P0", [128, 2048], F32))
        P2 = ctx.enter_context(nc.psum_tensor(pref + "P2", [128, 512], F32))
        P3 = ctx.enter_context(nc.psum_tensor(pref + "P3", [128, 512], F32))
        P4 = ctx.enter_context(nc.psum_tensor(pref + "P4", [128, 512], F32))
        P5 = ctx.enter_context(nc.psum_tensor(pref + "P5", [128, 512], F32))

        p = fctx["p"] if fctx else P(nc, ctx)

        def ld_w(e, p_):
            for dst, srcw in ((lng_s, lng_t), (lnb_s, lnb_t), (bng_s, bng_t),
                              (bnb_s, bnb_t), (bnm_s, bnm_t), (bnv_s, bnv_t)):
                p_.dma(e, out=dst[:, :], in_=srcw)
            p_.dma(e, out=wr_s[:, :].rearrange("p (a n) -> p a n", a=NDT),
                   in_=Wr.rearrange("(a p) n -> p a n", p=128))
            p_.dma(e, out=br_s[0:1, :], in_=br_row)
            p_.dma(e, out=wdn_s[:, :].rearrange("p (a n) -> p a n", a=8 * NDT),
                   in_=Wdn.rearrange("(a p) n -> p a n", p=128))
            p_.dma(e, out=bdn_s[:, :], in_=bdn)
            p_.dma(e, out=wup_s[:, :].rearrange("p (a n) -> p a n", a=8),
                   in_=Wup.rearrange("(a p) n -> p a n", p=64))
            p_.dma(e, out=bup_s[:, :], in_=bup)
        s_wld = p.step("gpsimd", ld_w, n_dma=12, tag="wld")

        def consts(e, p_):
            e.memset(c_onesf[:, :], 1.0)
            e.memset(c_onesrf[0:1, :], 1.0)
            e.memset(c_zer[:, :], 0.0)
            return e.memset(zsum[:, :], 0.0)
        s_const = p.step("vector", consts, tag="consts")

        def xcopy(e, p_):
            p_.dma(e, out=x4T, in_=xT)
        s_xcopy = p.step("sync", xcopy, n_dma=1, tag="xcopy")

        bca_m = P0[:, 0:CH]
        bca_r = P0[:, CH:2 * CH]

        s_zdma = [None] * NCHUNK
        prev = dict(statm=None, bca=None, stats_pe=None, xn=None, z=None)
        s_sqs = {}
        s_pes = {}

        def emit_stats(tagp, src_buf, extra_dep=None):
            for dt in range(NDT):
                def sq(e, p_, dt=dt, src_buf=src_buf):
                    return e.tensor_tensor(sqf[dt % 2][:, :], src_buf[:, dt * CH:(dt + 1) * CH],
                                           src_buf[:, dt * CH:(dt + 1) * CH], ALU.mult)
                s_sq = p.step("vector", sq, tag=f"{tagp}_sq{dt}",
                              deps=[extra_dep, s_pes.get((tagp, dt - 2)),
                                    prev["stats_pe"] if dt < 2 else None])
                s_sqs[(tagp, dt)] = s_sq

                def pe2(e, p_, dt=dt, src_buf=src_buf):
                    e.matmul(P4[0:1, :], c_onesf[:, :], src_buf[:, dt * CH:(dt + 1) * CH],
                             start=(dt == 0), stop=(dt == NDT - 1))
                    return e.matmul(P5[0:1, :], c_onesf[:, :], sqf[dt % 2][:, :],
                                    start=(dt == 0), stop=(dt == NDT - 1))
                s_pes[(tagp, dt)] = p.step("tensor", pe2, tag=f"{tagp}_pe{dt}",
                                           deps=[s_sq, s_const,
                                                 prev["statm"] if dt == 0 else None])
            s_stats = s_pes[(tagp, NDT - 1)]
            prev["stats_pe"] = s_stats

            def statm1(e, p_):
                e.tensor_scalar(mrow[0:1, :], P4[0:1, :], 1.0 / D, None, ALU.mult)
                e.tensor_scalar(rrow[0:1, 0:CH], P5[0:1, :], 1.0 / D, None, ALU.mult)
                e.tensor_tensor(rrow[0:1, CH:], mrow[0:1, :], mrow[0:1, :], ALU.mult)
                e.tensor_tensor(rrow[0:1, 0:CH], rrow[0:1, 0:CH], rrow[0:1, CH:], ALU.subtract)
                return e.tensor_scalar(rrow[0:1, 0:CH], rrow[0:1, 0:CH], 1e-5, None, ALU.add)
            s_sm = p.step("vector", statm1, deps=[s_stats, prev["bca"]], tag=f"{tagp}_sm")

            def sqrtf(e, p_):
                return e.activation(rrow[0:1, CH:], rrow[0:1, 0:CH], AF.Sqrt)
            s_sq2 = p.step("scalar", sqrtf, deps=[s_sm], tag=f"{tagp}_sqrt")

            def recipf(e, p_):
                return e.reciprocal(rrow[0:1, 0:CH], rrow[0:1, CH:])
            s_rc = p.step("vector", recipf, deps=[s_sq2], tag=f"{tagp}_rec")
            prev["statm"] = s_rc

            def bcaf(e, p_):
                e.matmul(bca_m, c_onesrf[0:1, :], mrow[0:1, :], start=True, stop=True)
                return e.matmul(bca_r, c_onesrf[0:1, :], rrow[0:1, 0:CH], start=True, stop=True)
            s_b = p.step("tensor", bcaf, deps=[s_rc, prev["xn"], prev["z"]], tag=f"{tagp}_bca")
            prev["bca"] = s_b
            return s_b

        for c in range(NCHUNK):
            def ldx(e, p_, cc=c):
                p_.dma(e, out=sxf[:, :].rearrange("p (a n) -> p a n", a=NDT),
                       in_=xT[:, cc * CH:(cc + 1) * CH].rearrange("(a p) n -> p a n", p=128))
            s_ldx = p.step("sync", ldx, n_dma=1, tag=f"F{c}_ldx",
                           deps=[s_sqs.get((f"F{c-1}s1", NDT - 1)), prev["stats_pe"],
                                 prev["xn"]])

            s_b1 = emit_stats(f"F{c}s1", sxf, extra_dep=s_ldx)

            def xn(e, p_):
                for dt in range(NDT):
                    e.tensor_tensor(tmpf[:, :], sxf[:, dt * CH:(dt + 1) * CH], bca_m, ALU.subtract)
                    ins = e.tensor_tensor(xnf[:, dt * CH:(dt + 1) * CH], tmpf[:, :], bca_r, ALU.mult)
                return ins
            s_xn = p.step("vector", xn, deps=[s_b1, prev["z"]], tag=f"F{c}_xn")
            prev["xn"] = s_xn

            s_b2 = emit_stats(f"F{c}s2", xnf, extra_dep=s_xn)

            def zf(e, p_, cc=c):
                def red(dt):
                    return e.tensor_reduce(
                        zsum_all[:, dt * NCHUNK + cc:dt * NCHUNK + cc + 1],
                        zbf[:, dt * CH:(dt + 1) * CH], AX.X, ALU.add)
                for dt in range(NDT):
                    e.tensor_tensor(tmpf[:, :], xnf[:, dt * CH:(dt + 1) * CH], bca_m, ALU.subtract)
                    e.tensor_tensor(tmpf[:, :], tmpf[:, :], bca_r, ALU.mult)
                    e.tensor_scalar(zbf[:, dt * CH:(dt + 1) * CH], tmpf[:, :],
                                    lng_s[:, dt:dt + 1], lnb_s[:, dt:dt + 1],
                                    ALU.mult, ALU.add)
                    if dt > 0:
                        red(dt - 1)
                return red(NDT - 1)
            s_z = p.step("vector", zf, deps=[s_b2, s_zdma[c - 1] if c else None, s_wld],
                         tag=f"F{c}_z")
            prev["z"] = s_z

            def zdma(e, p_, cc=c):
                p_.dma(e, out=z_h[cc * D:(cc + 1) * D, :].rearrange("(a p) n -> p a n", p=128),
                       in_=zbf[:, :].rearrange("p (a n) -> p a n", a=NDT))
            s_zdma[c] = p.step("gpsimd", zdma, n_dma=1, deps=[s_z], tag=f"F{c}_zdma")

        zsum2 = sb("zsum2", 128, NDT, F32)
        tmpq = sb("tmpq", 128, 3 * NDT, F32)
        padb = [sb(f"padb{i}", 128, 1024) for i in range(2)]

        def padv(e):
            return e.tensor_copy(padb[0][:, :], padb[1][:, :])

        def zbn1(e, p_):
            e.tensor_reduce(zsum[:, :], zsum_all[:, :].rearrange("p (a n) -> p a n", n=NCHUNK),
                            AX.X, ALU.add)
            e.tensor_scalar(tmpq[:, 0:NDT], bnv_s[:, :], 1e-5, None, ALU.add)
            return padv(e)
        s_z1 = p.step("vector", zbn1, deps=[prev["z"], prev["statm"]], tag="zbn1")

        def zbn2(e, p_):
            e.copy(zsum2[:, :], zsum[:, :])
            e.activation(tmpq[:, NDT:2 * NDT], tmpq[:, 0:NDT], AF.Ln)
            return e.activation(tmpq[:, 2 * NDT:3 * NDT], tmpq[:, NDT:2 * NDT],
                                AF.Exp, scale=-0.5)
        s_z2 = p.step("scalar", zbn2, deps=[s_z1], tag="zbn2")

        zbn2b = sb("zbn2b", 128, NDT, F32)

        def zbn5(e, p_):
            e.tensor_scalar(zbn[:, :], zsum2[:, :], 1.0 / T, None, ALU.mult)
            padv(e)
            e.tensor_tensor(zbn2b[:, :], zbn[:, :], bnm_s[:, :], ALU.subtract)
            padv(e)
            e.tensor_tensor(zbn[:, :], zbn2b[:, :], bng_s[:, :], ALU.mult)
            padv(e)
            e.tensor_tensor(zbn2b[:, :], zbn[:, :], tmpq[:, 2 * NDT:3 * NDT], ALU.mult)
            padv(e)
            e.tensor_tensor(zbn[:, :], zbn2b[:, :], bnb_s[:, :], ALU.add)
            return padv(e)
        s_z3 = p.step("vector", zbn5, deps=[s_z2], tag="zbn5")

        def l8_pe(e, p_):
            for dt in range(NDT):
                e.matmul(P0[0:1, 1024:1032], zbn[:, dt:dt + 1], wr_s[:, dt * 8:(dt + 1) * 8],
                         start=(dt == 0), stop=False)
            return e.matmul(P0[0:1, 1024:1032], c_onesf[0:1, 0:1], br_s[0:1, :],
                            start=False, stop=True)
        s_l8 = p.step("tensor", l8_pe, deps=[s_z3, prev["bca"], s_wld], tag="l8")

        def l8m(e, p_):
            e.tensor_copy(l8f[0:1, 0:8], P0[0:1, 1024:1032])
            e.tensor_reduce(l8f[0:1, 8:9], P0[0:1, 1024:1032], AX.X, ALU.max)
            return padv(e)
        s_l8m = p.step("vector", l8m, deps=[s_l8], tag="l8m")

        def mxcp(e, p_):
            return e.copy(l8f[0:1, 20:21], l8f[0:1, 8:9])
        s_mx = p.step("scalar", mxcp, deps=[s_l8m], tag="mxcp")

        def oh(e, p_):
            e.tensor_tensor(oh_row[0:1, :], l8f[0:1, 0:8],
                            l8f[0:1, 20:21].broadcast_to((1, 8)), ALU.is_ge)
            e.tensor_tensor(l8f[0:1, 9:17], l8f[0:1, 0:8],
                            l8f[0:1, 20:21].broadcast_to((1, 8)), ALU.subtract)
            return padv(e)
        s_oh = p.step("vector", oh, deps=[s_mx], tag="oh")

        def ohdma1(e, p_):
            p_.dma(e, out=oh_d, in_=oh_row[0:1, :])
        s_ohd1 = p.step("gpsimd", ohdma1, n_dma=1, deps=[s_oh], tag="ohd1")

        def ohdma2(e, p_):
            p_.dma(e, out=ohT[:, :], in_=oh_d)
        s_ohd2 = p.step("gpsimd", ohdma2, n_dma=1, deps=[s_ohd1], tag="ohd2")

        def gexp(e, p_):
            return e.activation(l8f[0:1, 9:17], l8f[0:1, 9:17], AF.Exp)
        s_ge = p.step("scalar", gexp, deps=[s_oh], tag="gexp")

        def gsum(e, p_):
            e.tensor_reduce(l8f[0:1, 17:18], l8f[0:1, 9:17], AX.X, ALU.add)
            return padv(e)
        s_gs = p.step("vector", gsum, deps=[s_ge], tag="gsum")

        def grec(e, p_):
            e.activation(l8f[0:1, 18:19], l8f[0:1, 17:18], AF.Ln)
            return e.activation(l8f[0:1, 19:20], l8f[0:1, 18:19], AF.Exp, scale=-1.0)
        s_gr = p.step("scalar", grec, deps=[s_gs], tag="grec")

        def sel_pe(e, p_):
            e.matmul(P0[:, 1024:1032], c_onesrf[0:1, :], oh_row[0:1, :], start=True, stop=True)
            e.matmul(P0[0:64, 1537:1538], bdn_s[:, :], ohT[:, :], start=True, stop=True)
            for dt in range(NDT):
                e.matmul(P2[:, 8 + dt:9 + dt], bup_s[:, dt * 128:(dt + 1) * 128], ohT[:, :],
                         start=(dt == 0), stop=False)
            return e.matmul(P2[:, 16:17], c_onesrf[0:1, :], l8f[0:1, 19:20],
                            start=False, stop=True)
        s_sel = p.step("tensor", sel_pe, deps=[s_oh, s_gr, s_ohd2], tag="sel_pe")

        selc_s = sb("selc_s", 128, 10, F32)

        def sel_ev(e, p_):
            e.copy(selc_s[:, 0:9], P2[:, 8:17])
            return e.copy(selc_s[0:64, 9:10], P0[0:64, 1537:1538])
        s_selev = p.step("scalar", sel_ev, deps=[s_sel], tag="sel_ev")

        if dbg:
            def dmpr(e, p_):
                e.tensor_copy(sqf[0][0:1, 0:21], l8f[0:1, 0:21])
                e.tensor_copy(sqf[0][:, 21:22], zsum2[:, 0:1])
                e.tensor_copy(sqf[0][:, 22:23], tmpq[:, 0:1])
                e.tensor_copy(sqf[0][:, 23:24], tmpq[:, 2 * NDT:2 * NDT + 1])
                e.tensor_copy(sqf[0][0:1, 24:32], oh_row[0:1, :])
                e.tensor_copy(sqf[0][:, 30:31], zbn[:, 0:1])
                return e.tensor_copy(sqf[0][:, 31:32], zsum[:, 0:1])
            s_dmp = p.step("vector", dmpr, deps=[s_gr, s_oh, s_z3], tag="dmpr")
            def dmpd(e, p_):
                p_.dma(e, out=rdbg[:, :], in_=sqf[0][:, 0:32])
            s_dmpd = p.step("gpsimd", dmpd, n_dma=1, deps=[s_dmp], tag="dmpd")

        s_ws = None
        for ei in range(8):
            def wsel_acc(e, p_, ei=ei):
                oc = P0[:, 1024 + ei:1025 + ei]
                if ei == 0:
                    e.tensor_scalar(wsel[0][:, :], wdn_s[:, ei * 512:(ei + 1) * 512],
                                    oc, None, ALU.mult)
                    return e.tensor_scalar(wusel[0][:, :], wup_s[:, ei * D:(ei + 1) * D],
                                           oc[0:64, :], None, ALU.mult)
                src, dst = wsel[(ei - 1) % 2], wsel[ei % 2]
                usrc, udst = wusel[(ei - 1) % 2], wusel[ei % 2]
                e.scalar_tensor_tensor(dst[:, :], wdn_s[:, ei * 512:(ei + 1) * 512],
                                       oc, src[:, :], ALU.mult, ALU.add)
                return e.scalar_tensor_tensor(udst[:, :], wup_s[:, ei * D:(ei + 1) * D],
                                              oc[0:64, :], usrc[:, :], ALU.mult, ALU.add)
            s_ws = p.step("vector", wsel_acc, deps=[s_sel if ei == 0 else s_ws],
                          tag=f"wsel{ei}")
        wfin = wsel[7 % 2]
        ufin = wusel[7 % 2]

        s_cout = None
        s_hev = None
        s_yev = None
        for c in range(NCHUNK):
            def ldz(e, p_, cc=c):
                p_.dma(e, out=zbf[:, :].rearrange("p (a n) -> p a n", a=NDT),
                       in_=z_h[cc * D:(cc + 1) * D, :].rearrange("(a p) n -> p a n", p=128))
            s_ldz = p.step("sync", ldz, n_dma=1, tag=f"G{c}_ldz",
                           deps=[s_zdma[c], s_zdma[NCHUNK - 1] if c == 0 else s_hev])

            def h_pe(e, p_):
                for dt in range(NDT):
                    ins = e.matmul(P3[0:64, :], wfin[:, dt * 64:(dt + 1) * 64],
                                   zbf[:, dt * CH:(dt + 1) * CH],
                                   start=(dt == 0), stop=(dt == NDT - 1))
                return ins
            s_hpe = p.step("tensor", h_pe, deps=[s_ldz, s_ws, s_hev], tag=f"G{c}_hpe")

            def h_ev(e, p_):
                return e.scalar_tensor_tensor(hbf[:, :], P3[0:64, :], selc_s[0:64, 9:10],
                                              c_zer[:, :], ALU.add, ALU.max)
            s_hev = p.step("vector", h_ev, deps=[s_hpe, s_yev, s_selev], tag=f"G{c}_hev")

            for dt in range(NDT):
                def y_pe(e, p_, dt=dt):
                    yo = P4[:, :] if dt % 2 == 0 else P5[:, :]
                    return e.matmul(yo, ufin[:, dt * 128:(dt + 1) * 128], hbf[:, :],
                                    start=True, stop=True)
                dep_war = s_yev if dt >= 2 else (prev["statm"] if c == 0 else None)
                s_ype = p.step("tensor", y_pe, deps=[s_hev, dep_war], tag=f"G{c}_ype{dt}")

                def y_ev(e, p_, dt=dt):
                    yo = P4[:, :] if dt % 2 == 0 else P5[:, :]
                    return e.tensor_scalar(resf[:, :], yo, selc_s[:, dt:dt + 1],
                                           selc_s[:, 8:9], ALU.add, ALU.mult)
                s_yev = p.step("vector", y_ev, deps=[s_ype, s_cout, s_selev], tag=f"G{c}_yev{dt}")

                def c_out(e, p_, dt=dt, cc=c):
                    p_.dma(e, out=x4T[dt * 128:(dt + 1) * 128, cc * CH:(cc + 1) * CH],
                           in_=resf[:, :], accum_op=ALU.add)
                s_cout = p.step("gpsimd", c_out, n_dma=1, deps=[s_yev, s_xcopy],
                                tag=f"G{c}_out{dt}")

        p.wait_only("gpsimd", [s_cout, s_xcopy], tag="final")
        if fctx is not None:
            p.fence()
            return nc
        with nc.Block() as block:
            p.emit(block)
    return nc


def host_l1_ffn_inputs(ln_g, ln_b, bn_g, bn_b, bn_mean, bn_var, Wr, br,
                       W_down, b_down, W_up, b_up, xs):
    def col(v):
        return np.ascontiguousarray(np.asarray(v, np.float32).reshape(NDT, 128).T)
    maps = []
    for c in range(NC):
        maps.append({
            "xT": np.ascontiguousarray(np.asarray(xs[c], np.float32).T),
            "lng_t": col(ln_g), "lnb_t": col(ln_b), "bng_t": col(bn_g),
            "bnb_t": col(bn_b), "bnm_t": col(bn_mean), "bnv_t": col(bn_var),
            "Wr": np.asarray(Wr, np.float32),
            "br_row": np.asarray(br, np.float32).reshape(1, 8),
            "Wdn": np.asarray(W_down, np.float32).reshape(8 * D, 64),
            "bdn": np.asarray(b_down, np.float32),
            "Wup": np.asarray(W_up, np.float32).reshape(8 * 64, D),
            "bup": np.asarray(b_up, np.float32),
        })
    return maps
NEXP = 8
CAPE = 1024
NGRP = 32
INVB = float(1 << 23)


def build_l0_moe(dbg=False, upto="all", fctx=None):
    nc = fctx["nc"] if fctx else bass.Bass("TRN2", target_bir_lowering=False, debug=False, detect_race_conditions=False)
    xT = fctx["xin"] if fctx else nc.dram_tensor("xT", [D, T], F32, kind="ExternalInput").ap()
    rmsw_t = nc.dram_tensor("rmsw_t", [128, NDT], F32, kind="ExternalInput").ap()
    gwt = nc.dram_tensor("gwt", [D, 64], F32, kind="ExternalInput").ap()
    w13s = nc.dram_tensor("w13s", [NEXP * D, D], F32, kind="ExternalInput").ap()
    w2s = nc.dram_tensor("w2s", [NEXP * 512, D], F32, kind="ExternalInput").ap()
    swg = nc.dram_tensor("swg", [D, 512], F32, kind="ExternalInput").ap()
    swu = nc.dram_tensor("swu", [D, 512], F32, kind="ExternalInput").ap()
    swd = nc.dram_tensor("swd", [512, D], F32, kind="ExternalInput").ap()
    identb = nc.dram_tensor("identb", [128, 128], BF16, kind="ExternalInput").ap()
    iotabig = nc.dram_tensor("iotabig", [128, 512], F32, kind="ExternalInput").ap()
    tokv = nc.dram_tensor("tokv", [128, 256], F32, kind="ExternalInput").ap()
    idx8q = nc.dram_tensor("idx8q", [128, 1], I32, kind="ExternalInput").ap()
    idxrt = nc.dram_tensor("idxrt", [128, 32], I32, kind="ExternalInput").ap()
    core8 = nc.dram_tensor("core8", [128, 1], F32, kind="ExternalInput").ap()
    identf = nc.dram_tensor("identf", [128, 128], F32, kind="ExternalInput").ap()
    x2T = fctx["xout"] if fctx else nc.dram_tensor("x2T", [D, T], F32, kind="ExternalOutput").ap()
    if dbg:
        rdbg = nc.dram_tensor("rdbg", [128, 96], F32, kind="ExternalOutput").ap()
    h_h = nc.dram_tensor("h_h", [NCHUNK * D, CH], F32).ap()
    sh_h = nc.dram_tensor("sh_h", [NCHUNK * D, CH], BF16).ap()
    xnn_b = nc.dram_tensor("xnn_b", [T, D], BF16)
    xn_full = nc.dram_tensor("xn_full", [NC * T, D], BF16)
    lg_b = nc.dram_tensor("lg_b", [T, 64], F32)
    lg_full = nc.dram_tensor("lg_full", [NC * T, 64], F32)
    rt2 = nc.dram_tensor("rt2", [NC * T, 8], F32,
                         kind="ExternalOutput" if dbg else "Internal").ap()
    mo_h = nc.dram_tensor("mo_h", [T, D], BF16).ap()
    table = nc.dram_tensor("table", [NEXP * CAPE, 1], F32).ap()
    yb_b = nc.dram_tensor("yb_b", [NEXP * CAPE, D], BF16)
    yb_fullA = nc.dram_tensor("yb_fullA", [NC * NEXP // 2 * CAPE, D], BF16)
    yb_fullB = nc.dram_tensor("yb_fullB", [NC * NEXP // 2 * CAPE, D], BF16)
    if dbg:
        ydbgA = nc.dram_tensor("ydbgA", [NC * NEXP // 2 * CAPE, D], BF16, kind="ExternalOutput").ap()
        ydbgB = nc.dram_tensor("ydbgB", [NC * NEXP // 2 * CAPE, D], BF16, kind="ExternalOutput").ap()

    with contextlib.ExitStack() as ctx:
        pref = fctx["tag"] if fctx else ""
        sb = lambda name, p_, f, dt=BF16: ctx.enter_context(nc.sbuf_tensor(pref + name, [p_, f], dt))
        c_ones = sb("c_ones", 128, 1)
        c_onesrf = sb("c_onesrf", 1, 128, F32)
        c_identb = sb("c_identb", 128, 128)
        c_identf = sb("c_identf", 128, 128, F32)
        c_iotab = sb("c_iotab", 128, 512, F32)
        c_tokv = sb("c_tokv", 128, 256, F32)
        c_idx8q = sb("c_idx8q", 128, 1, I32)
        c_idxrt = sb("c_idxrt", 128, 32, I32)
        c_core8 = sb("c_core8", 128, 1, F32)
        c_zer64 = sb("c_zer64", 64, 1024)
        rmsw_s = sb("rmsw_s", 128, NDT, F32)
        gwt_s = sb("gwt_s", 128, NDT * 64, F32)
        swg_s = [sb(f"swg{i}", 128, 512) for i in range(NDT)]
        swu_s = [sb(f"swu{i}", 128, 512) for i in range(NDT)]
        swd_s = [sb(f"swd{i}", 128, D) for i in range(4)]
        padb = [sb(f"padb{i}", 128, 1024) for i in range(2)]
        dum = sb("dum", 128, 16)
        sxb = sb("sxb", 128, NDT * CH)
        sqp = [sb(f"sqp{i}", 128, CH) for i in range(2)]
        hF = sb("hF", 128, NDT * CH, F32)
        xnp = [sb(f"xnp{i}", 128, CH, F32) for i in range(2)]
        xnb = sb("xnb", 128, NDT * CH)
        tmpf = sb("tmpf", 128, CH, F32)
        mrow = sb("mrow", 1, CH, F32)
        rrow = sb("rrow", 1, 2 * CH, F32)
        lgev = sb("lgev", 128, 256, F32)
        gsb = sb("gsb", 128, 4 * CH)
        hub = sb("hub", 128, 4 * CH)
        shp = [sb(f"shp{i}", 128, CH) for i in range(2)]
        xtn = sb("xtn", 128, D)
        lgb2 = [sb(f"lgb{i}", 128, 512, F32) for i in range(2)]
        lgb = lgb2[0]
        eq1 = sb("eq1", 128, 512, F32)
        eq2 = sb("eq2", 128, 512, F32)
        uw = sb("uw", 128, 512, F32)
        Ab = sb("Ab", 128, 512)
        AT = sb("AT", 64, 1024)
        CT = sb("CT", 64, 1024, F32)
        Cx = sb("Cx", 128, 512, F32)
        carry = sb("carry", 64, 2, F32)
        m1B = sb("m1B", 128, 256, F32)
        m2B = sb("m2B", 128, 256, F32)
        e1B = sb("e1B", 128, 256, F32)
        e2B = sb("e2B", 128, 256, F32)
        p1B = sb("p1B", 128, 256, F32)
        p2B = sb("p2B", 128, 256, F32)
        wA = sb("wA", 128, 512, F32)
        gA = sb("gA", 128, 512, F32)
        gB = sb("gB", 128, 512, F32)
        pxi = sb("pxi", 128, 512, I32)
        pxi2 = sb("pxi2", 128, 512, I32)
        rtpack = sb("rtpack", 128, 2048, F32)
        ix1 = sb("ix1", 128, 512, I32)
        ix2 = sb("ix2", 128, 512, I32)
        zc = sb("zc", 128, 512, F32)
        zrow = sb("zrow", 128, 16, F32)
        tblf = sb("tblf", 128, 64, F32)
        tbli = sb("tbli", 128, 64, I32)
        xbn = sb("xbn", 128, D)
        xbT = sb("xbT", 128, NDT * D)
        w13b = sb("w13b", 128, NDT * D)
        w2b = xnb
        hTb = sb("hTb", 128, 4 * D)
        ybT = sb("ybT", 128, NDT * D)
        ybn = sb("ybn", 128, D)
        r1b = sxb[:, CH * 2:CH * 4]
        r2b = sxb[:, CH * 4:CH * 6]
        rtc = sb("rtc", 128, 32, F32)
        rti = sb("rti", 128, 16, I32)
        mot4 = xnb
        mdt = sb("mdt", 128, 2 * CH)
        x1c = hF[:, 0:2 * CH]
        hc = hF[:, 2 * CH:4 * CH]
        shc = sxb[:, 0:2 * CH]

        P2 = ctx.enter_context(nc.psum_tensor(pref + "P2", [128, 512], F32))
        P3 = ctx.enter_context(nc.psum_tensor(pref + "P3", [128, 512], F32))
        P4 = ctx.enter_context(nc.psum_tensor(pref + "P4", [128, 512], F32))
        P5 = ctx.enter_context(nc.psum_tensor(pref + "P5", [128, 512], F32))
        PT0 = ctx.enter_context(nc.psum_tensor(pref + "PT0", [128, 1024], BF16))
        PT1 = ctx.enter_context(nc.psum_tensor(pref + "PT1", [128, 1024], BF16))
        _p0cm = nc.psum_tensor(pref + "P0", [128, 1024], F32)
        P0 = _p0cm.__enter__()

        cc1 = ctx.enter_context(nc.semaphore("cc1"))
        cc2 = ctx.enter_context(nc.semaphore("cc2"))
        cc3 = ctx.enter_context(nc.semaphore("cc3"))
        cc4 = ctx.enter_context(nc.semaphore("cc4"))

        p = fctx["p"] if fctx else P(nc, ctx)
        regs = {}

        def padv(e):
            return e.tensor_copy(padb[0][:, :], padb[1][:, :])

        def ld_w(e, p_):
            p_.dma(e, out=rmsw_s[:, :], in_=rmsw_t)
            p_.dma(e, out=c_identb[:, :], in_=identb)
            p_.dma(e, out=c_identf[:, :], in_=identf)
            p_.dma(e, out=c_iotab[:, :], in_=iotabig)
            p_.dma(e, out=c_tokv[:, :], in_=tokv)
            p_.dma(e, out=c_idx8q[:, :], in_=idx8q)
            p_.dma(e, out=c_idxrt[:, :], in_=idxrt)
            p_.dma(e, out=c_core8[:, :], in_=core8)
            p_.dma(e, out=gwt_s[:, :].rearrange("p (a n) -> p a n", a=NDT),
                   in_=gwt.rearrange("(a p) n -> p a n", p=128))
            for i in range(NDT):
                p_.dma(e, out=swg_s[i][:, :], in_=swg[i * 128:(i + 1) * 128, :])
                p_.dma(e, out=swu_s[i][:, :], in_=swu[i * 128:(i + 1) * 128, :])
            for i in range(4):
                p_.dma(e, out=swd_s[i][:, :], in_=swd[i * 128:(i + 1) * 128, :])
        s_wld = p.step("gpsimd", ld_w, n_dma=9 + 2 * NDT + 4, tag="wld")

        def mkregs(e, p_):
            regs["r32k"] = e.to_reg(NC * T - 1)
            regs["r64k"] = e.to_reg(64 * CAPE - 1)
            regs["r8k"] = e.to_reg(NEXP * CAPE - 1)
            return e.memset(dum[:, :], 0.0)
        s_regs = p.step("gpsimd", mkregs, tag="regs")

        def consts(e, p_):
            e.memset(c_ones[:, :], 1.0)
            e.memset(c_onesrf[0:1, :], 1.0)
            e.memset(c_zer64[:, :], 0.0)
            e.memset(carry[:, :], 0.0)
            e.memset(padb[1][:, :], 0.0)
            return e.memset(zc[:, :], 0.0)
        s_const = p.step("vector", consts, tag="consts")

        def tblzero(e, p_):
            p_.dma(e, out=table.rearrange("(q a) b -> q a b", q=128).rearrange("q a b -> q (a b)"),
                   in_=zc[:, 0:NEXP * CAPE // 128])
        s_tblz = p.step("gpsimd", tblzero, n_dma=1, deps=[s_const], tag="tblz")

        bca_m = P0[:, 0:CH]
        bca_r = P0[:, CH:2 * CH]
        prev = dict(statm=None, stats_pe=None)
        s_sqs, s_pes, s_hdma, s_shdma = {}, {}, {}, {}
        s_xnl = [None] * NCHUNK
        s_lgdma = [None] * NCHUNK
        s_hlast = [None] * NCHUNK
        s_lge_prev = None
        s_sh_prev = None
        s_ptchain = None
        s_xtnd_prev = None
        s_bca_prev = None
        s_lgdma_last = None
        s_xtnd_all = []

        for c in range(NCHUNK):
            def ldxb(e, p_, cc=c):
                p_.dma(e, out=sxb[:, :].rearrange("p (a n) -> p a n", a=NDT),
                       in_=xT[:, cc * CH:(cc + 1) * CH].rearrange("(a p) n -> p a n", p=128))
            s_ldxb = p.step("gpsimd", ldxb, n_dma=1, tag=f"M1{c}_ldxb",
                            deps=[s_hlast[c - 1] if c else None, prev["stats_pe"]])

            for dt in range(NDT):
                def sq(e, p_, dt=dt):
                    return e.tensor_tensor(sqp[dt % 2][:, :], sxb[:, dt * CH:(dt + 1) * CH],
                                           sxb[:, dt * CH:(dt + 1) * CH], ALU.mult)
                s_sqs[(c, dt)] = p.step("vector", sq, tag=f"M1{c}_sq{dt}",
                                        deps=[s_ldxb, s_pes.get((c, dt - 2)),
                                              prev["stats_pe"] if dt < 2 else None])

                def pe2(e, p_, dt=dt):
                    e.matmul(P4[0:1, :], c_ones[:, :], sxb[:, dt * CH:(dt + 1) * CH],
                             start=(dt == 0), stop=(dt == NDT - 1))
                    return e.matmul(P5[0:1, :], c_ones[:, :], sqp[dt % 2][:, :],
                                    start=(dt == 0), stop=(dt == NDT - 1))
                s_pes[(c, dt)] = p.step("tensor", pe2, tag=f"M1{c}_pe{dt}",
                                        deps=[s_sqs[(c, dt)], s_const,
                                              prev["statm"] if dt == 0 else None])
            s_stats_pe = s_pes[(c, NDT - 1)]
            prev["stats_pe"] = s_stats_pe

            def statm1(e, p_):
                e.tensor_scalar(mrow[0:1, :], P4[0:1, :], 1.0 / D, None, ALU.mult)
                e.tensor_scalar(rrow[0:1, 0:CH], P5[0:1, :], 1.0 / D, None, ALU.mult)
                e.tensor_tensor(rrow[0:1, CH:], mrow[0:1, :], mrow[0:1, :], ALU.mult)
                e.tensor_tensor(rrow[0:1, 0:CH], rrow[0:1, 0:CH], rrow[0:1, CH:], ALU.subtract)
                return e.tensor_scalar(rrow[0:1, 0:CH], rrow[0:1, 0:CH], 1e-5, None, ALU.add)
            s_sm1 = p.step("vector", statm1, deps=[s_stats_pe, s_bca_prev], tag=f"M1{c}_sm1")

            def sqrtf(e, p_):
                return e.activation(rrow[0:1, CH:], rrow[0:1, 0:CH], AF.Sqrt)
            s_sq2 = p.step("scalar", sqrtf, deps=[s_sm1], tag=f"M1{c}_sqrt")

            def recipf(e, p_):
                return e.reciprocal(rrow[0:1, 0:CH], rrow[0:1, CH:])
            s_rc = p.step("vector", recipf, deps=[s_sq2], tag=f"M1{c}_rec")
            prev["statm"] = s_rc

            def bcaf(e, p_):
                e.matmul(bca_m, c_onesrf[0:1, :], mrow[0:1, :], start=True, stop=True)
                return e.matmul(bca_r, c_onesrf[0:1, :], rrow[0:1, 0:CH], start=True, stop=True)
            s_bca = p.step("tensor", bcaf, deps=[s_rc, s_sh_prev], tag=f"M1{c}_bca")
            s_bca_prev = s_bca

            s_h = None
            for dt in range(NDT):
                def hstep(e, p_, dt=dt):
                    e.tensor_tensor(tmpf[:, :], sxb[:, dt * CH:(dt + 1) * CH], bca_m, ALU.subtract)
                    e.tensor_tensor(hF[:, dt * CH:(dt + 1) * CH], tmpf[:, :], bca_r, ALU.mult)
                    return e.tensor_tensor(sqp[dt % 2][:, :], hF[:, dt * CH:(dt + 1) * CH],
                                           hF[:, dt * CH:(dt + 1) * CH], ALU.mult)
                s_h = p.step("vector", hstep, tag=f"M1{c}_h{dt}",
                             deps=[s_bca, s_pes.get((c, "r", dt - 2)),
                                   s_hdma.get((c - 1, dt))])

                def hdma(e, p_, dt=dt, cc=c):
                    p_.dma(e, out=h_h[cc * D + dt * 128: cc * D + (dt + 1) * 128, :],
                           in_=hF[:, dt * CH:(dt + 1) * CH])
                s_hdma[(c, dt)] = p.step("gpsimd", hdma, n_dma=1, deps=[s_h], tag=f"M1{c}_hd{dt}")

                def rpe(e, p_, dt=dt):
                    return e.matmul(P5[0:1, :], c_ones[:, :], sqp[dt % 2][:, :],
                                    start=(dt == 0), stop=(dt == NDT - 1))
                s_pes[(c, "r", dt)] = p.step("tensor", rpe, deps=[s_h], tag=f"M1{c}_rpe{dt}")
            s_hlast[c] = s_h

            def rmsrow(e, p_):
                return e.tensor_scalar(rrow[0:1, 0:CH], P5[0:1, :], 1.0 / D, 1e-6,
                                       ALU.mult, ALU.add)
            s_rr = p.step("vector", rmsrow, deps=[s_pes[(c, "r", NDT - 1)]], tag=f"M1{c}_rr")
            prev["statm"] = s_rr

            def rmsact(e, p_):
                e.activation(rrow[0:1, CH:], rrow[0:1, 0:CH], AF.Ln)
                return e.activation(rrow[0:1, 0:CH], rrow[0:1, CH:], AF.Exp, scale=-0.5)
            s_ra = p.step("scalar", rmsact, deps=[s_rr], tag=f"M1{c}_ra")

            def r2bca(e, p_):
                return e.matmul(P4[:, :], c_onesrf[0:1, :], rrow[0:1, 0:CH],
                                start=True, stop=True)
            s_r2b = p.step("tensor", r2bca, deps=[s_ra, s_sm1], tag=f"M1{c}_r2b")

            s_lgpe = {}
            s_xc = None
            for dt in range(NDT):
                def xns(e, p_, dt=dt):
                    e.tensor_tensor(xnp[dt % 2][:, :], hF[:, dt * CH:(dt + 1) * CH],
                                    P4[:, :], ALU.mult)
                    return e.tensor_scalar(xnp[dt % 2][:, :], xnp[dt % 2][:, :],
                                           rmsw_s[:, dt:dt + 1], None, ALU.mult)
                s_x = p.step("vector", xns, deps=[s_r2b, s_lgpe.get(dt - 2)], tag=f"M1{c}_xn{dt}")

                def xnc(e, p_, dt=dt):
                    return e.tensor_copy(xnb[:, dt * CH:(dt + 1) * CH], xnp[dt % 2][:, :])
                s_xc = p.step("vector", xnc, deps=[s_x], tag=f"M1{c}_xnc{dt}")

                def lgpe(e, p_, dt=dt):
                    for sub in range(4):
                        ins = e.matmul(P2[:, sub * 64:(sub + 1) * 64],
                                       xnp[dt % 2][:, sub * 128:(sub + 1) * 128],
                                       gwt_s[:, dt * 64:(dt + 1) * 64],
                                       start=(dt == 0 and sub == 0),
                                       stop=(dt == NDT - 1 and sub == 3))
                    return ins
                s_lgpe[dt] = p.step("tensor", lgpe, tag=f"M1{c}_lg{dt}",
                                    deps=[s_x, s_wld, s_lge_prev if dt < 2 else None])
            s_xnl[c] = s_xc

            def lgevf(e, p_):
                return e.tensor_copy(lgev[:, :], P2[:, 0:256])
            s_lge = p.step("vector", lgevf, deps=[s_lgpe[NDT - 1], s_lgdma_last], tag=f"M1{c}_lgev")
            s_lge_prev = s_lge

            def lgdma(e, p_, cc=c):
                p_.dma(e, out=lg_b.ap()[cc * CH:(cc + 1) * CH, :].rearrange("(s q) e -> q s e", q=128),
                       in_=lgev[:, :].rearrange("q (s e) -> q s e", e=64))
            s_lgdma_last = p.step("gpsimd", lgdma, n_dma=1, deps=[s_lge], tag=f"M1{c}_lgd")
            s_lgdma[c] = s_lgdma_last

            s_uev = {}
            for m in range(4):
                def gu_pe(e, p_, m=m):
                    for kt in range(NDT):
                        e.matmul(P0[:, 0:CH], swg_s[kt][:, m * 128:(m + 1) * 128],
                                 xnb[:, kt * CH:(kt + 1) * CH],
                                 start=(kt == 0), stop=(kt == NDT - 1))
                    for kt in range(NDT):
                        ins = e.matmul(P0[:, CH:2 * CH], swu_s[kt][:, m * 128:(m + 1) * 128],
                                       xnb[:, kt * CH:(kt + 1) * CH],
                                       start=(kt == 0), stop=(kt == NDT - 1))
                    return ins
                s_gu = p.step("tensor", gu_pe, tag=f"M1{c}_gu{m}",
                              deps=[s_xnl[c], s_uev.get(m - 1)])

                def gsilu(e, p_, m=m):
                    return e.activation(gsb[:, m * CH:(m + 1) * CH], P0[:, 0:CH], AF.Silu)
                s_gs = p.step("scalar", gsilu, deps=[s_gu], tag=f"M1{c}_gs{m}")

                def uevf(e, p_, m=m):
                    return e.tensor_tensor(hub[:, m * CH:(m + 1) * CH],
                                           gsb[:, m * CH:(m + 1) * CH],
                                           P0[:, CH:2 * CH], ALU.mult)
                s_uev[m] = p.step("vector", uevf, deps=[s_gu, s_gs], tag=f"M1{c}_ue{m}")

            s_she_hist = {}
            for mo in range(NDT):
                def sh_pe(e, p_, mo=mo):
                    PS = P3 if mo % 2 == 0 else P4
                    for kt in range(4):
                        ins = e.matmul(PS[:, :], swd_s[kt][:, mo * 128:(mo + 1) * 128],
                                       hub[:, kt * CH:(kt + 1) * CH],
                                       start=(kt == 0), stop=(kt == 3))
                    return ins
                s_shp = p.step("tensor", sh_pe, tag=f"M1{c}_shp{mo}",
                               deps=[s_uev[3], s_she_hist.get(mo - 2),
                                     s_sh_prev if mo == 0 else None])

                def sh_ev(e, p_, mo=mo):
                    PS = P3 if mo % 2 == 0 else P4
                    return e.tensor_copy(shp[mo % 2][:, :], PS[:, :])
                s_she = p.step("vector", sh_ev, deps=[s_shp, s_shdma.get((c, mo - 2))],
                               tag=f"M1{c}_she{mo}")
                s_she_hist[mo] = s_she
                s_sh_prev = s_she

                def sh_dma(e, p_, mo=mo, cc=c):
                    p_.dma(e, out=sh_h[cc * D + mo * 128: cc * D + (mo + 1) * 128, :],
                           in_=shp[mo % 2][:, :])
                s_shdma[(c, mo)] = p.step("gpsimd", sh_dma, n_dma=1, deps=[s_she],
                                          tag=f"M1{c}_shd{mo}")

            for sub in range(4):
                s_evl = None
                for dt2 in range(0, NDT, 2):
                    def trx(e, p_, sub=sub, dt2=dt2):
                        e.transpose(PT0[:, 0:128],
                                    xnb[:, dt2 * CH + sub * 128: dt2 * CH + sub * 128 + 128],
                                    c_identb[:, :])
                        return e.transpose(PT1[:, 0:128],
                                           xnb[:, (dt2 + 1) * CH + sub * 128: (dt2 + 1) * CH + sub * 128 + 128],
                                           c_identb[:, :])
                    s_tr = p.step("tensor", trx, tag=f"M2{c}_t{sub}_{dt2}",
                                  deps=[s_xnl[c], s_ptchain])

                    def trev(e, p_, dt2=dt2):
                        e.tensor_copy(xtn[:, dt2 * 128:(dt2 + 1) * 128], PT0[:, 0:128])
                        return e.tensor_copy(xtn[:, (dt2 + 1) * 128:(dt2 + 2) * 128], PT1[:, 0:128])
                    s_evl = p.step("vector", trev, tag=f"M2{c}_e{sub}_{dt2}",
                                   deps=[s_tr, s_xtnd_prev if dt2 == 0 else None])
                    s_ptchain = s_evl

                def xtdma(e, p_, sub=sub, cc=c):
                    p_.dma(e, out=xnn_b.ap()[cc * CH + sub * 128: cc * CH + (sub + 1) * 128, :],
                           in_=xtn[:, :])
                s_xtnd_prev = p.step("gpsimd", xtdma, n_dma=1, deps=[s_evl], tag=f"M2{c}_d{sub}")
                s_xtnd_all.append(s_xtnd_prev)

        _p0cm.__exit__(None, None, None)
        PT2 = ctx.enter_context(nc.psum_tensor(pref + "PT2", [128, 1024], BF16))
        PT3 = ctx.enter_context(nc.psum_tensor(pref + "PT3", [128, 1024], BF16))
        PTS = [PT0, PT1, PT2, PT3]
        ptq = {"k": 0, "ev": [None, None]}

        def pt_next():
            b0 = PTS[ptq["k"] % 4]
            b1 = PTS[(ptq["k"] + 1) % 4]
            dep = ptq["ev"][0]
            ptq["k"] += 2
            return b0, b1, dep

        def pt_done(ev):
            ptq["ev"] = [ptq["ev"][1], ev]

        def agi(e, p_):
            e.collective_compute("AllGather", ALU.bypass,
                                 replica_groups=[list(range(NC))],
                                 ins=[lg_b.ap()], outs=[lg_full.ap()]).then_inc(cc2)
            e.collective_compute("AllGather", ALU.bypass,
                                 replica_groups=[list(range(NC))],
                                 ins=[xnn_b.ap()], outs=[xn_full.ap()]).then_inc(cc1)
            return e.memset(dum[:, :], 1.0)
        s_agi = p.step("gpsimd", agi, deps=[s_xtnd_all[-1], s_lgdma[NCHUNK - 1]], tag="agi")
        s_ag_xn = (cc1, 1)
        s_ag_lg = (cc2, 1)

        def _early_exit():
            def dummy_out(e, p_):
                for cc in range(NCHUNK):
                    for dt in range(NDT):
                        p_.dma(e, out=x2T[dt * 128:(dt + 1) * 128, cc * CH:(cc + 1) * CH],
                               in_=zc[:, :])
            s_do = p.step("sync", dummy_out, n_dma=64, deps=[s_const], tag="dummy_out")
            p.wait_only("sync", [s_do, (p.dsem["gpsimd"], p.dma_val["gpsimd"])], tag="fin")
            with nc.Block() as block:
                p.emit(block)

        if upto == "M2":
            def agwait(e, p_):
                e.wait_ge(cc1, 1)
                return e.memset(dum[:, :], 3.0)
            p.step("gpsimd", agwait, deps=[s_ag_lg], tag="agw")
            _early_exit()
            return nc

        s_pos_prev = None
        s_eq1_prev = None
        s_eq2_prev = None
        s_ccp_prev = None
        s_pos_hist = {}
        s_tA_hist = {}
        s_evl2_prev = None
        for g in range(NGRP):
            lgb = lgb2[g % 2]

            def lgld(e, p_, g=g, lgb=lgb):
                p_.dma(e, out=lgb[:, :].rearrange("q (a n) -> q a n", n=64),
                       in_=lg_full.ap()[g * 1024:(g + 1) * 1024, :].rearrange("(a q) n -> q a n", q=128))
            s_lgld = p.step("sync", lgld, n_dma=1, tag=f"M3{g}_ld",
                            deps=[s_ag_lg, s_pos_hist.get(g - 1)])

            def r_m1(e, p_, g=g, lgb=lgb):
                e.tensor_reduce(m1B[:, g * 8:(g + 1) * 8],
                                lgb[:, :].rearrange("q (a n) -> q a n", n=64), AX.X, ALU.max)
                return padv(e)
            s_m1 = p.step("vector", r_m1, deps=[s_lgld], tag=f"M3{g}_m1")

            def c_m1(e, p_, g=g):
                return e.copy(zrow[:, 0:8], m1B[:, g * 8:(g + 1) * 8])
            s_cm1 = p.step("scalar", c_m1, deps=[s_m1, s_eq1_prev], tag=f"M3{g}_cm1")

            def f_eq1(e, p_, g=g, lgb=lgb):
                for t_ in range(8):
                    e.tensor_scalar(eq1[:, t_ * 64:(t_ + 1) * 64], lgb[:, t_ * 64:(t_ + 1) * 64],
                                    zrow[:, t_:t_ + 1], None, ALU.is_ge)
                e.scalar_tensor_tensor(lgb[:, :], eq1[:, :], -1e30, lgb[:, :],
                                       ALU.mult, ALU.add)
                e.tensor_reduce(m2B[:, g * 8:(g + 1) * 8],
                                lgb[:, :].rearrange("q (a n) -> q a n", n=64), AX.X, ALU.max)
                return padv(e)
            s_eq1 = p.step("vector", f_eq1, deps=[s_cm1], tag=f"M3{g}_eq1")
            s_eq1_prev = s_eq1

            def c_m2(e, p_, g=g):
                return e.copy(zrow[:, 8:16], m2B[:, g * 8:(g + 1) * 8])
            s_cm2 = p.step("scalar", c_m2, deps=[s_eq1, s_eq2_prev], tag=f"M3{g}_cm2")

            def f_eq2(e, p_, g=g, lgb=lgb):
                for t_ in range(8):
                    e.tensor_scalar(eq2[:, t_ * 64:(t_ + 1) * 64], lgb[:, t_ * 64:(t_ + 1) * 64],
                                    zrow[:, 8 + t_:9 + t_], None, ALU.is_ge)
                e.tensor_tensor(Ab[:, :], eq1[:, :], eq2[:, :], ALU.add)
                e.scalar_tensor_tensor(uw[:, :], eq1[:, :], -float(1 << 20), c_iotab[:, :],
                                       ALU.mult, ALU.add)
                e.tensor_reduce(e1B[:, g * 8:(g + 1) * 8],
                                uw[:, :].rearrange("q (a n) -> q a n", n=64), AX.X, ALU.min)
                e.scalar_tensor_tensor(uw[:, :], eq2[:, :], -float(1 << 20), c_iotab[:, :],
                                       ALU.mult, ALU.add)
                e.tensor_reduce(e2B[:, g * 8:(g + 1) * 8],
                                uw[:, :].rearrange("q (a n) -> q a n", n=64), AX.X, ALU.min)
                return padv(e)
            s_eq2 = p.step("vector", f_eq2, deps=[s_cm2, s_tA_hist.get(g - 1)],
                           tag=f"M3{g}_eq2")
            s_eq2_prev = s_eq2

            s_evl = None
            for t2 in range(0, 8, 2):
                ba, bb, pdep = pt_next()

                def trA(e, p_, t2=t2, ba=ba, bb=bb):
                    e.transpose(ba[0:64, 0:128], Ab[:, t2 * 64:(t2 + 1) * 64], c_identb[:, :])
                    return e.transpose(bb[0:64, 0:128], Ab[:, (t2 + 1) * 64:(t2 + 2) * 64],
                                       c_identb[:, :])
                s_tA = p.step("tensor", trA, deps=[s_eq2, pdep], tag=f"M3{g}_tA{t2}")
                s_tA_hist[g] = s_tA

                def trAe(e, p_, t2=t2, ba=ba, bb=bb):
                    e.tensor_copy(AT[:, t2 * 128:(t2 + 1) * 128], ba[0:64, 0:128])
                    return e.tensor_copy(AT[:, (t2 + 1) * 128:(t2 + 2) * 128], bb[0:64, 0:128])
                s_evl = p.step("vector", trAe, deps=[s_tA], tag=f"M3{g}_tAe{t2}")
                pt_done(s_evl)

            def scan(e, p_):
                e.tensor_tensor_scan(CT[:, :], AT[:, :], c_zer64[:, :], carry[:, 0:1],
                                     ALU.add, ALU.add)
                padv(e)
                e.tensor_copy(carry[:, 1:2], CT[:, 1023:1024])
                padv(e)
                e.tensor_tensor(CT[:, :], CT[:, :], AT[:, :], ALU.subtract)
                return padv(e)
            s_scan = p.step("vector", scan, deps=[s_evl, s_ccp_prev], tag=f"M3{g}_scan")

            def carrycp(e, p_):
                return e.copy(carry[:, 0:1], carry[:, 1:2])
            s_ccp_prev = p.step("scalar", carrycp, deps=[s_scan], tag=f"M3{g}_ccp")

            s_evl2 = s_evl2_prev
            for t2 in range(0, 8, 2):
                def trC(e, p_, t2=t2):
                    e.transpose(P2[0:128, 0:64], CT[0:64, t2 * 128:(t2 + 1) * 128],
                                c_identf[0:64, 0:64])
                    return e.transpose(P3[0:128, 0:64], CT[0:64, (t2 + 1) * 128:(t2 + 2) * 128],
                                       c_identf[0:64, 0:64])
                s_tC = p.step("tensor", trC, deps=[s_scan, s_evl2], tag=f"M3{g}_tC{t2}")

                def trCe(e, p_, t2=t2):
                    e.tensor_copy(Cx[:, t2 * 64:(t2 + 1) * 64], P2[0:128, 0:64])
                    return e.tensor_copy(Cx[:, (t2 + 1) * 64:(t2 + 2) * 64], P3[0:128, 0:64])
                s_evl2 = p.step("vector", trCe, deps=[s_tC], tag=f"M3{g}_tCe{t2}")

            def posf(e, p_, g=g):
                e.tensor_tensor(uw[:, :], Cx[:, :], eq1[:, :], ALU.mult)
                e.tensor_reduce(p1B[:, g * 8:(g + 1) * 8],
                                uw[:, :].rearrange("q (a n) -> q a n", n=64), AX.X, ALU.add)
                e.tensor_tensor(uw[:, :], Cx[:, :], eq2[:, :], ALU.mult)
                e.tensor_reduce(p2B[:, g * 8:(g + 1) * 8],
                                uw[:, :].rearrange("q (a n) -> q a n", n=64), AX.X, ALU.add)
                return padv(e)
            s_pos_prev = p.step("vector", posf, deps=[s_evl2], tag=f"M3{g}_pos")
            s_pos_hist[g] = s_pos_prev
            s_evl2_prev = s_evl2

        def fin1(e, p_):
            e.tensor_tensor(gA[:, 0:256], m1B[:, :], m2B[:, :], ALU.subtract)
            return padv(e)
        s_f1 = p.step("vector", fin1, deps=[s_pos_prev], tag="fin1")

        def fin2(e, p_):
            e.activation(wA[:, 0:256], gA[:, 0:256], AF.Sigmoid)
            return e.activation(wA[:, 256:512], gA[:, 0:256], AF.Sigmoid, scale=-1.0)
        s_f2 = p.step("scalar", fin2, deps=[s_f1], tag="fin2")

        def fin3(e, p_):
            e.tensor_scalar(gA[:, 0:256], e1B[:, :], 1024.0, None, ALU.mult)
            e.tensor_scalar(gA[:, 256:512], e2B[:, :], 1024.0, None, ALU.mult)
            padv(e)
            e.tensor_tensor(gA[:, 0:256], gA[:, 0:256], p1B[:, :], ALU.add)
            e.tensor_tensor(gA[:, 256:512], gA[:, 256:512], p2B[:, :], ALU.add)
            padv(e)
            e.tensor_scalar(gB[:, 0:256], p1B[:, :], 1023.5, None, ALU.is_gt)
            e.tensor_scalar(gB[:, 256:512], p2B[:, :], 1023.5, None, ALU.is_gt)
            padv(e)
            e.scalar_tensor_tensor(gB[:, :], gB[:, :], INVB, gA[:, :], ALU.mult, ALU.add)
            return padv(e)
        s_f3 = p.step("vector", fin3, deps=[s_f2], tag="fin3")

        def fin4(e, p_):
            e.tensor_copy(eq1[:, 0:256], e1B[:, :])
            e.tensor_copy(eq1[:, 256:512], e2B[:, :])
            e.tensor_copy(eq2[:, 0:256], p1B[:, :])
            e.tensor_copy(eq2[:, 256:512], p2B[:, :])
            padv(e)
            e.tensor_scalar(uw[:, :], eq1[:, :], c_core8[:, 0:1], None, ALU.subtract)
            padv(e)
            e.tensor_scalar(Cx[:, :], uw[:, :], -0.5, None, ALU.is_lt)
            e.tensor_scalar(lgb[:, :], uw[:, :], 7.5, None, ALU.is_gt)
            padv(e)
            e.tensor_tensor(Cx[:, :], Cx[:, :], lgb[:, :], ALU.add)
            e.tensor_scalar(lgb[:, :], eq2[:, :], 1023.5, None, ALU.is_gt)
            padv(e)
            e.tensor_tensor(Cx[:, :], Cx[:, :], lgb[:, :], ALU.add)
            padv(e)
            e.tensor_copy(pxi[:, :], eq2[:, :])
            e.tensor_copy(pxi2[:, :], eq2[:, :])
            padv(e)
            e.tensor_scalar(pxi[:, :], pxi[:, :], 127, None, ALU.bitwise_and)
            e.tensor_scalar(pxi2[:, :], pxi2[:, :], 7, None, ALU.logical_shift_right)
            padv(e)
            e.tensor_scalar(pxi[:, :], pxi[:, :], 6, None, ALU.logical_shift_left)
            padv(e)
            e.tensor_tensor(pxi[:, :], pxi[:, :], pxi2[:, :], ALU.add)
            e.tensor_copy(pxi2[:, :], uw[:, :])
            padv(e)
            e.tensor_scalar(pxi2[:, :], pxi2[:, :], 3, None, ALU.arith_shift_left)
            padv(e)
            e.tensor_tensor(pxi[:, :], pxi[:, :], pxi2[:, :], ALU.add)
            e.tensor_copy(pxi2[:, :], Cx[:, :])
            padv(e)
            e.tensor_scalar(pxi2[:, :], pxi2[:, :], 20, None, ALU.logical_shift_left)
            padv(e)
            e.tensor_tensor(pxi[:, :], pxi[:, :], pxi2[:, :], ALU.add)
            return padv(e)
        s_f4 = p.step("vector", fin4, deps=[s_f3], tag="fin4")

        def fin4b(e, p_):
            e.tensor_copy(ix1[:, :], gB[:, :])
            padv(e)
            e.tensor_scalar(pxi2[:, :], ix1[:, :], 12, None, ALU.logical_shift_right)
            padv(e)
            e.tensor_scalar(pxi2[:, :], pxi2[:, :], 1, None, ALU.bitwise_and)
            e.tensor_scalar(ix2[:, :], ix1[:, :], 13, None, ALU.logical_shift_right)
            padv(e)
            e.tensor_scalar(pxi2[:, :], pxi2[:, :], 12, None, ALU.logical_shift_left)
            e.tensor_scalar(ix2[:, :], ix2[:, :], 12, None, ALU.logical_shift_left)
            padv(e)
            e.tensor_tensor(ix1[:, :], ix1[:, :], ix2[:, :], ALU.subtract)
            padv(e)
            e.tensor_tensor(ix1[:, :], ix1[:, :], pxi2[:, :], ALU.subtract)
            e.tensor_scalar(pxi2[:, :], pxi2[:, :], 11, None, ALU.logical_shift_left)
            padv(e)
            e.tensor_tensor(ix2[:, :], ix1[:, :], pxi2[:, :], ALU.add)
            e.tensor_scalar(ix1[:, :], ix1[:, :], 1 << 23, None, ALU.add)
            padv(e)
            e.tensor_tensor(ix1[:, :], ix1[:, :], pxi2[:, :], ALU.subtract)
            return padv(e)
        s_f4b = p.step("vector", fin4b, deps=[s_f4], tag="fin4b")

        def fin5(e, p_):
            rp = rtpack[:, :].rearrange("q (t f) -> q t f", f=8)
            one = lambda v: v.rearrange("q (t o) -> q t o", o=1)
            e.tensor_copy(rp[:, :, 0:1], one(ix2[:, 0:256]))
            e.tensor_copy(rp[:, :, 1:2], one(ix1[:, 0:256]))
            e.tensor_copy(rp[:, :, 2:3], one(ix2[:, 256:512]))
            e.tensor_copy(rp[:, :, 3:4], one(ix1[:, 256:512]))
            e.tensor_copy(rp[:, :, 4:5], one(wA[:, 0:256]))
            e.tensor_copy(rp[:, :, 5:6], one(wA[:, 256:512]))
            e.tensor_copy(rp[:, :, 6:7], one(wA[:, 0:256]))
            e.tensor_copy(rp[:, :, 7:8], one(wA[:, 256:512]))
            return padv(e)
        s_f5 = p.step("vector", fin5, deps=[s_f4b], tag="fin5")

        def rtdma(e, p_):
            p_.dma(e, out=rt2.rearrange("(q t) f -> q (t f)", q=128),
                   in_=rtpack[:, :])
        s_rtd = p.step("gpsimd", rtdma, n_dma=1, deps=[s_f5], tag="rtdma")

        def scatter_all(e, p_):
            for t_ in range(256):
                for ch_ in range(2):
                    e.indirect_dma_start(
                        out=table,
                        out_offset=IndirectOffsetOnAxis(pxi[:, ch_ * 256 + t_: ch_ * 256 + t_ + 1], 0),
                        in_=c_tokv[:, t_:t_ + 1], in_offset=None,
                        bounds_check=regs["r8k"], oob_is_err=False,
                    ).then_inc(p_.dsem["gpsimd"], 16)
                    p_._dma_seen += 1
        s_sc = p.step("gpsimd", scatter_all, n_dma=512, deps=[s_f4, s_tblz, s_regs], tag="scat")

        def tblgather(e, p_):
            e.indirect_dma_start(
                out=tblf[:, :], out_offset=None,
                in_=table.rearrange("(x o) b -> x (o b)", o=64),
                in_offset=IndirectOffsetOnAxis(c_idx8q[:, :], 0),
            ).then_inc(p_.dsem["gpsimd"], 16)
            p_._dma_seen += 1
        s_tg = p.step("gpsimd", tblgather, n_dma=1, deps=[s_sc], tag="tblg")

        def tokidx(e, p_):
            e.tensor_scalar(tblf[:, :], tblf[:, :], -1.0, None, ALU.add)
            padv(e)
            e.tensor_scalar(uw[:, 0:64], tblf[:, :], -0.5, None, ALU.is_lt)
            padv(e)
            e.scalar_tensor_tensor(tblf[:, :], uw[:, 0:64], 1e9, tblf[:, :], ALU.mult, ALU.add)
            padv(e)
            e.tensor_copy(tbli[:, :], tblf[:, :])
            return padv(e)
        s_ti = p.step("vector", tokidx, deps=[s_tg], tag="toki")

        if upto == "M4":
            p.wait_only("vector", [s_ti], tag="tiw")
            _early_exit()
            return nc

        if dbg:
            def dbg2(e, p_):
                p_.dma(e, out=rdbg[:, 0:8], in_=m1B[:, 0:8])
                p_.dma(e, out=rdbg[:, 8:16], in_=m2B[:, 0:8])
                p_.dma(e, out=rdbg[:, 16:24], in_=e1B[:, 0:8])
                p_.dma(e, out=rdbg[:, 24:32], in_=e2B[:, 0:8])
                p_.dma(e, out=rdbg[:, 32:40], in_=p1B[:, 0:8])
                p_.dma(e, out=rdbg[:, 40:48], in_=p2B[:, 0:8])
                p_.dma(e, out=rdbg[:, 48:56], in_=wA[:, 0:8])
                p_.dma(e, out=rdbg[:, 56:64], in_=gB[:, 0:8])
                p_.dma(e, out=rdbg[:, 64:96], in_=tblf[:, 0:32])
            p.step("sync", dbg2, n_dma=9, deps=[s_ti, s_f5], tag="dbgd")

        s_dn_last = None
        s_gu_last = None
        s_trx_last = None
        s_ytr_last = None
        s_ybdma_last = None
        for el in range(NEXP):
            def wline(e, p_, el=el):
                p_.dma(e, out=w13b[:, :].rearrange("q (a n) -> q a n", a=NDT),
                       in_=w13s[el * D:(el + 1) * D, :].rearrange("(a q) n -> q a n", q=128))
                p_.dma(e, out=w2b[:, :].rearrange("q (a n) -> q a n", a=4),
                       in_=w2s[el * 512:(el + 1) * 512, :].rearrange("(a q) n -> q a n", q=128))
            s_wl = p.step("gpsimd", wline, n_dma=2, tag=f"M5{el}_w",
                          deps=[s_dn_last])

            s_txe_last = None
            for sl in range(8):
                def xgat(e, p_, el=el, sl=sl):
                    e.memset(xbn[:, :], 0.0)
                    e.indirect_dma_start(
                        out=xbn[:, :], out_offset=None,
                        in_=xn_full.ap(),
                        in_offset=IndirectOffsetOnAxis(tbli[:, el * 8 + sl: el * 8 + sl + 1], 0),
                        bounds_check=regs["r32k"], oob_is_err=False,
                    ).then_inc(p_.dsem["gpsimd"], 16)
                    p_._dma_seen += 1
                s_xg = p.step("gpsimd", xgat, n_dma=1, tag=f"M5{el}_xg{sl}",
                              deps=[s_ag_xn, s_ti, s_regs, s_trx_last])

                for dt2 in range(0, NDT, 2):
                    ba, bb, pdep = pt_next()

                    def trxb(e, p_, dt2=dt2, ba=ba, bb=bb):
                        e.transpose(ba[:, 0:128], xbn[:, dt2 * 128:(dt2 + 1) * 128],
                                    c_identb[:, :])
                        return e.transpose(bb[:, 0:128], xbn[:, (dt2 + 1) * 128:(dt2 + 2) * 128],
                                           c_identb[:, :])
                    s_txb = p.step("tensor", trxb, tag=f"M5{el}_tx{sl}_{dt2}",
                                   deps=[s_xg, pdep,
                                         s_gu_last if (sl == 0 and dt2 == 0) else None])
                    s_trx_last = s_txb

                    def trxbe(e, p_, sl=sl, dt2=dt2, ba=ba, bb=bb):
                        e.tensor_copy(xbT[:, dt2 * D + sl * 128: dt2 * D + sl * 128 + 128],
                                      ba[:, 0:128])
                        return e.tensor_copy(xbT[:, (dt2 + 1) * D + sl * 128: (dt2 + 1) * D + sl * 128 + 128],
                                             bb[:, 0:128])
                    s_txe_last = p.step("vector", trxbe, deps=[s_txb], tag=f"M5{el}_te{sl}_{dt2}")
                    pt_done(s_txe_last)

            s_hm_prev = None
            s_silu_prev = None
            for i2 in range(8):
                m, th = divmod(i2, 2)

                def gu2(e, p_, m=m, th=th):
                    for kt in range(NDT):
                        e.matmul(P2[:, :], w13b[:, kt * D + m * 128: kt * D + m * 128 + 128],
                                 xbT[:, kt * D + th * 512: kt * D + th * 512 + 512],
                                 start=(kt == 0), stop=(kt == NDT - 1))
                    for kt in range(NDT):
                        ins = e.matmul(P3[:, :], w13b[:, kt * D + 512 + m * 128: kt * D + 512 + m * 128 + 128],
                                       xbT[:, kt * D + th * 512: kt * D + th * 512 + 512],
                                       start=(kt == 0), stop=(kt == NDT - 1))
                    return ins
                s_g2 = p.step("tensor", gu2, tag=f"M5{el}_gu{i2}",
                              deps=[s_txe_last, s_wl, s_silu_prev, s_hm_prev])
                s_gu_last = s_g2

                def gsilu2(e, p_, i2=i2):
                    return e.activation(shp[i2 % 2][:, :], P2[:, :], AF.Silu)
                s_silu_prev = p.step("scalar", gsilu2, deps=[s_g2], tag=f"M5{el}_sl{i2}")

                def hmul(e, p_, m=m, th=th, i2=i2):
                    return e.tensor_tensor(hTb[:, m * D + th * 512: m * D + th * 512 + 512],
                                           shp[i2 % 2][:, :], P3[:, :], ALU.mult)
                s_hm_prev = p.step("vector", hmul, tag=f"M5{el}_hm{i2}",
                                   deps=[s_g2, s_silu_prev,
                                         s_dn_last if i2 == 0 else None])

            s_ybev_prev = {}
            s_ybev_last = None
            for i2 in range(16):
                m, th = divmod(i2, 2)

                def dn2(e, p_, m=m, th=th, i2=i2):
                    PS = P4 if i2 % 2 == 0 else P5
                    for kt in range(4):
                        ins = e.matmul(PS[:, :], w2b[:, kt * D + m * 128: kt * D + m * 128 + 128],
                                       hTb[:, kt * D + th * 512: kt * D + th * 512 + 512],
                                       start=(kt == 0), stop=(kt == 3))
                    return ins
                s_dn = p.step("tensor", dn2, tag=f"M5{el}_dn{i2}",
                              deps=[s_hm_prev, s_wl, s_ybev_prev.get(i2 - 2)])
                s_dn_last = s_dn

                def ybev(e, p_, m=m, th=th, i2=i2):
                    PS = P4 if i2 % 2 == 0 else P5
                    return e.tensor_copy(ybT[:, m * D + th * 512: m * D + th * 512 + 512], PS[:, :])
                s_ybev_prev[i2] = p.step("vector", ybev, tag=f"M5{el}_ye{i2}",
                                         deps=[s_dn, s_ytr_last if i2 == 0 else None])
                s_ybev_last = s_ybev_prev[i2]

            for st in range(8):
                s_evl3 = None
                for dt2 in range(0, NDT, 2):
                    ba, bb, pdep = pt_next()

                    def ytr(e, p_, st=st, dt2=dt2, ba=ba, bb=bb):
                        e.transpose(ba[:, 0:128], ybT[:, dt2 * D + st * 128: dt2 * D + st * 128 + 128],
                                    c_identb[:, :])
                        return e.transpose(bb[:, 0:128],
                                           ybT[:, (dt2 + 1) * D + st * 128: (dt2 + 1) * D + st * 128 + 128],
                                           c_identb[:, :])
                    s_yt = p.step("tensor", ytr, tag=f"M5{el}_yt{st}_{dt2}",
                                  deps=[s_ybev_last, pdep])
                    s_ytr_last = s_yt

                    def ytre(e, p_, dt2=dt2, ba=ba, bb=bb):
                        e.tensor_copy(ybn[:, dt2 * 128:(dt2 + 1) * 128], ba[:, 0:128])
                        return e.tensor_copy(ybn[:, (dt2 + 1) * 128:(dt2 + 2) * 128], bb[:, 0:128])
                    s_evl3 = p.step("vector", ytre, tag=f"M5{el}_yv{st}_{dt2}",
                                    deps=[s_yt, s_ybdma_last if dt2 == 0 else None])
                    pt_done(s_evl3)

                def ybdma(e, p_, el=el, st=st):
                    p_.dma(e, out=yb_b.ap()[el * CAPE + st * 128: el * CAPE + (st + 1) * 128, :],
                           in_=ybn[:, :])
                s_ybdma_last = p.step("gpsimd", ybdma, n_dma=1, deps=[s_evl3], tag=f"M5{el}_yd{st}")

            if el == NEXP // 2 - 1 and st == 7:
                def agi2a(e, p_):
                    e.collective_compute(
                        "AllGather", ALU.bypass,
                        replica_groups=[list(range(NC))],
                        ins=[yb_b.ap()[0:NEXP // 2 * CAPE, :]],
                        outs=[yb_fullA.ap()],
                    ).then_inc(cc3)
                    return e.memset(dum[:, :], 2.0)
                p.step("gpsimd", agi2a, deps=[s_ybdma_last], tag="agi2a")

        def agi2(e, p_):
            e.collective_compute(
                "AllGather", ALU.bypass,
                replica_groups=[list(range(NC))],
                ins=[yb_b.ap()[NEXP // 2 * CAPE:, :]],
                outs=[yb_fullB.ap()],
            ).then_inc(cc4)
            return e.memset(dum[:, :], 3.0)
        s_agi2 = p.step("gpsimd", agi2, deps=[s_ybdma_last], tag="agi2")
        s_ag_yb = (cc4, 1)
        s_ag_yb1 = (cc3, 1)
        if dbg:
            def ydbg(e, p_):
                p_.dma(e, out=ydbgA, in_=yb_fullA.ap())
                p_.dma(e, out=ydbgB, in_=yb_fullB.ap())
            p.step("gpsimd", ydbg, n_dma=2, deps=[s_ag_yb1, s_ag_yb], tag="ydbg")

        if upto == "M6":
            def agwait2(e, p_):
                e.wait_ge(cc3, 1)
                return e.memset(dum[:, :], 4.0)
            p.step("gpsimd", agwait2, tag="agw2")
            _early_exit()
            return nc

        s_comb_prev = None
        s_add_hist = {}
        s_out_hist = {}
        s_modma = {}
        for c in range(NCHUNK):
            def rtg(e, p_, cc=c):
                for t_ in range(4):
                    e.indirect_dma_start(
                        out=rtc[:, t_ * 8:(t_ + 1) * 8], out_offset=None,
                        in_=rt2,
                        in_offset=IndirectOffsetOnAxis(c_idxrt[:, cc * 4 + t_: cc * 4 + t_ + 1], 0),
                    ).then_inc(p_.dsem["gpsimd"], 16)
                    p_._dma_seen += 1
            s_rtg = p.step("gpsimd", rtg, n_dma=4, tag=f"M7{c}_rtg",
                           deps=[s_rtd, s_comb_prev])

            def rticv(e, p_):
                e.tensor_copy(rti[:, :].rearrange("q (t f) -> q t f", f=4),
                              rtc[:, :].rearrange("q (t f) -> q t f", f=8)[:, :, 0:4])
                return padv(e)
            s_rti = p.step("vector", rticv, deps=[s_rtg], tag=f"M7{c}_rti")

            for t_ in range(4):
                def ygat(e, p_, t_=t_):
                    e.memset(r1b[:, :], 0.0)
                    e.memset(r2b[:, :], 0.0)
                    for dst, f0 in ((r1b, 0), (r2b, 2)):
                        for buf, f in ((yb_fullA, f0), (yb_fullB, f0 + 1)):
                            e.indirect_dma_start(
                                out=dst[:, :], out_offset=None,
                                in_=buf.ap(),
                                in_offset=IndirectOffsetOnAxis(rti[:, t_ * 4 + f: t_ * 4 + f + 1], 0),
                                bounds_check=regs["r32k"], oob_is_err=False,
                            ).then_inc(p_.dsem["gpsimd"], 16)
                            p_._dma_seen += 1
                s_yg = p.step("gpsimd", ygat, n_dma=4, tag=f"M7{c}_yg{t_}",
                              deps=[s_ag_yb1, s_ag_yb, s_rti, s_comb_prev])

                def comb(e, p_, t_=t_):
                    e.tensor_scalar(mot4[:, t_ * D:(t_ + 1) * D], r1b[:, :],
                                    rtc[:, t_ * 8 + 4: t_ * 8 + 5], None, ALU.mult)
                    e.scalar_tensor_tensor(mot4[:, t_ * D:(t_ + 1) * D], r2b[:, :],
                                           rtc[:, t_ * 8 + 5: t_ * 8 + 6],
                                           mot4[:, t_ * D:(t_ + 1) * D], ALU.mult, ALU.add)
                    return padv(e)
                s_comb_prev = p.step("vector", comb, tag=f"M7{c}_cb{t_}",
                                     deps=[s_yg, s_modma.get((c - 1, t_))])

                def modma(e, p_, cc=c, t_=t_):
                    p_.dma(e, out=mo_h[cc * CH + t_ * 128: cc * CH + (t_ + 1) * 128, :],
                           in_=mot4[:, t_ * D:(t_ + 1) * D])
                s_modma[(c, t_)] = p.step("gpsimd", modma, n_dma=1, deps=[s_comb_prev],
                                          tag=f"M7{c}_md{t_}")

            if upto == "M7":
                continue
            for dt in range(NDT):
                def ldx(e, p_, cc=c, dt=dt):
                    hh = (dt % 2) * CH
                    p_.dma(e, out=x1c[:, hh:hh + CH], in_=xT[dt * 128:(dt + 1) * 128,
                                                             cc * CH:(cc + 1) * CH])
                    p_.dma(e, out=hc[:, hh:hh + CH], in_=h_h[cc * D + dt * 128: cc * D + (dt + 1) * 128, :])
                    p_.dma(e, out=shc[:, hh:hh + CH], in_=sh_h[cc * D + dt * 128: cc * D + (dt + 1) * 128, :])
                s_ldx = p.step("sync", ldx, n_dma=3, tag=f"M8{c}_ld{dt}",
                               deps=[s_add_hist.get((c, dt - 2)), s_hdma[(c, dt)], s_shdma[(c, dt)]])

                def mld(e, p_, cc=c, dt=dt):
                    p_.dma_t(e, out=mdt[:, (dt % 2) * CH:(dt % 2) * CH + CH],
                             in_=mo_h[cc * CH:(cc + 1) * CH, dt * 128:(dt + 1) * 128])
                s_mld = p.step("sync", mld, n_dma=1, tag=f"M8{c}_mld{dt}",
                               deps=[s_modma[(c, 3)], s_add_hist.get((c, dt - 2))])

                def addf(e, p_, dt=dt):
                    hh = (dt % 2) * CH
                    outb = tmpf if dt % 2 == 0 else lgb
                    e.tensor_tensor(xnp[0][:, :], x1c[:, hh:hh + CH], hc[:, hh:hh + CH], ALU.add)
                    e.tensor_tensor(xnp[1][:, :], shc[:, hh:hh + CH],
                                    mdt[:, hh:hh + CH], ALU.add)
                    return e.tensor_tensor(outb[:, 0:CH], xnp[0][:, :], xnp[1][:, :], ALU.add)
                s_add_hist[(c, dt)] = p.step("vector", addf, tag=f"M8{c}_add{dt}",
                                             deps=[s_ldx, s_mld, s_out_hist.get((c, dt - 2)),
                                                   s_out_hist.get((c - 1, dt + 6))])

                def outd(e, p_, cc=c, dt=dt):
                    outb = tmpf if dt % 2 == 0 else lgb
                    p_.dma(e, out=x2T[dt * 128:(dt + 1) * 128, cc * CH:(cc + 1) * CH],
                           in_=outb[:, 0:CH])
                s_out_hist[(c, dt)] = p.step("sync", outd, n_dma=1,
                                             deps=[s_add_hist[(c, dt)]], tag=f"M8{c}_o{dt}")

        if upto == "M7":
            p.wait_only("vector", [s_comb_prev], tag="cbw")
            _early_exit()
            return nc

        p.wait_only("sync", [s_out_hist[(NCHUNK - 1, NDT - 1)],
                             (p.dsem["gpsimd"], p.dma_val["gpsimd"])], tag="fin")
        if fctx is not None:
            p.fence()
            return nc
        with nc.Block() as block:
            p.emit(block)
    return nc


def host_l0_moe_inputs(x, rms_w, gate_w, w13, w2, sw_gate, sw_up, sw_down):
    import ml_dtypes
    ident_bf = np.eye(128, dtype=np.float32).astype(ml_dtypes.bfloat16)
    ident_f = np.eye(128, dtype=np.float32)
    rmsw_t = np.ascontiguousarray(np.asarray(rms_w, np.float32).reshape(NDT, 128).T)
    gwt = np.ascontiguousarray(np.asarray(gate_w, np.float32).T)
    iota = (np.arange(64, dtype=np.float32) + float(1 << 20))
    iotabig = np.ascontiguousarray(np.tile(iota, (128, 8)).astype(np.float32))
    tokv = (np.arange(256, dtype=np.float32)[None, :] * 128.0
            + np.arange(128, dtype=np.float32)[:, None] + 1.0).astype(np.float32)
    pp = np.arange(128, dtype=np.int64)
    maps = []
    for c in range(NC):
        idx8q = pp.astype(np.int32).reshape(128, 1)
        core8 = np.full((128, 1), 8.0 * c, np.float32)
        idxrt = (pp[:, None] * 256
                 + (c * 32 + np.arange(32, dtype=np.int64))[None, :]).astype(np.int32)
        maps.append({
            "xT": np.ascontiguousarray(np.asarray(x[c], np.float32).T),
            "rmsw_t": rmsw_t, "gwt": gwt,
            "w13s": np.ascontiguousarray(np.asarray(w13, np.float32)[8 * c:8 * c + 8]
                                         .reshape(NEXP * D, D)),
            "w2s": np.ascontiguousarray(np.asarray(w2, np.float32)[8 * c:8 * c + 8]
                                        .reshape(NEXP * 512, D)),
            "swg": np.asarray(sw_gate, np.float32), "swu": np.asarray(sw_up, np.float32),
            "swd": np.asarray(sw_down, np.float32),
            "identb": ident_bf, "identf": ident_f,
            "iotabig": iotabig, "tokv": tokv,
            "idx8q": idx8q, "idxrt": idxrt, "core8": core8,
        })
    return maps
